# version 11
# baseline (speedup 1.0000x reference)
import numpy as np
import ml_dtypes
from contextlib import ExitStack

import concourse.bass as bass
import concourse.mybir as mybir
from concourse.bass_utils import run_bass_kernel_spmd

F32 = mybir.dt.float32
BF16 = mybir.dt.bfloat16
I32 = mybir.dt.int32
AF = mybir.ActivationFunctionType
OP = mybir.AluOpType
BF = ml_dtypes.bfloat16

RSQ2 = 0.70710678118654752

SEM_ALIAS = {}
for _n in ("amm", "bmm", "cmm", "dmm", "emm", "fmm", "gmm", "hmm", "jmm"):
    SEM_ALIAS[_n] = "Tmm"
for _t in ("a", "e"):
    SEM_ALIAS[_t + "lx"] = "Glx"
    SEM_ALIAS[_t + "lb"] = "Glb"
    SEM_ALIAS[_t + "lw0"] = "Glw0"
    SEM_ALIAS[_t + "lw1"] = "Glw1"
    for _i in range(4):
        SEM_ALIAS[_t + "st" + str(_i)] = "So" + str(_i)
    for _i in range(2):
        SEM_ALIAS[_t + "stv" + str(_i)] = "Sv" + str(_i)
SEM_ALIAS.update({"bkld": "Gkl", "hkld": "Gkl", "bvld": "Gvl", "hvld": "Gvl",
                  "dlw": "Gw", "dlw2": "Gw", "jlw": "Gw", "dli": "Gli",
                  "jlx": "Gli", "hgld": "Sg", "gst": "Sgst"})
for _i in range(4):
    SEM_ALIAS["jld" + str(_i)] = "Gd" + str(_i)
    SEM_ALIAS["jlh" + str(_i)] = "Sjh" + str(_i)
for _i in range(2):
    SEM_ALIAS["bq" + str(_i)] = "SQ" + str(_i)
    SEM_ALIAS["hq" + str(_i)] = "SQ" + str(_i)
    SEM_ALIAS["br" + str(_i)] = "Sw" + str(_i)
    SEM_ALIAS["hr" + str(_i)] = "Sw" + str(_i)
    SEM_ALIAS["jlg" + str(_i)] = "Gg" + str(_i)
    SEM_ALIAS["jlu" + str(_i)] = "Gu" + str(_i)
    SEM_ALIAS["jld" + str(_i)] = "Gd" + str(_i)
    SEM_ALIAS["jlh" + str(_i)] = "Gh" + str(_i)
    SEM_ALIAS["gld" + str(_i)] = "Gw2" + str(_i)
    SEM_ALIAS["pli" + str(_i)] = "Gp" + str(_i)
    SEM_ALIAS["pgi" + str(_i)] = "Gi" + str(_i)
    SEM_ALIAS["pst" + str(_i)] = "Sp" + str(_i)
    SEM_ALIAS["jsh" + str(_i)] = "Sh" + str(_i)
    SEM_ALIAS["jso" + str(_i)] = "Sy" + str(_i)
    SEM_ALIAS["dso" + str(_i)] = "Sy" + str(_i)
    SEM_ALIAS["cso" + str(_i)] = "Sy" + str(_i)
    SEM_ALIAS["fso" + str(_i)] = "Sy" + str(_i)
    SEM_ALIAS["jlr" + str(_i)] = "Sr" + str(_i)
    SEM_ALIAS["dlr" + str(_i)] = "Sr" + str(_i)
    SEM_ALIAS["clr" + str(_i)] = "Sr" + str(_i)
    SEM_ALIAS["flr" + str(_i)] = "Sr" + str(_i)
    SEM_ALIAS["clx" + str(_i)] = "Sx" + str(_i)
    SEM_ALIAS["flx" + str(_i)] = "Sx" + str(_i)
for _i in range(4):
    SEM_ALIAS["clq" + str(_i)] = "Sq" + str(_i)
    SEM_ALIAS["flq" + str(_i)] = "Sq" + str(_i)
SEM_ALIAS.update({"cld": "Scl", "fld": "Scl", "clw": "Scw", "flw": "Scw",
                  "cst": "Sct", "fst": "Sct", "dlx": "Gdx", "dsx": "Sdx2"})
DBG_PHASES = ("q", "k", "v")
DBG_MLIM = 8
DBG_TBLIM = 32
DBG_TAULIM = 8
DBG_S3TAU = 8
DBG_S5HP = 8
DBG_S5GLOB = 1
ENGS = ("tensor", "scalar", "vector", "gpsimd", "sync")

LAUNCH_PLAN = [["s0", "s1", "s2", "s3", "sp", "s4", "s5", "s6", "s7"]]

ACT_SHAPES = {
    "x0T": ([1024, 4096], F32),
    "q0": ([1024, 4096], BF16),
    "k0": ([1024, 4096], BF16),
    "v0": ([32, 128, 1040], BF16),
    "raw0": ([16, 8, 65, 512], F32),
    "x1": ([1024, 4096], F32),
    "x2": ([1024, 4096], F32),
    "x2p": ([1024, 4096], F32),
    "q1": ([1024, 2048], BF16),
    "k1": ([1024, 4096], BF16),
    "v1": ([32, 128, 1040], BF16),
    "vg1": ([64, 1040], BF16),
    "vsum1": ([65, 16], F32),
    "raw1": ([16, 4, 65, 512], F32),
    "x3": ([1024, 2048], F32),
    "hb1": ([64, 4, 128, 512], BF16),
    "xout": ([1024, 2048], F32),
    "permidx": ([2048, 1], I32),
    "c_rd": ([128, 512], F32),
    "intg": ([1024, 4096], BF16),
    "f_rd": ([64, 512], F32),
}

WEIGHT_SHAPES = {
    "w_attn": [1024, 3072], "w_proj0": [1024, 1024],
    "wq1": [1024, 1024], "wk1": [1024, 1024], "wv1": [1024, 1024], "wo1": [1024, 1024],
    "r_w1": [1024, 256], "r_w2": [256, 8], "h_w1": [1024, 256], "h_w2": [256, 1],
    "e_w1": [2048, 512], "e_w2": [512, 3072], "e_b2": [8, 3072],
    "s_w1": [2048, 64], "s_w2": [64, 3072],
    "f_w1": [1024, 2048], "f_w2": [2048, 1024],
    "rt1_w": [1024, 16], "up1": [1024, 8192], "gate1": [1024, 8192],
    "down1": [8192, 1024], "shg1": [1024, 1024], "shu1": [1024, 1024],
    "shd1": [1024, 1024],
    "bv_row": [1, 1024], "bv1_row": [1, 1024],
}

STAGES = {}



BIAS_COLS = {}
N_BIAS_COLS = 0
CONST_COLS = {}
N_CONSTB = 0


def _build_bias_pack(inp):
    global BIAS_COLS, N_BIAS_COLS
    BIAS_COLS = {}
    cols = []

    def addvec(name, v):
        v = np.asarray(v, np.float32).reshape(-1)
        ncol = (v.size + 127) // 128
        a = np.zeros((ncol, 128), np.float32)
        a.reshape(-1)[:v.size] = v
        BIAS_COLS[name] = len(cols)
        cols.extend(list(a))

    addvec("r_b1", inp["L0_ffn_router_b1"])
    addvec("h_b1", inp["L0_ffn_halt_b1"])
    addvec("e_b1", inp["L0_ffn_expert_b1"].reshape(-1))
    addvec("s_b1", inp["L0_ffn_shared_b1"])
    addvec("f_b1", inp["L0_ffn_refine_b1"])
    BIAS_COLS["_gelu_end"] = len(cols)
    addvec("b_attn", inp["L0_attn_b_attn"])
    addvec("b_proj0", inp["L0_attn_b_proj"])
    addvec("r_b2", inp["L0_ffn_router_b2"])
    addvec("h_b2", inp["L0_ffn_halt_b2"])
    addvec("s_b2", inp["L0_ffn_shared_b2"])
    addvec("f_b2", inp["L0_ffn_refine_b2"])
    addvec("mu", inp["L0_ffn_mu"])
    addvec("iw", inp["L0_ffn_integration_weight"])
    addvec("sw", inp["L0_ffn_shared_weight"])
    addvec("bq1", inp["L1_attn_bq"])
    addvec("bk1", inp["L1_attn_bk"])
    addvec("bv1", inp["L1_attn_bv"])
    addvec("bo1", inp["L1_attn_bo"])
    addvec("rt1_b", inp["L1_ffn_router_bias"])
    N_BIAS_COLS = len(cols)
    return np.ascontiguousarray(np.stack(cols, axis=1), np.float32)


def _build_constb():
    global CONST_COLS, N_CONSTB
    CONST_COLS = {}
    parts = []
    off = 0

    def add(name, arr):
        nonlocal off
        a = np.zeros((128, arr.shape[1]), np.float32)
        a[:arr.shape[0], :] = arr
        CONST_COLS[name] = (off, arr.shape[1])
        parts.append(a)
        off += arr.shape[1]

    p = np.arange(128).reshape(128, 1)
    col = np.arange(896).reshape(1, 896)
    add("maskCM", (p <= col - 384).astype(np.float32))
    add("eye128", np.eye(128, dtype=np.float32))
    j = np.arange(512)
    add("B8", ((j // 64)[None, :] == np.arange(8)[:, None]).astype(np.float32))
    j = np.arange(8192)
    add("B16", ((j // 512)[None, :] == np.arange(16)[:, None]).astype(np.float32))
    add("ones_row", np.ones((1, 128), np.float32))
    add("ones_col", np.ones((128, 1), np.float32))
    m = np.zeros((64, 32), np.float32)
    for g in range(64):
        jj = 2 * g if g < 32 else 2 * (g - 32) + 1
        if jj < 32:
            for i in range(32):
                if i // 8 == jj // 8:
                    m[g, i] = 1.0
    add("corrmask", m)
    N_CONSTB = off
    return np.ascontiguousarray(np.concatenate(parts, axis=1).astype(BF))


def _build_constf():
    global CONSTF_COLS, N_CONSTF
    CONSTF_COLS = {}
    parts = []
    off = 0

    def add(name, arr):
        nonlocal off
        a = np.zeros((128, arr.shape[1]), np.float32)
        a[:arr.shape[0], :] = arr
        CONSTF_COLS[name] = (off, arr.shape[1])
        parts.append(a)
        off += arr.shape[1]

    add("eye128f", np.eye(128, dtype=np.float32))
    add("ones_rowf", np.ones((1, 128), np.float32))
    add("scale_rowf", np.full((1, 128), 1.0 / 1024.0, np.float32))
    N_CONSTF = off
    return np.ascontiguousarray(np.concatenate(parts, axis=1), np.float32)


def host_prep(inp):
    sh = {}
    sh["w_attn"] = np.ascontiguousarray(inp["L0_attn_w_attn"], np.float32)
    sh["w_proj0"] = np.ascontiguousarray(inp["L0_attn_w_proj"], np.float32)
    sh["wq1"] = np.ascontiguousarray(inp["L1_attn_wq"].T, np.float32)
    sh["wk1"] = np.ascontiguousarray(inp["L1_attn_wk"].T, np.float32)
    sh["wv1"] = np.ascontiguousarray(inp["L1_attn_wv"].T, np.float32)
    sh["wo1"] = np.ascontiguousarray(inp["L1_attn_wo"].T, np.float32)
    sh["r_w1"] = np.ascontiguousarray(inp["L0_ffn_router_w1"], np.float32)
    sh["r_w2"] = np.ascontiguousarray(inp["L0_ffn_router_w2"], np.float32)
    sh["h_w1"] = np.ascontiguousarray(inp["L0_ffn_halt_w1"], np.float32)
    sh["h_w2"] = np.ascontiguousarray(inp["L0_ffn_halt_w2"], np.float32)
    sh["e_w1"] = np.ascontiguousarray(
        inp["L0_ffn_expert_w1"].transpose(1, 0, 2).reshape(2048, 512), np.float32)
    sh["e_w2"] = np.ascontiguousarray(
        inp["L0_ffn_expert_w2"].reshape(512, 3072), np.float32)
    sh["e_b2"] = np.ascontiguousarray(inp["L0_ffn_expert_b2"], np.float32)
    sh["s_w1"] = np.ascontiguousarray(inp["L0_ffn_shared_w1"], np.float32)
    sh["s_w2"] = np.ascontiguousarray(inp["L0_ffn_shared_w2"], np.float32)
    sh["f_w1"] = np.ascontiguousarray(inp["L0_ffn_refine_w1"], np.float32)
    sh["f_w2"] = np.ascontiguousarray(inp["L0_ffn_refine_w2"], np.float32)
    sh["rt1_w"] = np.ascontiguousarray(inp["L1_ffn_router_w"].T, np.float32)
    sh["up1"] = np.ascontiguousarray(
        inp["L1_ffn_up_proj"].transpose(1, 0, 2).reshape(1024, 8192), np.float32)
    sh["gate1"] = np.ascontiguousarray(
        inp["L1_ffn_gate_proj"].transpose(1, 0, 2).reshape(1024, 8192), np.float32)
    sh["down1"] = np.ascontiguousarray(
        inp["L1_ffn_down_proj"].reshape(8192, 1024), np.float32)
    sh["shg1"] = np.ascontiguousarray(inp["L1_ffn_sh_gate"].T, np.float32)
    sh["shu1"] = np.ascontiguousarray(inp["L1_ffn_sh_up"].T, np.float32)
    sh["shd1"] = np.ascontiguousarray(inp["L1_ffn_sh_down"].T, np.float32)
    sh["bv_row"] = np.ascontiguousarray(
        inp["L0_attn_b_attn"][2048:3072].reshape(1, 1024), np.float32)
    sh["bv1_row"] = np.ascontiguousarray(
        inp["L1_attn_bv"].reshape(1, 1024), np.float32)
    sh["bias_pack"] = _build_bias_pack(inp)
    sh["constb"] = _build_constb()
    sh["constf"] = _build_constf()

    per = []
    for c in range(8):
        b, h = c // 2, c % 2
        d = {}
        d["x0T"] = np.ascontiguousarray(inp["L0_attn_hidden_states"][b].T, np.float32)
        d["permidx"] = (np.arange(2048, dtype=np.int32) ^ h).reshape(2048, 1)
        per.append(d)
    return sh, per



class Prog:
    def __init__(self):
        self.nc = bass.Bass("TRN2", target_bir_lowering=False)
        self.es = ExitStack()
        self.inputs = set()
        self.outputs = set()
        self.dram = {}
        self.sems = {}
        self.streams = {e: [] for e in ENGS}
        self.waited = {e: {} for e in ENGS}
        self.guards = {}
        self.AR = self.es.enter_context(self.nc.sbuf_tensor("AR", [128, 101376], BF16))
        self.ar_free = 0
        self.ar_base_stack = []
        self.PSH = [self.es.enter_context(
            self.nc.psum_tensor(f"PSH{i}", [128, 1024], F32)) for i in range(4)]
        self.PS = [self.PSH[i // 2][:, (i % 2) * 512:(i % 2) * 512 + 512]
                   for i in range(8)]

    def din(self, name):
        if name not in self.dram:
            if name == "bias_pack":
                shape, dt = [128, N_BIAS_COLS], F32
            elif name == "constb":
                shape, dt = [128, N_CONSTB], BF16
            elif name == "constf":
                shape, dt = [128, N_CONSTF], F32
            elif name in ACT_SHAPES:
                shape, dt = ACT_SHAPES[name]
            else:
                shape, dt = WEIGHT_SHAPES[name], F32
            self.dram[name] = self.nc.dram_tensor(
                name, list(shape), dt, kind="ExternalInput").ap()
            self.inputs.add(name)
        return self.dram[name]

    def dout(self, name):
        if name not in self.dram:
            shape, dt = ACT_SHAPES[name]
            self.dram[name] = self.nc.dram_tensor(
                name, list(shape), dt, kind="ExternalOutput").ap()
            self.outputs.add(name)
        return self.dram[name]

    def dsrc(self, name):
        if name in self.dram:
            return self.dram[name]
        return self.din(name)

    def fence(self):
        for sname, rec in list(self.sems.items()):
            if rec[1] > 0:
                for eng in ENGS:
                    self.wait(eng, sname, rec[1])

    def dtmp(self, name):
        if name not in self.dram:
            shape, dt = ACT_SHAPES[name]
            self.dram[name] = self.nc.dram_tensor(
                name, list(shape), dt, kind="Internal").ap()
        return self.dram[name]

    def alloc_b(self, n):
        off = self.ar_free
        self.ar_free += (n + 1) & ~1
        assert self.ar_free <= 101376, f"SBUF arena overflow: {self.ar_free}"
        return self.AR[:, off:off + n]

    def alloc_f(self, n):
        return self.alloc_b(2 * n).bitcast(F32)

    def push(self):
        self.ar_base_stack.append(self.ar_free)

    def pop(self):
        self.ar_free = self.ar_base_stack.pop()

    def sem(self, name):
        name = SEM_ALIAS.get(name, name)
        if name not in self.sems:
            h = self.es.enter_context(self.nc.semaphore(name))
            self.sems[name] = [h, 0]
        return self.sems[name][0]

    def tick(self, name, amt, agent=None):
        name = SEM_ALIAS.get(name, name)
        self.sem(name)
        s = self.sems[name]
        if agent is not None:
            if len(s) < 3:
                s.append(agent)
            else:
                assert s[2] == agent, (
                    f"semaphore {name} incremented by {agent} and {s[2]}")
        s[1] += amt
        return s[1]

    def val(self, name):
        name = SEM_ALIAS.get(name, name)
        return self.sems.get(name, [None, 0])[1]

    def emit(self, eng, fn):
        self.streams[eng].append((getattr(self, "cur_scope", None), fn))

    def wait(self, eng, name, v):
        name = SEM_ALIAS.get(name, name)
        if v is None or v <= 0:
            return
        w = self.waited[eng]
        if w.get(name, 0) >= v:
            return
        w[name] = v
        h = self.sem(name)
        self.emit(eng, lambda e, h=h, v=v: e.wait_ge(h, v))

    def wguard(self, eng, key):
        g = self.guards.get(key)
        if g is not None:
            self.wait(eng, g[0], g[1])

    def setguard(self, key, name, v):
        self.guards[key] = (name, v)

    def idma(self, out, in_, idx_ap, sname):
        h = self.sem(sname)
        self.emit("gpsimd", lambda e, o=out, i=in_, x=idx_ap, h=h:
                  e.indirect_dma_start(
                      out=o, out_offset=None, in_=i,
                      in_offset=bass.IndirectOffsetOnAxis(x, 0)).then_inc(h, 16))
        return self.tick(sname, 16, agent="dma-gpsimd")

    def dma(self, eng, out, in_, sname):
        h = self.sem(sname)
        self.emit(eng, lambda e, o=out, i=in_, h=h: e.dma_start(out=o, in_=i).then_inc(h, 16))
        return self.tick(sname, 16, agent="dma-" + eng)

    def mm(self, bank, lhsT, rhs, start, stop, sname=None):
        if sname is None:
            self.emit("tensor", lambda t, b=bank, l=lhsT, r=rhs, s1=start, s2=stop:
                      t.matmul(b, l, r, start=s1, stop=s2))
            return None
        h = self.sem(sname)
        self.emit("tensor", lambda t, b=bank, l=lhsT, r=rhs, s1=start, s2=stop, h=h:
                  t.matmul(b, l, r, start=s1, stop=s2).then_inc(h, 1))
        return self.tick(sname, 1, agent="tensor")

    def act(self, eng, out, in_, func, bias=0.0, scale=1.0, sname=None, accum=None):
        ch = "ch" + eng
        self.wait(eng, ch, self.val(ch))
        def f(e, o=out, i=in_, fu=func, b=bias, s=scale, a=accum):
            kw = {}
            if a is not None:
                kw["accum_out"] = a
            return e.activation(out=o, in_=i, func=fu, bias=b, scale=s, **kw)
        h = self.sem(ch)
        self.emit(eng, lambda e, h=h: f(e).then_inc(h, 1))
        return self.tick(ch, 1, agent=eng)

    def op(self, eng, meth, *args, sname=None, **kw):
        ch = "ch" + eng
        if eng != "tensor":
            self.wait(eng, ch, self.val(ch))
            assert sname in (None, ch), (eng, sname)
            sname = ch
        use = sname if sname is not None else ch
        h = self.sem(use)
        self.emit(eng, lambda e, a=args, k=kw, m=meth, h=h:
                  getattr(e, m)(*a, **k).then_inc(h, 1))
        return self.tick(use, 1, agent="tensor" if eng == "tensor" else eng)

    def finalize(self):
        def run(e, fns):
            cur = None
            cur_id = None
            for sc, f in fns:
                if sc != cur:
                    if cur is not None:
                        self.nc.leave_named_scope(cur, cur_id, False)
                    if sc is not None:
                        cur_id, _ = self.nc.enter_named_scope(sc, False)
                    cur = sc
                f(e)
            if cur is not None:
                self.nc.leave_named_scope(cur, cur_id, False)
        with self.nc.Block() as block:
            for eng in ENGS:
                fns = self.streams[eng]
                if not fns:
                    continue
                getattr(block, eng)(lambda e, fns=fns: run(e, fns))
        return self.nc



def emit_consts(P):
    cb = P.din("constb")
    cf = P.din("constf")
    bp = P.din("bias_pack")
    P.CB = P.alloc_b(N_CONSTB)
    P.CF = P.alloc_f(N_CONSTF)
    P.BP = P.alloc_f(N_BIAS_COLS)
    ng = BIAS_COLS["_gelu_end"]
    P.BPg_sc = P.alloc_f(ng)
    P.BPg_h = P.alloc_f(ng)
    P.BPn = P.alloc_f(24)
    P.dma("sync", P.CB[:, :], cb[:, :], "cld")
    P.dma("sync", P.CF[:, :], cf[:, :], "cld")
    v = P.dma("sync", P.BP[:, :], bp[:, :], "cld")
    P.wait("vector", "cld", v)
    P.op("vector", "tensor_scalar_mul", P.BPg_sc[:, :], P.BP[:, 0:ng], RSQ2)
    P.op("vector", "tensor_scalar_mul", P.BPg_h[:, :], P.BP[:, 0:ng], 0.5)
    sb2c0 = BIAS_COLS["s_b2"]
    P.op("vector", "tensor_scalar_mul", P.BPn[:, :],
         P.BP[:, sb2c0:sb2c0 + 24], -1.0)
    for eng in ENGS:
        P.wait(eng, "cld", P.val("cld"))
        P.wait(eng, "chvector", P.val("chvector"))


def CB(P, name, r0=0, r1=128, c0=0, c1=None):
    off, n = CONST_COLS[name]
    c1 = n if c1 is None else c1
    return P.CB[r0:r1, off + c0:off + c1]


def CFc(P, name, r0=0, r1=128, c0=0, c1=None):
    off, n = CONSTF_COLS[name]
    c1 = n if c1 is None else c1
    return P.CF[r0:r1, off + c0:off + c1]


def BPc(P, name, j=0, r0=0, r1=128):
    return P.BP[r0:r1, BIAS_COLS[name] + j: BIAS_COLS[name] + j + 1]



def emit_qkv(P, tag, xname, wq, wk, wv, bq, bk, bvrow_name, qname, kname, vname,
             nq_tiles):
    P.push()
    xb = P.alloc_b(8 * 4096)
    wbuf = [P.alloc_b(8 * 1024), P.alloc_b(8 * 1024)]
    ev = P.alloc_b(4 * 512)
    vsb = [P.alloc_b(1040), P.alloc_b(1040)]
    bvb = P.alloc_b(1024)
    sMM, sEV, sVV, sST = tag + "mm", "chscalar", "chvector", tag + "st"
    sLX, sLB = tag + "lx", tag + "lb"
    sLW = [tag + "lw0", tag + "lw1"]

    xsrc = P.dsrc(xname)
    bvrow = P.din(bvrow_name)
    qd, kd, vd = P.dout_or_tmp(qname), P.dout_or_tmp(kname), P.dout_or_tmp(vname)

    for kc in range(8):
        P.dma("gpsimd", xb[:, kc * 4096:(kc + 1) * 4096],
              xsrc[128 * kc:128 * (kc + 1), :], sLX)
    P.dma("gpsimd", bvb[0:1, :], bvrow[0:1, :], sLB)

    def load_w(phase, wspec):
        dname, col0 = wspec
        wd = P.din(dname)
        buf = wbuf[phase % 2]
        P.wguard("gpsimd", ("wbuf", tag, phase % 2))
        sem = sLW[phase % 2]
        for kc in range(8):
            P.dma("gpsimd", buf[:, kc * 1024:(kc + 1) * 1024],
                  wd[128 * kc:128 * (kc + 1), col0:col0 + 1024], sem)
        return buf, sem, P.val(sem)

    _phases = [("q", wq, nq_tiles, qd, bq), ("k", wk, 8, kd, bk)]
    _phases = [p for p in _phases if p[0] in DBG_PHASES]
    for phase, (_pn, wspec, ntile, outd, bcol) in enumerate(_phases):
        buf, wsem, wtot = load_w(phase, wspec)
        P.wait("tensor", wsem, wtot)
        P.wait("tensor", sLX, P.val(sLX))
        bname, bj = bcol
        for m in range(DBG_MLIM):
            for g0 in range(0, ntile, 4):
                gt = list(range(g0, min(g0 + 4, ntile)))
                bi = ((m * ((ntile + 3) // 4) + g0 // 4) % 2) * 4
                for ti in range(len(gt)):
                    P.wguard("tensor", ("bank", bi + ti))
                for kc in range(8):
                    lhs = buf[:, kc * 1024 + m * 128: kc * 1024 + (m + 1) * 128]
                    for ti, tile in enumerate(gt):
                        P.mm(P.PS[bi + ti][:, 0:512], lhs,
                             xb[:, kc * 4096 + tile * 512: kc * 4096 + tile * 512 + 512],
                             kc == 0, kc == 7,
                             sname=sMM if (kc == 7 and ti == len(gt) - 1) else None)
                mmv = P.val(sMM)
                P.wait("scalar", sMM, mmv)
                for ti, tile in enumerate(gt):
                    evslot = ev[:, ti * 512:(ti + 1) * 512]
                    P.wguard("scalar", ("ev", tag, ti))
                    e = P.act("scalar", evslot, P.PS[bi + ti][:, 0:512], AF.Identity,
                              bias=BPc(P, bname, bj + m), sname=sEV)
                    P.setguard(("bank", bi + ti), sEV, e)
                    P.wait("sync", sEV, e)
                    st = P.dma("sync", outd[128 * m:128 * (m + 1),
                                            tile * 512:(tile + 1) * 512], evslot,
                               sST + str(ti))
                    P.setguard(("ev", tag, ti), sST + str(ti), st)
        P.setguard(("wbuf", tag, phase % 2), sMM, P.val(sMM))

    if "v" not in DBG_PHASES:
        P.pop()
        return
    buf, wsem, wtot = load_w(len(_phases), wv)
    P.wait("tensor", wsem, wtot)
    P.wait("tensor", sLX, P.val(sLX))
    P.wait("tensor", sLB, P.val(sLB))
    for tb in range(DBG_TBLIM):
        sb = vsb[tb % 2]
        P.wguard("vector", ("vsb", tag, tb % 2))
        P.op("vector", "memset", sb[:, :], 1.0, sname=sVV)
        msetv = P.val(sVV)
        for dv in range(2):
            bank = P.PS[4 + 2 * (tb % 2) + dv]
            P.wguard("tensor", ("bank", 4 + 2 * (tb % 2) + dv))
            for kc in range(8):
                lhs = xb[:, kc * 4096 + tb * 128: kc * 4096 + (tb + 1) * 128]
                P.mm(bank[:, 0:512], lhs,
                     buf[:, kc * 1024 + dv * 512: kc * 1024 + dv * 512 + 512],
                     kc == 0, False)
            mmv = P.mm(bank[:, 0:512], CB(P, "ones_row", 0, 1),
                       bvb[0:1, dv * 512: dv * 512 + 512], False, True, sname=sMM)
            P.wait("vector", sMM, mmv)
            P.wait("vector", sVV, msetv)
            for hh in range(8):
                h = dv * 8 + hh
                P.op("vector", "tensor_copy",
                     sb[:, h * 65: h * 65 + 64],
                     bank[:, hh * 64:(hh + 1) * 64],
                     sname=sVV if hh == 7 else None)
            P.setguard(("bank", 4 + 2 * (tb % 2) + dv), sVV, P.val(sVV))
        P.wait("sync", sVV, P.val(sVV))
        st = P.dma("sync", vd[tb, :, :], sb[:, :], sST + "v" + str(tb % 2))
        P.setguard(("vsb", tag, tb % 2), sST + "v" + str(tb % 2), st)
    P.pop()



def emit_attn0(P):
    P.push()
    kb = P.alloc_b(8 * 4096)
    vb = P.alloc_b(32 * 1040)
    qb = [P.alloc_b(4096), P.alloc_b(4096)]
    expp = [P.alloc_b(1024) for _ in range(3)]
    rawst = [P.alloc_f(512), P.alloc_f(512)]
    qd, kd, vd = P.dsrc("q0"), P.dsrc("k0"), P.dsrc("v0")
    rawd = P.dout_or_tmp("raw0")
    sMM, sSC, sVV = "bmm", "chscalar", "chvector"

    for m in range(8):
        P.dma("gpsimd", kb[:, m * 4096:(m + 1) * 4096], kd[128 * m:128 * (m + 1), :], "bkld")
    for tb in range(32):
        P.dma("gpsimd", vb[:, tb * 1040:(tb + 1) * 1040], vd[tb, :, :], "bvld")
    P.wait("tensor", "bkld", P.val("bkld"))
    P.wait("tensor", "bvld", P.val("bvld"))

    CM = CB(P, "maskCM")
    SPAIR = [0, 2, 3]
    for hp in range(8):
        qs = qb[hp % 2]
        P.wguard("sync", ("qslot", hp % 2))
        P.dma("sync", qs[:, :], qd[128 * hp:128 * (hp + 1), :], "bq" + str(hp % 2))
        P.wait("tensor", "bq" + str(hp % 2), P.val("bq" + str(hp % 2)))
        for h2 in range(2):
            h = 2 * hp + h2

            def emit_pair(tau, pi):
                sp = SPAIR[pi % 3]
                ph = P.PSH[sp]
                P.wguard("tensor", ("spair", sp))
                for half in range(2):
                    kc = 2 * pi + half
                    P.mm(ph[:, half * 512:(half + 1) * 512],
                         kb[64 * h2:64 * h2 + 64,
                            hp * 4096 + kc * 128: hp * 4096 + (kc + 1) * 128],
                         qs[64 * h2:64 * h2 + 64, tau * 512:(tau + 1) * 512],
                         True, True, sname=sMM if half == 1 else None)
                smm = P.val(sMM)
                eslot = pi % 3
                P.wguard("scalar", ("expp", eslot))
                P.wait("scalar", sMM, smm)
                eex = P.act("scalar", expp[eslot][:, :], ph[:, 0:1024], AF.Exp,
                            scale=0.125, sname=sSC)
                P.setguard(("spair", sp), sSC, eex)
                gate = (sSC, eex)
                for half in range(2):
                    kc = 2 * pi + half
                    if kc >= 4 * tau:
                        i = kc - 4 * tau
                        P.wait("vector", sSC, eex)
                        emk = P.op("vector", "tensor_mul",
                                   expp[eslot][:, half * 512:(half + 1) * 512],
                                   expp[eslot][:, half * 512:(half + 1) * 512],
                                   CM[:, 384 - 128 * i: 384 - 128 * i + 512],
                                   sname=sVV)
                        gate = (sVV, emk)
                return gate

            for tau in range(DBG_TAULIM):
                npair = 2 * tau + 2
                avi = 2 + ((h2 * 8 + tau) % 2)
                av = P.PS[avi]
                P.wguard("tensor", ("bank", avi))
                gates = {}
                for pi in range(min(2, npair)):
                    gates[pi] = emit_pair(tau, pi)
                for pi in range(npair):
                    if pi + 2 < npair:
                        gates[pi + 2] = emit_pair(tau, pi + 2)
                    gate = gates.pop(pi)
                    eslot = pi % 3
                    P.wait("tensor", gate[0], gate[1])
                    for half in range(2):
                        kc = 2 * pi + half
                        P.mm(av[0:65, 0:512],
                             vb[:, kc * 1040 + h * 65: kc * 1040 + h * 65 + 65],
                             expp[eslot][:, half * 512:(half + 1) * 512],
                             kc == 0, kc == 4 * tau + 3,
                             sname=sMM if half == 1 else None)
                    P.setguard(("expp", eslot), sMM, P.val(sMM))
                rsl = (h2 * 8 + tau) % 2
                P.wguard("vector", ("rawst", rsl))
                P.wait("vector", sMM, P.val(sMM))
                ee = P.op("vector", "tensor_copy", rawst[rsl][0:65, 0:512],
                          av[0:65, 0:512])
                P.setguard(("bank", avi), "chvector", ee)
                P.wait("sync", "chvector", ee)
                stv = P.dma("sync", rawd[h, tau, :, :], rawst[rsl][0:65, 0:512],
                            "br" + str(rsl))
                P.setguard(("rawst", rsl), "br" + str(rsl), stv)
        P.setguard(("qslot", hp % 2), sMM, P.val(sMM))
    P.pop()


def s1(P):
    emit_attn0(P)


STAGES["s1"] = (s1, ["q0", "k0", "v0"], ["raw0"])



def emit_proj_ln(P, tag, raw_name, ntile, wname, bcol_name, res_name, out_name):
    P.push()
    nh = 16
    rawd = P.dsrc(raw_name)
    resd = P.dsrc(res_name)
    outd = P.dout_or_tmp(out_name)
    wd = P.din(wname)
    rdd = P.dtmp(tag + "_rd")
    sMM, sSC, sVV = tag + "mm", "chscalar", "chvector"

    wpb = P.alloc_b(8 * 1024)
    denf = P.alloc_f(512)
    recf = P.alloc_f(512)
    rdrow = [P.alloc_f(512) for _ in range(4)]
    rawt = [P.alloc_f(512) for _ in range(2)]
    attnb = P.alloc_b(8 * 512)
    ysb = P.alloc_f(8 * 512)
    ybf = P.alloc_b(512)
    y2b = P.alloc_b(512)
    xrt = [P.alloc_f(512) for _ in range(2)]
    ssum = P.alloc_f(512)
    sq = P.alloc_f(512)
    vrx = P.alloc_f(512)
    rstd = P.alloc_f(512)
    xo = [P.alloc_f(512) for _ in range(2)]

    nrow = nh * ntile
    for kc in range(8):
        P.dma("gpsimd", wpb[:, kc * 1024:(kc + 1) * 1024],
              wd[128 * kc:128 * (kc + 1), :], tag + "lw")
    P.dma("sync", denf[0:nrow, 0:512], rawd[:, :, 64, :], tag + "ld")
    P.wait("vector", tag + "ld", P.val(tag + "ld"))
    P.op("vector", "reciprocal", recf[0:nrow, 0:512], denf[0:nrow, 0:512], sname=sVV)
    P.wait("sync", sVV, P.val(sVV))
    P.dma("sync", rdd[0:nrow, :], recf[0:nrow, 0:512], tag + "st")
    P.wait("sync", tag + "st", P.val(tag + "st"))
    P.wait("tensor", tag + "lw", P.val(tag + "lw"))

    ROWF = CFc(P, "ones_rowf", 0, 1)
    SCLF = CFc(P, "scale_rowf", 0, 1)
    OCOL = CB(P, "ones_col")

    for tau in range(ntile):
        for m in range(8):
            rs = m % 2
            P.wguard("sync", ("rawt", tag, rs))
            for half in range(2):
                h = 2 * m + half
                P.dma("sync", rawt[rs][64 * half:64 * half + 64, 0:512],
                      rawd[h, tau, 0:64, :], tag + "lr" + str(rs))
                P.wguard("sync", ("rdrow", tag, 2 * rs + half))
                P.dma("sync", rdrow[2 * rs + half][0:1, 0:512],
                      rdd[h * ntile + tau: h * ntile + tau + 1, :],
                      tag + "lq" + str(2 * rs + half))
            P.wait("tensor", tag + "lq" + str(2 * rs), P.val(tag + "lq" + str(2 * rs)))
            P.wait("tensor", tag + "lq" + str(2 * rs + 1),
                   P.val(tag + "lq" + str(2 * rs + 1)))
            bcb = P.PS[6 + rs]
            P.wguard("tensor", ("bank", 6 + rs))
            P.mm(bcb[0:64, 0:512], ROWF[0:1, 0:64], rdrow[2 * rs][0:1, 0:512],
                 True, True)
            bm = P.mm(bcb[64:128, 0:512], ROWF[0:1, 0:64],
                      rdrow[2 * rs + 1][0:1, 0:512], True, True, sname=sMM)
            P.setguard(("rdrow", tag, 2 * rs), sMM, bm)
            P.setguard(("rdrow", tag, 2 * rs + 1), sMM, bm)
            P.wait("vector", sMM, bm)
            P.wait("vector", tag + "lr" + str(rs), P.val(tag + "lr" + str(rs)))
            vm = P.op("vector", "tensor_mul", attnb[:, m * 512:(m + 1) * 512],
                      rawt[rs][:, 0:512], bcb[0:128, 0:512], sname=sVV)
            P.setguard(("rawt", tag, rs), sVV, vm)
            P.setguard(("bank", 6 + rs), sVV, vm)
        attn_done = P.val(sVV)
        P.wait("tensor", sVV, attn_done)
        for dmg in range(2):
            for ti in range(4):
                P.wguard("tensor", ("bank", ti))
            for m in range(8):
                for ti in range(4):
                    dm = dmg * 4 + ti
                    P.mm(P.PS[ti][:, 0:512],
                         wpb[:, m * 1024 + dm * 128: m * 1024 + (dm + 1) * 128],
                         attnb[:, m * 512:(m + 1) * 512], m == 0, m == 7,
                         sname=sMM if (m == 7 and ti == 3) else None)
            mmv = P.val(sMM)
            P.wait("scalar", sMM, mmv)
            for ti in range(4):
                dm = dmg * 4 + ti
                rs = dm % 2
                P.wguard("sync", ("xrt", tag, rs))
                P.dma("sync", xrt[rs][:, 0:512],
                      resd[128 * dm:128 * (dm + 1), tau * 512:(tau + 1) * 512],
                      tag + "lx" + str(rs))
                ev = P.act("scalar", ysb[:, dm * 512:(dm + 1) * 512],
                           P.PS[ti][:, 0:512], AF.Identity,
                           bias=BPc(P, bcol_name, dm), sname=sSC)
                P.setguard(("bank", ti), sSC, ev)
                P.wait("vector", sSC, ev)
                P.wait("vector", tag + "lx" + str(rs), P.val(tag + "lx" + str(rs)))
                av = P.op("vector", "tensor_add", ysb[:, dm * 512:(dm + 1) * 512],
                          ysb[:, dm * 512:(dm + 1) * 512], xrt[rs][:, 0:512],
                          sname=sVV)
                P.setguard(("xrt", tag, rs), sVV, av)
                P.wguard("vector", ("ybf", tag))
                P.op("vector", "tensor_copy", ybf[:, :],
                     ysb[:, dm * 512:(dm + 1) * 512])
                cv = P.op("vector", "tensor_mul", y2b[:, :], ybf[:, :], ybf[:, :],
                          sname=sVV)
                P.wait("tensor", sVV, cv)
                if dm == 0:
                    P.wguard("tensor", ("bank", 4))
                    P.wguard("tensor", ("bank", 5))
                P.mm(P.PS[4][0:1, 0:512], OCOL, ybf[:, :], dm == 0, dm == 7)
                sm = P.mm(P.PS[5][0:1, 0:512], OCOL, y2b[:, :], dm == 0, dm == 7,
                          sname=sMM)
                P.setguard(("ybf", tag), sMM, sm)
        stat_mm = P.val(sMM)
        P.wait("scalar", sMM, stat_mm)
        c1 = P.act("scalar", ssum[0:1, 0:512], P.PS[4][0:1, 0:512], AF.Copy,
                   sname=sSC)
        c2 = P.act("scalar", sq[0:1, 0:512], P.PS[5][0:1, 0:512], AF.Copy,
                   sname=sSC)
        P.setguard(("bank", 4), sSC, c2)
        P.setguard(("bank", 5), sSC, c2)
        P.wait("vector", sSC, c2)
        P.op("vector", "tensor_mul", vrx[0:1, 0:512], ssum[0:1, 0:512],
             ssum[0:1, 0:512])
        sv = P.op("vector", "scalar_tensor_tensor", vrx[0:1, 0:512],
                  vrx[0:1, 0:512], -1.0 / 1024.0, sq[0:1, 0:512],
                  OP.mult, OP.add, sname=sVV)
        P.op("vector", "tensor_scalar", vrx[0:1, 0:512], vrx[0:1, 0:512],
             1.0 / 1024.0, 1e-5, OP.mult, OP.add, sname=sVV)
        P.wait("scalar", sVV, P.val(sVV))
        P.act("scalar", vrx[0:1, 0:512], vrx[0:1, 0:512], AF.Ln)
        rv = P.act("scalar", rstd[0:1, 0:512], vrx[0:1, 0:512], AF.Exp,
                   scale=-0.5, sname=sSC)
        P.wait("tensor", sSC, rv)
        P.wguard("tensor", ("bank", 6))
        P.wguard("tensor", ("bank", 7))
        P.mm(P.PS[6][0:128, 0:512], SCLF, ssum[0:1, 0:512], True, True)
        bm2 = P.mm(P.PS[7][0:128, 0:512], ROWF, rstd[0:1, 0:512], True, True,
                   sname=sMM)
        P.wait("vector", sMM, bm2)
        for dm in range(8):
            xs = xo[dm % 2]
            P.wguard("vector", ("xo", tag, dm % 2))
            P.op("vector", "tensor_sub", xs[:, 0:512],
                 ysb[:, dm * 512:(dm + 1) * 512], P.PS[6][0:128, 0:512])
            xv = P.op("vector", "tensor_mul", xs[:, 0:512], xs[:, 0:512],
                      P.PS[7][0:128, 0:512], sname=sVV)
            P.wait("sync", sVV, xv)
            so_ = P.dma("sync", outd[128 * dm:128 * (dm + 1),
                                     tau * 512:(tau + 1) * 512], xs[:, 0:512],
                        tag + "so" + str(dm % 2))
            P.setguard(("xo", tag, dm % 2), tag + "so" + str(dm % 2), so_)
        P.setguard(("bank", 6), sVV, P.val(sVV))
        P.setguard(("bank", 7), sVV, P.val(sVV))
    P.pop()


def s2(P):
    emit_proj_ln(P, "c", "raw0", 8, "w_proj0", "b_proj0", "x0T", "x1")


STAGES["s2"] = (s2, ["raw0", "w_proj0", "x0T"], ["x1"])



AX = mybir.AxisListType.X


def gelu_evac(P, outb, bank, gcol, tmpb, tmph, rows=128):
    return P.act("scalar", outb[0:rows, :], bank, AF.Gelu,
                 bias=P.BP[0:rows, gcol:gcol + 1])


def emit_ffn0(P):
    P.push()
    rw1b = P.alloc_b(2048)
    rw2b = P.alloc_b(16)
    hw1b = P.alloc_b(2048)
    hw2b = P.alloc_b(2)
    ew1b = P.alloc_b(16 * 512)
    ew2b = P.alloc_b(4 * 3072)
    eb2b = P.alloc_b(3072)
    sw1b = P.alloc_b(16 * 64)
    sw2b = P.alloc_b(3072)
    for (buf, dn, nk, ncol) in ((rw1b, "r_w1", 8, 256), (rw2b, "r_w2", 2, 8),
                                (hw1b, "h_w1", 8, 256), (hw2b, "h_w2", 2, 1),
                                (ew1b, "e_w1", 16, 512), (ew2b, "e_w2", 4, 3072),
                                (sw1b, "s_w1", 16, 64)):
        wd = P.din(dn)
        for kc in range(nk):
            P.dma("gpsimd", buf[:, kc * ncol:(kc + 1) * ncol],
                  wd[128 * kc:128 * (kc + 1), :], "dlw")
    P.dma("gpsimd", eb2b[0:8, :], P.din("e_b2")[:, :], "dlw")
    P.dma("gpsimd", sw2b[0:64, :], P.din("s_w2")[:, :], "dlw")
    P.wait("tensor", "dlw", P.val("dlw"))

    x1d = P.dsrc("x1")
    intd = P.dout_or_tmp("intg")

    intg = P.alloc_b(8 * 512)
    vst = P.alloc_b(8 * 512)
    habc = P.alloc_b(512)
    h1b = P.alloc_b(4 * 512)
    h2b = P.alloc_b(4 * 512)
    hsb = P.alloc_b(512)
    hhb = P.alloc_b(2 * 512)
    rhb = P.alloc_b(2 * 512)
    tmpb = P.alloc_b(512)
    tmph = P.alloc_b(512)
    tmpf = P.alloc_f(512)
    logf = P.alloc_f(512)
    hrow = P.alloc_f(512)
    ltm = P.alloc_f(32)
    eqa = P.alloc_f(32)
    eqb = P.alloc_f(32)
    mx1 = P.alloc_f(1)
    mx2 = P.alloc_f(1)
    w1c = P.alloc_f(1)
    m1fm = P.alloc_b(512)
    m2fm = P.alloc_b(512)
    rowa = P.alloc_f(512)
    rowb = P.alloc_f(512)
    swsig = P.alloc_f(1)
    onemsw = P.alloc_f(1)
    swcol = P.alloc_f(1)
    wAb = P.alloc_b(512)
    wBb = P.alloc_b(512)
    haltb = P.alloc_b(512)
    haltxb = P.alloc_b(512)
    dtb = [P.alloc_b(512) for _ in range(2)]
    Ab = [[P.alloc_b(512) for _ in range(3)] for _ in range(2)]
    Bb = [[P.alloc_b(512) for _ in range(3)] for _ in range(2)]
    Gb = [[P.alloc_b(512) for _ in range(3)] for _ in range(2)]
    tmpf3 = [[P.alloc_f(512) for _ in range(3)] for _ in range(2)]
    vnb = [P.alloc_b(512) for _ in range(3)]
    gnb = [P.alloc_b(512) for _ in range(3)]
    prod = P.alloc_b(512)
    mixv = P.alloc_b(512)
    mixg = P.alloc_b(512)

    EYE = CFc(P, "eye128f")
    ROWF = CFc(P, "ones_rowf", 0, 1)
    B8 = CB(P, "B8", 0, 8)
    sMM = "dmm"

    P.act("scalar", swsig[0:1, 0:1], BPc(P, "sw", 0, 0, 1), AF.Sigmoid)
    P.wait("vector", "chscalar", P.val("chscalar"))
    P.op("vector", "tensor_scalar", onemsw[0:1, 0:1], swsig[0:1, 0:1], -1.0, 1.0,
         OP.mult, OP.add)
    ov = P.op("vector", "tensor_copy", rowa[0:1, 0:128], ROWF)
    P.wait("tensor", "chvector", ov)
    bmv = P.mm(P.PS[7][0:128, 0:1], rowa[0:1, 0:128], swsig[0:1, 0:1], True, True,
               sname=sMM)
    P.wait("scalar", sMM, bmv)
    scv = P.act("scalar", swcol[0:128, 0:1], P.PS[7][0:128, 0:1], AF.Copy)
    P.setguard(("bank", 7), "chscalar", scv)

    gc_r = 0
    gc_h = 2
    gc_e = 4
    gc_s = 8
    gc_f = 9

    for tau in range(DBG_S3TAU):
        t0 = tau * 512
        for f in range(8):
            P.wguard("gpsimd", ("s3x", f))
            P.dma("gpsimd", intg[:, f * 512:(f + 1) * 512],
                  x1d[128 * f:128 * (f + 1), t0:t0 + 512], "dlx")
        P.wait("vector", "dlx", P.val("dlx"))
        P.wait("tensor", "dlx", P.val("dlx"))

        for mh in range(2):
            P.wguard("tensor", ("bank", mh))
        for kc in range(8):
            for mh in range(2):
                P.mm(P.PS[mh][:, 0:512],
                     rw1b[:, kc * 256 + mh * 128: kc * 256 + (mh + 1) * 128],
                     intg[:, kc * 512:(kc + 1) * 512], kc == 0, kc == 7,
                     sname=sMM if mh == 1 else None)
        P.wait("scalar", sMM, P.val(sMM))
        for mh in range(2):
            gv = gelu_evac(P, rhb[:, mh * 512:(mh + 1) * 512],
                           P.PS[mh][:, 0:512], gc_r + mh, tmpb, tmph)
            P.setguard(("bank", mh), "chscalar", gv)
        P.wguard("tensor", ("bank", 2))
        P.wait("tensor", "chscalar", P.val("chscalar"))
        for kc2 in range(2):
            P.mm(P.PS[2][0:8, 0:512], rw2b[:, kc2 * 8:(kc2 + 1) * 8],
                 rhb[:, kc2 * 512:(kc2 + 1) * 512], kc2 == 0, kc2 == 1,
                 sname=sMM if kc2 == 1 else None)
        P.wait("scalar", sMM, P.val(sMM))
        lv = P.act("scalar", logf[0:8, 0:512], P.PS[2][0:8, 0:512], AF.Identity,
                   bias=BPc(P, "r_b2", 0, 0, 8))
        P.setguard(("bank", 2), "chscalar", lv)
        P.wait("tensor", "chscalar", lv)
        P.wguard("tensor", ("bank", 3))
        for j in range(4):
            tv = P.op("tensor", "transpose", P.PS[3][0:128, j * 8:(j + 1) * 8],
                      logf[0:8, 128 * j:128 * (j + 1)], EYE[0:8, 0:8], sname=sMM)
        P.wait("vector", sMM, tv)
        P.op("vector", "tensor_copy", ltm[0:128, 0:32], P.PS[3][0:128, 0:32])
        P.setguard(("bank", 3), "chvector", P.val("chvector"))
        P.wguard("tensor", ("bank", 4))
        P.wguard("tensor", ("bank", 5))
        P.wguard("tensor", ("bank", 6))
        for j in range(4):
            lj = ltm[0:128, j * 8:(j + 1) * 8]
            if j > 0:
                P.wait("vector", "chscalar", P.val("chscalar"))
                P.wait("vector", sMM, P.val(sMM))
            P.op("vector", "reduce_max", mx1[0:128, 0:1], lj, AX)
            P.op("vector", "tensor_scalar", eqa[0:128, 0:8], lj, mx1[0:128, 0:1],
                 None, OP.is_equal)
            P.op("vector", "scalar_tensor_tensor", eqb[0:128, 0:8], eqa[0:128, 0:8],
                 -1e30, lj, OP.mult, OP.add)
            P.op("vector", "reduce_max", mx2[0:128, 0:1], eqb[0:128, 0:8], AX)
            P.op("vector", "tensor_scalar", eqb[0:128, 0:8], eqb[0:128, 0:8],
                 mx2[0:128, 0:1], None, OP.is_equal)
            wv_ = P.op("vector", "tensor_sub", w1c[0:128, 0:1], mx1[0:128, 0:1],
                       mx2[0:128, 0:1])
            P.wait("scalar", "chvector", wv_)
            sv_ = P.act("scalar", w1c[0:128, 0:1], w1c[0:128, 0:1], AF.Sigmoid)
            P.wait("tensor", "chvector", wv_)
            P.wait("tensor", "chscalar", sv_)
            P.op("tensor", "transpose", P.PS[4][0:8, j * 128:(j + 1) * 128],
                 eqa[0:128, 0:8], EYE[0:128, 0:128], sname=sMM)
            P.op("tensor", "transpose", P.PS[5][0:8, j * 128:(j + 1) * 128],
                 eqb[0:128, 0:8], EYE[0:128, 0:128], sname=sMM)
            tv2 = P.op("tensor", "transpose", P.PS[6][0:1, j * 128:(j + 1) * 128],
                       w1c[0:128, 0:1], EYE[0:128, 0:128], sname=sMM)
        P.wait("vector", sMM, tv2)
        P.op("vector", "tensor_copy", m1fm[0:8, :], P.PS[4][0:8, 0:512])
        P.op("vector", "tensor_copy", m2fm[0:8, :], P.PS[5][0:8, 0:512])
        P.op("vector", "tensor_copy", rowa[0:1, 0:512], P.PS[6][0:1, 0:512])
        P.op("vector", "tensor_scalar", rowb[0:1, 0:512], rowa[0:1, 0:512],
             onemsw[0:1, 0:1], None, OP.mult)
        P.op("vector", "tensor_scalar", rowa[0:1, 0:512], rowa[0:1, 0:512],
             -1.0, 1.0, OP.mult, OP.add)
        rv_ = P.op("vector", "tensor_scalar", rowa[0:1, 0:512], rowa[0:1, 0:512],
                   onemsw[0:1, 0:1], None, OP.mult)
        P.setguard(("bank", 4), "chvector", rv_)
        P.setguard(("bank", 5), "chvector", rv_)
        P.setguard(("bank", 6), "chvector", rv_)
        P.wait("tensor", "chvector", rv_)
        P.wguard("tensor", ("bank", 4))
        P.wguard("tensor", ("bank", 5))
        P.mm(P.PS[4][0:128, 0:512], ROWF, rowb[0:1, 0:512], True, True)
        bc2 = P.mm(P.PS[5][0:128, 0:512], ROWF, rowa[0:1, 0:512], True, True,
                   sname=sMM)
        P.wait("vector", sMM, bc2)
        P.op("vector", "tensor_copy", wAb[0:128, 0:512], P.PS[4][0:128, 0:512])
        wv2 = P.op("vector", "tensor_copy", wBb[0:128, 0:512], P.PS[5][0:128, 0:512])
        P.setguard(("bank", 4), "chvector", wv2)
        P.setguard(("bank", 5), "chvector", wv2)

        for it in range(2):
            nkc = 8 if it == 0 else 16
            for b_ in range(7):
                P.wguard("tensor", ("bank", b_))
            P.wait("tensor", "chvector", P.val("chvector"))
            for kc in range(nkc):
                srcv = intg if kc < 8 else vst
                f8 = kc % 8
                ctx = srcv[:, f8 * 512:(f8 + 1) * 512]
                last = kc == nkc - 1
                for mh in range(4):
                    P.mm(P.PS[mh][:, 0:512],
                         ew1b[:, kc * 512 + mh * 128: kc * 512 + (mh + 1) * 128],
                         ctx, kc == 0, last)
                P.mm(P.PS[4][0:64, 0:512], sw1b[:, kc * 64:(kc + 1) * 64],
                     ctx, kc == 0, last,
                     sname=sMM if kc >= 8 else None)
                if kc < 8:
                    for mh2 in range(2):
                        P.mm(P.PS[5 + mh2][:, 0:512],
                             hw1b[:, kc * 256 + mh2 * 128: kc * 256 + (mh2 + 1) * 128],
                             ctx, kc == 0, kc == 7,
                             sname=sMM if mh2 == 1 else None)
            mmv = P.val(sMM)
            P.wait("scalar", sMM, mmv)
            for mh in range(4):
                P.wguard("scalar", ("habc",))
                gv = gelu_evac(P, habc, P.PS[mh][:, 0:512], gc_e + mh, tmpb, tmph)
                P.wguard("tensor", ("bank", 7))
                P.wait("tensor", "chscalar", gv)
                bm1 = P.mm(P.PS[7][0:128, 0:512], B8[0:8, mh * 128:(mh + 1) * 128],
                           m1fm[0:8, :], True, True, sname=sMM)
                P.wait("vector", sMM, bm1)
                P.wait("vector", "chscalar", gv)
                u1 = P.op("vector", "tensor_mul", h1b[:, mh * 512:(mh + 1) * 512],
                          habc[:, :], P.PS[7][0:128, 0:512])
                P.wait("tensor", "chvector", u1)
                bm2 = P.mm(P.PS[7][0:128, 0:512], B8[0:8, mh * 128:(mh + 1) * 128],
                           m2fm[0:8, :], True, True, sname=sMM)
                P.wait("vector", sMM, bm2)
                u2 = P.op("vector", "tensor_mul", h2b[:, mh * 512:(mh + 1) * 512],
                          habc[:, :], P.PS[7][0:128, 0:512])
                P.setguard(("bank", 7), "chvector", u2)
                P.setguard(("bank", mh), "chvector", u2)
                P.setguard(("habc",), "chvector", u2)
            gvs = gelu_evac(P, hsb, P.PS[4][0:64, 0:512], gc_s, tmpb, tmph, rows=64)
            P.setguard(("bank", 4), "chscalar", gvs)
            for mh2 in range(2):
                gvh = gelu_evac(P, hhb[:, mh2 * 512:(mh2 + 1) * 512],
                                P.PS[5 + mh2][:, 0:512], gc_h + mh2, tmpb, tmph)
                P.setguard(("bank", 5 + mh2), "chscalar", gvh)
            P.wguard("tensor", ("bank", 7))
            P.wait("tensor", "chscalar", P.val("chscalar"))
            for kc2 in range(2):
                P.mm(P.PS[7][0:1, 0:512], hw2b[:, kc2:kc2 + 1],
                     hhb[:, kc2 * 512:(kc2 + 1) * 512], kc2 == 0, kc2 == 1,
                     sname=sMM if kc2 == 1 else None)
            P.wait("scalar", sMM, P.val(sMM))
            hv = P.act("scalar", hrow[0:1, 0:512], P.PS[7][0:1, 0:512], AF.Sigmoid,
                       bias=BPc(P, "h_b2", 0, 0, 1))
            P.setguard(("bank", 7), "chscalar", hv)
            P.wait("tensor", "chscalar", hv)
            P.wguard("tensor", ("bank", 7))
            hb_ = P.mm(P.PS[7][0:128, 0:512], ROWF, hrow[0:1, 0:512], True, True,
                       sname=sMM)
            P.wait("vector", sMM, hb_)
            P.op("vector", "tensor_copy", haltb[0:128, 0:512], P.PS[7][0:128, 0:512])
            hx_ = P.op("vector", "tensor_scalar", haltxb[0:128, 0:512],
                       haltb[0:128, 0:512], 0.1, None, OP.mult)
            P.setguard(("bank", 7), "chvector", hx_)
            PB = [0, 1, 5]
            for fp in range(4):
                fpair = (2 * fp, 2 * fp + 1)
                for j, f in enumerate(fpair):
                    P.op("vector", "tensor_scalar", dtb[j][:, :],
                         intg[:, f * 512:(f + 1) * 512], BPc(P, "mu", f), None,
                         OP.subtract)
                parts = []
                for j, f in enumerate(fpair):
                    for pt in (0, 2, 1):
                        for si in range(3):
                            if it == 0 and pt == 0:
                                continue
                            parts.append((j, f, si, pt))

                def emit_part(idx):
                    j, f, si, pt = parts[idx]
                    ch = (0, 8, 16)[pt] + f
                    bk = P.PS[PB[idx % 3]]
                    P.wguard("tensor", ("bank", PB[idx % 3]))
                    if si == 0:
                        P.wait("tensor", "chscalar", P.val("chscalar"))
                        P.mm(bk[:, 0:512],
                             sw2b[0:64, ch * 128:(ch + 1) * 128], hsb[0:64, :],
                             True, True, sname=sMM)
                    else:
                        hx = h1b if si == 1 else h2b
                        mf = m1fm if si == 1 else m2fm
                        for kc2 in range(4):
                            P.mm(bk[:, 0:512],
                                 ew2b[:, kc2 * 3072 + ch * 128:
                                      kc2 * 3072 + (ch + 1) * 128],
                                 hx[:, kc2 * 512:(kc2 + 1) * 512],
                                 kc2 == 0, False)
                        P.mm(bk[:, 0:512], eb2b[0:8, ch * 128:(ch + 1) * 128],
                             mf[0:8, :], False, True, sname=sMM)
                    return P.val(sMM)

                def emit_act(idx, mk):
                    j, f, si, pt = parts[idx]
                    bk = P.PS[PB[idx % 3]]
                    P.wait("scalar", sMM, mk)
                    P.wguard("scalar", ("abg", j))
                    sb2c = BPc(P, "s_b2", (0, 8, 16)[pt] + f) if si == 0 else 0.0
                    if pt == 0:
                        ev_ = P.act("scalar", Ab[j][si][:, :], bk[:, 0:512],
                                    AF.Sigmoid, bias=sb2c)
                    elif pt == 2:
                        ev_ = P.act("scalar", Gb[j][si][:, :], bk[:, 0:512],
                                    AF.Sigmoid, bias=sb2c)
                    else:
                        ev_ = P.act("scalar", tmpf3[j][si][:, 0:512], bk[:, 0:512],
                                    AF.Sigmoid, scale=-1.0,
                                    bias=P.BPn[:, 8 + f:8 + f + 1] if si == 0 else 0.0)
                    P.setguard(("bank", PB[idx % 3]), "chscalar", ev_)
                    return ev_

                marks = {}
                n = len(parts)
                for idx in range(min(2, n)):
                    marks[idx] = emit_part(idx)
                acts = {}
                for idx in range(n):
                    if idx + 2 < n:
                        marks[idx + 2] = emit_part(idx + 2)
                    acts[idx] = emit_act(idx, marks.pop(idx))
                lns = {}
                for j in range(2):
                    for si in range(3):
                        lns[(j, si)] = P.act("scalar", Bb[j][si][:, :],
                                             tmpf3[j][si][:, 0:512], AF.Ln)
                for j, f in enumerate(fpair):
                    for si in range(3):
                        P.wait("vector", "chscalar", lns[(j, si)])
                        if it == 0:
                            P.op("vector", "tensor_mul", vnb[si][:, :],
                                 Bb[j][si][:, :], dtb[j][:, :])
                        else:
                            P.op("vector", "tensor_mul", vnb[si][:, :],
                                 Ab[j][si][:, :], vst[:, f * 512:(f + 1) * 512])
                            P.op("vector", "tensor_mul", prod[:, :],
                                 Bb[j][si][:, :], dtb[j][:, :])
                            P.op("vector", "tensor_add", vnb[si][:, :],
                                 vnb[si][:, :], prod[:, :])
                        gn_ = P.op("vector", "tensor_mul", gnb[si][:, :],
                                   Gb[j][si][:, :], vnb[si][:, :])
                    P.setguard(("abg", j), "chvector", P.val("chvector"))
                    for (mix, src3) in ((mixv, vnb), (mixg, gnb)):
                        P.op("vector", "tensor_scalar", mix[0:128, 0:512],
                             src3[0][:, :], swcol[0:128, 0:1], None, OP.mult)
                        P.op("vector", "tensor_mul", prod[:, :], src3[1][:, :],
                             wAb[0:128, 0:512])
                        P.op("vector", "tensor_add", mix[0:128, 0:512],
                             mix[0:128, 0:512], prod[:, :])
                        P.op("vector", "tensor_mul", prod[:, :], src3[2][:, :],
                             wBb[0:128, 0:512])
                        P.op("vector", "tensor_add", mix[0:128, 0:512],
                             mix[0:128, 0:512], prod[:, :])
                    if it == 0:
                        P.op("vector", "tensor_mul", vst[:, f * 512:(f + 1) * 512],
                             haltb[0:128, 0:512], mixv[0:128, 0:512])
                    else:
                        P.op("vector", "tensor_sub", mixv[0:128, 0:512],
                             mixv[0:128, 0:512], vst[:, f * 512:(f + 1) * 512])
                        P.op("vector", "tensor_mul", mixv[0:128, 0:512],
                             haltb[0:128, 0:512], mixv[0:128, 0:512])
                        P.op("vector", "tensor_add", vst[:, f * 512:(f + 1) * 512],
                             vst[:, f * 512:(f + 1) * 512], mixv[0:128, 0:512])
                    P.op("vector", "tensor_mul", mixg[0:128, 0:512],
                         haltxb[0:128, 0:512], mixg[0:128, 0:512])
                    uv = P.op("vector", "tensor_add", intg[:, f * 512:(f + 1) * 512],
                              intg[:, f * 512:(f + 1) * 512], mixg[0:128, 0:512])
        P.wait("sync", "chvector", P.val("chvector"))
        for f in range(8):
            sv = P.dma("sync", intd[128 * f:128 * (f + 1), t0:t0 + 512],
                       intg[:, f * 512:(f + 1) * 512], "dsx")
            P.setguard(("s3x", f), "dsx", sv)
    P.pop()
    P.fence()

    P.push()
    fw1b = P.alloc_b(8 * 2048)
    fw2b = P.alloc_b(16 * 1024)
    for (buf, dn, nk, ncol) in ((fw1b, "f_w1", 8, 2048), (fw2b, "f_w2", 16, 1024)):
        wd = P.din(dn)
        for kc in range(nk):
            P.dma("gpsimd", buf[:, kc * ncol:(kc + 1) * ncol],
                  wd[128 * kc:128 * (kc + 1), :], "dlw2")
    P.wait("tensor", "dlw2", P.val("dlw2"))
    x1d = P.dsrc("x1")
    intd = P.dsrc("intg")
    x2d = P.dout_or_tmp("x2")

    inb8 = P.alloc_b(8 * 512)
    r1b = P.alloc_b(16 * 512)
    tmpb2 = P.alloc_b(512)
    tmph2 = P.alloc_b(512)
    ysb = P.alloc_f(8 * 512)
    x1t = [P.alloc_f(512), P.alloc_f(512)]
    ybf = P.alloc_b(512)
    y2b = P.alloc_b(512)
    ssum = P.alloc_f(512)
    sq = P.alloc_f(512)
    vrx = P.alloc_f(512)
    rstd = P.alloc_f(512)
    xo = [P.alloc_f(512), P.alloc_f(512)]
    ROWF = CFc(P, "ones_rowf", 0, 1)
    SCLF = CFc(P, "scale_rowf", 0, 1)
    OCOL = CB(P, "ones_col")
    sMM = "dmm"

    for tau in range(DBG_S3TAU):
        t0 = tau * 512
        P.wguard("gpsimd", ("s3in",))
        for kc in range(8):
            P.dma("gpsimd", inb8[:, kc * 512:(kc + 1) * 512],
                  intd[128 * kc:128 * (kc + 1), t0:t0 + 512], "dli")
        P.wait("tensor", "dli", P.val("dli"))
        for mh in range(16):
            bk = P.PS[mh % 2]
            P.wguard("tensor", ("bank", mh % 2))
            for kc in range(8):
                P.mm(bk[:, 0:512],
                     fw1b[:, kc * 2048 + mh * 128: kc * 2048 + (mh + 1) * 128],
                     inb8[:, kc * 512:(kc + 1) * 512], kc == 0, kc == 7,
                     sname=sMM if kc == 7 else None)
            if mh == 15:
                P.setguard(("s3in",), sMM, P.val(sMM))
            P.wait("scalar", sMM, P.val(sMM))
            gv = gelu_evac(P, r1b[:, mh * 512:(mh + 1) * 512], bk[:, 0:512],
                           9 + mh, tmpb2, tmph2)
            P.setguard(("bank", mh % 2), "chscalar", gv)
        for dm in range(8):
            bk = P.PS[2 + dm % 2]
            P.wguard("tensor", ("bank", 2 + dm % 2))
            P.wait("tensor", "chscalar", P.val("chscalar"))
            for kc in range(16):
                P.mm(bk[:, 0:512],
                     fw2b[:, kc * 1024 + dm * 128: kc * 1024 + (dm + 1) * 128],
                     r1b[:, kc * 512:(kc + 1) * 512], kc == 0, kc == 15,
                     sname=sMM if kc == 15 else None)
            mmv = P.val(sMM)
            rs = dm % 2
            P.wguard("sync", ("s3r", rs))
            P.dma("sync", x1t[rs][:, 0:512],
                  x1d[128 * dm:128 * (dm + 1), t0:t0 + 512], "dlr" + str(rs))
            P.wait("scalar", sMM, mmv)
            ev = P.act("scalar", ysb[:, dm * 512:(dm + 1) * 512], bk[:, 0:512],
                       AF.Identity, bias=BPc(P, "f_b2", dm))
            P.setguard(("bank", 2 + dm % 2), "chscalar", ev)
            P.wait("vector", "chscalar", ev)
            P.op("vector", "tensor_scalar", ysb[:, dm * 512:(dm + 1) * 512],
                 ysb[:, dm * 512:(dm + 1) * 512], BPc(P, "iw", dm), None, OP.mult)
            P.wait("vector", "dlr" + str(rs), P.val("dlr" + str(rs)))
            av = P.op("vector", "scalar_tensor_tensor",
                      ysb[:, dm * 512:(dm + 1) * 512], x1t[rs][:, 0:512], 2.0,
                      ysb[:, dm * 512:(dm + 1) * 512], OP.mult, OP.add)
            P.setguard(("s3r", rs), "chvector", av)
            P.op("vector", "tensor_copy", ybf[:, :], ysb[:, dm * 512:(dm + 1) * 512])
            cv = P.op("vector", "tensor_mul", y2b[:, :], ybf[:, :], ybf[:, :])
            P.wait("tensor", "chvector", cv)
            if dm == 0:
                P.wguard("tensor", ("bank", 4))
                P.wguard("tensor", ("bank", 5))
            P.mm(P.PS[4][0:1, 0:512], OCOL, ybf[:, :], dm == 0, dm == 7)
            P.mm(P.PS[5][0:1, 0:512], OCOL, y2b[:, :], dm == 0, dm == 7,
                 sname=sMM if dm == 7 else None)
        P.wait("scalar", sMM, P.val(sMM))
        P.act("scalar", ssum[0:1, 0:512], P.PS[4][0:1, 0:512], AF.Copy)
        c2 = P.act("scalar", sq[0:1, 0:512], P.PS[5][0:1, 0:512], AF.Copy)
        P.setguard(("bank", 4), "chscalar", c2)
        P.setguard(("bank", 5), "chscalar", c2)
        P.wait("vector", "chscalar", c2)
        P.op("vector", "tensor_mul", vrx[0:1, 0:512], ssum[0:1, 0:512],
             ssum[0:1, 0:512])
        P.op("vector", "scalar_tensor_tensor", vrx[0:1, 0:512], vrx[0:1, 0:512],
             -1.0 / 1024.0, sq[0:1, 0:512], OP.mult, OP.add)
        sv_ = P.op("vector", "tensor_scalar", vrx[0:1, 0:512], vrx[0:1, 0:512],
                   1.0 / 1024.0, 1e-5, OP.mult, OP.add)
        P.wait("scalar", "chvector", sv_)
        P.act("scalar", vrx[0:1, 0:512], vrx[0:1, 0:512], AF.Ln)
        rv = P.act("scalar", rstd[0:1, 0:512], vrx[0:1, 0:512], AF.Exp, scale=-0.5)
        P.wait("tensor", "chscalar", rv)
        P.wguard("tensor", ("bank", 6))
        P.wguard("tensor", ("bank", 7))
        P.mm(P.PS[6][0:128, 0:512], SCLF, ssum[0:1, 0:512], True, True)
        bm2 = P.mm(P.PS[7][0:128, 0:512], ROWF, rstd[0:1, 0:512], True, True,
                   sname=sMM)
        P.wait("vector", sMM, bm2)
        for dm in range(8):
            xs = xo[dm % 2]
            P.wguard("vector", ("s3o", dm % 2))
            P.op("vector", "tensor_sub", xs[:, 0:512],
                 ysb[:, dm * 512:(dm + 1) * 512], P.PS[6][0:128, 0:512])
            xv = P.op("vector", "tensor_mul", xs[:, 0:512], xs[:, 0:512],
                      P.PS[7][0:128, 0:512])
            P.wait("sync", "chvector", xv)
            so_ = P.dma("sync", x2d[128 * dm:128 * (dm + 1), t0:t0 + 512],
                        xs[:, 0:512], "dso" + str(dm % 2))
            P.setguard(("s3o", dm % 2), "dso" + str(dm % 2), so_)
        P.setguard(("bank", 6), "chvector", P.val("chvector"))
        P.setguard(("bank", 7), "chvector", P.val("chvector"))
    P.pop()


LMARK = {}


def s3(P):
    emit_ffn0(P)


STAGES["s3"] = (s3, ["x1", "r_w1", "r_w2", "h_w1", "h_w2", "e_w1", "e_w2", "e_b2",
                     "s_w1", "s_w2", "f_w1", "f_w2"], ["x2", "intg"])



def emit_perm(P):
    P.push()
    x2d = P.dsrc("x2")
    x2pd = P.dout_or_tmp("x2p")
    idxd = P.din("permidx")
    buf = [P.alloc_f(2048), P.alloc_f(2048)]
    idxt = [P.alloc_b(2), P.alloc_b(2)]
    x2v = x2d.rearrange("p (h t) -> (p h) t", h=2)
    x2pv = x2pd.rearrange("p (h t) -> (p h) t", h=2)
    for k in range(16):
        s = k % 2
        P.wguard("gpsimd", ("pidx", s))
        P.dma("gpsimd", idxt[s][:, 0:2].bitcast(I32),
              idxd[128 * k:128 * (k + 1), :], "pli" + str(s))
        P.wait("gpsimd", "pli" + str(s), P.val("pli" + str(s)))
        P.wguard("gpsimd", ("pbuf", s))
        P.idma(buf[s][:, 0:2048], x2v[:, :], idxt[s][:, 0:2].bitcast(I32),
               "pgi" + str(s))
        P.wait("sync", "pgi" + str(s), P.val("pgi" + str(s)))
        sv = P.dma("sync", x2pv[128 * k:128 * (k + 1), :], buf[s][:, 0:2048],
                   "pst" + str(s))
        P.setguard(("pbuf", s), "pst" + str(s), sv)
        P.setguard(("pidx", s), "pst" + str(s), sv)
    P.pop()


def sp(P):
    emit_perm(P)


STAGES["sp"] = (sp, ["x2", "permidx"], ["x2p"])



def emit_vsum(P):
    P.push()
    v1d = P.dsrc("v1")
    vsd = P.dout_or_tmp("vsum1")
    vgd = P.dout_or_tmp("vg1")
    vt = [P.alloc_b(1040), P.alloc_b(1040)]
    vsum = P.alloc_f(1040)
    vs65 = P.alloc_f(16)
    sMM = "gmm"
    P.dma("sync", vgd[0:32, :], v1d[:, 0, :], "gst")
    P.dma("sync", vgd[32:64, :], v1d[:, 64, :], "gst")
    for tb in range(32):
        s = tb % 2
        P.wguard("gpsimd", ("vsb", s))
        P.dma("gpsimd", vt[s][:, :], v1d[tb, :, :], "gld" + str(s))
        P.wait("tensor", "gld" + str(s), P.val("gld" + str(s)))
        for sl, (c0, c1) in enumerate(((0, 512), (512, 1024), (1024, 1040))):
            mk = P.mm(P.PS[sl][0:1, 0:c1 - c0], CB(P, "ones_col"),
                      vt[s][:, c0:c1], tb == 0, tb == 31,
                      sname=sMM if sl == 2 else None)
        P.setguard(("vsb", s), sMM, P.val(sMM))
    P.wait("scalar", sMM, P.val(sMM))
    for sl, (c0, c1) in enumerate(((0, 512), (512, 1024), (1024, 1040))):
        sv = P.act("scalar", vsum[0:1, c0:c1], P.PS[sl][0:1, 0:c1 - c0], AF.Copy)
    P.wait("tensor", "chscalar", sv)
    EY1 = CFc(P, "eye128f", 0, 1, 0, 1)
    for h in range(16):
        tv = P.op("tensor", "transpose", P.PS[3][0:64, h:h + 1],
                  vsum[0:1, h * 65: h * 65 + 64], EY1, sname=sMM)
    P.wait("vector", sMM, tv)
    P.op("vector", "tensor_copy", vs65[0:64, 0:16], P.PS[3][0:64, 0:16])
    P.op("vector", "memset", vs65[64:65, 0:16], 4096.0)
    P.wait("sync", "chvector", P.val("chvector"))
    P.dma("sync", vsd[:, :], vs65[0:65, 0:16], "gst")
    P.pop()


def s4(P):
    emit_qkv(P, "e", "x2p", ("wq1", 0), ("wk1", 0), ("wv1", 0),
             ("bq1", 0), ("bk1", 0), "bv1_row", "q1", "k1", "v1", nq_tiles=4)
    P.fence()
    emit_vsum(P)


STAGES["s4"] = (s4, ["x2p", "wq1", "wk1", "wv1", "bv1_row"],
                ["q1", "k1", "v1", "vsum1", "vg1"])



def emit_attn1(P):
    P.push()
    kb = P.alloc_b(8 * 4096)
    vb = P.alloc_b(32 * 1040)
    qb = [P.alloc_b(2048), P.alloc_b(2048)]
    vgbE = P.alloc_b(1040)
    vgbO = P.alloc_b(1040)
    pgE = P.alloc_b(32)
    pgO = P.alloc_b(32)
    vs65 = P.alloc_f(16)
    kgb = P.alloc_b(64)
    qgb = P.alloc_b(32)
    e1 = P.alloc_b(32)
    e2 = P.alloc_b(32)
    corr = P.alloc_b(32)
    pg = P.alloc_b(32)
    expb = [P.alloc_b(512) for _ in range(3)]
    rawst = [P.alloc_f(512), P.alloc_f(512)]
    qd, kd, vd = P.dsrc("q1"), P.dsrc("k1"), P.dsrc("v1")
    vgd, vsd = P.dsrc("vg1"), P.dsrc("vsum1")
    rawd = P.dout_or_tmp("raw1")
    sMM = "hmm"

    for m in range(8):
        P.dma("gpsimd", kb[:, m * 4096:(m + 1) * 4096], kd[128 * m:128 * (m + 1), :],
              "hkld")
    for tb in range(32):
        P.dma("gpsimd", vb[:, tb * 1040:(tb + 1) * 1040], vd[tb, :, :], "hvld")
    P.dma("sync", vgbE[0:32, :], vgd[0:32, :], "hgld")
    P.dma("sync", vgbO[0:32, :], vgd[32:64, :], "hgld")
    P.dma("sync", vs65[0:65, 0:16], vsd[:, :], "hgld")
    P.wait("tensor", "hkld", P.val("hkld"))
    P.wait("tensor", "hvld", P.val("hvld"))
    P.wait("tensor", "hgld", P.val("hgld"))
    P.wait("vector", "hkld", P.val("hkld"))
    P.wait("vector", "hgld", P.val("hgld"))
    CMK = CB(P, "corrmask", 0, 64)

    for hp in range(DBG_S5HP):
        qs = qb[hp % 2]
        P.wguard("sync", ("q1slot", hp % 2))
        P.dma("sync", qs[:, :], qd[128 * hp:128 * (hp + 1), :], "hq" + str(hp % 2))
        P.wait("tensor", "hq" + str(hp % 2), P.val("hq" + str(hp % 2)))
        P.wait("vector", "hq" + str(hp % 2), P.val("hq" + str(hp % 2)))
        for h2 in range(2):
            h = 2 * hp + h2
            if not DBG_S5GLOB:
                pass
            P.wguard("vector", ("g1",))
            P.op("vector", "tensor_copy", kgb[0:64, 0:32],
                 kb[64 * h2:64 * h2 + 64, hp * 4096: hp * 4096 + 4096: 128])
            P.op("vector", "tensor_copy", kgb[0:64, 32:64],
                 kb[64 * h2:64 * h2 + 64, hp * 4096 + 64: hp * 4096 + 4096: 128])
            cq = P.op("vector", "tensor_copy", qgb[0:64, 0:32],
                      qs[64 * h2:64 * h2 + 64, 0:2048:64])
            P.wait("tensor", "chvector", cq)
            P.wguard("tensor", ("bank", 5))
            gv = P.mm(P.PS[5][0:64, 0:32], kgb[0:64, 0:64], qgb[0:64, 0:32],
                      True, True, sname=sMM)
            P.wait("scalar", sMM, gv)
            P.wguard("scalar", ("g1v",))
            s1_ = P.act("scalar", e1[0:64, 0:32], P.PS[5][0:64, 0:32], AF.Exp,
                        scale=0.125)
            s2_ = P.act("scalar", e2[0:64, 0:32], P.PS[5][0:64, 0:32], AF.Exp,
                        scale=0.25)
            P.setguard(("bank", 5), "chscalar", s2_)
            P.wait("vector", "chscalar", s2_)
            P.op("vector", "scalar_tensor_tensor", corr[0:64, 0:32], e1[0:64, 0:32],
                 -2.0, e2[0:64, 0:32], OP.mult, OP.add)
            P.op("vector", "tensor_scalar", corr[0:64, 0:32], corr[0:64, 0:32],
                 1.0, None, OP.add)
            P.op("vector", "tensor_mul", corr[0:64, 0:32], corr[0:64, 0:32], CMK[:, :])
            P.op("vector", "tensor_scalar", pg[0:64, 0:32], e1[0:64, 0:32],
                 -1.0, None, OP.add)
            P.op("vector", "tensor_add", pg[0:64, 0:32], pg[0:64, 0:32],
                 corr[0:64, 0:32])
            P.op("vector", "tensor_copy", pgE[0:32, 0:32], pg[0:32, 0:32])
            pgv = P.op("vector", "tensor_copy", pgO[0:32, 0:32], pg[32:64, 0:32])
            P.setguard(("g1v",), "chvector", pgv)
            P.wait("tensor", "chvector", pgv)
            for tau in range(4):
                avi = 2 + ((h2 * 4 + tau) % 2)
                av = P.PS[avi]
                P.wguard("tensor", ("bank", avi))

                def emit_sc1(i, tau=tau):
                    kc = 4 * tau + i
                    si = [0, 1, 6][i % 3]
                    P.wguard("tensor", ("sbank1", si))
                    smm = P.mm(P.PS[si][:, 0:512],
                               kb[64 * h2:64 * h2 + 64,
                                  hp * 4096 + kc * 128: hp * 4096 + (kc + 1) * 128],
                               qs[64 * h2:64 * h2 + 64, tau * 512:(tau + 1) * 512],
                               True, True, sname=sMM)
                    es = i % 3
                    P.wguard("scalar", ("exp1", es))
                    P.wait("scalar", sMM, smm)
                    ee = P.act("scalar", expb[es][:, :], P.PS[si][:, 0:512], AF.Exp,
                               scale=0.125)
                    P.setguard(("sbank1", si), "chscalar", ee)
                    P.wait("vector", "chscalar", ee)
                    em = P.op("vector", "tensor_scalar", expb[es][:, :],
                              expb[es][:, :], -1.0, None, OP.add)
                    return ("chvector", em)

                gts = {0: emit_sc1(0), 1: emit_sc1(1)}
                for i in range(4):
                    kc = 4 * tau + i
                    if i + 2 < 4:
                        gts[i + 2] = emit_sc1(i + 2)
                    gate = gts.pop(i)
                    es = i % 3
                    P.wait("tensor", gate[0], gate[1])
                    amv = P.mm(av[0:65, 0:512],
                               vb[:, kc * 1040 + h * 65: kc * 1040 + h * 65 + 65],
                               expb[es][:, :], i == 0, False, sname=sMM)
                    P.setguard(("exp1", es), sMM, amv)
                if DBG_S5GLOB:
                    P.mm(av[0:65, 0:512:64], vgbE[0:32, h * 65: h * 65 + 65],
                         pgE[0:32, 8 * tau: 8 * tau + 8], False, False, sname=sMM)
                    gmv = P.mm(av[0:65, 0:512:64],
                               vgbO[0:32, h * 65: h * 65 + 65],
                               pgO[0:32, 8 * tau: 8 * tau + 8], False, True,
                               sname=sMM)
                else:
                    gmv = P.mm(av[0:65, 0:512], vb[:, 4 * tau * 1040 + h * 65:
                                                   4 * tau * 1040 + h * 65 + 65],
                               expb[(3) % 3][:, :], False, True, sname=sMM)
                rsl = (h2 * 4 + tau) % 2
                P.wguard("scalar", ("rawst1", rsl))
                P.wait("scalar", sMM, gmv)
                ev = P.act("scalar", rawst[rsl][0:65, 0:512], av[0:65, 0:512],
                           AF.Identity, bias=vs65[0:65, h:h + 1])
                P.setguard(("bank", avi), "chscalar", ev)
                P.wait("sync", "chscalar", ev)
                stv = P.dma("sync", rawd[h, tau, :, :], rawst[rsl][0:65, 0:512],
                            "hr" + str(rsl))
                P.setguard(("rawst1", rsl), "hr" + str(rsl), stv)
            P.setguard(("g1",), sMM, P.val(sMM))
        P.setguard(("q1slot", hp % 2), sMM, P.val(sMM))
    P.pop()


def s5(P):
    emit_attn1(P)


STAGES["s5"] = (s5, ["q1", "k1", "v1", "vg1", "vsum1"], ["raw1"])


def s6(P):
    emit_proj_ln(P, "f", "raw1", 4, "wo1", "bo1", "x2p", "x3")


STAGES["s6"] = (s6, ["raw1", "wo1", "x2p"], ["x3"])



def emit_ffn1(P):
    P.push()
    x3b = P.alloc_b(8 * 2048)
    shgb = P.alloc_b(8 * 1024)
    shub = P.alloc_b(8 * 1024)
    shdb = P.alloc_b(8 * 1024)
    rt1wb = P.alloc_b(8 * 16)
    cwb = P.alloc_b(4 * 512)
    gateg = [P.alloc_b(8 * 512), P.alloc_b(8 * 512)]
    upg = [P.alloc_b(8 * 512), P.alloc_b(8 * 512)]
    downb = [P.alloc_b(1024) for _ in range(4)]
    hbt = [P.alloc_b(512) for _ in range(4)]
    gb = P.alloc_b(512)
    h1 = P.alloc_b(512)
    shb = P.alloc_b(8 * 512)
    logf = P.alloc_f(512)
    ltm = P.alloc_f(64)
    eqa = P.alloc_f(16)
    eqb = P.alloc_f(16)
    mx1 = P.alloc_f(1)
    mx2 = P.alloc_f(1)
    w1c = P.alloc_f(1)
    w2c = P.alloc_f(1)
    cwtm = P.alloc_f(16)
    ysb = P.alloc_f(8 * 512)
    x3t = [P.alloc_f(512), P.alloc_f(512)]
    ybf = P.alloc_b(512)
    y2b = P.alloc_b(512)
    ssum = P.alloc_f(512)
    sq = P.alloc_f(512)
    vrx = P.alloc_f(512)
    rstd = P.alloc_f(512)
    xo = [P.alloc_f(512), P.alloc_f(512)]

    x3d = P.dsrc("x3")
    hbd = P.dtmp("hb1")
    outd = P.dout_or_tmp("xout")
    EYE = CFc(P, "eye128f")
    ROWF = CFc(P, "ones_rowf", 0, 1)
    SCLF = CFc(P, "scale_rowf", 0, 1)
    OCOL = CB(P, "ones_col")
    B16 = CB(P, "B16", 0, 16)
    sMM = "jmm"

    for kc in range(8):
        P.dma("gpsimd", x3b[:, kc * 2048:(kc + 1) * 2048],
              x3d[128 * kc:128 * (kc + 1), :], "jlx")
        P.dma("gpsimd", shgb[:, kc * 1024:(kc + 1) * 1024],
              P.din("shg1")[128 * kc:128 * (kc + 1), :], "jlw")
        P.dma("gpsimd", shub[:, kc * 1024:(kc + 1) * 1024],
              P.din("shu1")[128 * kc:128 * (kc + 1), :], "jlw")
        P.dma("gpsimd", shdb[:, kc * 1024:(kc + 1) * 1024],
              P.din("shd1")[128 * kc:128 * (kc + 1), :], "jlw")
        P.dma("gpsimd", rt1wb[:, kc * 16:(kc + 1) * 16],
              P.din("rt1_w")[128 * kc:128 * (kc + 1), :], "jlw")
    P.wait("tensor", "jlx", P.val("jlx"))
    P.wait("tensor", "jlw", P.val("jlw"))

    for tau in range(4):
        P.wguard("tensor", ("bank", 0))
        for kc in range(8):
            P.mm(P.PS[0][0:16, 0:512], rt1wb[:, kc * 16:(kc + 1) * 16],
                 x3b[:, kc * 2048 + tau * 512: kc * 2048 + tau * 512 + 512],
                 kc == 0, kc == 7, sname=sMM if kc == 7 else None)
        P.wait("scalar", sMM, P.val(sMM))
        lv = P.act("scalar", logf[0:16, 0:512], P.PS[0][0:16, 0:512], AF.Identity,
                   bias=BPc(P, "rt1_b", 0, 0, 16))
        P.setguard(("bank", 0), "chscalar", lv)
        P.wait("tensor", "chscalar", lv)
        P.wguard("tensor", ("bank", 1))
        for j in range(4):
            tv = P.op("tensor", "transpose", P.PS[1][0:128, j * 16:(j + 1) * 16],
                      logf[0:16, 128 * j:128 * (j + 1)], EYE[0:16, 0:16], sname=sMM)
        P.wait("vector", sMM, tv)
        lc = P.op("vector", "tensor_copy", ltm[0:128, 0:64], P.PS[1][0:128, 0:64])
        P.setguard(("bank", 1), "chvector", lc)
        P.wguard("tensor", ("bank", 2))
        for j in range(4):
            lj = ltm[0:128, j * 16:(j + 1) * 16]
            if j > 0:
                P.wait("vector", "chscalar", P.val("chscalar"))
                P.wait("vector", sMM, P.val(sMM))
            P.op("vector", "reduce_max", mx1[0:128, 0:1], lj, AX)
            P.op("vector", "tensor_scalar", eqa[0:128, 0:16], lj, mx1[0:128, 0:1],
                 None, OP.is_equal)
            P.op("vector", "scalar_tensor_tensor", eqb[0:128, 0:16], eqa[0:128, 0:16],
                 -1e30, lj, OP.mult, OP.add)
            P.op("vector", "reduce_max", mx2[0:128, 0:1], eqb[0:128, 0:16], AX)
            P.op("vector", "tensor_scalar", eqb[0:128, 0:16], eqb[0:128, 0:16],
                 mx2[0:128, 0:1], None, OP.is_equal)
            wv_ = P.op("vector", "tensor_sub", w1c[0:128, 0:1], mx1[0:128, 0:1],
                       mx2[0:128, 0:1])
            P.wait("scalar", "chvector", wv_)
            sv_ = P.act("scalar", w1c[0:128, 0:1], w1c[0:128, 0:1], AF.Sigmoid)
            P.wait("vector", "chscalar", sv_)
            P.op("vector", "tensor_scalar", w2c[0:128, 0:1], w1c[0:128, 0:1],
                 -1.0, 1.0, OP.mult, OP.add)
            P.op("vector", "tensor_scalar", cwtm[0:128, 0:16], eqa[0:128, 0:16],
                 w1c[0:128, 0:1], None, OP.mult)
            P.op("vector", "tensor_scalar", eqb[0:128, 0:16], eqb[0:128, 0:16],
                 w2c[0:128, 0:1], None, OP.mult)
            cv_ = P.op("vector", "tensor_add", cwtm[0:128, 0:16], cwtm[0:128, 0:16],
                       eqb[0:128, 0:16])
            P.wait("tensor", "chvector", cv_)
            P.op("tensor", "transpose", P.PS[2][0:16, j * 128:(j + 1) * 128],
                 cwtm[0:128, 0:16], EYE[0:128, 0:128], sname=sMM)
        tv2 = P.val(sMM)
        P.wait("vector", sMM, tv2)
        cc_ = P.op("vector", "tensor_copy", cwb[0:16, tau * 512:(tau + 1) * 512],
                   P.PS[2][0:16, 0:512])
        P.setguard(("bank", 2), "chvector", cc_)

    P.wait("tensor", "chvector", P.val("chvector"))
    for g in range(16):
        s = g % 2
        P.wguard("gpsimd", ("jgu", s))
        for kc in range(8):
            P.dma("gpsimd", gateg[s][:, kc * 512:(kc + 1) * 512],
                  P.din("gate1")[128 * kc:128 * (kc + 1), g * 512:(g + 1) * 512],
                  "jlg" + str(s))
            P.dma("gpsimd", upg[s][:, kc * 512:(kc + 1) * 512],
                  P.din("up1")[128 * kc:128 * (kc + 1), g * 512:(g + 1) * 512],
                  "jlu" + str(s))
        P.wait("tensor", "jlg" + str(s), P.val("jlg" + str(s)))
        P.wait("tensor", "jlu" + str(s), P.val("jlu" + str(s)))
        for mq in range(4):
            mh = g * 4 + mq
            for tp in range(4):
                bG = P.PS[0 + 2 * (mh % 2)]
                bU = P.PS[1 + 2 * (mh % 2)]
                bM = P.PS[4 + (mh % 2)]
                P.wguard("tensor", ("bank", 0 + 2 * (mh % 2)))
                P.wguard("tensor", ("bank", 1 + 2 * (mh % 2)))
                P.wguard("tensor", ("bank", 4 + (mh % 2)))
                for kc in range(8):
                    P.mm(bG[:, 0:512],
                         gateg[s][:, kc * 512 + mq * 128: kc * 512 + (mq + 1) * 128],
                         x3b[:, kc * 2048 + tp * 512: kc * 2048 + tp * 512 + 512],
                         kc == 0, kc == 7)
                    P.mm(bU[:, 0:512],
                         upg[s][:, kc * 512 + mq * 128: kc * 512 + (mq + 1) * 128],
                         x3b[:, kc * 2048 + tp * 512: kc * 2048 + tp * 512 + 512],
                         kc == 0, kc == 7)
                bmv = P.mm(bM[:, 0:512], B16[0:16, mh * 128:(mh + 1) * 128],
                           cwb[0:16, tp * 512:(tp + 1) * 512], True, True, sname=sMM)
                P.wait("scalar", sMM, bmv)
                P.wguard("scalar", ("jgb",))
                sv_ = P.act("scalar", gb[:, :], bG[:, 0:512], AF.Silu)
                P.setguard(("bank", 0 + 2 * (mh % 2)), "chscalar", sv_)
                P.wait("vector", "chscalar", sv_)
                P.op("vector", "tensor_mul", h1[:, :], gb[:, :], bU[:, 0:512])
                P.wguard("vector", ("jhb", mh % 2))
                hv_ = P.op("vector", "tensor_mul", hbt[mh % 2][:, :], h1[:, :],
                           bM[:, 0:512])
                P.setguard(("jgb",), "chvector", hv_)
                P.setguard(("bank", 1 + 2 * (mh % 2)), "chvector", hv_)
                P.setguard(("bank", 4 + (mh % 2)), "chvector", hv_)
                P.wguard("sync", ("jhb", mh % 2))
                P.wait("sync", "chvector", hv_)
                so_ = P.dma("sync", hbd[mh, tp, :, :], hbt[mh % 2][:, :],
                            "jsh" + str(mh % 2))
                P.setguard(("jhb", mh % 2), "jsh" + str(mh % 2), so_)
        P.setguard(("jgu", s), sMM, P.val(sMM))
    P.fence()

    for tau in range(4):
        for m in range(8):
            bG = P.PS[0 + 2 * (m % 2)]
            bU = P.PS[1 + 2 * (m % 2)]
            P.wguard("tensor", ("bank", 0 + 2 * (m % 2)))
            P.wguard("tensor", ("bank", 1 + 2 * (m % 2)))
            for kc in range(8):
                P.mm(bG[:, 0:512],
                     shgb[:, kc * 1024 + m * 128: kc * 1024 + (m + 1) * 128],
                     x3b[:, kc * 2048 + tau * 512: kc * 2048 + tau * 512 + 512],
                     kc == 0, kc == 7)
                P.mm(bU[:, 0:512],
                     shub[:, kc * 1024 + m * 128: kc * 1024 + (m + 1) * 128],
                     x3b[:, kc * 2048 + tau * 512: kc * 2048 + tau * 512 + 512],
                     kc == 0, kc == 7, sname=sMM if kc == 7 else None)
            smv = P.val(sMM)
            P.wait("scalar", sMM, smv)
            P.wguard("scalar", ("jshb",))
            sv_ = P.act("scalar", gb[:, :], bG[:, 0:512], AF.Silu)
            P.setguard(("bank", 0 + 2 * (m % 2)), "chscalar", sv_)
            P.wait("vector", "chscalar", sv_)
            shv = P.op("vector", "tensor_mul", shb[:, m * 512:(m + 1) * 512],
                       gb[:, :], bU[:, 0:512])
            P.setguard(("bank", 1 + 2 * (m % 2)), "chvector", shv)
            P.setguard(("jshb",), "chvector", shv)
        P.wait("tensor", "chvector", P.val("chvector"))
        for dm in range(8):
            P.wguard("tensor", ("bank", dm))
        def emit_dload(mh):
            s = mh % 4
            P.wguard("gpsimd", ("jdw", s))
            dl = P.dma("gpsimd", downb[s][:, :],
                       P.din("down1")[128 * mh:128 * (mh + 1), :], "jld" + str(s))
            P.wguard("sync", ("jht", s))
            hl = P.dma("sync", hbt[s][:, :], hbd[mh, tau, :, :], "jlh" + str(s))
            return dl, hl

        lds = {0: emit_dload(0), 1: emit_dload(1), 2: emit_dload(2)}
        for mh in range(64):
            s = mh % 4
            if mh + 3 < 64:
                lds[mh + 3] = emit_dload(mh + 3)
            dl, hl = lds.pop(mh)
            P.wait("tensor", "jld" + str(s), dl)
            P.wait("tensor", "jlh" + str(s), hl)
            for dm in range(8):
                mk = P.mm(P.PS[dm][:, 0:512],
                          downb[s][:, dm * 128:(dm + 1) * 128], hbt[s][:, :],
                          mh == 0, False, sname=sMM if dm == 7 else None)
            P.setguard(("jdw", s), sMM, P.val(sMM))
            P.setguard(("jht", s), sMM, P.val(sMM))
        for kc in range(8):
            for dm in range(8):
                P.mm(P.PS[dm][:, 0:512],
                     shdb[:, kc * 1024 + dm * 128: kc * 1024 + (dm + 1) * 128],
                     shb[:, kc * 512:(kc + 1) * 512], False, kc == 7,
                     sname=sMM if dm == 7 else None)
        dnv = P.val(sMM)
        P.wait("scalar", sMM, dnv)
        for dm in range(8):
            rs = dm % 2
            P.wguard("sync", ("jxr", rs))
            P.dma("sync", x3t[rs][:, 0:512],
                  x3d[128 * dm:128 * (dm + 1), tau * 512:(tau + 1) * 512],
                  "jlr" + str(rs))
            ev = P.act("scalar", ysb[:, dm * 512:(dm + 1) * 512],
                       P.PS[dm][:, 0:512], AF.Copy)
            P.setguard(("bank", dm), "chscalar", ev)
            P.wait("vector", "chscalar", ev)
            P.wait("vector", "jlr" + str(rs), P.val("jlr" + str(rs)))
            av = P.op("vector", "tensor_add", ysb[:, dm * 512:(dm + 1) * 512],
                      ysb[:, dm * 512:(dm + 1) * 512], x3t[rs][:, 0:512])
            P.setguard(("jxr", rs), "chvector", av)
        for dm in range(8):
            P.wguard("vector", ("jyb",))
            P.op("vector", "tensor_copy", ybf[:, :], ysb[:, dm * 512:(dm + 1) * 512])
            cv = P.op("vector", "tensor_mul", y2b[:, :], ybf[:, :], ybf[:, :])
            P.wait("tensor", "chvector", cv)
            if dm == 0:
                P.wguard("tensor", ("bank", 4))
                P.wguard("tensor", ("bank", 5))
            P.mm(P.PS[4][0:1, 0:512], OCOL, ybf[:, :], dm == 0, dm == 7)
            sm = P.mm(P.PS[5][0:1, 0:512], OCOL, y2b[:, :], dm == 0, dm == 7,
                      sname=sMM)
            P.setguard(("jyb",), sMM, sm)
        stat_mm = P.val(sMM)
        P.wait("scalar", sMM, stat_mm)
        P.act("scalar", ssum[0:1, 0:512], P.PS[4][0:1, 0:512], AF.Copy)
        c2 = P.act("scalar", sq[0:1, 0:512], P.PS[5][0:1, 0:512], AF.Copy)
        P.setguard(("bank", 4), "chscalar", c2)
        P.setguard(("bank", 5), "chscalar", c2)
        P.wait("vector", "chscalar", c2)
        P.op("vector", "tensor_mul", vrx[0:1, 0:512], ssum[0:1, 0:512],
             ssum[0:1, 0:512])
        P.op("vector", "scalar_tensor_tensor", vrx[0:1, 0:512], vrx[0:1, 0:512],
             -1.0 / 1024.0, sq[0:1, 0:512], OP.mult, OP.add)
        sv2 = P.op("vector", "tensor_scalar", vrx[0:1, 0:512], vrx[0:1, 0:512],
                   1.0 / 1024.0, 1e-5, OP.mult, OP.add)
        P.wait("scalar", "chvector", sv2)
        P.act("scalar", vrx[0:1, 0:512], vrx[0:1, 0:512], AF.Ln)
        rv = P.act("scalar", rstd[0:1, 0:512], vrx[0:1, 0:512], AF.Exp, scale=-0.5)
        P.wait("tensor", "chscalar", rv)
        P.wguard("tensor", ("bank", 6))
        P.wguard("tensor", ("bank", 7))
        P.mm(P.PS[6][0:128, 0:512], SCLF, ssum[0:1, 0:512], True, True)
        bm2 = P.mm(P.PS[7][0:128, 0:512], ROWF, rstd[0:1, 0:512], True, True,
                   sname=sMM)
        P.wait("vector", sMM, bm2)
        for dm in range(8):
            xs = xo[dm % 2]
            P.wguard("vector", ("jxo", dm % 2))
            P.op("vector", "tensor_sub", xs[:, 0:512],
                 ysb[:, dm * 512:(dm + 1) * 512], P.PS[6][0:128, 0:512])
            xv = P.op("vector", "tensor_mul", xs[:, 0:512], xs[:, 0:512],
                      P.PS[7][0:128, 0:512])
            P.wait("sync", "chvector", xv)
            so_ = P.dma("sync", outd[128 * dm:128 * (dm + 1),
                                     tau * 512:(tau + 1) * 512], xs[:, 0:512],
                        "jso" + str(dm % 2))
            P.setguard(("jxo", dm % 2), "jso" + str(dm % 2), so_)
        P.setguard(("bank", 6), "chvector", P.val("chvector"))
        P.setguard(("bank", 7), "chvector", P.val("chvector"))
    P.pop()


def s7(P):
    emit_ffn1(P)


STAGES["s7"] = (s7, ["x3", "rt1_w", "up1", "gate1", "down1", "shg1", "shu1",
                     "shd1"], ["xout"])


def _dout_or_tmp(P, name):
    if name in P.export_names:
        return P.dout(name)
    return P.dtmp(name)


Prog.dout_or_tmp = _dout_or_tmp



def s0(P):
    emit_qkv(P, "a", "x0T",
             ("w_attn", 0), ("w_attn", 1024), ("w_attn", 2048),
             ("b_attn", 0), ("b_attn", 8), "bv_row",
             "q0", "k0", "v0", nq_tiles=8)


STAGES["s0"] = (s0, ["x0T", "w_attn", "bv_row"], ["q0", "k0", "v0"])


def build_launch(stages, export_names):
    P = Prog()
    P.export_names = export_names
    emit_consts(P)
    P.stage_bounds = {}
    for s in stages:
        P.cur_scope = s
        STAGES[s][0](P)
        P.fence()
        P.stage_bounds[s] = {e: len(P.streams[e]) for e in ENGS}
    P.cur_scope = None
    for name, rec in list(P.sems.items()):
        if rec[1] > 0:
            P.wait("sync", name, rec[1])
    P.finalize()
    return P


_BUILD_CACHE = {}


def kernel(**inputs):
    sh, per = host_prep(inputs)
    env = [dict() for _ in range(8)]
    produced_by_launch = []
    all_stage_list = [s for grp in LAUNCH_PLAN for s in grp]
    for li, grp in enumerate(LAUNCH_PLAN):
        later = set(s for g in LAUNCH_PLAN[li + 1:] for s in g)
        produces = set(p for s in grp for p in STAGES[s][2])
        consumed_later = set()
        for s in later:
            consumed_later.update(STAGES[s][1])
        exports = (produces & consumed_later) | (produces & {"xout"})
        if li == len(LAUNCH_PLAN) - 1:
            exports |= produces
        key = (li, tuple(grp), tuple(sorted(exports)))
        if key not in _BUILD_CACHE:
            _BUILD_CACHE[key] = build_launch(grp, exports)
        P = _BUILD_CACHE[key]
        in_maps = []
        for c in range(8):
            m = {}
            for name in P.inputs:
                if name in sh:
                    m[name] = sh[name]
                elif name in per[c]:
                    m[name] = per[c][name]
                else:
                    m[name] = env[c][name]
            in_maps.append(m)
        res = run_bass_kernel_spmd(P.nc, in_maps, core_ids=list(range(8)))
        for c in range(8):
            env[c].update(res.results[c])
    out = np.zeros((4, 4096, 1024), np.float32)
    for c in range(8):
        b, h = c // 2, c % 2
        if "xout" in env[c]:
            out[b, h * 2048:(h + 1) * 2048, :] = env[c]["xout"].T
    kernel.env = env
    return out


# version 13
# speedup vs baseline: 1.0438x; 1.0438x over previous
import numpy as np
import ml_dtypes
from contextlib import ExitStack

import concourse.bass as bass
import concourse.mybir as mybir
from concourse.bass_utils import run_bass_kernel_spmd

F32 = mybir.dt.float32
BF16 = mybir.dt.bfloat16
I32 = mybir.dt.int32
AF = mybir.ActivationFunctionType
OP = mybir.AluOpType
BF = ml_dtypes.bfloat16

RSQ2 = 0.70710678118654752

SEM_ALIAS = {}
for _n in ("amm", "bmm", "cmm", "dmm", "emm", "fmm", "gmm", "hmm", "jmm"):
    SEM_ALIAS[_n] = "Tmm"
SEM_ALIAS["bms"] = "Tms"
for _t in ("a", "e"):
    SEM_ALIAS[_t + "lx"] = "Glx"
    SEM_ALIAS[_t + "lb"] = "Glb"
    SEM_ALIAS[_t + "lw0"] = "Glw0"
    SEM_ALIAS[_t + "lw1"] = "Glw1"
    for _i in range(4):
        SEM_ALIAS[_t + "st" + str(_i)] = "So" + str(_i)
    for _i in range(2):
        SEM_ALIAS[_t + "stv" + str(_i)] = "Sv" + str(_i)
SEM_ALIAS.update({"bkld": "Gkl", "hkld": "Gkl", "bvld": "Gvl", "hvld": "Gvl",
                  "dlw": "Gw", "dlw2": "Gw", "jlw": "Gw", "dli": "Gli",
                  "jlx": "Gli", "hgld": "Sg", "gst": "Sgst"})
for _i in range(4):
    SEM_ALIAS["jld" + str(_i)] = "Gd" + str(_i)
    SEM_ALIAS["jlh" + str(_i)] = "Sjh" + str(_i)
for _i in range(2):
    SEM_ALIAS["bq" + str(_i)] = "SQ" + str(_i)
    SEM_ALIAS["hq" + str(_i)] = "SQ" + str(_i)
    SEM_ALIAS["br" + str(_i)] = "Sw" + str(_i)
    SEM_ALIAS["hr" + str(_i)] = "Sw" + str(_i)
    SEM_ALIAS["jlg" + str(_i)] = "Gg" + str(_i)
    SEM_ALIAS["jlu" + str(_i)] = "Gu" + str(_i)
    SEM_ALIAS["jld" + str(_i)] = "Gd" + str(_i)
    SEM_ALIAS["jlh" + str(_i)] = "Gh" + str(_i)
    SEM_ALIAS["gld" + str(_i)] = "Gw2" + str(_i)
    SEM_ALIAS["pli" + str(_i)] = "Gp" + str(_i)
    SEM_ALIAS["pgi" + str(_i)] = "Gi" + str(_i)
    SEM_ALIAS["pst" + str(_i)] = "Sp" + str(_i)
    SEM_ALIAS["jsh" + str(_i)] = "Sh" + str(_i)
    SEM_ALIAS["jso" + str(_i)] = "Sy" + str(_i)
    SEM_ALIAS["dso" + str(_i)] = "Sy" + str(_i)
    SEM_ALIAS["cso" + str(_i)] = "Sy" + str(_i)
    SEM_ALIAS["fso" + str(_i)] = "Sy" + str(_i)
    SEM_ALIAS["jlr" + str(_i)] = "Sr" + str(_i)
    SEM_ALIAS["dlr" + str(_i)] = "Sr" + str(_i)
    SEM_ALIAS["clr" + str(_i)] = "Sr" + str(_i)
    SEM_ALIAS["flr" + str(_i)] = "Sr" + str(_i)
    SEM_ALIAS["clx" + str(_i)] = "Sx" + str(_i)
    SEM_ALIAS["flx" + str(_i)] = "Sx" + str(_i)
for _i in range(4):
    SEM_ALIAS["clq" + str(_i)] = "Sq" + str(_i)
    SEM_ALIAS["flq" + str(_i)] = "Sq" + str(_i)
SEM_ALIAS.update({"cld": "Scl", "fld": "Scl", "clw": "Scw", "flw": "Scw",
                  "cst": "Sct", "fst": "Sct", "dlx": "Gdx", "dsx": "Sdx2"})
DBG_PHASES = ("q", "k", "v")
DBG_MLIM = 8
DBG_TBLIM = 32
DBG_TAULIM = 8
DBG_S3TAU = 8
DBG_S5HP = 8
DBG_S5GLOB = 1
ENGS = ("tensor", "scalar", "vector", "gpsimd", "sync")

LAUNCH_PLAN = [["s0", "s1", "s2", "s3", "sp", "s4", "s5", "s6", "s7"]]

ACT_SHAPES = {
    "x0T": ([1024, 4096], F32),
    "q0": ([1024, 4096], BF16),
    "k0": ([1024, 4096], BF16),
    "v0": ([32, 128, 1040], BF16),
    "raw0": ([16, 8, 65, 512], F32),
    "x1": ([1024, 4096], F32),
    "x2": ([1024, 4096], F32),
    "x2p": ([1024, 4096], F32),
    "q1": ([1024, 2048], BF16),
    "k1": ([1024, 4096], BF16),
    "v1": ([32, 128, 1040], BF16),
    "vg1": ([64, 1040], BF16),
    "vsum1": ([65, 16], F32),
    "raw1": ([16, 4, 65, 512], F32),
    "x3": ([1024, 2048], F32),
    "hb1": ([64, 4, 128, 512], BF16),
    "xout": ([1024, 2048], F32),
    "permidx": ([2048, 1], I32),
    "c_rd": ([128, 512], F32),
    "intg": ([1024, 4096], BF16),
    "f_rd": ([64, 512], F32),
}

WEIGHT_SHAPES = {
    "w_attn": [1024, 3072], "w_proj0": [1024, 1024],
    "wq1": [1024, 1024], "wk1": [1024, 1024], "wv1": [1024, 1024], "wo1": [1024, 1024],
    "r_w1": [1024, 256], "r_w2": [256, 8], "h_w1": [1024, 256], "h_w2": [256, 1],
    "e_w1": [2048, 512], "e_w2": [512, 3072], "e_b2": [8, 3072],
    "s_w1": [2048, 64], "s_w2": [64, 3072],
    "f_w1": [1024, 2048], "f_w2": [2048, 1024],
    "rt1_w": [1024, 16], "up1": [1024, 8192], "gate1": [1024, 8192],
    "down1": [8192, 1024], "shg1": [1024, 1024], "shu1": [1024, 1024],
    "shd1": [1024, 1024],
    "bv_row": [1, 1024], "bv1_row": [1, 1024],
}

STAGES = {}



BIAS_COLS = {}
N_BIAS_COLS = 0
CONST_COLS = {}
N_CONSTB = 0


def _build_bias_pack(inp):
    global BIAS_COLS, N_BIAS_COLS
    BIAS_COLS = {}
    cols = []

    def addvec(name, v):
        v = np.asarray(v, np.float32).reshape(-1)
        ncol = (v.size + 127) // 128
        a = np.zeros((ncol, 128), np.float32)
        a.reshape(-1)[:v.size] = v
        BIAS_COLS[name] = len(cols)
        cols.extend(list(a))

    addvec("r_b1", inp["L0_ffn_router_b1"])
    addvec("h_b1", inp["L0_ffn_halt_b1"])
    addvec("e_b1", inp["L0_ffn_expert_b1"].reshape(-1))
    addvec("s_b1", inp["L0_ffn_shared_b1"])
    addvec("f_b1", inp["L0_ffn_refine_b1"])
    BIAS_COLS["_gelu_end"] = len(cols)
    addvec("b_attn", inp["L0_attn_b_attn"])
    addvec("b_proj0", inp["L0_attn_b_proj"])
    addvec("r_b2", inp["L0_ffn_router_b2"])
    addvec("h_b2", inp["L0_ffn_halt_b2"])
    addvec("s_b2", inp["L0_ffn_shared_b2"])
    addvec("f_b2", inp["L0_ffn_refine_b2"])
    addvec("mu", inp["L0_ffn_mu"])
    addvec("iw", inp["L0_ffn_integration_weight"])
    addvec("sw", inp["L0_ffn_shared_weight"])
    addvec("bq1", inp["L1_attn_bq"])
    addvec("bk1", inp["L1_attn_bk"])
    addvec("bv1", inp["L1_attn_bv"])
    addvec("bo1", inp["L1_attn_bo"])
    addvec("rt1_b", inp["L1_ffn_router_bias"])
    N_BIAS_COLS = len(cols)
    return np.ascontiguousarray(np.stack(cols, axis=1), np.float32)


def _build_constb():
    global CONST_COLS, N_CONSTB
    CONST_COLS = {}
    parts = []
    off = 0

    def add(name, arr):
        nonlocal off
        a = np.zeros((128, arr.shape[1]), np.float32)
        a[:arr.shape[0], :] = arr
        CONST_COLS[name] = (off, arr.shape[1])
        parts.append(a)
        off += arr.shape[1]

    p = np.arange(128).reshape(128, 1)
    col = np.arange(896).reshape(1, 896)
    add("maskCM", (p <= col - 384).astype(np.float32))
    add("eye128", np.eye(128, dtype=np.float32))
    j = np.arange(512)
    add("B8", ((j // 64)[None, :] == np.arange(8)[:, None]).astype(np.float32))
    j = np.arange(8192)
    add("B16", ((j // 512)[None, :] == np.arange(16)[:, None]).astype(np.float32))
    add("ones_row", np.ones((1, 128), np.float32))
    add("ones_col", np.ones((128, 1), np.float32))
    m = np.zeros((64, 32), np.float32)
    for g in range(64):
        jj = 2 * g if g < 32 else 2 * (g - 32) + 1
        if jj < 32:
            for i in range(32):
                if i // 8 == jj // 8:
                    m[g, i] = 1.0
    add("corrmask", m)
    N_CONSTB = off
    return np.ascontiguousarray(np.concatenate(parts, axis=1).astype(BF))


def _build_constf():
    global CONSTF_COLS, N_CONSTF
    CONSTF_COLS = {}
    parts = []
    off = 0

    def add(name, arr):
        nonlocal off
        a = np.zeros((128, arr.shape[1]), np.float32)
        a[:arr.shape[0], :] = arr
        CONSTF_COLS[name] = (off, arr.shape[1])
        parts.append(a)
        off += arr.shape[1]

    add("eye128f", np.eye(128, dtype=np.float32))
    add("ones_rowf", np.ones((1, 128), np.float32))
    add("scale_rowf", np.full((1, 128), 1.0 / 1024.0, np.float32))
    N_CONSTF = off
    return np.ascontiguousarray(np.concatenate(parts, axis=1), np.float32)


def host_prep(inp):
    sh = {}
    sh["w_attn"] = np.ascontiguousarray(inp["L0_attn_w_attn"], np.float32)
    sh["w_proj0"] = np.ascontiguousarray(inp["L0_attn_w_proj"], np.float32)
    sh["wq1"] = np.ascontiguousarray(inp["L1_attn_wq"].T, np.float32)
    sh["wk1"] = np.ascontiguousarray(inp["L1_attn_wk"].T, np.float32)
    sh["wv1"] = np.ascontiguousarray(inp["L1_attn_wv"].T, np.float32)
    sh["wo1"] = np.ascontiguousarray(inp["L1_attn_wo"].T, np.float32)
    sh["r_w1"] = np.ascontiguousarray(inp["L0_ffn_router_w1"], np.float32)
    sh["r_w2"] = np.ascontiguousarray(inp["L0_ffn_router_w2"], np.float32)
    sh["h_w1"] = np.ascontiguousarray(inp["L0_ffn_halt_w1"], np.float32)
    sh["h_w2"] = np.ascontiguousarray(inp["L0_ffn_halt_w2"], np.float32)
    sh["e_w1"] = np.ascontiguousarray(
        inp["L0_ffn_expert_w1"].transpose(1, 0, 2).reshape(2048, 512), np.float32)
    sh["e_w2"] = np.ascontiguousarray(
        inp["L0_ffn_expert_w2"].reshape(512, 3072), np.float32)
    sh["e_b2"] = np.ascontiguousarray(inp["L0_ffn_expert_b2"], np.float32)
    sh["s_w1"] = np.ascontiguousarray(inp["L0_ffn_shared_w1"], np.float32)
    sh["s_w2"] = np.ascontiguousarray(inp["L0_ffn_shared_w2"], np.float32)
    sh["f_w1"] = np.ascontiguousarray(inp["L0_ffn_refine_w1"], np.float32)
    sh["f_w2"] = np.ascontiguousarray(inp["L0_ffn_refine_w2"], np.float32)
    sh["rt1_w"] = np.ascontiguousarray(inp["L1_ffn_router_w"].T, np.float32)
    sh["up1"] = np.ascontiguousarray(
        inp["L1_ffn_up_proj"].transpose(1, 0, 2).reshape(1024, 8192), np.float32)
    sh["gate1"] = np.ascontiguousarray(
        inp["L1_ffn_gate_proj"].transpose(1, 0, 2).reshape(1024, 8192), np.float32)
    sh["down1"] = np.ascontiguousarray(
        inp["L1_ffn_down_proj"].reshape(8192, 1024), np.float32)
    sh["shg1"] = np.ascontiguousarray(inp["L1_ffn_sh_gate"].T, np.float32)
    sh["shu1"] = np.ascontiguousarray(inp["L1_ffn_sh_up"].T, np.float32)
    sh["shd1"] = np.ascontiguousarray(inp["L1_ffn_sh_down"].T, np.float32)
    sh["bv_row"] = np.ascontiguousarray(
        inp["L0_attn_b_attn"][2048:3072].reshape(1, 1024), np.float32)
    sh["bv1_row"] = np.ascontiguousarray(
        inp["L1_attn_bv"].reshape(1, 1024), np.float32)
    sh["bias_pack"] = _build_bias_pack(inp)
    sh["constb"] = _build_constb()
    sh["constf"] = _build_constf()

    per = []
    for c in range(8):
        b, h = c // 2, c % 2
        d = {}
        d["x0T"] = np.ascontiguousarray(inp["L0_attn_hidden_states"][b].T, np.float32)
        d["permidx"] = (np.arange(2048, dtype=np.int32) ^ h).reshape(2048, 1)
        per.append(d)
    return sh, per



class Prog:
    def __init__(self):
        self.nc = bass.Bass("TRN2", target_bir_lowering=False)
        self.es = ExitStack()
        self.inputs = set()
        self.outputs = set()
        self.dram = {}
        self.sems = {}
        self.streams = {e: [] for e in ENGS}
        self.waited = {e: {} for e in ENGS}
        self.guards = {}
        self.AR = self.es.enter_context(self.nc.sbuf_tensor("AR", [128, 101376], BF16))
        self.ar_free = 0
        self.ar_base_stack = []
        self.PSH = [self.es.enter_context(
            self.nc.psum_tensor(f"PSH{i}", [128, 1024], F32)) for i in range(4)]
        self.PS = [self.PSH[i // 2][:, (i % 2) * 512:(i % 2) * 512 + 512]
                   for i in range(8)]

    def din(self, name):
        if name not in self.dram:
            if name == "bias_pack":
                shape, dt = [128, N_BIAS_COLS], F32
            elif name == "constb":
                shape, dt = [128, N_CONSTB], BF16
            elif name == "constf":
                shape, dt = [128, N_CONSTF], F32
            elif name in ACT_SHAPES:
                shape, dt = ACT_SHAPES[name]
            else:
                shape, dt = WEIGHT_SHAPES[name], F32
            self.dram[name] = self.nc.dram_tensor(
                name, list(shape), dt, kind="ExternalInput").ap()
            self.inputs.add(name)
        return self.dram[name]

    def dout(self, name):
        if name not in self.dram:
            shape, dt = ACT_SHAPES[name]
            self.dram[name] = self.nc.dram_tensor(
                name, list(shape), dt, kind="ExternalOutput").ap()
            self.outputs.add(name)
        return self.dram[name]

    def dsrc(self, name):
        if name in self.dram:
            return self.dram[name]
        return self.din(name)

    def fence(self):
        for sname, rec in list(self.sems.items()):
            if rec[1] > 0:
                for eng in ENGS:
                    self.wait(eng, sname, rec[1])

    def dtmp(self, name):
        if name not in self.dram:
            shape, dt = ACT_SHAPES[name]
            self.dram[name] = self.nc.dram_tensor(
                name, list(shape), dt, kind="Internal").ap()
        return self.dram[name]

    def alloc_b(self, n):
        off = self.ar_free
        self.ar_free += (n + 1) & ~1
        assert self.ar_free <= 101376, f"SBUF arena overflow: {self.ar_free}"
        return self.AR[:, off:off + n]

    def alloc_f(self, n):
        return self.alloc_b(2 * n).bitcast(F32)

    def push(self):
        self.ar_base_stack.append(self.ar_free)

    def pop(self):
        self.ar_free = self.ar_base_stack.pop()

    def sem(self, name):
        name = SEM_ALIAS.get(name, name)
        if name not in self.sems:
            h = self.es.enter_context(self.nc.semaphore(name))
            self.sems[name] = [h, 0]
        return self.sems[name][0]

    def tick(self, name, amt, agent=None):
        name = SEM_ALIAS.get(name, name)
        self.sem(name)
        s = self.sems[name]
        if agent is not None:
            if len(s) < 3:
                s.append(agent)
            else:
                assert s[2] == agent, (
                    f"semaphore {name} incremented by {agent} and {s[2]}")
        s[1] += amt
        return s[1]

    def val(self, name):
        name = SEM_ALIAS.get(name, name)
        return self.sems.get(name, [None, 0])[1]

    def emit(self, eng, fn):
        self.streams[eng].append((getattr(self, "cur_scope", None), fn))

    def wait(self, eng, name, v):
        name = SEM_ALIAS.get(name, name)
        if v is None or v <= 0:
            return
        w = self.waited[eng]
        if w.get(name, 0) >= v:
            return
        w[name] = v
        h = self.sem(name)
        self.emit(eng, lambda e, h=h, v=v: e.wait_ge(h, v))

    def wguard(self, eng, key):
        g = self.guards.get(key)
        if g is not None:
            self.wait(eng, g[0], g[1])

    def setguard(self, key, name, v):
        self.guards[key] = (name, v)

    def idma(self, out, in_, idx_ap, sname):
        h = self.sem(sname)
        self.emit("gpsimd", lambda e, o=out, i=in_, x=idx_ap, h=h:
                  e.indirect_dma_start(
                      out=o, out_offset=None, in_=i,
                      in_offset=bass.IndirectOffsetOnAxis(x, 0)).then_inc(h, 16))
        return self.tick(sname, 16, agent="dma-gpsimd")

    def dma(self, eng, out, in_, sname):
        h = self.sem(sname)
        self.emit(eng, lambda e, o=out, i=in_, h=h: e.dma_start(out=o, in_=i).then_inc(h, 16))
        return self.tick(sname, 16, agent="dma-" + eng)

    def mm(self, bank, lhsT, rhs, start, stop, sname=None):
        if sname is None:
            self.emit("tensor", lambda t, b=bank, l=lhsT, r=rhs, s1=start, s2=stop:
                      t.matmul(b, l, r, start=s1, stop=s2))
            return None
        h = self.sem(sname)
        self.emit("tensor", lambda t, b=bank, l=lhsT, r=rhs, s1=start, s2=stop, h=h:
                  t.matmul(b, l, r, start=s1, stop=s2).then_inc(h, 1))
        return self.tick(sname, 1, agent="tensor")

    def act(self, eng, out, in_, func, bias=0.0, scale=1.0, sname=None, accum=None):
        ch = "ch" + eng
        self.wait(eng, ch, self.val(ch))
        def f(e, o=out, i=in_, fu=func, b=bias, s=scale, a=accum):
            kw = {}
            if a is not None:
                kw["accum_out"] = a
            return e.activation(out=o, in_=i, func=fu, bias=b, scale=s, **kw)
        h = self.sem(ch)
        self.emit(eng, lambda e, h=h: f(e).then_inc(h, 1))
        return self.tick(ch, 1, agent=eng)

    def op(self, eng, meth, *args, sname=None, **kw):
        ch = "ch" + eng
        if eng != "tensor":
            self.wait(eng, ch, self.val(ch))
            assert sname in (None, ch), (eng, sname)
            sname = ch
        use = sname if sname is not None else ch
        h = self.sem(use)
        self.emit(eng, lambda e, a=args, k=kw, m=meth, h=h:
                  getattr(e, m)(*a, **k).then_inc(h, 1))
        return self.tick(use, 1, agent="tensor" if eng == "tensor" else eng)

    def finalize(self):
        def run(e, fns):
            cur = None
            cur_id = None
            for sc, f in fns:
                if sc != cur:
                    if cur is not None:
                        self.nc.leave_named_scope(cur, cur_id, False)
                    if sc is not None:
                        cur_id, _ = self.nc.enter_named_scope(sc, False)
                    cur = sc
                f(e)
            if cur is not None:
                self.nc.leave_named_scope(cur, cur_id, False)
        with self.nc.Block() as block:
            for eng in ENGS:
                fns = self.streams[eng]
                if not fns:
                    continue
                getattr(block, eng)(lambda e, fns=fns: run(e, fns))
        return self.nc



def emit_consts(P):
    cb = P.din("constb")
    cf = P.din("constf")
    bp = P.din("bias_pack")
    P.CB = P.alloc_b(N_CONSTB)
    P.CF = P.alloc_f(N_CONSTF)
    P.BP = P.alloc_f(N_BIAS_COLS)
    ng = BIAS_COLS["_gelu_end"]
    P.BPg_sc = P.alloc_f(ng)
    P.BPg_h = P.alloc_f(ng)
    P.BPn = P.alloc_f(24)
    P.dma("sync", P.CB[:, :], cb[:, :], "cld")
    P.dma("sync", P.CF[:, :], cf[:, :], "cld")
    v = P.dma("sync", P.BP[:, :], bp[:, :], "cld")
    P.wait("vector", "cld", v)
    P.op("vector", "tensor_scalar_mul", P.BPg_sc[:, :], P.BP[:, 0:ng], RSQ2)
    P.op("vector", "tensor_scalar_mul", P.BPg_h[:, :], P.BP[:, 0:ng], 0.5)
    sb2c0 = BIAS_COLS["s_b2"]
    P.op("vector", "tensor_scalar_mul", P.BPn[:, :],
         P.BP[:, sb2c0:sb2c0 + 24], -1.0)
    for eng in ENGS:
        P.wait(eng, "cld", P.val("cld"))
        P.wait(eng, "chvector", P.val("chvector"))


def CB(P, name, r0=0, r1=128, c0=0, c1=None):
    off, n = CONST_COLS[name]
    c1 = n if c1 is None else c1
    return P.CB[r0:r1, off + c0:off + c1]


def CFc(P, name, r0=0, r1=128, c0=0, c1=None):
    off, n = CONSTF_COLS[name]
    c1 = n if c1 is None else c1
    return P.CF[r0:r1, off + c0:off + c1]


def BPc(P, name, j=0, r0=0, r1=128):
    return P.BP[r0:r1, BIAS_COLS[name] + j: BIAS_COLS[name] + j + 1]



def emit_qkv(P, tag, xname, wq, wk, wv, bq, bk, bvrow_name, qname, kname, vname,
             nq_tiles):
    P.push()
    xb = P.alloc_b(8 * 4096)
    wbuf = [P.alloc_b(8 * 1024), P.alloc_b(8 * 1024)]
    ev = P.alloc_b(4 * 512)
    vsb = [P.alloc_b(1040), P.alloc_b(1040)]
    bvb = P.alloc_b(1024)
    sMM, sEV, sVV, sST = tag + "mm", "chscalar", "chvector", tag + "st"
    sLX, sLB = tag + "lx", tag + "lb"
    sLW = [tag + "lw0", tag + "lw1"]

    xsrc = P.dsrc(xname)
    bvrow = P.din(bvrow_name)
    qd, kd, vd = P.dout_or_tmp(qname), P.dout_or_tmp(kname), P.dout_or_tmp(vname)

    for kc in range(8):
        P.dma("gpsimd", xb[:, kc * 4096:(kc + 1) * 4096],
              xsrc[128 * kc:128 * (kc + 1), :], sLX)
    P.dma("gpsimd", bvb[0:1, :], bvrow[0:1, :], sLB)

    def load_w(phase, wspec):
        dname, col0 = wspec
        wd = P.din(dname)
        buf = wbuf[phase % 2]
        P.wguard("gpsimd", ("wbuf", tag, phase % 2))
        sem = sLW[phase % 2]
        for kc in range(8):
            P.dma("gpsimd", buf[:, kc * 1024:(kc + 1) * 1024],
                  wd[128 * kc:128 * (kc + 1), col0:col0 + 1024], sem)
        return buf, sem, P.val(sem)

    _phases = [("q", wq, nq_tiles, qd, bq), ("k", wk, 8, kd, bk)]
    _phases = [p for p in _phases if p[0] in DBG_PHASES]
    for phase, (_pn, wspec, ntile, outd, bcol) in enumerate(_phases):
        buf, wsem, wtot = load_w(phase, wspec)
        P.wait("tensor", wsem, wtot)
        P.wait("tensor", sLX, P.val(sLX))
        bname, bj = bcol
        for m in range(DBG_MLIM):
            for g0 in range(0, ntile, 4):
                gt = list(range(g0, min(g0 + 4, ntile)))
                bi = ((m * ((ntile + 3) // 4) + g0 // 4) % 2) * 4
                for ti in range(len(gt)):
                    P.wguard("tensor", ("bank", bi + ti))
                for kc in range(8):
                    lhs = buf[:, kc * 1024 + m * 128: kc * 1024 + (m + 1) * 128]
                    for ti, tile in enumerate(gt):
                        P.mm(P.PS[bi + ti][:, 0:512], lhs,
                             xb[:, kc * 4096 + tile * 512: kc * 4096 + tile * 512 + 512],
                             kc == 0, kc == 7,
                             sname=sMM if (kc == 7 and ti == len(gt) - 1) else None)
                mmv = P.val(sMM)
                P.wait("scalar", sMM, mmv)
                for ti, tile in enumerate(gt):
                    evslot = ev[:, ti * 512:(ti + 1) * 512]
                    P.wguard("scalar", ("ev", tag, ti))
                    e = P.act("scalar", evslot, P.PS[bi + ti][:, 0:512], AF.Identity,
                              bias=BPc(P, bname, bj + m), sname=sEV)
                    P.setguard(("bank", bi + ti), sEV, e)
                    P.wait("sync", sEV, e)
                    st = P.dma("sync", outd[128 * m:128 * (m + 1),
                                            tile * 512:(tile + 1) * 512], evslot,
                               sST + str(ti))
                    P.setguard(("ev", tag, ti), sST + str(ti), st)
        P.setguard(("wbuf", tag, phase % 2), sMM, P.val(sMM))

    if "v" not in DBG_PHASES:
        P.pop()
        return
    buf, wsem, wtot = load_w(len(_phases), wv)
    P.wait("tensor", wsem, wtot)
    P.wait("tensor", sLX, P.val(sLX))
    P.wait("tensor", sLB, P.val(sLB))
    for tb in range(DBG_TBLIM):
        sb = vsb[tb % 2]
        P.wguard("vector", ("vsb", tag, tb % 2))
        P.op("vector", "memset", sb[:, :], 1.0, sname=sVV)
        msetv = P.val(sVV)
        for dv in range(2):
            bank = P.PS[4 + 2 * (tb % 2) + dv]
            P.wguard("tensor", ("bank", 4 + 2 * (tb % 2) + dv))
            for kc in range(8):
                lhs = xb[:, kc * 4096 + tb * 128: kc * 4096 + (tb + 1) * 128]
                P.mm(bank[:, 0:512], lhs,
                     buf[:, kc * 1024 + dv * 512: kc * 1024 + dv * 512 + 512],
                     kc == 0, False)
            mmv = P.mm(bank[:, 0:512], CB(P, "ones_row", 0, 1),
                       bvb[0:1, dv * 512: dv * 512 + 512], False, True, sname=sMM)
            P.wait("vector", sMM, mmv)
            P.wait("vector", sVV, msetv)
            for hh in range(8):
                h = dv * 8 + hh
                P.op("vector", "tensor_copy",
                     sb[:, h * 65: h * 65 + 64],
                     bank[:, hh * 64:(hh + 1) * 64],
                     sname=sVV if hh == 7 else None)
            P.setguard(("bank", 4 + 2 * (tb % 2) + dv), sVV, P.val(sVV))
        P.wait("sync", sVV, P.val(sVV))
        st = P.dma("sync", vd[tb, :, :], sb[:, :], sST + "v" + str(tb % 2))
        P.setguard(("vsb", tag, tb % 2), sST + "v" + str(tb % 2), st)
    P.pop()



def emit_attn0(P):
    P.push()
    vb = P.alloc_b(32 * 1040)
    kpb = [P.alloc_b(8192), P.alloc_b(8192)]
    qpb = [P.alloc_b(8192), P.alloc_b(8192)]
    expp = [P.alloc_b(1024) for _ in range(3)]
    rawst = [P.alloc_f(512), P.alloc_f(512)]
    qd, kd, vd = P.dsrc("q0"), P.dsrc("k0"), P.dsrc("v0")
    rawd = P.dout_or_tmp("raw0")
    sMM, sSC, sVV = "bmm", "chscalar", "chvector"

    for tb in range(32):
        P.dma("gpsimd", vb[:, tb * 1040:(tb + 1) * 1040], vd[tb, :, :], "bvld")
    for buf in (kpb[0], kpb[1], qpb[0], qpb[1]):
        for c in range(16):
            P.op("vector", "memset", buf[64:128, c * 512:(c + 1) * 512], 0.0)
    P.wait("tensor", "chvector", P.val("chvector"))
    P.wait("tensor", "bvld", P.val("bvld"))

    CM = CB(P, "maskCM")
    SPAIR = [0, 2, 3]
    for hp in range(8):
        sl = hp % 2
        kp, qp = kpb[sl], qpb[sl]
        P.wguard("sync", ("qslot", sl))
        P.dma("sync", kp[0:64, 0:4096], kd[128 * hp:128 * hp + 64, :], "bq" + str(sl))
        P.dma("sync", kp[0:64, 4096:8192], kd[128 * hp + 64:128 * hp + 128, :],
              "bq" + str(sl))
        P.dma("sync", qp[0:64, 0:4096], qd[128 * hp:128 * hp + 64, :], "bq" + str(sl))
        P.dma("sync", qp[0:64, 4096:8192], qd[128 * hp + 64:128 * hp + 128, :],
              "bq" + str(sl))
        P.wait("tensor", "bq" + str(sl), P.val("bq" + str(sl)))
        for h2 in range(2):
            h = 2 * hp + h2

            def emit_pair(tau, pi):
                sp = SPAIR[pi % 3]
                ph = P.PSH[sp]
                P.wguard("tensor", ("spair", sp))
                for half in range(2):
                    kc = 2 * pi + half
                    P.mm(ph[:, half * 512:(half + 1) * 512],
                         kp[0:128, h2 * 4096 + kc * 128: h2 * 4096 + (kc + 1) * 128],
                         qp[0:128, h2 * 4096 + tau * 512: h2 * 4096 + (tau + 1) * 512],
                         True, True, sname="bms" if half == 1 else None)
                smm = P.val("bms")
                eslot = pi % 3
                P.wguard("scalar", ("expp", eslot))
                P.wait("scalar", "bms", smm)
                eex = P.act("scalar", expp[eslot][:, :], ph[:, 0:1024], AF.Exp,
                            scale=0.125, sname=sSC)
                P.setguard(("spair", sp), sSC, eex)
                gate = (sSC, eex)
                for half in range(2):
                    kc = 2 * pi + half
                    if kc >= 4 * tau:
                        i = kc - 4 * tau
                        P.wait("vector", sSC, eex)
                        emk = P.op("vector", "tensor_mul",
                                   expp[eslot][:, half * 512:(half + 1) * 512],
                                   expp[eslot][:, half * 512:(half + 1) * 512],
                                   CM[:, 384 - 128 * i: 384 - 128 * i + 512],
                                   sname=sVV)
                        gate = (sVV, emk)
                return gate

            for tau in range(DBG_TAULIM):
                npair = 2 * tau + 2
                avi = 2 + ((h2 * 8 + tau) % 2)
                av = P.PS[avi]
                P.wguard("tensor", ("bank", avi))
                gates = {}
                for pi in range(min(2, npair)):
                    gates[pi] = emit_pair(tau, pi)
                for pi in range(npair):
                    if pi + 2 < npair:
                        gates[pi + 2] = emit_pair(tau, pi + 2)
                    gate = gates.pop(pi)
                    eslot = pi % 3
                    P.wait("tensor", gate[0], gate[1])
                    for half in range(2):
                        kc = 2 * pi + half
                        P.mm(av[0:65, 0:512],
                             vb[:, kc * 1040 + h * 65: kc * 1040 + h * 65 + 65],
                             expp[eslot][:, half * 512:(half + 1) * 512],
                             kc == 0, kc == 4 * tau + 3,
                             sname=sMM if half == 1 else None)
                    P.setguard(("expp", eslot), sMM, P.val(sMM))
                rsl = (h2 * 8 + tau) % 2
                P.wguard("vector", ("rawst", rsl))
                P.wait("vector", sMM, P.val(sMM))
                ee = P.op("vector", "tensor_copy", rawst[rsl][0:65, 0:512],
                          av[0:65, 0:512])
                P.setguard(("bank", avi), "chvector", ee)
                P.wait("sync", "chvector", ee)
                stv = P.dma("sync", rawd[h, tau, :, :], rawst[rsl][0:65, 0:512],
                            "br" + str(rsl))
                P.setguard(("rawst", rsl), "br" + str(rsl), stv)
        P.setguard(("qslot", hp % 2), "bms", P.val("bms"))
    P.pop()


def s1(P):
    emit_attn0(P)


STAGES["s1"] = (s1, ["q0", "k0", "v0"], ["raw0"])



def emit_proj_ln(P, tag, raw_name, ntile, wname, bcol_name, res_name, out_name):
    P.push()
    nh = 16
    rawd = P.dsrc(raw_name)
    resd = P.dsrc(res_name)
    outd = P.dout_or_tmp(out_name)
    wd = P.din(wname)
    rdd = P.dtmp(tag + "_rd")
    sMM, sSC, sVV = tag + "mm", "chscalar", "chvector"

    wpb = P.alloc_b(8 * 1024)
    denf = P.alloc_f(512)
    recf = P.alloc_f(512)
    rdrow = [P.alloc_f(512) for _ in range(4)]
    rawt = [P.alloc_f(512) for _ in range(2)]
    attnb = P.alloc_b(8 * 512)
    ysb = P.alloc_f(8 * 512)
    ybf = P.alloc_b(512)
    y2b = P.alloc_b(512)
    xrt = [P.alloc_f(512) for _ in range(2)]
    ssum = P.alloc_f(512)
    sq = P.alloc_f(512)
    vrx = P.alloc_f(512)
    rstd = P.alloc_f(512)
    xo = [P.alloc_f(512) for _ in range(2)]

    nrow = nh * ntile
    for kc in range(8):
        P.dma("gpsimd", wpb[:, kc * 1024:(kc + 1) * 1024],
              wd[128 * kc:128 * (kc + 1), :], tag + "lw")
    P.dma("sync", denf[0:nrow, 0:512], rawd[:, :, 64, :], tag + "ld")
    P.wait("vector", tag + "ld", P.val(tag + "ld"))
    P.op("vector", "reciprocal", recf[0:nrow, 0:512], denf[0:nrow, 0:512], sname=sVV)
    P.wait("sync", sVV, P.val(sVV))
    P.dma("sync", rdd[0:nrow, :], recf[0:nrow, 0:512], tag + "st")
    P.wait("sync", tag + "st", P.val(tag + "st"))
    P.wait("tensor", tag + "lw", P.val(tag + "lw"))

    ROWF = CFc(P, "ones_rowf", 0, 1)
    SCLF = CFc(P, "scale_rowf", 0, 1)
    OCOL = CB(P, "ones_col")

    for tau in range(ntile):
        for m in range(8):
            rs = m % 2
            P.wguard("sync", ("rawt", tag, rs))
            for half in range(2):
                h = 2 * m + half
                P.dma("sync", rawt[rs][64 * half:64 * half + 64, 0:512],
                      rawd[h, tau, 0:64, :], tag + "lr" + str(rs))
                P.wguard("sync", ("rdrow", tag, 2 * rs + half))
                P.dma("sync", rdrow[2 * rs + half][0:1, 0:512],
                      rdd[h * ntile + tau: h * ntile + tau + 1, :],
                      tag + "lq" + str(2 * rs + half))
            P.wait("tensor", tag + "lq" + str(2 * rs), P.val(tag + "lq" + str(2 * rs)))
            P.wait("tensor", tag + "lq" + str(2 * rs + 1),
                   P.val(tag + "lq" + str(2 * rs + 1)))
            bcb = P.PS[6 + rs]
            P.wguard("tensor", ("bank", 6 + rs))
            P.mm(bcb[0:64, 0:512], ROWF[0:1, 0:64], rdrow[2 * rs][0:1, 0:512],
                 True, True)
            bm = P.mm(bcb[64:128, 0:512], ROWF[0:1, 0:64],
                      rdrow[2 * rs + 1][0:1, 0:512], True, True, sname=sMM)
            P.setguard(("rdrow", tag, 2 * rs), sMM, bm)
            P.setguard(("rdrow", tag, 2 * rs + 1), sMM, bm)
            P.wait("vector", sMM, bm)
            P.wait("vector", tag + "lr" + str(rs), P.val(tag + "lr" + str(rs)))
            vm = P.op("vector", "tensor_mul", attnb[:, m * 512:(m + 1) * 512],
                      rawt[rs][:, 0:512], bcb[0:128, 0:512], sname=sVV)
            P.setguard(("rawt", tag, rs), sVV, vm)
            P.setguard(("bank", 6 + rs), sVV, vm)
        attn_done = P.val(sVV)
        P.wait("tensor", sVV, attn_done)
        for dmg in range(2):
            for ti in range(4):
                P.wguard("tensor", ("bank", ti))
            for m in range(8):
                for ti in range(4):
                    dm = dmg * 4 + ti
                    P.mm(P.PS[ti][:, 0:512],
                         wpb[:, m * 1024 + dm * 128: m * 1024 + (dm + 1) * 128],
                         attnb[:, m * 512:(m + 1) * 512], m == 0, m == 7,
                         sname=sMM if (m == 7 and ti == 3) else None)
            mmv = P.val(sMM)
            P.wait("scalar", sMM, mmv)
            for ti in range(4):
                dm = dmg * 4 + ti
                rs = dm % 2
                P.wguard("sync", ("xrt", tag, rs))
                P.dma("sync", xrt[rs][:, 0:512],
                      resd[128 * dm:128 * (dm + 1), tau * 512:(tau + 1) * 512],
                      tag + "lx" + str(rs))
                ev = P.act("scalar", ysb[:, dm * 512:(dm + 1) * 512],
                           P.PS[ti][:, 0:512], AF.Identity,
                           bias=BPc(P, bcol_name, dm), sname=sSC)
                P.setguard(("bank", ti), sSC, ev)
                P.wait("vector", sSC, ev)
                P.wait("vector", tag + "lx" + str(rs), P.val(tag + "lx" + str(rs)))
                av = P.op("vector", "tensor_add", ysb[:, dm * 512:(dm + 1) * 512],
                          ysb[:, dm * 512:(dm + 1) * 512], xrt[rs][:, 0:512],
                          sname=sVV)
                P.setguard(("xrt", tag, rs), sVV, av)
                P.wguard("vector", ("ybf", tag))
                P.op("vector", "tensor_copy", ybf[:, :],
                     ysb[:, dm * 512:(dm + 1) * 512])
                cv = P.op("vector", "tensor_mul", y2b[:, :], ybf[:, :], ybf[:, :],
                          sname=sVV)
                P.wait("tensor", sVV, cv)
                if dm == 0:
                    P.wguard("tensor", ("bank", 4))
                    P.wguard("tensor", ("bank", 5))
                P.mm(P.PS[4][0:1, 0:512], OCOL, ybf[:, :], dm == 0, dm == 7)
                sm = P.mm(P.PS[5][0:1, 0:512], OCOL, y2b[:, :], dm == 0, dm == 7,
                          sname=sMM)
                P.setguard(("ybf", tag), sMM, sm)
        stat_mm = P.val(sMM)
        P.wait("scalar", sMM, stat_mm)
        c1 = P.act("scalar", ssum[0:1, 0:512], P.PS[4][0:1, 0:512], AF.Copy,
                   sname=sSC)
        c2 = P.act("scalar", sq[0:1, 0:512], P.PS[5][0:1, 0:512], AF.Copy,
                   sname=sSC)
        P.setguard(("bank", 4), sSC, c2)
        P.setguard(("bank", 5), sSC, c2)
        P.wait("vector", sSC, c2)
        P.op("vector", "tensor_mul", vrx[0:1, 0:512], ssum[0:1, 0:512],
             ssum[0:1, 0:512])
        sv = P.op("vector", "scalar_tensor_tensor", vrx[0:1, 0:512],
                  vrx[0:1, 0:512], -1.0 / 1024.0, sq[0:1, 0:512],
                  OP.mult, OP.add, sname=sVV)
        P.op("vector", "tensor_scalar", vrx[0:1, 0:512], vrx[0:1, 0:512],
             1.0 / 1024.0, 1e-5, OP.mult, OP.add, sname=sVV)
        P.wait("scalar", sVV, P.val(sVV))
        P.act("scalar", vrx[0:1, 0:512], vrx[0:1, 0:512], AF.Ln)
        rv = P.act("scalar", rstd[0:1, 0:512], vrx[0:1, 0:512], AF.Exp,
                   scale=-0.5, sname=sSC)
        P.wait("tensor", sSC, rv)
        P.wguard("tensor", ("bank", 6))
        P.wguard("tensor", ("bank", 7))
        P.mm(P.PS[6][0:128, 0:512], SCLF, ssum[0:1, 0:512], True, True)
        bm2 = P.mm(P.PS[7][0:128, 0:512], ROWF, rstd[0:1, 0:512], True, True,
                   sname=sMM)
        P.wait("vector", sMM, bm2)
        for dm in range(8):
            xs = xo[dm % 2]
            P.wguard("vector", ("xo", tag, dm % 2))
            P.op("vector", "tensor_sub", xs[:, 0:512],
                 ysb[:, dm * 512:(dm + 1) * 512], P.PS[6][0:128, 0:512])
            xv = P.op("vector", "tensor_mul", xs[:, 0:512], xs[:, 0:512],
                      P.PS[7][0:128, 0:512], sname=sVV)
            P.wait("sync", sVV, xv)
            so_ = P.dma("sync", outd[128 * dm:128 * (dm + 1),
                                     tau * 512:(tau + 1) * 512], xs[:, 0:512],
                        tag + "so" + str(dm % 2))
            P.setguard(("xo", tag, dm % 2), tag + "so" + str(dm % 2), so_)
        P.setguard(("bank", 6), sVV, P.val(sVV))
        P.setguard(("bank", 7), sVV, P.val(sVV))
    P.pop()


def s2(P):
    emit_proj_ln(P, "c", "raw0", 8, "w_proj0", "b_proj0", "x0T", "x1")


STAGES["s2"] = (s2, ["raw0", "w_proj0", "x0T"], ["x1"])



AX = mybir.AxisListType.X


def gelu_evac(P, outb, bank, gcol, tmpb, tmph, rows=128):
    return P.act("scalar", outb[0:rows, :], bank, AF.Gelu,
                 bias=P.BP[0:rows, gcol:gcol + 1])


def emit_ffn0(P):
    P.push()
    rw1b = P.alloc_b(2048)
    rw2b = P.alloc_b(16)
    hw1b = P.alloc_b(2048)
    hw2b = P.alloc_b(2)
    ew1b = P.alloc_b(16 * 512)
    ew2b = P.alloc_b(4 * 3072)
    eb2b = P.alloc_b(3072)
    sw1b = P.alloc_b(16 * 64)
    sw2b = P.alloc_b(3072)
    for (buf, dn, nk, ncol) in ((rw1b, "r_w1", 8, 256), (rw2b, "r_w2", 2, 8),
                                (hw1b, "h_w1", 8, 256), (hw2b, "h_w2", 2, 1),
                                (ew1b, "e_w1", 16, 512), (ew2b, "e_w2", 4, 3072),
                                (sw1b, "s_w1", 16, 64)):
        wd = P.din(dn)
        for kc in range(nk):
            P.dma("gpsimd", buf[:, kc * ncol:(kc + 1) * ncol],
                  wd[128 * kc:128 * (kc + 1), :], "dlw")
    P.dma("gpsimd", eb2b[0:8, :], P.din("e_b2")[:, :], "dlw")
    P.dma("gpsimd", sw2b[0:64, :], P.din("s_w2")[:, :], "dlw")
    P.wait("tensor", "dlw", P.val("dlw"))

    x1d = P.dsrc("x1")
    intd = P.dout_or_tmp("intg")

    intg = P.alloc_b(8 * 512)
    vst = P.alloc_b(8 * 512)
    habc = P.alloc_b(512)
    h1b = P.alloc_b(4 * 512)
    h2b = P.alloc_b(4 * 512)
    hsb = P.alloc_b(512)
    hhb = P.alloc_b(2 * 512)
    rhb = P.alloc_b(2 * 512)
    tmpb = P.alloc_b(512)
    tmph = P.alloc_b(512)
    tmpf = P.alloc_f(512)
    logf = P.alloc_f(512)
    hrow = P.alloc_f(512)
    ltm = P.alloc_f(32)
    eqa = P.alloc_f(32)
    eqb = P.alloc_f(32)
    mx1 = P.alloc_f(1)
    mx2 = P.alloc_f(1)
    w1c = P.alloc_f(1)
    m1fm = P.alloc_b(512)
    m2fm = P.alloc_b(512)
    rowa = P.alloc_f(512)
    rowb = P.alloc_f(512)
    swsig = P.alloc_f(1)
    onemsw = P.alloc_f(1)
    swcol = P.alloc_f(1)
    wAb = P.alloc_b(512)
    wBb = P.alloc_b(512)
    haltb = P.alloc_b(512)
    haltxb = P.alloc_b(512)
    dtb = [P.alloc_b(512) for _ in range(2)]
    Ab = [[P.alloc_b(512) for _ in range(3)] for _ in range(2)]
    Bb = [[P.alloc_b(512) for _ in range(3)] for _ in range(2)]
    Gb = [[P.alloc_b(512) for _ in range(3)] for _ in range(2)]
    tmpf3 = [[P.alloc_f(512) for _ in range(3)] for _ in range(2)]
    vnb = [P.alloc_b(512) for _ in range(3)]
    gnb = [P.alloc_b(512) for _ in range(3)]
    prod = P.alloc_b(512)
    mixv = P.alloc_b(512)
    mixg = P.alloc_b(512)

    EYE = CFc(P, "eye128f")
    ROWF = CFc(P, "ones_rowf", 0, 1)
    B8 = CB(P, "B8", 0, 8)
    sMM = "dmm"

    P.act("scalar", swsig[0:1, 0:1], BPc(P, "sw", 0, 0, 1), AF.Sigmoid)
    P.wait("vector", "chscalar", P.val("chscalar"))
    P.op("vector", "tensor_scalar", onemsw[0:1, 0:1], swsig[0:1, 0:1], -1.0, 1.0,
         OP.mult, OP.add)
    ov = P.op("vector", "tensor_copy", rowa[0:1, 0:128], ROWF)
    P.wait("tensor", "chvector", ov)
    bmv = P.mm(P.PS[7][0:128, 0:1], rowa[0:1, 0:128], swsig[0:1, 0:1], True, True,
               sname=sMM)
    P.wait("scalar", sMM, bmv)
    scv = P.act("scalar", swcol[0:128, 0:1], P.PS[7][0:128, 0:1], AF.Copy)
    P.setguard(("bank", 7), "chscalar", scv)

    gc_r = 0
    gc_h = 2
    gc_e = 4
    gc_s = 8
    gc_f = 9

    for tau in range(DBG_S3TAU):
        t0 = tau * 512
        for f in range(8):
            P.wguard("gpsimd", ("s3x", f))
            P.dma("gpsimd", intg[:, f * 512:(f + 1) * 512],
                  x1d[128 * f:128 * (f + 1), t0:t0 + 512], "dlx")
        P.wait("vector", "dlx", P.val("dlx"))
        P.wait("tensor", "dlx", P.val("dlx"))

        for mh in range(2):
            P.wguard("tensor", ("bank", mh))
        for kc in range(8):
            for mh in range(2):
                P.mm(P.PS[mh][:, 0:512],
                     rw1b[:, kc * 256 + mh * 128: kc * 256 + (mh + 1) * 128],
                     intg[:, kc * 512:(kc + 1) * 512], kc == 0, kc == 7,
                     sname=sMM if mh == 1 else None)
        P.wait("scalar", sMM, P.val(sMM))
        for mh in range(2):
            gv = gelu_evac(P, rhb[:, mh * 512:(mh + 1) * 512],
                           P.PS[mh][:, 0:512], gc_r + mh, tmpb, tmph)
            P.setguard(("bank", mh), "chscalar", gv)
        P.wguard("tensor", ("bank", 2))
        P.wait("tensor", "chscalar", P.val("chscalar"))
        for kc2 in range(2):
            P.mm(P.PS[2][0:8, 0:512], rw2b[:, kc2 * 8:(kc2 + 1) * 8],
                 rhb[:, kc2 * 512:(kc2 + 1) * 512], kc2 == 0, kc2 == 1,
                 sname=sMM if kc2 == 1 else None)
        P.wait("scalar", sMM, P.val(sMM))
        lv = P.act("scalar", logf[0:8, 0:512], P.PS[2][0:8, 0:512], AF.Identity,
                   bias=BPc(P, "r_b2", 0, 0, 8))
        P.setguard(("bank", 2), "chscalar", lv)
        P.wait("tensor", "chscalar", lv)
        P.wguard("tensor", ("bank", 3))
        for j in range(4):
            tv = P.op("tensor", "transpose", P.PS[3][0:128, j * 8:(j + 1) * 8],
                      logf[0:8, 128 * j:128 * (j + 1)], EYE[0:8, 0:8], sname=sMM)
        P.wait("vector", sMM, tv)
        P.op("vector", "tensor_copy", ltm[0:128, 0:32], P.PS[3][0:128, 0:32])
        P.setguard(("bank", 3), "chvector", P.val("chvector"))
        P.wguard("tensor", ("bank", 4))
        P.wguard("tensor", ("bank", 5))
        P.wguard("tensor", ("bank", 6))
        for j in range(4):
            lj = ltm[0:128, j * 8:(j + 1) * 8]
            if j > 0:
                P.wait("vector", "chscalar", P.val("chscalar"))
                P.wait("vector", sMM, P.val(sMM))
            P.op("vector", "reduce_max", mx1[0:128, 0:1], lj, AX)
            P.op("vector", "tensor_scalar", eqa[0:128, 0:8], lj, mx1[0:128, 0:1],
                 None, OP.is_equal)
            P.op("vector", "scalar_tensor_tensor", eqb[0:128, 0:8], eqa[0:128, 0:8],
                 -1e30, lj, OP.mult, OP.add)
            P.op("vector", "reduce_max", mx2[0:128, 0:1], eqb[0:128, 0:8], AX)
            P.op("vector", "tensor_scalar", eqb[0:128, 0:8], eqb[0:128, 0:8],
                 mx2[0:128, 0:1], None, OP.is_equal)
            wv_ = P.op("vector", "tensor_sub", w1c[0:128, 0:1], mx1[0:128, 0:1],
                       mx2[0:128, 0:1])
            P.wait("scalar", "chvector", wv_)
            sv_ = P.act("scalar", w1c[0:128, 0:1], w1c[0:128, 0:1], AF.Sigmoid)
            P.wait("tensor", "chvector", wv_)
            P.wait("tensor", "chscalar", sv_)
            P.op("tensor", "transpose", P.PS[4][0:8, j * 128:(j + 1) * 128],
                 eqa[0:128, 0:8], EYE[0:128, 0:128], sname=sMM)
            P.op("tensor", "transpose", P.PS[5][0:8, j * 128:(j + 1) * 128],
                 eqb[0:128, 0:8], EYE[0:128, 0:128], sname=sMM)
            tv2 = P.op("tensor", "transpose", P.PS[6][0:1, j * 128:(j + 1) * 128],
                       w1c[0:128, 0:1], EYE[0:128, 0:128], sname=sMM)
        P.wait("vector", sMM, tv2)
        P.op("vector", "tensor_copy", m1fm[0:8, :], P.PS[4][0:8, 0:512])
        P.op("vector", "tensor_copy", m2fm[0:8, :], P.PS[5][0:8, 0:512])
        P.op("vector", "tensor_copy", rowa[0:1, 0:512], P.PS[6][0:1, 0:512])
        P.op("vector", "tensor_scalar", rowb[0:1, 0:512], rowa[0:1, 0:512],
             onemsw[0:1, 0:1], None, OP.mult)
        P.op("vector", "tensor_scalar", rowa[0:1, 0:512], rowa[0:1, 0:512],
             -1.0, 1.0, OP.mult, OP.add)
        rv_ = P.op("vector", "tensor_scalar", rowa[0:1, 0:512], rowa[0:1, 0:512],
                   onemsw[0:1, 0:1], None, OP.mult)
        P.setguard(("bank", 4), "chvector", rv_)
        P.setguard(("bank", 5), "chvector", rv_)
        P.setguard(("bank", 6), "chvector", rv_)
        P.wait("tensor", "chvector", rv_)
        P.wguard("tensor", ("bank", 4))
        P.wguard("tensor", ("bank", 5))
        P.mm(P.PS[4][0:128, 0:512], ROWF, rowb[0:1, 0:512], True, True)
        bc2 = P.mm(P.PS[5][0:128, 0:512], ROWF, rowa[0:1, 0:512], True, True,
                   sname=sMM)
        P.wait("vector", sMM, bc2)
        P.op("vector", "tensor_copy", wAb[0:128, 0:512], P.PS[4][0:128, 0:512])
        wv2 = P.op("vector", "tensor_copy", wBb[0:128, 0:512], P.PS[5][0:128, 0:512])
        P.setguard(("bank", 4), "chvector", wv2)
        P.setguard(("bank", 5), "chvector", wv2)

        for it in range(2):
            nkc = 8 if it == 0 else 16
            for b_ in range(7):
                P.wguard("tensor", ("bank", b_))
            P.wait("tensor", "chvector", P.val("chvector"))
            for kc in range(nkc):
                srcv = intg if kc < 8 else vst
                f8 = kc % 8
                ctx = srcv[:, f8 * 512:(f8 + 1) * 512]
                last = kc == nkc - 1
                for mh in range(4):
                    P.mm(P.PS[mh][:, 0:512],
                         ew1b[:, kc * 512 + mh * 128: kc * 512 + (mh + 1) * 128],
                         ctx, kc == 0, last)
                P.mm(P.PS[4][0:64, 0:512], sw1b[:, kc * 64:(kc + 1) * 64],
                     ctx, kc == 0, last,
                     sname=sMM if kc >= 8 else None)
                if kc < 8:
                    for mh2 in range(2):
                        P.mm(P.PS[5 + mh2][:, 0:512],
                             hw1b[:, kc * 256 + mh2 * 128: kc * 256 + (mh2 + 1) * 128],
                             ctx, kc == 0, kc == 7,
                             sname=sMM if mh2 == 1 else None)
            mmv = P.val(sMM)
            P.wait("scalar", sMM, mmv)
            for mh in range(4):
                P.wguard("scalar", ("habc",))
                gv = gelu_evac(P, habc, P.PS[mh][:, 0:512], gc_e + mh, tmpb, tmph)
                P.wguard("tensor", ("bank", 7))
                P.wait("tensor", "chscalar", gv)
                bm1 = P.mm(P.PS[7][0:128, 0:512], B8[0:8, mh * 128:(mh + 1) * 128],
                           m1fm[0:8, :], True, True, sname=sMM)
                P.wait("vector", sMM, bm1)
                P.wait("vector", "chscalar", gv)
                u1 = P.op("vector", "tensor_mul", h1b[:, mh * 512:(mh + 1) * 512],
                          habc[:, :], P.PS[7][0:128, 0:512])
                P.wait("tensor", "chvector", u1)
                bm2 = P.mm(P.PS[7][0:128, 0:512], B8[0:8, mh * 128:(mh + 1) * 128],
                           m2fm[0:8, :], True, True, sname=sMM)
                P.wait("vector", sMM, bm2)
                u2 = P.op("vector", "tensor_mul", h2b[:, mh * 512:(mh + 1) * 512],
                          habc[:, :], P.PS[7][0:128, 0:512])
                P.setguard(("bank", 7), "chvector", u2)
                P.setguard(("bank", mh), "chvector", u2)
                P.setguard(("habc",), "chvector", u2)
            gvs = gelu_evac(P, hsb, P.PS[4][0:64, 0:512], gc_s, tmpb, tmph, rows=64)
            P.setguard(("bank", 4), "chscalar", gvs)
            for mh2 in range(2):
                gvh = gelu_evac(P, hhb[:, mh2 * 512:(mh2 + 1) * 512],
                                P.PS[5 + mh2][:, 0:512], gc_h + mh2, tmpb, tmph)
                P.setguard(("bank", 5 + mh2), "chscalar", gvh)
            P.wguard("tensor", ("bank", 7))
            P.wait("tensor", "chscalar", P.val("chscalar"))
            for kc2 in range(2):
                P.mm(P.PS[7][0:1, 0:512], hw2b[:, kc2:kc2 + 1],
                     hhb[:, kc2 * 512:(kc2 + 1) * 512], kc2 == 0, kc2 == 1,
                     sname=sMM if kc2 == 1 else None)
            P.wait("scalar", sMM, P.val(sMM))
            hv = P.act("scalar", hrow[0:1, 0:512], P.PS[7][0:1, 0:512], AF.Sigmoid,
                       bias=BPc(P, "h_b2", 0, 0, 1))
            P.setguard(("bank", 7), "chscalar", hv)
            P.wait("tensor", "chscalar", hv)
            P.wguard("tensor", ("bank", 7))
            hb_ = P.mm(P.PS[7][0:128, 0:512], ROWF, hrow[0:1, 0:512], True, True,
                       sname=sMM)
            P.wait("vector", sMM, hb_)
            P.op("vector", "tensor_copy", haltb[0:128, 0:512], P.PS[7][0:128, 0:512])
            hx_ = P.op("vector", "tensor_scalar", haltxb[0:128, 0:512],
                       haltb[0:128, 0:512], 0.1, None, OP.mult)
            P.setguard(("bank", 7), "chvector", hx_)
            PB = [0, 1, 5]
            for fp in range(4):
                fpair = (2 * fp, 2 * fp + 1)
                for j, f in enumerate(fpair):
                    P.op("vector", "tensor_scalar", dtb[j][:, :],
                         intg[:, f * 512:(f + 1) * 512], BPc(P, "mu", f), None,
                         OP.subtract)
                parts = []
                for j, f in enumerate(fpair):
                    for pt in (0, 2, 1):
                        for si in range(3):
                            if it == 0 and pt == 0:
                                continue
                            parts.append((j, f, si, pt))

                def emit_part(idx):
                    j, f, si, pt = parts[idx]
                    ch = (0, 8, 16)[pt] + f
                    bk = P.PS[PB[idx % 3]]
                    P.wguard("tensor", ("bank", PB[idx % 3]))
                    if si == 0:
                        P.wait("tensor", "chscalar", P.val("chscalar"))
                        P.mm(bk[:, 0:512],
                             sw2b[0:64, ch * 128:(ch + 1) * 128], hsb[0:64, :],
                             True, True, sname=sMM)
                    else:
                        hx = h1b if si == 1 else h2b
                        mf = m1fm if si == 1 else m2fm
                        for kc2 in range(4):
                            P.mm(bk[:, 0:512],
                                 ew2b[:, kc2 * 3072 + ch * 128:
                                      kc2 * 3072 + (ch + 1) * 128],
                                 hx[:, kc2 * 512:(kc2 + 1) * 512],
                                 kc2 == 0, False)
                        P.mm(bk[:, 0:512], eb2b[0:8, ch * 128:(ch + 1) * 128],
                             mf[0:8, :], False, True, sname=sMM)
                    return P.val(sMM)

                def emit_act(idx, mk):
                    j, f, si, pt = parts[idx]
                    bk = P.PS[PB[idx % 3]]
                    P.wait("scalar", sMM, mk)
                    P.wguard("scalar", ("abg", j))
                    sb2c = BPc(P, "s_b2", (0, 8, 16)[pt] + f) if si == 0 else 0.0
                    if pt == 0:
                        ev_ = P.act("scalar", Ab[j][si][:, :], bk[:, 0:512],
                                    AF.Sigmoid, bias=sb2c)
                    elif pt == 2:
                        ev_ = P.act("scalar", Gb[j][si][:, :], bk[:, 0:512],
                                    AF.Sigmoid, bias=sb2c)
                    else:
                        ev_ = P.act("scalar", tmpf3[j][si][:, 0:512], bk[:, 0:512],
                                    AF.Sigmoid, scale=-1.0,
                                    bias=P.BPn[:, 8 + f:8 + f + 1] if si == 0 else 0.0)
                    P.setguard(("bank", PB[idx % 3]), "chscalar", ev_)
                    return ev_

                marks = {}
                n = len(parts)
                for idx in range(min(2, n)):
                    marks[idx] = emit_part(idx)
                acts = {}
                for idx in range(n):
                    if idx + 2 < n:
                        marks[idx + 2] = emit_part(idx + 2)
                    acts[idx] = emit_act(idx, marks.pop(idx))
                lns = {}
                for j in range(2):
                    for si in range(3):
                        lns[(j, si)] = P.act("scalar", Bb[j][si][:, :],
                                             tmpf3[j][si][:, 0:512], AF.Ln)
                for j, f in enumerate(fpair):
                    for si in range(3):
                        P.wait("vector", "chscalar", lns[(j, si)])
                        if it == 0:
                            P.op("vector", "tensor_mul", vnb[si][:, :],
                                 Bb[j][si][:, :], dtb[j][:, :])
                        else:
                            P.op("vector", "tensor_mul", vnb[si][:, :],
                                 Ab[j][si][:, :], vst[:, f * 512:(f + 1) * 512])
                            P.op("vector", "tensor_mul", prod[:, :],
                                 Bb[j][si][:, :], dtb[j][:, :])
                            P.op("vector", "tensor_add", vnb[si][:, :],
                                 vnb[si][:, :], prod[:, :])
                        gn_ = P.op("vector", "tensor_mul", gnb[si][:, :],
                                   Gb[j][si][:, :], vnb[si][:, :])
                    P.setguard(("abg", j), "chvector", P.val("chvector"))
                    for (mix, src3) in ((mixv, vnb), (mixg, gnb)):
                        P.op("vector", "tensor_scalar", mix[0:128, 0:512],
                             src3[0][:, :], swcol[0:128, 0:1], None, OP.mult)
                        P.op("vector", "tensor_mul", prod[:, :], src3[1][:, :],
                             wAb[0:128, 0:512])
                        P.op("vector", "tensor_add", mix[0:128, 0:512],
                             mix[0:128, 0:512], prod[:, :])
                        P.op("vector", "tensor_mul", prod[:, :], src3[2][:, :],
                             wBb[0:128, 0:512])
                        P.op("vector", "tensor_add", mix[0:128, 0:512],
                             mix[0:128, 0:512], prod[:, :])
                    if it == 0:
                        P.op("vector", "tensor_mul", vst[:, f * 512:(f + 1) * 512],
                             haltb[0:128, 0:512], mixv[0:128, 0:512])
                    else:
                        P.op("vector", "tensor_sub", mixv[0:128, 0:512],
                             mixv[0:128, 0:512], vst[:, f * 512:(f + 1) * 512])
                        P.op("vector", "tensor_mul", mixv[0:128, 0:512],
                             haltb[0:128, 0:512], mixv[0:128, 0:512])
                        P.op("vector", "tensor_add", vst[:, f * 512:(f + 1) * 512],
                             vst[:, f * 512:(f + 1) * 512], mixv[0:128, 0:512])
                    P.op("vector", "tensor_mul", mixg[0:128, 0:512],
                         haltxb[0:128, 0:512], mixg[0:128, 0:512])
                    uv = P.op("vector", "tensor_add", intg[:, f * 512:(f + 1) * 512],
                              intg[:, f * 512:(f + 1) * 512], mixg[0:128, 0:512])
        P.wait("sync", "chvector", P.val("chvector"))
        for f in range(8):
            sv = P.dma("sync", intd[128 * f:128 * (f + 1), t0:t0 + 512],
                       intg[:, f * 512:(f + 1) * 512], "dsx")
            P.setguard(("s3x", f), "dsx", sv)
    P.pop()
    P.fence()

    P.push()
    fw1b = P.alloc_b(8 * 2048)
    fw2b = P.alloc_b(16 * 1024)
    for (buf, dn, nk, ncol) in ((fw1b, "f_w1", 8, 2048), (fw2b, "f_w2", 16, 1024)):
        wd = P.din(dn)
        for kc in range(nk):
            P.dma("gpsimd", buf[:, kc * ncol:(kc + 1) * ncol],
                  wd[128 * kc:128 * (kc + 1), :], "dlw2")
    P.wait("tensor", "dlw2", P.val("dlw2"))
    x1d = P.dsrc("x1")
    intd = P.dsrc("intg")
    x2d = P.dout_or_tmp("x2")

    inb8 = P.alloc_b(8 * 512)
    r1b = P.alloc_b(16 * 512)
    tmpb2 = P.alloc_b(512)
    tmph2 = P.alloc_b(512)
    ysb = P.alloc_f(8 * 512)
    x1t = [P.alloc_f(512), P.alloc_f(512)]
    ybf = P.alloc_b(512)
    y2b = P.alloc_b(512)
    ssum = P.alloc_f(512)
    sq = P.alloc_f(512)
    vrx = P.alloc_f(512)
    rstd = P.alloc_f(512)
    xo = [P.alloc_f(512), P.alloc_f(512)]
    ROWF = CFc(P, "ones_rowf", 0, 1)
    SCLF = CFc(P, "scale_rowf", 0, 1)
    OCOL = CB(P, "ones_col")
    sMM = "dmm"

    for tau in range(DBG_S3TAU):
        t0 = tau * 512
        P.wguard("gpsimd", ("s3in",))
        for kc in range(8):
            P.dma("gpsimd", inb8[:, kc * 512:(kc + 1) * 512],
                  intd[128 * kc:128 * (kc + 1), t0:t0 + 512], "dli")
        P.wait("tensor", "dli", P.val("dli"))
        for mh in range(16):
            bk = P.PS[mh % 2]
            P.wguard("tensor", ("bank", mh % 2))
            for kc in range(8):
                P.mm(bk[:, 0:512],
                     fw1b[:, kc * 2048 + mh * 128: kc * 2048 + (mh + 1) * 128],
                     inb8[:, kc * 512:(kc + 1) * 512], kc == 0, kc == 7,
                     sname=sMM if kc == 7 else None)
            if mh == 15:
                P.setguard(("s3in",), sMM, P.val(sMM))
            P.wait("scalar", sMM, P.val(sMM))
            gv = gelu_evac(P, r1b[:, mh * 512:(mh + 1) * 512], bk[:, 0:512],
                           9 + mh, tmpb2, tmph2)
            P.setguard(("bank", mh % 2), "chscalar", gv)
        for dm in range(8):
            bk = P.PS[2 + dm % 2]
            P.wguard("tensor", ("bank", 2 + dm % 2))
            P.wait("tensor", "chscalar", P.val("chscalar"))
            for kc in range(16):
                P.mm(bk[:, 0:512],
                     fw2b[:, kc * 1024 + dm * 128: kc * 1024 + (dm + 1) * 128],
                     r1b[:, kc * 512:(kc + 1) * 512], kc == 0, kc == 15,
                     sname=sMM if kc == 15 else None)
            mmv = P.val(sMM)
            rs = dm % 2
            P.wguard("sync", ("s3r", rs))
            P.dma("sync", x1t[rs][:, 0:512],
                  x1d[128 * dm:128 * (dm + 1), t0:t0 + 512], "dlr" + str(rs))
            P.wait("scalar", sMM, mmv)
            ev = P.act("scalar", ysb[:, dm * 512:(dm + 1) * 512], bk[:, 0:512],
                       AF.Identity, bias=BPc(P, "f_b2", dm))
            P.setguard(("bank", 2 + dm % 2), "chscalar", ev)
            P.wait("vector", "chscalar", ev)
            P.op("vector", "tensor_scalar", ysb[:, dm * 512:(dm + 1) * 512],
                 ysb[:, dm * 512:(dm + 1) * 512], BPc(P, "iw", dm), None, OP.mult)
            P.wait("vector", "dlr" + str(rs), P.val("dlr" + str(rs)))
            av = P.op("vector", "scalar_tensor_tensor",
                      ysb[:, dm * 512:(dm + 1) * 512], x1t[rs][:, 0:512], 2.0,
                      ysb[:, dm * 512:(dm + 1) * 512], OP.mult, OP.add)
            P.setguard(("s3r", rs), "chvector", av)
            P.op("vector", "tensor_copy", ybf[:, :], ysb[:, dm * 512:(dm + 1) * 512])
            cv = P.op("vector", "tensor_mul", y2b[:, :], ybf[:, :], ybf[:, :])
            P.wait("tensor", "chvector", cv)
            if dm == 0:
                P.wguard("tensor", ("bank", 4))
                P.wguard("tensor", ("bank", 5))
            P.mm(P.PS[4][0:1, 0:512], OCOL, ybf[:, :], dm == 0, dm == 7)
            P.mm(P.PS[5][0:1, 0:512], OCOL, y2b[:, :], dm == 0, dm == 7,
                 sname=sMM if dm == 7 else None)
        P.wait("scalar", sMM, P.val(sMM))
        P.act("scalar", ssum[0:1, 0:512], P.PS[4][0:1, 0:512], AF.Copy)
        c2 = P.act("scalar", sq[0:1, 0:512], P.PS[5][0:1, 0:512], AF.Copy)
        P.setguard(("bank", 4), "chscalar", c2)
        P.setguard(("bank", 5), "chscalar", c2)
        P.wait("vector", "chscalar", c2)
        P.op("vector", "tensor_mul", vrx[0:1, 0:512], ssum[0:1, 0:512],
             ssum[0:1, 0:512])
        P.op("vector", "scalar_tensor_tensor", vrx[0:1, 0:512], vrx[0:1, 0:512],
             -1.0 / 1024.0, sq[0:1, 0:512], OP.mult, OP.add)
        sv_ = P.op("vector", "tensor_scalar", vrx[0:1, 0:512], vrx[0:1, 0:512],
                   1.0 / 1024.0, 1e-5, OP.mult, OP.add)
        P.wait("scalar", "chvector", sv_)
        P.act("scalar", vrx[0:1, 0:512], vrx[0:1, 0:512], AF.Ln)
        rv = P.act("scalar", rstd[0:1, 0:512], vrx[0:1, 0:512], AF.Exp, scale=-0.5)
        P.wait("tensor", "chscalar", rv)
        P.wguard("tensor", ("bank", 6))
        P.wguard("tensor", ("bank", 7))
        P.mm(P.PS[6][0:128, 0:512], SCLF, ssum[0:1, 0:512], True, True)
        bm2 = P.mm(P.PS[7][0:128, 0:512], ROWF, rstd[0:1, 0:512], True, True,
                   sname=sMM)
        P.wait("vector", sMM, bm2)
        for dm in range(8):
            xs = xo[dm % 2]
            P.wguard("vector", ("s3o", dm % 2))
            P.op("vector", "tensor_sub", xs[:, 0:512],
                 ysb[:, dm * 512:(dm + 1) * 512], P.PS[6][0:128, 0:512])
            xv = P.op("vector", "tensor_mul", xs[:, 0:512], xs[:, 0:512],
                      P.PS[7][0:128, 0:512])
            P.wait("sync", "chvector", xv)
            so_ = P.dma("sync", x2d[128 * dm:128 * (dm + 1), t0:t0 + 512],
                        xs[:, 0:512], "dso" + str(dm % 2))
            P.setguard(("s3o", dm % 2), "dso" + str(dm % 2), so_)
        P.setguard(("bank", 6), "chvector", P.val("chvector"))
        P.setguard(("bank", 7), "chvector", P.val("chvector"))
    P.pop()


LMARK = {}


def s3(P):
    emit_ffn0(P)


STAGES["s3"] = (s3, ["x1", "r_w1", "r_w2", "h_w1", "h_w2", "e_w1", "e_w2", "e_b2",
                     "s_w1", "s_w2", "f_w1", "f_w2"], ["x2", "intg"])



def emit_perm(P):
    P.push()
    x2d = P.dsrc("x2")
    x2pd = P.dout_or_tmp("x2p")
    idxd = P.din("permidx")
    buf = [P.alloc_f(2048), P.alloc_f(2048)]
    idxt = [P.alloc_b(2), P.alloc_b(2)]
    x2v = x2d.rearrange("p (h t) -> (p h) t", h=2)
    x2pv = x2pd.rearrange("p (h t) -> (p h) t", h=2)
    for k in range(16):
        s = k % 2
        P.wguard("gpsimd", ("pidx", s))
        P.dma("gpsimd", idxt[s][:, 0:2].bitcast(I32),
              idxd[128 * k:128 * (k + 1), :], "pli" + str(s))
        P.wait("gpsimd", "pli" + str(s), P.val("pli" + str(s)))
        P.wguard("gpsimd", ("pbuf", s))
        P.idma(buf[s][:, 0:2048], x2v[:, :], idxt[s][:, 0:2].bitcast(I32),
               "pgi" + str(s))
        P.wait("sync", "pgi" + str(s), P.val("pgi" + str(s)))
        sv = P.dma("sync", x2pv[128 * k:128 * (k + 1), :], buf[s][:, 0:2048],
                   "pst" + str(s))
        P.setguard(("pbuf", s), "pst" + str(s), sv)
        P.setguard(("pidx", s), "pst" + str(s), sv)
    P.pop()


def sp(P):
    emit_perm(P)


STAGES["sp"] = (sp, ["x2", "permidx"], ["x2p"])



def emit_vsum(P):
    P.push()
    v1d = P.dsrc("v1")
    vsd = P.dout_or_tmp("vsum1")
    vgd = P.dout_or_tmp("vg1")
    vt = [P.alloc_b(1040), P.alloc_b(1040)]
    vsum = P.alloc_f(1040)
    vs65 = P.alloc_f(16)
    sMM = "gmm"
    P.dma("sync", vgd[0:32, :], v1d[:, 0, :], "gst")
    P.dma("sync", vgd[32:64, :], v1d[:, 64, :], "gst")
    for tb in range(32):
        s = tb % 2
        P.wguard("gpsimd", ("vsb", s))
        P.dma("gpsimd", vt[s][:, :], v1d[tb, :, :], "gld" + str(s))
        P.wait("tensor", "gld" + str(s), P.val("gld" + str(s)))
        for sl, (c0, c1) in enumerate(((0, 512), (512, 1024), (1024, 1040))):
            mk = P.mm(P.PS[sl][0:1, 0:c1 - c0], CB(P, "ones_col"),
                      vt[s][:, c0:c1], tb == 0, tb == 31,
                      sname=sMM if sl == 2 else None)
        P.setguard(("vsb", s), sMM, P.val(sMM))
    P.wait("scalar", sMM, P.val(sMM))
    for sl, (c0, c1) in enumerate(((0, 512), (512, 1024), (1024, 1040))):
        sv = P.act("scalar", vsum[0:1, c0:c1], P.PS[sl][0:1, 0:c1 - c0], AF.Copy)
    P.wait("tensor", "chscalar", sv)
    EY1 = CFc(P, "eye128f", 0, 1, 0, 1)
    for h in range(16):
        tv = P.op("tensor", "transpose", P.PS[3][0:64, h:h + 1],
                  vsum[0:1, h * 65: h * 65 + 64], EY1, sname=sMM)
    P.wait("vector", sMM, tv)
    P.op("vector", "tensor_copy", vs65[0:64, 0:16], P.PS[3][0:64, 0:16])
    P.op("vector", "memset", vs65[64:65, 0:16], 4096.0)
    P.wait("sync", "chvector", P.val("chvector"))
    P.dma("sync", vsd[:, :], vs65[0:65, 0:16], "gst")
    P.pop()


def s4(P):
    emit_qkv(P, "e", "x2p", ("wq1", 0), ("wk1", 0), ("wv1", 0),
             ("bq1", 0), ("bk1", 0), "bv1_row", "q1", "k1", "v1", nq_tiles=4)
    P.fence()
    emit_vsum(P)


STAGES["s4"] = (s4, ["x2p", "wq1", "wk1", "wv1", "bv1_row"],
                ["q1", "k1", "v1", "vsum1", "vg1"])



def emit_attn1(P):
    P.push()
    kb = P.alloc_b(8 * 4096)
    vb = P.alloc_b(32 * 1040)
    qb = [P.alloc_b(2048), P.alloc_b(2048)]
    vgbE = P.alloc_b(1040)
    vgbO = P.alloc_b(1040)
    pgE = P.alloc_b(32)
    pgO = P.alloc_b(32)
    vs65 = P.alloc_f(16)
    kgb = P.alloc_b(64)
    qgb = P.alloc_b(32)
    e1 = P.alloc_b(32)
    e2 = P.alloc_b(32)
    corr = P.alloc_b(32)
    pg = P.alloc_b(32)
    expb = [P.alloc_b(512) for _ in range(3)]
    rawst = [P.alloc_f(512), P.alloc_f(512)]
    qd, kd, vd = P.dsrc("q1"), P.dsrc("k1"), P.dsrc("v1")
    vgd, vsd = P.dsrc("vg1"), P.dsrc("vsum1")
    rawd = P.dout_or_tmp("raw1")
    sMM = "hmm"

    for m in range(8):
        P.dma("gpsimd", kb[:, m * 4096:(m + 1) * 4096], kd[128 * m:128 * (m + 1), :],
              "hkld")
    for tb in range(32):
        P.dma("gpsimd", vb[:, tb * 1040:(tb + 1) * 1040], vd[tb, :, :], "hvld")
    P.dma("sync", vgbE[0:32, :], vgd[0:32, :], "hgld")
    P.dma("sync", vgbO[0:32, :], vgd[32:64, :], "hgld")
    P.dma("sync", vs65[0:65, 0:16], vsd[:, :], "hgld")
    P.wait("tensor", "hkld", P.val("hkld"))
    P.wait("tensor", "hvld", P.val("hvld"))
    P.wait("tensor", "hgld", P.val("hgld"))
    P.wait("vector", "hkld", P.val("hkld"))
    P.wait("vector", "hgld", P.val("hgld"))
    CMK = CB(P, "corrmask", 0, 64)

    for hp in range(DBG_S5HP):
        qs = qb[hp % 2]
        P.wguard("sync", ("q1slot", hp % 2))
        P.dma("sync", qs[:, :], qd[128 * hp:128 * (hp + 1), :], "hq" + str(hp % 2))
        P.wait("tensor", "hq" + str(hp % 2), P.val("hq" + str(hp % 2)))
        P.wait("vector", "hq" + str(hp % 2), P.val("hq" + str(hp % 2)))
        for h2 in range(2):
            h = 2 * hp + h2
            if not DBG_S5GLOB:
                pass
            P.wguard("vector", ("g1",))
            P.op("vector", "tensor_copy", kgb[0:64, 0:32],
                 kb[64 * h2:64 * h2 + 64, hp * 4096: hp * 4096 + 4096: 128])
            P.op("vector", "tensor_copy", kgb[0:64, 32:64],
                 kb[64 * h2:64 * h2 + 64, hp * 4096 + 64: hp * 4096 + 4096: 128])
            cq = P.op("vector", "tensor_copy", qgb[0:64, 0:32],
                      qs[64 * h2:64 * h2 + 64, 0:2048:64])
            P.wait("tensor", "chvector", cq)
            P.wguard("tensor", ("bank", 5))
            gv = P.mm(P.PS[5][0:64, 0:32], kgb[0:64, 0:64], qgb[0:64, 0:32],
                      True, True, sname=sMM)
            P.wait("scalar", sMM, gv)
            P.wguard("scalar", ("g1v",))
            s1_ = P.act("scalar", e1[0:64, 0:32], P.PS[5][0:64, 0:32], AF.Exp,
                        scale=0.125)
            s2_ = P.act("scalar", e2[0:64, 0:32], P.PS[5][0:64, 0:32], AF.Exp,
                        scale=0.25)
            P.setguard(("bank", 5), "chscalar", s2_)
            P.wait("vector", "chscalar", s2_)
            P.op("vector", "scalar_tensor_tensor", corr[0:64, 0:32], e1[0:64, 0:32],
                 -2.0, e2[0:64, 0:32], OP.mult, OP.add)
            P.op("vector", "tensor_scalar", corr[0:64, 0:32], corr[0:64, 0:32],
                 1.0, None, OP.add)
            P.op("vector", "tensor_mul", corr[0:64, 0:32], corr[0:64, 0:32], CMK[:, :])
            P.op("vector", "tensor_scalar", pg[0:64, 0:32], e1[0:64, 0:32],
                 -1.0, None, OP.add)
            P.op("vector", "tensor_add", pg[0:64, 0:32], pg[0:64, 0:32],
                 corr[0:64, 0:32])
            P.op("vector", "tensor_copy", pgE[0:32, 0:32], pg[0:32, 0:32])
            pgv = P.op("vector", "tensor_copy", pgO[0:32, 0:32], pg[32:64, 0:32])
            P.setguard(("g1v",), "chvector", pgv)
            P.wait("tensor", "chvector", pgv)
            for tau in range(4):
                avi = 2 + ((h2 * 4 + tau) % 2)
                av = P.PS[avi]
                P.wguard("tensor", ("bank", avi))

                def emit_sc1(i, tau=tau):
                    kc = 4 * tau + i
                    si = [0, 1, 6][i % 3]
                    P.wguard("tensor", ("sbank1", si))
                    smm = P.mm(P.PS[si][:, 0:512],
                               kb[64 * h2:64 * h2 + 64,
                                  hp * 4096 + kc * 128: hp * 4096 + (kc + 1) * 128],
                               qs[64 * h2:64 * h2 + 64, tau * 512:(tau + 1) * 512],
                               True, True, sname=sMM)
                    es = i % 3
                    P.wguard("scalar", ("exp1", es))
                    P.wait("scalar", sMM, smm)
                    ee = P.act("scalar", expb[es][:, :], P.PS[si][:, 0:512], AF.Exp,
                               scale=0.125)
                    P.setguard(("sbank1", si), "chscalar", ee)
                    P.wait("vector", "chscalar", ee)
                    em = P.op("vector", "tensor_scalar", expb[es][:, :],
                              expb[es][:, :], -1.0, None, OP.add)
                    return ("chvector", em)

                gts = {0: emit_sc1(0), 1: emit_sc1(1)}
                for i in range(4):
                    kc = 4 * tau + i
                    if i + 2 < 4:
                        gts[i + 2] = emit_sc1(i + 2)
                    gate = gts.pop(i)
                    es = i % 3
                    P.wait("tensor", gate[0], gate[1])
                    amv = P.mm(av[0:65, 0:512],
                               vb[:, kc * 1040 + h * 65: kc * 1040 + h * 65 + 65],
                               expb[es][:, :], i == 0, False, sname=sMM)
                    P.setguard(("exp1", es), sMM, amv)
                if DBG_S5GLOB:
                    P.mm(av[0:65, 0:512:64], vgbE[0:32, h * 65: h * 65 + 65],
                         pgE[0:32, 8 * tau: 8 * tau + 8], False, False, sname=sMM)
                    gmv = P.mm(av[0:65, 0:512:64],
                               vgbO[0:32, h * 65: h * 65 + 65],
                               pgO[0:32, 8 * tau: 8 * tau + 8], False, True,
                               sname=sMM)
                else:
                    gmv = P.mm(av[0:65, 0:512], vb[:, 4 * tau * 1040 + h * 65:
                                                   4 * tau * 1040 + h * 65 + 65],
                               expb[(3) % 3][:, :], False, True, sname=sMM)
                rsl = (h2 * 4 + tau) % 2
                P.wguard("scalar", ("rawst1", rsl))
                P.wait("scalar", sMM, gmv)
                ev = P.act("scalar", rawst[rsl][0:65, 0:512], av[0:65, 0:512],
                           AF.Identity, bias=vs65[0:65, h:h + 1])
                P.setguard(("bank", avi), "chscalar", ev)
                P.wait("sync", "chscalar", ev)
                stv = P.dma("sync", rawd[h, tau, :, :], rawst[rsl][0:65, 0:512],
                            "hr" + str(rsl))
                P.setguard(("rawst1", rsl), "hr" + str(rsl), stv)
            P.setguard(("g1",), sMM, P.val(sMM))
        P.setguard(("q1slot", hp % 2), sMM, P.val(sMM))
    P.pop()


def s5(P):
    emit_attn1(P)


STAGES["s5"] = (s5, ["q1", "k1", "v1", "vg1", "vsum1"], ["raw1"])


def s6(P):
    emit_proj_ln(P, "f", "raw1", 4, "wo1", "bo1", "x2p", "x3")


STAGES["s6"] = (s6, ["raw1", "wo1", "x2p"], ["x3"])



def emit_ffn1(P):
    P.push()
    x3b = P.alloc_b(8 * 2048)
    shgb = P.alloc_b(8 * 1024)
    shub = P.alloc_b(8 * 1024)
    shdb = P.alloc_b(8 * 1024)
    rt1wb = P.alloc_b(8 * 16)
    cwb = P.alloc_b(4 * 512)
    gateg = [P.alloc_b(8 * 512), P.alloc_b(8 * 512)]
    upg = [P.alloc_b(8 * 512), P.alloc_b(8 * 512)]
    downb = [P.alloc_b(1024) for _ in range(4)]
    hbt = [P.alloc_b(512) for _ in range(4)]
    gb = P.alloc_b(512)
    h1 = P.alloc_b(512)
    shb = P.alloc_b(8 * 512)
    logf = P.alloc_f(512)
    ltm = P.alloc_f(64)
    eqa = P.alloc_f(16)
    eqb = P.alloc_f(16)
    mx1 = P.alloc_f(1)
    mx2 = P.alloc_f(1)
    w1c = P.alloc_f(1)
    w2c = P.alloc_f(1)
    cwtm = P.alloc_f(16)
    ysb = P.alloc_f(8 * 512)
    x3t = [P.alloc_f(512), P.alloc_f(512)]
    ybf = P.alloc_b(512)
    y2b = P.alloc_b(512)
    ssum = P.alloc_f(512)
    sq = P.alloc_f(512)
    vrx = P.alloc_f(512)
    rstd = P.alloc_f(512)
    xo = [P.alloc_f(512), P.alloc_f(512)]

    x3d = P.dsrc("x3")
    hbd = P.dtmp("hb1")
    outd = P.dout_or_tmp("xout")
    EYE = CFc(P, "eye128f")
    ROWF = CFc(P, "ones_rowf", 0, 1)
    SCLF = CFc(P, "scale_rowf", 0, 1)
    OCOL = CB(P, "ones_col")
    B16 = CB(P, "B16", 0, 16)
    sMM = "jmm"

    for kc in range(8):
        P.dma("gpsimd", x3b[:, kc * 2048:(kc + 1) * 2048],
              x3d[128 * kc:128 * (kc + 1), :], "jlx")
        P.dma("gpsimd", shgb[:, kc * 1024:(kc + 1) * 1024],
              P.din("shg1")[128 * kc:128 * (kc + 1), :], "jlw")
        P.dma("gpsimd", shub[:, kc * 1024:(kc + 1) * 1024],
              P.din("shu1")[128 * kc:128 * (kc + 1), :], "jlw")
        P.dma("gpsimd", shdb[:, kc * 1024:(kc + 1) * 1024],
              P.din("shd1")[128 * kc:128 * (kc + 1), :], "jlw")
        P.dma("gpsimd", rt1wb[:, kc * 16:(kc + 1) * 16],
              P.din("rt1_w")[128 * kc:128 * (kc + 1), :], "jlw")
    P.wait("tensor", "jlx", P.val("jlx"))
    P.wait("tensor", "jlw", P.val("jlw"))

    for tau in range(4):
        P.wguard("tensor", ("bank", 0))
        for kc in range(8):
            P.mm(P.PS[0][0:16, 0:512], rt1wb[:, kc * 16:(kc + 1) * 16],
                 x3b[:, kc * 2048 + tau * 512: kc * 2048 + tau * 512 + 512],
                 kc == 0, kc == 7, sname=sMM if kc == 7 else None)
        P.wait("scalar", sMM, P.val(sMM))
        lv = P.act("scalar", logf[0:16, 0:512], P.PS[0][0:16, 0:512], AF.Identity,
                   bias=BPc(P, "rt1_b", 0, 0, 16))
        P.setguard(("bank", 0), "chscalar", lv)
        P.wait("tensor", "chscalar", lv)
        P.wguard("tensor", ("bank", 1))
        for j in range(4):
            tv = P.op("tensor", "transpose", P.PS[1][0:128, j * 16:(j + 1) * 16],
                      logf[0:16, 128 * j:128 * (j + 1)], EYE[0:16, 0:16], sname=sMM)
        P.wait("vector", sMM, tv)
        lc = P.op("vector", "tensor_copy", ltm[0:128, 0:64], P.PS[1][0:128, 0:64])
        P.setguard(("bank", 1), "chvector", lc)
        P.wguard("tensor", ("bank", 2))
        for j in range(4):
            lj = ltm[0:128, j * 16:(j + 1) * 16]
            if j > 0:
                P.wait("vector", "chscalar", P.val("chscalar"))
                P.wait("vector", sMM, P.val(sMM))
            P.op("vector", "reduce_max", mx1[0:128, 0:1], lj, AX)
            P.op("vector", "tensor_scalar", eqa[0:128, 0:16], lj, mx1[0:128, 0:1],
                 None, OP.is_equal)
            P.op("vector", "scalar_tensor_tensor", eqb[0:128, 0:16], eqa[0:128, 0:16],
                 -1e30, lj, OP.mult, OP.add)
            P.op("vector", "reduce_max", mx2[0:128, 0:1], eqb[0:128, 0:16], AX)
            P.op("vector", "tensor_scalar", eqb[0:128, 0:16], eqb[0:128, 0:16],
                 mx2[0:128, 0:1], None, OP.is_equal)
            wv_ = P.op("vector", "tensor_sub", w1c[0:128, 0:1], mx1[0:128, 0:1],
                       mx2[0:128, 0:1])
            P.wait("scalar", "chvector", wv_)
            sv_ = P.act("scalar", w1c[0:128, 0:1], w1c[0:128, 0:1], AF.Sigmoid)
            P.wait("vector", "chscalar", sv_)
            P.op("vector", "tensor_scalar", w2c[0:128, 0:1], w1c[0:128, 0:1],
                 -1.0, 1.0, OP.mult, OP.add)
            P.op("vector", "tensor_scalar", cwtm[0:128, 0:16], eqa[0:128, 0:16],
                 w1c[0:128, 0:1], None, OP.mult)
            P.op("vector", "tensor_scalar", eqb[0:128, 0:16], eqb[0:128, 0:16],
                 w2c[0:128, 0:1], None, OP.mult)
            cv_ = P.op("vector", "tensor_add", cwtm[0:128, 0:16], cwtm[0:128, 0:16],
                       eqb[0:128, 0:16])
            P.wait("tensor", "chvector", cv_)
            P.op("tensor", "transpose", P.PS[2][0:16, j * 128:(j + 1) * 128],
                 cwtm[0:128, 0:16], EYE[0:128, 0:128], sname=sMM)
        tv2 = P.val(sMM)
        P.wait("vector", sMM, tv2)
        cc_ = P.op("vector", "tensor_copy", cwb[0:16, tau * 512:(tau + 1) * 512],
                   P.PS[2][0:16, 0:512])
        P.setguard(("bank", 2), "chvector", cc_)

    P.wait("tensor", "chvector", P.val("chvector"))
    for g in range(16):
        s = g % 2
        P.wguard("gpsimd", ("jgu", s))
        for kc in range(8):
            P.dma("gpsimd", gateg[s][:, kc * 512:(kc + 1) * 512],
                  P.din("gate1")[128 * kc:128 * (kc + 1), g * 512:(g + 1) * 512],
                  "jlg" + str(s))
            P.dma("gpsimd", upg[s][:, kc * 512:(kc + 1) * 512],
                  P.din("up1")[128 * kc:128 * (kc + 1), g * 512:(g + 1) * 512],
                  "jlu" + str(s))
        P.wait("tensor", "jlg" + str(s), P.val("jlg" + str(s)))
        P.wait("tensor", "jlu" + str(s), P.val("jlu" + str(s)))
        for mq in range(4):
            mh = g * 4 + mq
            for tp in range(4):
                bG = P.PS[0 + 2 * (mh % 2)]
                bU = P.PS[1 + 2 * (mh % 2)]
                bM = P.PS[4 + (mh % 2)]
                P.wguard("tensor", ("bank", 0 + 2 * (mh % 2)))
                P.wguard("tensor", ("bank", 1 + 2 * (mh % 2)))
                P.wguard("tensor", ("bank", 4 + (mh % 2)))
                for kc in range(8):
                    P.mm(bG[:, 0:512],
                         gateg[s][:, kc * 512 + mq * 128: kc * 512 + (mq + 1) * 128],
                         x3b[:, kc * 2048 + tp * 512: kc * 2048 + tp * 512 + 512],
                         kc == 0, kc == 7)
                    P.mm(bU[:, 0:512],
                         upg[s][:, kc * 512 + mq * 128: kc * 512 + (mq + 1) * 128],
                         x3b[:, kc * 2048 + tp * 512: kc * 2048 + tp * 512 + 512],
                         kc == 0, kc == 7)
                bmv = P.mm(bM[:, 0:512], B16[0:16, mh * 128:(mh + 1) * 128],
                           cwb[0:16, tp * 512:(tp + 1) * 512], True, True, sname=sMM)
                P.wait("scalar", sMM, bmv)
                P.wguard("scalar", ("jgb",))
                sv_ = P.act("scalar", gb[:, :], bG[:, 0:512], AF.Silu)
                P.setguard(("bank", 0 + 2 * (mh % 2)), "chscalar", sv_)
                P.wait("vector", "chscalar", sv_)
                P.op("vector", "tensor_mul", h1[:, :], gb[:, :], bU[:, 0:512])
                P.wguard("vector", ("jhb", mh % 2))
                hv_ = P.op("vector", "tensor_mul", hbt[mh % 2][:, :], h1[:, :],
                           bM[:, 0:512])
                P.setguard(("jgb",), "chvector", hv_)
                P.setguard(("bank", 1 + 2 * (mh % 2)), "chvector", hv_)
                P.setguard(("bank", 4 + (mh % 2)), "chvector", hv_)
                P.wguard("sync", ("jhb", mh % 2))
                P.wait("sync", "chvector", hv_)
                so_ = P.dma("sync", hbd[mh, tp, :, :], hbt[mh % 2][:, :],
                            "jsh" + str(mh % 2))
                P.setguard(("jhb", mh % 2), "jsh" + str(mh % 2), so_)
        P.setguard(("jgu", s), sMM, P.val(sMM))
    P.fence()

    for tau in range(4):
        for m in range(8):
            bG = P.PS[0 + 2 * (m % 2)]
            bU = P.PS[1 + 2 * (m % 2)]
            P.wguard("tensor", ("bank", 0 + 2 * (m % 2)))
            P.wguard("tensor", ("bank", 1 + 2 * (m % 2)))
            for kc in range(8):
                P.mm(bG[:, 0:512],
                     shgb[:, kc * 1024 + m * 128: kc * 1024 + (m + 1) * 128],
                     x3b[:, kc * 2048 + tau * 512: kc * 2048 + tau * 512 + 512],
                     kc == 0, kc == 7)
                P.mm(bU[:, 0:512],
                     shub[:, kc * 1024 + m * 128: kc * 1024 + (m + 1) * 128],
                     x3b[:, kc * 2048 + tau * 512: kc * 2048 + tau * 512 + 512],
                     kc == 0, kc == 7, sname=sMM if kc == 7 else None)
            smv = P.val(sMM)
            P.wait("scalar", sMM, smv)
            P.wguard("scalar", ("jshb",))
            sv_ = P.act("scalar", gb[:, :], bG[:, 0:512], AF.Silu)
            P.setguard(("bank", 0 + 2 * (m % 2)), "chscalar", sv_)
            P.wait("vector", "chscalar", sv_)
            shv = P.op("vector", "tensor_mul", shb[:, m * 512:(m + 1) * 512],
                       gb[:, :], bU[:, 0:512])
            P.setguard(("bank", 1 + 2 * (m % 2)), "chvector", shv)
            P.setguard(("jshb",), "chvector", shv)
        P.wait("tensor", "chvector", P.val("chvector"))
        for dm in range(8):
            P.wguard("tensor", ("bank", dm))
        def emit_dload(mh):
            s = mh % 4
            P.wguard("gpsimd", ("jdw", s))
            dl = P.dma("gpsimd", downb[s][:, :],
                       P.din("down1")[128 * mh:128 * (mh + 1), :], "jld" + str(s))
            P.wguard("sync", ("jht", s))
            hl = P.dma("sync", hbt[s][:, :], hbd[mh, tau, :, :], "jlh" + str(s))
            return dl, hl

        lds = {0: emit_dload(0), 1: emit_dload(1), 2: emit_dload(2)}
        for mh in range(64):
            s = mh % 4
            if mh + 3 < 64:
                lds[mh + 3] = emit_dload(mh + 3)
            dl, hl = lds.pop(mh)
            P.wait("tensor", "jld" + str(s), dl)
            P.wait("tensor", "jlh" + str(s), hl)
            for dm in range(8):
                mk = P.mm(P.PS[dm][:, 0:512],
                          downb[s][:, dm * 128:(dm + 1) * 128], hbt[s][:, :],
                          mh == 0, False, sname=sMM if dm == 7 else None)
            P.setguard(("jdw", s), sMM, P.val(sMM))
            P.setguard(("jht", s), sMM, P.val(sMM))
        for kc in range(8):
            for dm in range(8):
                P.mm(P.PS[dm][:, 0:512],
                     shdb[:, kc * 1024 + dm * 128: kc * 1024 + (dm + 1) * 128],
                     shb[:, kc * 512:(kc + 1) * 512], False, kc == 7,
                     sname=sMM if dm == 7 else None)
        dnv = P.val(sMM)
        P.wait("scalar", sMM, dnv)
        for dm in range(8):
            rs = dm % 2
            P.wguard("sync", ("jxr", rs))
            P.dma("sync", x3t[rs][:, 0:512],
                  x3d[128 * dm:128 * (dm + 1), tau * 512:(tau + 1) * 512],
                  "jlr" + str(rs))
            ev = P.act("scalar", ysb[:, dm * 512:(dm + 1) * 512],
                       P.PS[dm][:, 0:512], AF.Copy)
            P.setguard(("bank", dm), "chscalar", ev)
            P.wait("vector", "chscalar", ev)
            P.wait("vector", "jlr" + str(rs), P.val("jlr" + str(rs)))
            av = P.op("vector", "tensor_add", ysb[:, dm * 512:(dm + 1) * 512],
                      ysb[:, dm * 512:(dm + 1) * 512], x3t[rs][:, 0:512])
            P.setguard(("jxr", rs), "chvector", av)
        for dm in range(8):
            P.wguard("vector", ("jyb",))
            P.op("vector", "tensor_copy", ybf[:, :], ysb[:, dm * 512:(dm + 1) * 512])
            cv = P.op("vector", "tensor_mul", y2b[:, :], ybf[:, :], ybf[:, :])
            P.wait("tensor", "chvector", cv)
            if dm == 0:
                P.wguard("tensor", ("bank", 4))
                P.wguard("tensor", ("bank", 5))
            P.mm(P.PS[4][0:1, 0:512], OCOL, ybf[:, :], dm == 0, dm == 7)
            sm = P.mm(P.PS[5][0:1, 0:512], OCOL, y2b[:, :], dm == 0, dm == 7,
                      sname=sMM)
            P.setguard(("jyb",), sMM, sm)
        stat_mm = P.val(sMM)
        P.wait("scalar", sMM, stat_mm)
        P.act("scalar", ssum[0:1, 0:512], P.PS[4][0:1, 0:512], AF.Copy)
        c2 = P.act("scalar", sq[0:1, 0:512], P.PS[5][0:1, 0:512], AF.Copy)
        P.setguard(("bank", 4), "chscalar", c2)
        P.setguard(("bank", 5), "chscalar", c2)
        P.wait("vector", "chscalar", c2)
        P.op("vector", "tensor_mul", vrx[0:1, 0:512], ssum[0:1, 0:512],
             ssum[0:1, 0:512])
        P.op("vector", "scalar_tensor_tensor", vrx[0:1, 0:512], vrx[0:1, 0:512],
             -1.0 / 1024.0, sq[0:1, 0:512], OP.mult, OP.add)
        sv2 = P.op("vector", "tensor_scalar", vrx[0:1, 0:512], vrx[0:1, 0:512],
                   1.0 / 1024.0, 1e-5, OP.mult, OP.add)
        P.wait("scalar", "chvector", sv2)
        P.act("scalar", vrx[0:1, 0:512], vrx[0:1, 0:512], AF.Ln)
        rv = P.act("scalar", rstd[0:1, 0:512], vrx[0:1, 0:512], AF.Exp, scale=-0.5)
        P.wait("tensor", "chscalar", rv)
        P.wguard("tensor", ("bank", 6))
        P.wguard("tensor", ("bank", 7))
        P.mm(P.PS[6][0:128, 0:512], SCLF, ssum[0:1, 0:512], True, True)
        bm2 = P.mm(P.PS[7][0:128, 0:512], ROWF, rstd[0:1, 0:512], True, True,
                   sname=sMM)
        P.wait("vector", sMM, bm2)
        for dm in range(8):
            xs = xo[dm % 2]
            P.wguard("vector", ("jxo", dm % 2))
            P.op("vector", "tensor_sub", xs[:, 0:512],
                 ysb[:, dm * 512:(dm + 1) * 512], P.PS[6][0:128, 0:512])
            xv = P.op("vector", "tensor_mul", xs[:, 0:512], xs[:, 0:512],
                      P.PS[7][0:128, 0:512])
            P.wait("sync", "chvector", xv)
            so_ = P.dma("sync", outd[128 * dm:128 * (dm + 1),
                                     tau * 512:(tau + 1) * 512], xs[:, 0:512],
                        "jso" + str(dm % 2))
            P.setguard(("jxo", dm % 2), "jso" + str(dm % 2), so_)
        P.setguard(("bank", 6), "chvector", P.val("chvector"))
        P.setguard(("bank", 7), "chvector", P.val("chvector"))
    P.pop()


def s7(P):
    emit_ffn1(P)


STAGES["s7"] = (s7, ["x3", "rt1_w", "up1", "gate1", "down1", "shg1", "shu1",
                     "shd1"], ["xout"])


def _dout_or_tmp(P, name):
    if name in P.export_names:
        return P.dout(name)
    return P.dtmp(name)


Prog.dout_or_tmp = _dout_or_tmp



def s0(P):
    emit_qkv(P, "a", "x0T",
             ("w_attn", 0), ("w_attn", 1024), ("w_attn", 2048),
             ("b_attn", 0), ("b_attn", 8), "bv_row",
             "q0", "k0", "v0", nq_tiles=8)


STAGES["s0"] = (s0, ["x0T", "w_attn", "bv_row"], ["q0", "k0", "v0"])


def build_launch(stages, export_names):
    P = Prog()
    P.export_names = export_names
    emit_consts(P)
    P.stage_bounds = {}
    for s in stages:
        P.cur_scope = s
        STAGES[s][0](P)
        P.fence()
        P.stage_bounds[s] = {e: len(P.streams[e]) for e in ENGS}
    P.cur_scope = None
    for name, rec in list(P.sems.items()):
        if rec[1] > 0:
            P.wait("sync", name, rec[1])
    P.finalize()
    return P


_BUILD_CACHE = {}


def kernel(**inputs):
    sh, per = host_prep(inputs)
    env = [dict() for _ in range(8)]
    produced_by_launch = []
    all_stage_list = [s for grp in LAUNCH_PLAN for s in grp]
    for li, grp in enumerate(LAUNCH_PLAN):
        later = set(s for g in LAUNCH_PLAN[li + 1:] for s in g)
        produces = set(p for s in grp for p in STAGES[s][2])
        consumed_later = set()
        for s in later:
            consumed_later.update(STAGES[s][1])
        exports = (produces & consumed_later) | (produces & {"xout"})
        if li == len(LAUNCH_PLAN) - 1:
            exports |= produces
        key = (li, tuple(grp), tuple(sorted(exports)))
        if key not in _BUILD_CACHE:
            _BUILD_CACHE[key] = build_launch(grp, exports)
        P = _BUILD_CACHE[key]
        in_maps = []
        for c in range(8):
            m = {}
            for name in P.inputs:
                if name in sh:
                    m[name] = sh[name]
                elif name in per[c]:
                    m[name] = per[c][name]
                else:
                    m[name] = env[c][name]
            in_maps.append(m)
        res = run_bass_kernel_spmd(P.nc, in_maps, core_ids=list(range(8)))
        for c in range(8):
            env[c].update(res.results[c])
    out = np.zeros((4, 4096, 1024), np.float32)
    for c in range(8):
        b, h = c // 2, c % 2
        if "xout" in env[c]:
            out[b, h * 2048:(h + 1) * 2048, :] = env[c]["xout"].T
    kernel.env = env
    return out


# version 14
# speedup vs baseline: 1.2330x; 1.1812x over previous
import numpy as np
import ml_dtypes
from contextlib import ExitStack

import concourse.bass as bass
import concourse.mybir as mybir
from concourse.bass_utils import run_bass_kernel_spmd

F32 = mybir.dt.float32
BF16 = mybir.dt.bfloat16
I32 = mybir.dt.int32
AF = mybir.ActivationFunctionType
OP = mybir.AluOpType
BF = ml_dtypes.bfloat16

RSQ2 = 0.70710678118654752

SEM_ALIAS = {}
for _n in ("amm", "bmm", "cmm", "dmm", "emm", "fmm", "gmm", "hmm", "jmm"):
    SEM_ALIAS[_n] = "Tmm"
SEM_ALIAS["bms"] = "Tms"
for _t in ("a", "e"):
    SEM_ALIAS[_t + "lx"] = "Glx"
    SEM_ALIAS[_t + "lb"] = "Glb"
    SEM_ALIAS[_t + "lw0"] = "Glw0"
    SEM_ALIAS[_t + "lw1"] = "Glw1"
    for _i in range(4):
        SEM_ALIAS[_t + "st" + str(_i)] = "So" + str(_i)
    for _i in range(2):
        SEM_ALIAS[_t + "stv" + str(_i)] = "Sv" + str(_i)
SEM_ALIAS.update({"bkld": "Gkl", "hkld": "Gkl", "bvld": "Gvl", "hvld": "Gvl",
                  "dlw": "Gw", "dlw2": "Gw", "jlw": "Gw", "dli": "Gli",
                  "jlx": "Gli", "hgld": "Sg", "gst": "Sgst"})
for _i in range(4):
    SEM_ALIAS["jld" + str(_i)] = "Gd" + str(_i)
    SEM_ALIAS["jlh" + str(_i)] = "Sjh" + str(_i)
for _i in range(2):
    SEM_ALIAS["bq" + str(_i)] = "SQ" + str(_i)
    SEM_ALIAS["hq" + str(_i)] = "SQ" + str(_i)
    SEM_ALIAS["br" + str(_i)] = "Sw" + str(_i)
    SEM_ALIAS["hr" + str(_i)] = "Sw" + str(_i)
    SEM_ALIAS["jlg" + str(_i)] = "Gg" + str(_i)
    SEM_ALIAS["jlu" + str(_i)] = "Gu" + str(_i)
    SEM_ALIAS["jld" + str(_i)] = "Gd" + str(_i)
    SEM_ALIAS["jlh" + str(_i)] = "Gh" + str(_i)
    SEM_ALIAS["gld" + str(_i)] = "Gw2" + str(_i)
    SEM_ALIAS["pli" + str(_i)] = "Gp" + str(_i)
    SEM_ALIAS["pgi" + str(_i)] = "Gi" + str(_i)
    SEM_ALIAS["pst" + str(_i)] = "Sp" + str(_i)
    SEM_ALIAS["jsh" + str(_i)] = "Sh" + str(_i)
    SEM_ALIAS["jso" + str(_i)] = "Sy" + str(_i)
    SEM_ALIAS["dso" + str(_i)] = "Sy" + str(_i)
    SEM_ALIAS["cso" + str(_i)] = "Sy" + str(_i)
    SEM_ALIAS["fso" + str(_i)] = "Sy" + str(_i)
    SEM_ALIAS["jlr" + str(_i)] = "Sr" + str(_i)
    SEM_ALIAS["dlr" + str(_i)] = "Sr" + str(_i)
    SEM_ALIAS["clr" + str(_i)] = "Sr" + str(_i)
    SEM_ALIAS["flr" + str(_i)] = "Sr" + str(_i)
    SEM_ALIAS["clx" + str(_i)] = "Sx" + str(_i)
    SEM_ALIAS["flx" + str(_i)] = "Sx" + str(_i)
for _i in range(4):
    SEM_ALIAS["clq" + str(_i)] = "Sq" + str(_i)
    SEM_ALIAS["flq" + str(_i)] = "Sq" + str(_i)
SEM_ALIAS.update({"cld": "Scl", "fld": "Scl", "clw": "Scw", "flw": "Scw",
                  "cst": "Sct", "fst": "Sct", "dlx": "Gdx", "dsx": "Sdx2"})
DBG_PHASES = ("q", "k", "v")
DBG_MLIM = 8
DBG_TBLIM = 32
DBG_TAULIM = 8
DBG_S3TAU = 8
DBG_S5HP = 8
DBG_S5GLOB = 1
ENGS = ("tensor", "scalar", "vector", "gpsimd", "sync")

LAUNCH_PLAN = [["s0", "s1", "s2", "s3", "sp", "s4", "s5", "s6", "s7"]]

ACT_SHAPES = {
    "x0T": ([1024, 4096], F32),
    "q0": ([1024, 4096], BF16),
    "k0": ([1024, 4096], BF16),
    "v0": ([32, 128, 1040], BF16),
    "raw0": ([16, 8, 65, 512], F32),
    "x1": ([1024, 4096], F32),
    "x2": ([1024, 4096], F32),
    "x2p": ([1024, 4096], F32),
    "q1": ([1024, 2048], BF16),
    "k1": ([1024, 4096], BF16),
    "v1": ([32, 128, 1040], BF16),
    "vg1": ([64, 1040], BF16),
    "vsum1": ([65, 16], F32),
    "raw1": ([16, 4, 65, 512], F32),
    "x3": ([1024, 2048], F32),
    "hb1": ([64, 4, 128, 512], BF16),
    "xout": ([1024, 2048], F32),
    "permidx": ([2048, 1], I32),
    "c_rd": ([128, 512], F32),
    "intg": ([1024, 4096], BF16),
    "f_rd": ([64, 512], F32),
}

WEIGHT_SHAPES = {
    "w_attn": [1024, 3072], "w_proj0": [1024, 1024],
    "wq1": [1024, 1024], "wk1": [1024, 1024], "wv1": [1024, 1024], "wo1": [1024, 1024],
    "r_w1": [1024, 256], "r_w2": [256, 8], "h_w1": [1024, 256], "h_w2": [256, 1],
    "e_w1": [2048, 512], "e_w2": [512, 3072], "e_b2": [8, 3072],
    "s_w1": [2048, 64], "s_w2": [64, 3072],
    "f_w1": [1024, 2048], "f_w2": [2048, 1024],
    "rt1_w": [1024, 16], "up1": [1024, 8192], "gate1": [1024, 8192],
    "down1": [8192, 1024], "shg1": [1024, 1024], "shu1": [1024, 1024],
    "shd1": [1024, 1024],
    "bv_row": [1, 1024], "bv1_row": [1, 1024],
}

STAGES = {}



BIAS_COLS = {}
N_BIAS_COLS = 0
CONST_COLS = {}
N_CONSTB = 0


def _build_bias_pack(inp):
    global BIAS_COLS, N_BIAS_COLS
    BIAS_COLS = {}
    cols = []

    def addvec(name, v):
        v = np.asarray(v, np.float32).reshape(-1)
        ncol = (v.size + 127) // 128
        a = np.zeros((ncol, 128), np.float32)
        a.reshape(-1)[:v.size] = v
        BIAS_COLS[name] = len(cols)
        cols.extend(list(a))

    addvec("r_b1", inp["L0_ffn_router_b1"])
    addvec("h_b1", inp["L0_ffn_halt_b1"])
    addvec("e_b1", inp["L0_ffn_expert_b1"].reshape(-1))
    addvec("s_b1", inp["L0_ffn_shared_b1"])
    addvec("f_b1", inp["L0_ffn_refine_b1"])
    BIAS_COLS["_gelu_end"] = len(cols)
    addvec("b_attn", inp["L0_attn_b_attn"])
    addvec("b_proj0", inp["L0_attn_b_proj"])
    addvec("r_b2", inp["L0_ffn_router_b2"])
    addvec("h_b2", inp["L0_ffn_halt_b2"])
    addvec("s_b2", inp["L0_ffn_shared_b2"])
    addvec("f_b2", inp["L0_ffn_refine_b2"])
    addvec("mu", inp["L0_ffn_mu"])
    addvec("iw", inp["L0_ffn_integration_weight"])
    addvec("sw", inp["L0_ffn_shared_weight"])
    addvec("bq1", inp["L1_attn_bq"])
    addvec("bk1", inp["L1_attn_bk"])
    addvec("bv1", inp["L1_attn_bv"])
    addvec("bo1", inp["L1_attn_bo"])
    addvec("rt1_b", inp["L1_ffn_router_bias"])
    N_BIAS_COLS = len(cols)
    return np.ascontiguousarray(np.stack(cols, axis=1), np.float32)


def _build_constb():
    global CONST_COLS, N_CONSTB
    CONST_COLS = {}
    parts = []
    off = 0

    def add(name, arr):
        nonlocal off
        a = np.zeros((128, arr.shape[1]), np.float32)
        a[:arr.shape[0], :] = arr
        CONST_COLS[name] = (off, arr.shape[1])
        parts.append(a)
        off += arr.shape[1]

    p = np.arange(128).reshape(128, 1)
    col = np.arange(896).reshape(1, 896)
    add("maskCM", (p <= col - 384).astype(np.float32))
    add("eye128", np.eye(128, dtype=np.float32))
    j = np.arange(512)
    add("B8", ((j // 64)[None, :] == np.arange(8)[:, None]).astype(np.float32))
    j = np.arange(8192)
    add("B16", ((j // 512)[None, :] == np.arange(16)[:, None]).astype(np.float32))
    add("ones_row", np.ones((1, 128), np.float32))
    add("ones_col", np.ones((128, 1), np.float32))
    m = np.zeros((64, 32), np.float32)
    for g in range(64):
        jj = 2 * g if g < 32 else 2 * (g - 32) + 1
        if jj < 32:
            for i in range(32):
                if i // 8 == jj // 8:
                    m[g, i] = 1.0
    add("corrmask", m)
    N_CONSTB = off
    return np.ascontiguousarray(np.concatenate(parts, axis=1).astype(BF))


def _build_constf():
    global CONSTF_COLS, N_CONSTF
    CONSTF_COLS = {}
    parts = []
    off = 0

    def add(name, arr):
        nonlocal off
        a = np.zeros((128, arr.shape[1]), np.float32)
        a[:arr.shape[0], :] = arr
        CONSTF_COLS[name] = (off, arr.shape[1])
        parts.append(a)
        off += arr.shape[1]

    add("eye128f", np.eye(128, dtype=np.float32))
    add("ones_rowf", np.ones((1, 128), np.float32))
    add("scale_rowf", np.full((1, 128), 1.0 / 1024.0, np.float32))
    N_CONSTF = off
    return np.ascontiguousarray(np.concatenate(parts, axis=1), np.float32)


def host_prep(inp):
    sh = {}
    sh["w_attn"] = np.ascontiguousarray(inp["L0_attn_w_attn"], np.float32)
    sh["w_proj0"] = np.ascontiguousarray(inp["L0_attn_w_proj"], np.float32)
    sh["wq1"] = np.ascontiguousarray(inp["L1_attn_wq"].T, np.float32)
    sh["wk1"] = np.ascontiguousarray(inp["L1_attn_wk"].T, np.float32)
    sh["wv1"] = np.ascontiguousarray(inp["L1_attn_wv"].T, np.float32)
    sh["wo1"] = np.ascontiguousarray(inp["L1_attn_wo"].T, np.float32)
    sh["r_w1"] = np.ascontiguousarray(inp["L0_ffn_router_w1"], np.float32)
    sh["r_w2"] = np.ascontiguousarray(inp["L0_ffn_router_w2"], np.float32)
    sh["h_w1"] = np.ascontiguousarray(inp["L0_ffn_halt_w1"], np.float32)
    sh["h_w2"] = np.ascontiguousarray(inp["L0_ffn_halt_w2"], np.float32)
    sh["e_w1"] = np.ascontiguousarray(
        inp["L0_ffn_expert_w1"].transpose(1, 0, 2).reshape(2048, 512), np.float32)
    sh["e_w2"] = np.ascontiguousarray(
        inp["L0_ffn_expert_w2"].reshape(512, 3072), np.float32)
    sh["e_b2"] = np.ascontiguousarray(inp["L0_ffn_expert_b2"], np.float32)
    sh["s_w1"] = np.ascontiguousarray(inp["L0_ffn_shared_w1"], np.float32)
    sh["s_w2"] = np.ascontiguousarray(inp["L0_ffn_shared_w2"], np.float32)
    sh["f_w1"] = np.ascontiguousarray(inp["L0_ffn_refine_w1"], np.float32)
    sh["f_w2"] = np.ascontiguousarray(inp["L0_ffn_refine_w2"], np.float32)
    sh["rt1_w"] = np.ascontiguousarray(inp["L1_ffn_router_w"].T, np.float32)
    sh["up1"] = np.ascontiguousarray(
        inp["L1_ffn_up_proj"].transpose(1, 0, 2).reshape(1024, 8192), np.float32)
    sh["gate1"] = np.ascontiguousarray(
        inp["L1_ffn_gate_proj"].transpose(1, 0, 2).reshape(1024, 8192), np.float32)
    sh["down1"] = np.ascontiguousarray(
        inp["L1_ffn_down_proj"].reshape(8192, 1024), np.float32)
    sh["shg1"] = np.ascontiguousarray(inp["L1_ffn_sh_gate"].T, np.float32)
    sh["shu1"] = np.ascontiguousarray(inp["L1_ffn_sh_up"].T, np.float32)
    sh["shd1"] = np.ascontiguousarray(inp["L1_ffn_sh_down"].T, np.float32)
    sh["bv_row"] = np.ascontiguousarray(
        inp["L0_attn_b_attn"][2048:3072].reshape(1, 1024), np.float32)
    sh["bv1_row"] = np.ascontiguousarray(
        inp["L1_attn_bv"].reshape(1, 1024), np.float32)
    sh["bias_pack"] = _build_bias_pack(inp)
    sh["constb"] = _build_constb()
    sh["constf"] = _build_constf()

    per = []
    for c in range(8):
        b, h = c // 2, c % 2
        d = {}
        d["x0T"] = np.ascontiguousarray(inp["L0_attn_hidden_states"][b].T, np.float32)
        d["permidx"] = (np.arange(2048, dtype=np.int32) ^ h).reshape(2048, 1)
        per.append(d)
    return sh, per



class Prog:
    def __init__(self):
        self.nc = bass.Bass("TRN2", target_bir_lowering=False)
        self.es = ExitStack()
        self.inputs = set()
        self.outputs = set()
        self.dram = {}
        self.sems = {}
        self.streams = {e: [] for e in ENGS}
        self.waited = {e: {} for e in ENGS}
        self.guards = {}
        self.AR = self.es.enter_context(self.nc.sbuf_tensor("AR", [128, 101376], BF16))
        self.ar_free = 0
        self.ar_base_stack = []
        self.PSH = [self.es.enter_context(
            self.nc.psum_tensor(f"PSH{i}", [128, 1024], F32)) for i in range(4)]
        self.PS = [self.PSH[i // 2][:, (i % 2) * 512:(i % 2) * 512 + 512]
                   for i in range(8)]

    def din(self, name):
        if name not in self.dram:
            if name == "bias_pack":
                shape, dt = [128, N_BIAS_COLS], F32
            elif name == "constb":
                shape, dt = [128, N_CONSTB], BF16
            elif name == "constf":
                shape, dt = [128, N_CONSTF], F32
            elif name in ACT_SHAPES:
                shape, dt = ACT_SHAPES[name]
            else:
                shape, dt = WEIGHT_SHAPES[name], F32
            self.dram[name] = self.nc.dram_tensor(
                name, list(shape), dt, kind="ExternalInput").ap()
            self.inputs.add(name)
        return self.dram[name]

    def dout(self, name):
        if name not in self.dram:
            shape, dt = ACT_SHAPES[name]
            self.dram[name] = self.nc.dram_tensor(
                name, list(shape), dt, kind="ExternalOutput").ap()
            self.outputs.add(name)
        return self.dram[name]

    def dsrc(self, name):
        if name in self.dram:
            return self.dram[name]
        return self.din(name)

    def fence(self):
        for sname, rec in list(self.sems.items()):
            if rec[1] > 0:
                for eng in ENGS:
                    self.wait(eng, sname, rec[1])

    def dtmp(self, name):
        if name not in self.dram:
            shape, dt = ACT_SHAPES[name]
            self.dram[name] = self.nc.dram_tensor(
                name, list(shape), dt, kind="Internal").ap()
        return self.dram[name]

    def alloc_b(self, n):
        off = self.ar_free
        self.ar_free += (n + 1) & ~1
        assert self.ar_free <= 101376, f"SBUF arena overflow: {self.ar_free}"
        return self.AR[:, off:off + n]

    def alloc_f(self, n):
        return self.alloc_b(2 * n).bitcast(F32)

    def push(self):
        self.ar_base_stack.append(self.ar_free)

    def pop(self):
        self.ar_free = self.ar_base_stack.pop()

    def sem(self, name):
        name = SEM_ALIAS.get(name, name)
        if name not in self.sems:
            h = self.es.enter_context(self.nc.semaphore(name))
            self.sems[name] = [h, 0]
        return self.sems[name][0]

    def tick(self, name, amt, agent=None):
        name = SEM_ALIAS.get(name, name)
        self.sem(name)
        s = self.sems[name]
        if agent is not None:
            if len(s) < 3:
                s.append(agent)
            else:
                assert s[2] == agent, (
                    f"semaphore {name} incremented by {agent} and {s[2]}")
        s[1] += amt
        return s[1]

    def val(self, name):
        name = SEM_ALIAS.get(name, name)
        return self.sems.get(name, [None, 0])[1]

    def emit(self, eng, fn):
        self.streams[eng].append((getattr(self, "cur_scope", None), fn))

    def wait(self, eng, name, v):
        name = SEM_ALIAS.get(name, name)
        if v is None or v <= 0:
            return
        w = self.waited[eng]
        if w.get(name, 0) >= v:
            return
        w[name] = v
        h = self.sem(name)
        self.emit(eng, lambda e, h=h, v=v: e.wait_ge(h, v))

    def wguard(self, eng, key):
        g = self.guards.get(key)
        if g is not None:
            self.wait(eng, g[0], g[1])

    def setguard(self, key, name, v):
        self.guards[key] = (name, v)

    def idma(self, out, in_, idx_ap, sname):
        h = self.sem(sname)
        self.emit("gpsimd", lambda e, o=out, i=in_, x=idx_ap, h=h:
                  e.indirect_dma_start(
                      out=o, out_offset=None, in_=i,
                      in_offset=bass.IndirectOffsetOnAxis(x, 0)).then_inc(h, 16))
        return self.tick(sname, 16, agent="dma-gpsimd")

    def dma(self, eng, out, in_, sname):
        h = self.sem(sname)
        self.emit(eng, lambda e, o=out, i=in_, h=h: e.dma_start(out=o, in_=i).then_inc(h, 16))
        return self.tick(sname, 16, agent="dma-" + eng)

    def mm(self, bank, lhsT, rhs, start, stop, sname=None):
        if sname is None:
            self.emit("tensor", lambda t, b=bank, l=lhsT, r=rhs, s1=start, s2=stop:
                      t.matmul(b, l, r, start=s1, stop=s2))
            return None
        h = self.sem(sname)
        self.emit("tensor", lambda t, b=bank, l=lhsT, r=rhs, s1=start, s2=stop, h=h:
                  t.matmul(b, l, r, start=s1, stop=s2).then_inc(h, 1))
        return self.tick(sname, 1, agent="tensor")

    def act(self, eng, out, in_, func, bias=0.0, scale=1.0, sname=None, accum=None):
        ch = "ch" + eng
        self.wait(eng, ch, self.val(ch))
        def f(e, o=out, i=in_, fu=func, b=bias, s=scale, a=accum):
            kw = {}
            if a is not None:
                kw["accum_out"] = a
            return e.activation(out=o, in_=i, func=fu, bias=b, scale=s, **kw)
        h = self.sem(ch)
        self.emit(eng, lambda e, h=h: f(e).then_inc(h, 1))
        return self.tick(ch, 1, agent=eng)

    def op(self, eng, meth, *args, sname=None, **kw):
        ch = "ch" + eng
        if eng != "tensor":
            self.wait(eng, ch, self.val(ch))
            assert sname in (None, ch), (eng, sname)
            sname = ch
        use = sname if sname is not None else ch
        h = self.sem(use)
        self.emit(eng, lambda e, a=args, k=kw, m=meth, h=h:
                  getattr(e, m)(*a, **k).then_inc(h, 1))
        return self.tick(use, 1, agent="tensor" if eng == "tensor" else eng)

    def finalize(self):
        def run(e, fns):
            cur = None
            cur_id = None
            for sc, f in fns:
                if sc != cur:
                    if cur is not None:
                        self.nc.leave_named_scope(cur, cur_id, False)
                    if sc is not None:
                        cur_id, _ = self.nc.enter_named_scope(sc, False)
                    cur = sc
                f(e)
            if cur is not None:
                self.nc.leave_named_scope(cur, cur_id, False)
        with self.nc.Block() as block:
            for eng in ENGS:
                fns = self.streams[eng]
                if not fns:
                    continue
                getattr(block, eng)(lambda e, fns=fns: run(e, fns))
        return self.nc



def emit_consts(P):
    cb = P.din("constb")
    cf = P.din("constf")
    bp = P.din("bias_pack")
    P.CB = P.alloc_b(N_CONSTB)
    P.CF = P.alloc_f(N_CONSTF)
    P.BP = P.alloc_f(N_BIAS_COLS)
    ng = BIAS_COLS["_gelu_end"]
    P.BPg_sc = P.alloc_f(ng)
    P.BPg_h = P.alloc_f(ng)
    P.BPn = P.alloc_f(24)
    P.dma("sync", P.CB[:, :], cb[:, :], "cld")
    P.dma("sync", P.CF[:, :], cf[:, :], "cld")
    v = P.dma("sync", P.BP[:, :], bp[:, :], "cld")
    P.wait("vector", "cld", v)
    P.op("vector", "tensor_scalar_mul", P.BPg_sc[:, :], P.BP[:, 0:ng], RSQ2)
    P.op("vector", "tensor_scalar_mul", P.BPg_h[:, :], P.BP[:, 0:ng], 0.5)
    sb2c0 = BIAS_COLS["s_b2"]
    P.op("vector", "tensor_scalar_mul", P.BPn[:, :],
         P.BP[:, sb2c0:sb2c0 + 24], -1.0)
    for eng in ENGS:
        P.wait(eng, "cld", P.val("cld"))
        P.wait(eng, "chvector", P.val("chvector"))


def CB(P, name, r0=0, r1=128, c0=0, c1=None):
    off, n = CONST_COLS[name]
    c1 = n if c1 is None else c1
    return P.CB[r0:r1, off + c0:off + c1]


def CFc(P, name, r0=0, r1=128, c0=0, c1=None):
    off, n = CONSTF_COLS[name]
    c1 = n if c1 is None else c1
    return P.CF[r0:r1, off + c0:off + c1]


def BPc(P, name, j=0, r0=0, r1=128):
    return P.BP[r0:r1, BIAS_COLS[name] + j: BIAS_COLS[name] + j + 1]



def emit_qkv(P, tag, xname, wq, wk, wv, bq, bk, bvrow_name, qname, kname, vname,
             nq_tiles):
    P.push()
    xb = P.alloc_b(8 * 4096)
    wbuf = [P.alloc_b(8 * 1024), P.alloc_b(8 * 1024)]
    ev = P.alloc_b(4 * 512)
    vsb = [P.alloc_b(1040), P.alloc_b(1040)]
    bvb = P.alloc_b(1024)
    sMM, sEV, sVV, sST = tag + "mm", "chscalar", "chvector", tag + "st"
    sLX, sLB = tag + "lx", tag + "lb"
    sLW = [tag + "lw0", tag + "lw1"]

    xsrc = P.dsrc(xname)
    bvrow = P.din(bvrow_name)
    qd, kd, vd = P.dout_or_tmp(qname), P.dout_or_tmp(kname), P.dout_or_tmp(vname)

    for kc in range(8):
        P.dma("gpsimd", xb[:, kc * 4096:(kc + 1) * 4096],
              xsrc[128 * kc:128 * (kc + 1), :], sLX)
    P.dma("gpsimd", bvb[0:1, :], bvrow[0:1, :], sLB)

    def load_w(phase, wspec):
        dname, col0 = wspec
        wd = P.din(dname)
        buf = wbuf[phase % 2]
        P.wguard("gpsimd", ("wbuf", tag, phase % 2))
        sem = sLW[phase % 2]
        for kc in range(8):
            P.dma("gpsimd", buf[:, kc * 1024:(kc + 1) * 1024],
                  wd[128 * kc:128 * (kc + 1), col0:col0 + 1024], sem)
        return buf, sem, P.val(sem)

    _phases = [("q", wq, nq_tiles, qd, bq), ("k", wk, 8, kd, bk)]
    _phases = [p for p in _phases if p[0] in DBG_PHASES]
    for phase, (_pn, wspec, ntile, outd, bcol) in enumerate(_phases):
        buf, wsem, wtot = load_w(phase, wspec)
        P.wait("tensor", wsem, wtot)
        P.wait("tensor", sLX, P.val(sLX))
        bname, bj = bcol
        for m in range(DBG_MLIM):
            for g0 in range(0, ntile, 4):
                gt = list(range(g0, min(g0 + 4, ntile)))
                bi = ((m * ((ntile + 3) // 4) + g0 // 4) % 2) * 4
                for ti in range(len(gt)):
                    P.wguard("tensor", ("bank", bi + ti))
                for kc in range(8):
                    lhs = buf[:, kc * 1024 + m * 128: kc * 1024 + (m + 1) * 128]
                    for ti, tile in enumerate(gt):
                        P.mm(P.PS[bi + ti][:, 0:512], lhs,
                             xb[:, kc * 4096 + tile * 512: kc * 4096 + tile * 512 + 512],
                             kc == 0, kc == 7,
                             sname=sMM if (kc == 7 and ti == len(gt) - 1) else None)
                mmv = P.val(sMM)
                P.wait("scalar", sMM, mmv)
                for ti, tile in enumerate(gt):
                    evslot = ev[:, ti * 512:(ti + 1) * 512]
                    P.wguard("scalar", ("ev", tag, ti))
                    e = P.act("scalar", evslot, P.PS[bi + ti][:, 0:512], AF.Identity,
                              bias=BPc(P, bname, bj + m), sname=sEV)
                    P.setguard(("bank", bi + ti), sEV, e)
                    P.wait("sync", sEV, e)
                    st = P.dma("sync", outd[128 * m:128 * (m + 1),
                                            tile * 512:(tile + 1) * 512], evslot,
                               sST + str(ti))
                    P.setguard(("ev", tag, ti), sST + str(ti), st)
        P.setguard(("wbuf", tag, phase % 2), sMM, P.val(sMM))

    if "v" not in DBG_PHASES:
        P.pop()
        return
    buf, wsem, wtot = load_w(len(_phases), wv)
    P.wait("tensor", wsem, wtot)
    P.wait("tensor", sLX, P.val(sLX))
    P.wait("tensor", sLB, P.val(sLB))
    for tb in range(DBG_TBLIM):
        sb = vsb[tb % 2]
        P.wguard("vector", ("vsb", tag, tb % 2))
        P.op("vector", "memset", sb[:, :], 1.0, sname=sVV)
        msetv = P.val(sVV)
        for dv in range(2):
            bank = P.PS[4 + 2 * (tb % 2) + dv]
            P.wguard("tensor", ("bank", 4 + 2 * (tb % 2) + dv))
            for kc in range(8):
                lhs = xb[:, kc * 4096 + tb * 128: kc * 4096 + (tb + 1) * 128]
                P.mm(bank[:, 0:512], lhs,
                     buf[:, kc * 1024 + dv * 512: kc * 1024 + dv * 512 + 512],
                     kc == 0, False)
            mmv = P.mm(bank[:, 0:512], CB(P, "ones_row", 0, 1),
                       bvb[0:1, dv * 512: dv * 512 + 512], False, True, sname=sMM)
            P.wait("vector", sMM, mmv)
            P.wait("vector", sVV, msetv)
            for hh in range(8):
                h = dv * 8 + hh
                P.op("vector", "tensor_copy",
                     sb[:, h * 65: h * 65 + 64],
                     bank[:, hh * 64:(hh + 1) * 64],
                     sname=sVV if hh == 7 else None)
            P.setguard(("bank", 4 + 2 * (tb % 2) + dv), sVV, P.val(sVV))
        P.wait("sync", sVV, P.val(sVV))
        st = P.dma("sync", vd[tb, :, :], sb[:, :], sST + "v" + str(tb % 2))
        P.setguard(("vsb", tag, tb % 2), sST + "v" + str(tb % 2), st)
    P.pop()



def emit_attn0(P):
    P.push()
    vb = P.alloc_b(32 * 1040)
    kpb = [P.alloc_b(8192), P.alloc_b(8192)]
    qpb = [P.alloc_b(8192), P.alloc_b(8192)]
    expp = [P.alloc_b(1024) for _ in range(3)]
    rawst = [P.alloc_f(512), P.alloc_f(512)]
    qd, kd, vd = P.dsrc("q0"), P.dsrc("k0"), P.dsrc("v0")
    rawd = P.dout_or_tmp("raw0")
    sMM, sSC, sVV = "bmm", "chscalar", "chvector"

    for tb in range(32):
        P.dma("gpsimd", vb[:, tb * 1040:(tb + 1) * 1040], vd[tb, :, :], "bvld")
    for buf in (kpb[0], kpb[1], qpb[0], qpb[1]):
        for c in range(16):
            P.op("vector", "memset", buf[64:128, c * 512:(c + 1) * 512], 0.0)
    P.wait("tensor", "chvector", P.val("chvector"))
    P.wait("tensor", "bvld", P.val("bvld"))

    CM = CB(P, "maskCM")
    SPAIR = [0, 2, 3]
    for hp in range(8):
        sl = hp % 2
        kp, qp = kpb[sl], qpb[sl]
        P.wguard("sync", ("qslot", sl))
        P.dma("sync", kp[0:64, 0:4096], kd[128 * hp:128 * hp + 64, :], "bq" + str(sl))
        P.dma("sync", kp[0:64, 4096:8192], kd[128 * hp + 64:128 * hp + 128, :],
              "bq" + str(sl))
        P.dma("sync", qp[0:64, 0:4096], qd[128 * hp:128 * hp + 64, :], "bq" + str(sl))
        P.dma("sync", qp[0:64, 4096:8192], qd[128 * hp + 64:128 * hp + 128, :],
              "bq" + str(sl))
        P.wait("tensor", "bq" + str(sl), P.val("bq" + str(sl)))
        for h2 in range(2):
            h = 2 * hp + h2

            def emit_pair(tau, pi):
                sp = SPAIR[pi % 3]
                ph = P.PSH[sp]
                P.wguard("tensor", ("spair", sp))
                for half in range(2):
                    kc = 2 * pi + half
                    P.mm(ph[:, half * 512:(half + 1) * 512],
                         kp[0:128, h2 * 4096 + kc * 128: h2 * 4096 + (kc + 1) * 128],
                         qp[0:128, h2 * 4096 + tau * 512: h2 * 4096 + (tau + 1) * 512],
                         True, True, sname="bms" if half == 1 else None)
                smm = P.val("bms")
                eslot = pi % 3
                P.wguard("scalar", ("expp", eslot))
                P.wait("scalar", "bms", smm)
                eex = P.act("scalar", expp[eslot][:, :], ph[:, 0:1024], AF.Exp,
                            scale=0.125, sname=sSC)
                P.setguard(("spair", sp), sSC, eex)
                gate = (sSC, eex)
                for half in range(2):
                    kc = 2 * pi + half
                    if kc >= 4 * tau:
                        i = kc - 4 * tau
                        P.wait("vector", sSC, eex)
                        emk = P.op("vector", "tensor_mul",
                                   expp[eslot][:, half * 512:(half + 1) * 512],
                                   expp[eslot][:, half * 512:(half + 1) * 512],
                                   CM[:, 384 - 128 * i: 384 - 128 * i + 512],
                                   sname=sVV)
                        gate = (sVV, emk)
                return gate

            for tau in range(DBG_TAULIM):
                npair = 2 * tau + 2
                avi = 2 + ((h2 * 8 + tau) % 2)
                av = P.PS[avi]
                P.wguard("tensor", ("bank", avi))
                gates = {}
                for pi in range(min(2, npair)):
                    gates[pi] = emit_pair(tau, pi)
                for pi in range(npair):
                    if pi + 2 < npair:
                        gates[pi + 2] = emit_pair(tau, pi + 2)
                    gate = gates.pop(pi)
                    eslot = pi % 3
                    P.wait("tensor", gate[0], gate[1])
                    for half in range(2):
                        kc = 2 * pi + half
                        P.mm(av[0:65, 0:512],
                             vb[:, kc * 1040 + h * 65: kc * 1040 + h * 65 + 65],
                             expp[eslot][:, half * 512:(half + 1) * 512],
                             kc == 0, kc == 4 * tau + 3,
                             sname=sMM if half == 1 else None)
                    P.setguard(("expp", eslot), sMM, P.val(sMM))
                rsl = (h2 * 8 + tau) % 2
                P.wguard("vector", ("rawst", rsl))
                P.wait("vector", sMM, P.val(sMM))
                ee = P.op("vector", "tensor_copy", rawst[rsl][0:65, 0:512],
                          av[0:65, 0:512])
                P.setguard(("bank", avi), "chvector", ee)
                P.wait("sync", "chvector", ee)
                stv = P.dma("sync", rawd[h, tau, :, :], rawst[rsl][0:65, 0:512],
                            "br" + str(rsl))
                P.setguard(("rawst", rsl), "br" + str(rsl), stv)
        P.setguard(("qslot", hp % 2), "bms", P.val("bms"))
    P.pop()


def s1(P):
    emit_attn0(P)


STAGES["s1"] = (s1, ["q0", "k0", "v0"], ["raw0"])



def emit_proj_ln(P, tag, raw_name, ntile, wname, bcol_name, res_name, out_name):
    P.push()
    nh = 16
    rawd = P.dsrc(raw_name)
    resd = P.dsrc(res_name)
    outd = P.dout_or_tmp(out_name)
    wd = P.din(wname)
    rdd = P.dtmp(tag + "_rd")
    sMM, sSC, sVV = tag + "mm", "chscalar", "chvector"

    wpb = P.alloc_b(8 * 1024)
    denf = P.alloc_f(512)
    recf = P.alloc_f(512)
    rdrow = [P.alloc_f(512) for _ in range(4)]
    rawt = [P.alloc_f(512) for _ in range(2)]
    attnb = P.alloc_b(8 * 512)
    ysb = P.alloc_f(8 * 512)
    ybf = P.alloc_b(512)
    y2b = P.alloc_b(512)
    xrt = [P.alloc_f(512) for _ in range(2)]
    ssum = P.alloc_f(512)
    sq = P.alloc_f(512)
    vrx = P.alloc_f(512)
    rstd = P.alloc_f(512)
    xo = [P.alloc_f(512) for _ in range(2)]

    nrow = nh * ntile
    for kc in range(8):
        P.dma("gpsimd", wpb[:, kc * 1024:(kc + 1) * 1024],
              wd[128 * kc:128 * (kc + 1), :], tag + "lw")
    P.dma("sync", denf[0:nrow, 0:512], rawd[:, :, 64, :], tag + "ld")
    P.wait("vector", tag + "ld", P.val(tag + "ld"))
    P.op("vector", "reciprocal", recf[0:nrow, 0:512], denf[0:nrow, 0:512], sname=sVV)
    P.wait("sync", sVV, P.val(sVV))
    P.dma("sync", rdd[0:nrow, :], recf[0:nrow, 0:512], tag + "st")
    P.wait("sync", tag + "st", P.val(tag + "st"))
    P.wait("tensor", tag + "lw", P.val(tag + "lw"))

    ROWF = CFc(P, "ones_rowf", 0, 1)
    SCLF = CFc(P, "scale_rowf", 0, 1)
    OCOL = CB(P, "ones_col")

    for tau in range(ntile):
        for m in range(8):
            rs = m % 2
            P.wguard("sync", ("rawt", tag, rs))
            for half in range(2):
                h = 2 * m + half
                P.dma("sync", rawt[rs][64 * half:64 * half + 64, 0:512],
                      rawd[h, tau, 0:64, :], tag + "lr" + str(rs))
                P.wguard("sync", ("rdrow", tag, 2 * rs + half))
                P.dma("sync", rdrow[2 * rs + half][0:1, 0:512],
                      rdd[h * ntile + tau: h * ntile + tau + 1, :],
                      tag + "lq" + str(2 * rs + half))
            P.wait("tensor", tag + "lq" + str(2 * rs), P.val(tag + "lq" + str(2 * rs)))
            P.wait("tensor", tag + "lq" + str(2 * rs + 1),
                   P.val(tag + "lq" + str(2 * rs + 1)))
            bcb = P.PS[6 + rs]
            P.wguard("tensor", ("bank", 6 + rs))
            P.mm(bcb[0:64, 0:512], ROWF[0:1, 0:64], rdrow[2 * rs][0:1, 0:512],
                 True, True)
            bm = P.mm(bcb[64:128, 0:512], ROWF[0:1, 0:64],
                      rdrow[2 * rs + 1][0:1, 0:512], True, True, sname=sMM)
            P.setguard(("rdrow", tag, 2 * rs), sMM, bm)
            P.setguard(("rdrow", tag, 2 * rs + 1), sMM, bm)
            P.wait("vector", sMM, bm)
            P.wait("vector", tag + "lr" + str(rs), P.val(tag + "lr" + str(rs)))
            vm = P.op("vector", "tensor_mul", attnb[:, m * 512:(m + 1) * 512],
                      rawt[rs][:, 0:512], bcb[0:128, 0:512], sname=sVV)
            P.setguard(("rawt", tag, rs), sVV, vm)
            P.setguard(("bank", 6 + rs), sVV, vm)
        attn_done = P.val(sVV)
        P.wait("tensor", sVV, attn_done)
        for dmg in range(2):
            for ti in range(4):
                P.wguard("tensor", ("bank", ti))
            for m in range(8):
                for ti in range(4):
                    dm = dmg * 4 + ti
                    P.mm(P.PS[ti][:, 0:512],
                         wpb[:, m * 1024 + dm * 128: m * 1024 + (dm + 1) * 128],
                         attnb[:, m * 512:(m + 1) * 512], m == 0, m == 7,
                         sname=sMM if (m == 7 and ti == 3) else None)
            mmv = P.val(sMM)
            P.wait("scalar", sMM, mmv)
            for ti in range(4):
                dm = dmg * 4 + ti
                rs = dm % 2
                P.wguard("sync", ("xrt", tag, rs))
                P.dma("sync", xrt[rs][:, 0:512],
                      resd[128 * dm:128 * (dm + 1), tau * 512:(tau + 1) * 512],
                      tag + "lx" + str(rs))
                ev = P.act("scalar", ysb[:, dm * 512:(dm + 1) * 512],
                           P.PS[ti][:, 0:512], AF.Identity,
                           bias=BPc(P, bcol_name, dm), sname=sSC)
                P.setguard(("bank", ti), sSC, ev)
                P.wait("vector", sSC, ev)
                P.wait("vector", tag + "lx" + str(rs), P.val(tag + "lx" + str(rs)))
                av = P.op("vector", "tensor_add", ysb[:, dm * 512:(dm + 1) * 512],
                          ysb[:, dm * 512:(dm + 1) * 512], xrt[rs][:, 0:512],
                          sname=sVV)
                P.setguard(("xrt", tag, rs), sVV, av)
                P.wguard("vector", ("ybf", tag))
                P.op("vector", "tensor_copy", ybf[:, :],
                     ysb[:, dm * 512:(dm + 1) * 512])
                cv = P.op("vector", "tensor_mul", y2b[:, :], ybf[:, :], ybf[:, :],
                          sname=sVV)
                P.wait("tensor", sVV, cv)
                if dm == 0:
                    P.wguard("tensor", ("bank", 4))
                    P.wguard("tensor", ("bank", 5))
                P.mm(P.PS[4][0:1, 0:512], OCOL, ybf[:, :], dm == 0, dm == 7)
                sm = P.mm(P.PS[5][0:1, 0:512], OCOL, y2b[:, :], dm == 0, dm == 7,
                          sname=sMM)
                P.setguard(("ybf", tag), sMM, sm)
        stat_mm = P.val(sMM)
        P.wait("scalar", sMM, stat_mm)
        c1 = P.act("scalar", ssum[0:1, 0:512], P.PS[4][0:1, 0:512], AF.Copy,
                   sname=sSC)
        c2 = P.act("scalar", sq[0:1, 0:512], P.PS[5][0:1, 0:512], AF.Copy,
                   sname=sSC)
        P.setguard(("bank", 4), sSC, c2)
        P.setguard(("bank", 5), sSC, c2)
        P.wait("vector", sSC, c2)
        P.op("vector", "tensor_mul", vrx[0:1, 0:512], ssum[0:1, 0:512],
             ssum[0:1, 0:512])
        sv = P.op("vector", "scalar_tensor_tensor", vrx[0:1, 0:512],
                  vrx[0:1, 0:512], -1.0 / 1024.0, sq[0:1, 0:512],
                  OP.mult, OP.add, sname=sVV)
        P.op("vector", "tensor_scalar", vrx[0:1, 0:512], vrx[0:1, 0:512],
             1.0 / 1024.0, 1e-5, OP.mult, OP.add, sname=sVV)
        P.wait("scalar", sVV, P.val(sVV))
        P.act("scalar", vrx[0:1, 0:512], vrx[0:1, 0:512], AF.Ln)
        rv = P.act("scalar", rstd[0:1, 0:512], vrx[0:1, 0:512], AF.Exp,
                   scale=-0.5, sname=sSC)
        P.wait("tensor", sSC, rv)
        P.wguard("tensor", ("bank", 6))
        P.wguard("tensor", ("bank", 7))
        P.mm(P.PS[6][0:128, 0:512], SCLF, ssum[0:1, 0:512], True, True)
        bm2 = P.mm(P.PS[7][0:128, 0:512], ROWF, rstd[0:1, 0:512], True, True,
                   sname=sMM)
        P.wait("vector", sMM, bm2)
        for dm in range(8):
            xs = xo[dm % 2]
            P.wguard("vector", ("xo", tag, dm % 2))
            P.op("vector", "tensor_sub", xs[:, 0:512],
                 ysb[:, dm * 512:(dm + 1) * 512], P.PS[6][0:128, 0:512])
            xv = P.op("vector", "tensor_mul", xs[:, 0:512], xs[:, 0:512],
                      P.PS[7][0:128, 0:512], sname=sVV)
            P.wait("sync", sVV, xv)
            so_ = P.dma("sync", outd[128 * dm:128 * (dm + 1),
                                     tau * 512:(tau + 1) * 512], xs[:, 0:512],
                        tag + "so" + str(dm % 2))
            P.setguard(("xo", tag, dm % 2), tag + "so" + str(dm % 2), so_)
        P.setguard(("bank", 6), sVV, P.val(sVV))
        P.setguard(("bank", 7), sVV, P.val(sVV))
    P.pop()


def s2(P):
    emit_proj_ln(P, "c", "raw0", 8, "w_proj0", "b_proj0", "x0T", "x1")


STAGES["s2"] = (s2, ["raw0", "w_proj0", "x0T"], ["x1"])



AX = mybir.AxisListType.X


def gelu_evac(P, outb, bank, gcol, tmpb, tmph, rows=128):
    return P.act("scalar", outb[0:rows, :], bank, AF.Gelu,
                 bias=P.BP[0:rows, gcol:gcol + 1])


def emit_ffn0(P):
    P.push()
    rw1b = P.alloc_b(2048)
    rw2b = P.alloc_b(16)
    hw1b = P.alloc_b(2048)
    hw2b = P.alloc_b(2)
    ew1b = P.alloc_b(16 * 512)
    ew2b = P.alloc_b(4 * 3072)
    eb2b = P.alloc_b(3072)
    sw1b = P.alloc_b(16 * 64)
    sw2b = P.alloc_b(3072)
    for (buf, dn, nk, ncol) in ((rw1b, "r_w1", 8, 256), (rw2b, "r_w2", 2, 8),
                                (hw1b, "h_w1", 8, 256), (hw2b, "h_w2", 2, 1),
                                (ew1b, "e_w1", 16, 512), (ew2b, "e_w2", 4, 3072),
                                (sw1b, "s_w1", 16, 64)):
        wd = P.din(dn)
        for kc in range(nk):
            P.dma("gpsimd", buf[:, kc * ncol:(kc + 1) * ncol],
                  wd[128 * kc:128 * (kc + 1), :], "dlw")
    P.dma("gpsimd", eb2b[0:8, :], P.din("e_b2")[:, :], "dlw")
    P.dma("gpsimd", sw2b[0:64, :], P.din("s_w2")[:, :], "dlw")
    P.wait("tensor", "dlw", P.val("dlw"))

    x1d = P.dsrc("x1")
    intd = P.dout_or_tmp("intg")

    intg = P.alloc_b(8 * 512)
    vst = P.alloc_b(8 * 512)
    habc = P.alloc_b(512)
    h1b = P.alloc_b(4 * 512)
    h2b = P.alloc_b(4 * 512)
    hsb = P.alloc_b(512)
    hhb = P.alloc_b(2 * 512)
    rhb = P.alloc_b(2 * 512)
    tmpb = P.alloc_b(512)
    tmph = P.alloc_b(512)
    tmpf = P.alloc_f(512)
    logf = P.alloc_f(512)
    hrow = P.alloc_f(512)
    ltm = P.alloc_f(32)
    eqa = P.alloc_f(32)
    eqb = P.alloc_f(32)
    mx1 = P.alloc_f(1)
    mx2 = P.alloc_f(1)
    w1c = P.alloc_f(1)
    m1fm = P.alloc_b(512)
    m2fm = P.alloc_b(512)
    rowa = P.alloc_f(512)
    rowb = P.alloc_f(512)
    swsig = P.alloc_f(1)
    onemsw = P.alloc_f(1)
    swcol = P.alloc_f(1)
    wAb = P.alloc_b(512)
    wBb = P.alloc_b(512)
    haltb = P.alloc_b(512)
    haltxb = P.alloc_b(512)
    dtb = [P.alloc_b(512) for _ in range(2)]
    Ab = [[P.alloc_b(512) for _ in range(3)] for _ in range(2)]
    Bb = [[P.alloc_b(512) for _ in range(3)] for _ in range(2)]
    Gb = [[P.alloc_b(512) for _ in range(3)] for _ in range(2)]
    tmpf3 = [[P.alloc_f(512) for _ in range(3)] for _ in range(2)]
    vnb = [P.alloc_b(512) for _ in range(3)]
    gnb = [P.alloc_b(512) for _ in range(3)]
    prod = P.alloc_b(512)
    mixv = P.alloc_b(512)
    mixg = P.alloc_b(512)

    EYE = CFc(P, "eye128f")
    ROWF = CFc(P, "ones_rowf", 0, 1)
    B8 = CB(P, "B8", 0, 8)
    sMM = "dmm"
    for c in range(6):
        P.op("vector", "memset", sw2b[64:128, c * 512:(c + 1) * 512], 0.0)
    P.op("vector", "memset", hsb[64:128, 0:512], 0.0)
    P.wait("tensor", "chvector", P.val("chvector"))

    P.act("scalar", swsig[0:1, 0:1], BPc(P, "sw", 0, 0, 1), AF.Sigmoid)
    P.wait("vector", "chscalar", P.val("chscalar"))
    P.op("vector", "tensor_scalar", onemsw[0:1, 0:1], swsig[0:1, 0:1], -1.0, 1.0,
         OP.mult, OP.add)
    ov = P.op("vector", "tensor_copy", rowa[0:1, 0:128], ROWF)
    P.wait("tensor", "chvector", ov)
    bmv = P.mm(P.PS[7][0:128, 0:1], rowa[0:1, 0:128], swsig[0:1, 0:1], True, True,
               sname=sMM)
    P.wait("scalar", sMM, bmv)
    scv = P.act("scalar", swcol[0:128, 0:1], P.PS[7][0:128, 0:1], AF.Copy)
    P.setguard(("bank", 7), "chscalar", scv)

    gc_r = 0
    gc_h = 2
    gc_e = 4
    gc_s = 8
    gc_f = 9

    for tau in range(DBG_S3TAU):
        t0 = tau * 512
        for f in range(8):
            P.wguard("gpsimd", ("s3x", f))
            P.dma("gpsimd", intg[:, f * 512:(f + 1) * 512],
                  x1d[128 * f:128 * (f + 1), t0:t0 + 512], "dlx")
        P.wait("vector", "dlx", P.val("dlx"))
        P.wait("tensor", "dlx", P.val("dlx"))

        for mh in range(2):
            P.wguard("tensor", ("bank", mh))
        for kc in range(8):
            for mh in range(2):
                P.mm(P.PS[mh][:, 0:512],
                     rw1b[:, kc * 256 + mh * 128: kc * 256 + (mh + 1) * 128],
                     intg[:, kc * 512:(kc + 1) * 512], kc == 0, kc == 7,
                     sname=sMM if mh == 1 else None)
        P.wait("scalar", sMM, P.val(sMM))
        for mh in range(2):
            gv = gelu_evac(P, rhb[:, mh * 512:(mh + 1) * 512],
                           P.PS[mh][:, 0:512], gc_r + mh, tmpb, tmph)
            P.setguard(("bank", mh), "chscalar", gv)
        P.wguard("tensor", ("bank", 2))
        P.wait("tensor", "chscalar", P.val("chscalar"))
        for kc2 in range(2):
            P.mm(P.PS[2][0:8, 0:512], rw2b[:, kc2 * 8:(kc2 + 1) * 8],
                 rhb[:, kc2 * 512:(kc2 + 1) * 512], kc2 == 0, kc2 == 1,
                 sname=sMM if kc2 == 1 else None)
        P.wait("scalar", sMM, P.val(sMM))
        lv = P.act("scalar", logf[0:8, 0:512], P.PS[2][0:8, 0:512], AF.Identity,
                   bias=BPc(P, "r_b2", 0, 0, 8))
        P.setguard(("bank", 2), "chscalar", lv)
        P.wait("tensor", "chscalar", lv)
        P.wguard("tensor", ("bank", 3))
        for j in range(4):
            tv = P.op("tensor", "transpose", P.PS[3][0:128, j * 8:(j + 1) * 8],
                      logf[0:8, 128 * j:128 * (j + 1)], EYE[0:8, 0:8], sname=sMM)
        P.wait("vector", sMM, tv)
        P.op("vector", "tensor_copy", ltm[0:128, 0:32], P.PS[3][0:128, 0:32])
        P.setguard(("bank", 3), "chvector", P.val("chvector"))
        P.wguard("tensor", ("bank", 4))
        P.wguard("tensor", ("bank", 5))
        P.wguard("tensor", ("bank", 6))
        for j in range(4):
            lj = ltm[0:128, j * 8:(j + 1) * 8]
            if j > 0:
                P.wait("vector", "chscalar", P.val("chscalar"))
                P.wait("vector", sMM, P.val(sMM))
            P.op("vector", "reduce_max", mx1[0:128, 0:1], lj, AX)
            P.op("vector", "tensor_scalar", eqa[0:128, 0:8], lj, mx1[0:128, 0:1],
                 None, OP.is_equal)
            P.op("vector", "scalar_tensor_tensor", eqb[0:128, 0:8], eqa[0:128, 0:8],
                 -1e30, lj, OP.mult, OP.add)
            P.op("vector", "reduce_max", mx2[0:128, 0:1], eqb[0:128, 0:8], AX)
            P.op("vector", "tensor_scalar", eqb[0:128, 0:8], eqb[0:128, 0:8],
                 mx2[0:128, 0:1], None, OP.is_equal)
            wv_ = P.op("vector", "tensor_sub", w1c[0:128, 0:1], mx1[0:128, 0:1],
                       mx2[0:128, 0:1])
            P.wait("scalar", "chvector", wv_)
            sv_ = P.act("scalar", w1c[0:128, 0:1], w1c[0:128, 0:1], AF.Sigmoid)
            P.wait("tensor", "chvector", wv_)
            P.wait("tensor", "chscalar", sv_)
            P.op("tensor", "transpose", P.PS[4][0:8, j * 128:(j + 1) * 128],
                 eqa[0:128, 0:8], EYE[0:128, 0:128], sname=sMM)
            P.op("tensor", "transpose", P.PS[5][0:8, j * 128:(j + 1) * 128],
                 eqb[0:128, 0:8], EYE[0:128, 0:128], sname=sMM)
            tv2 = P.op("tensor", "transpose", P.PS[6][0:1, j * 128:(j + 1) * 128],
                       w1c[0:128, 0:1], EYE[0:128, 0:128], sname=sMM)
        P.wait("vector", sMM, tv2)
        P.op("vector", "tensor_copy", m1fm[0:8, :], P.PS[4][0:8, 0:512])
        P.op("vector", "tensor_copy", m2fm[0:8, :], P.PS[5][0:8, 0:512])
        P.op("vector", "tensor_copy", rowa[0:1, 0:512], P.PS[6][0:1, 0:512])
        P.op("vector", "tensor_scalar", rowb[0:1, 0:512], rowa[0:1, 0:512],
             onemsw[0:1, 0:1], None, OP.mult)
        P.op("vector", "tensor_scalar", rowa[0:1, 0:512], rowa[0:1, 0:512],
             -1.0, 1.0, OP.mult, OP.add)
        rv_ = P.op("vector", "tensor_scalar", rowa[0:1, 0:512], rowa[0:1, 0:512],
                   onemsw[0:1, 0:1], None, OP.mult)
        P.setguard(("bank", 4), "chvector", rv_)
        P.setguard(("bank", 5), "chvector", rv_)
        P.setguard(("bank", 6), "chvector", rv_)
        P.wait("tensor", "chvector", rv_)
        P.wguard("tensor", ("bank", 4))
        P.wguard("tensor", ("bank", 5))
        P.mm(P.PS[4][0:128, 0:512], ROWF, rowb[0:1, 0:512], True, True)
        bc2 = P.mm(P.PS[5][0:128, 0:512], ROWF, rowa[0:1, 0:512], True, True,
                   sname=sMM)
        P.wait("vector", sMM, bc2)
        P.op("vector", "tensor_copy", wAb[0:128, 0:512], P.PS[4][0:128, 0:512])
        wv2 = P.op("vector", "tensor_copy", wBb[0:128, 0:512], P.PS[5][0:128, 0:512])
        P.setguard(("bank", 4), "chvector", wv2)
        P.setguard(("bank", 5), "chvector", wv2)

        for it in range(2):
            nkc = 8 if it == 0 else 16
            for b_ in range(7):
                P.wguard("tensor", ("bank", b_))
            P.wait("tensor", "chvector", P.val("chvector"))
            for kc in range(nkc):
                srcv = intg if kc < 8 else vst
                f8 = kc % 8
                ctx = srcv[:, f8 * 512:(f8 + 1) * 512]
                last = kc == nkc - 1
                for mh in range(4):
                    P.mm(P.PS[mh][:, 0:512],
                         ew1b[:, kc * 512 + mh * 128: kc * 512 + (mh + 1) * 128],
                         ctx, kc == 0, last)
                P.mm(P.PS[4][0:64, 0:512], sw1b[:, kc * 64:(kc + 1) * 64],
                     ctx, kc == 0, last,
                     sname=sMM if kc >= 8 else None)
                if kc < 8:
                    for mh2 in range(2):
                        P.mm(P.PS[5 + mh2][:, 0:512],
                             hw1b[:, kc * 256 + mh2 * 128: kc * 256 + (mh2 + 1) * 128],
                             ctx, kc == 0, kc == 7,
                             sname=sMM if mh2 == 1 else None)
            mmv = P.val(sMM)
            P.wait("scalar", sMM, mmv)
            for mh in range(4):
                P.wguard("scalar", ("habc",))
                gv = gelu_evac(P, habc, P.PS[mh][:, 0:512], gc_e + mh, tmpb, tmph)
                P.wguard("tensor", ("bank", 7))
                P.wait("tensor", "chscalar", gv)
                bm1 = P.mm(P.PS[7][0:128, 0:512], B8[0:8, mh * 128:(mh + 1) * 128],
                           m1fm[0:8, :], True, True, sname=sMM)
                P.wait("vector", sMM, bm1)
                P.wait("vector", "chscalar", gv)
                u1 = P.op("vector", "tensor_mul", h1b[:, mh * 512:(mh + 1) * 512],
                          habc[:, :], P.PS[7][0:128, 0:512])
                P.wait("tensor", "chvector", u1)
                bm2 = P.mm(P.PS[7][0:128, 0:512], B8[0:8, mh * 128:(mh + 1) * 128],
                           m2fm[0:8, :], True, True, sname=sMM)
                P.wait("vector", sMM, bm2)
                u2 = P.op("vector", "tensor_mul", h2b[:, mh * 512:(mh + 1) * 512],
                          habc[:, :], P.PS[7][0:128, 0:512])
                P.setguard(("bank", 7), "chvector", u2)
                P.setguard(("bank", mh), "chvector", u2)
                P.setguard(("habc",), "chvector", u2)
            gvs = gelu_evac(P, hsb, P.PS[4][0:64, 0:512], gc_s, tmpb, tmph, rows=64)
            P.setguard(("bank", 4), "chscalar", gvs)
            for mh2 in range(2):
                gvh = gelu_evac(P, hhb[:, mh2 * 512:(mh2 + 1) * 512],
                                P.PS[5 + mh2][:, 0:512], gc_h + mh2, tmpb, tmph)
                P.setguard(("bank", 5 + mh2), "chscalar", gvh)
            P.wguard("tensor", ("bank", 7))
            P.wait("tensor", "chscalar", P.val("chscalar"))
            for kc2 in range(2):
                P.mm(P.PS[7][0:1, 0:512], hw2b[:, kc2:kc2 + 1],
                     hhb[:, kc2 * 512:(kc2 + 1) * 512], kc2 == 0, kc2 == 1,
                     sname=sMM if kc2 == 1 else None)
            P.wait("scalar", sMM, P.val(sMM))
            hv = P.act("scalar", hrow[0:1, 0:512], P.PS[7][0:1, 0:512], AF.Sigmoid,
                       bias=BPc(P, "h_b2", 0, 0, 1))
            P.setguard(("bank", 7), "chscalar", hv)
            P.wait("tensor", "chscalar", hv)
            P.wguard("tensor", ("bank", 7))
            hb_ = P.mm(P.PS[7][0:128, 0:512], ROWF, hrow[0:1, 0:512], True, True,
                       sname=sMM)
            P.wait("vector", sMM, hb_)
            P.op("vector", "tensor_copy", haltb[0:128, 0:512], P.PS[7][0:128, 0:512])
            hx_ = P.op("vector", "tensor_scalar", haltxb[0:128, 0:512],
                       haltb[0:128, 0:512], 0.1, None, OP.mult)
            P.setguard(("bank", 7), "chvector", hx_)
            PB = [0, 1, 5]
            for fp in range(4):
                fpair = (2 * fp, 2 * fp + 1)
                for j, f in enumerate(fpair):
                    P.op("vector", "tensor_scalar", dtb[j][:, :],
                         intg[:, f * 512:(f + 1) * 512], BPc(P, "mu", f), None,
                         OP.subtract)
                parts = []
                for j, f in enumerate(fpair):
                    for pt in (0, 2, 1):
                        for si in range(3):
                            if it == 0 and pt == 0:
                                continue
                            parts.append((j, f, si, pt))

                def emit_part(idx):
                    j, f, si, pt = parts[idx]
                    ch = (0, 8, 16)[pt] + f
                    bk = P.PS[PB[idx % 3]]
                    P.wguard("tensor", ("bank", PB[idx % 3]))
                    if si == 0:
                        P.wait("tensor", "chscalar", P.val("chscalar"))
                        P.mm(bk[:, 0:512],
                             sw2b[0:128, ch * 128:(ch + 1) * 128], hsb[0:128, :],
                             True, True, sname=sMM)
                    else:
                        hx = h1b if si == 1 else h2b
                        mf = m1fm if si == 1 else m2fm
                        for kc2 in range(4):
                            P.mm(bk[:, 0:512],
                                 ew2b[:, kc2 * 3072 + ch * 128:
                                      kc2 * 3072 + (ch + 1) * 128],
                                 hx[:, kc2 * 512:(kc2 + 1) * 512],
                                 kc2 == 0, False)
                        P.mm(bk[:, 0:512], eb2b[0:8, ch * 128:(ch + 1) * 128],
                             mf[0:8, :], False, True, sname=sMM)
                    return P.val(sMM)

                def emit_act(idx, mk):
                    j, f, si, pt = parts[idx]
                    bk = P.PS[PB[idx % 3]]
                    P.wait("scalar", sMM, mk)
                    P.wguard("scalar", ("abg", j))
                    sb2c = BPc(P, "s_b2", (0, 8, 16)[pt] + f) if si == 0 else 0.0
                    if pt == 0:
                        ev_ = P.act("scalar", Ab[j][si][:, :], bk[:, 0:512],
                                    AF.Sigmoid, bias=sb2c)
                    elif pt == 2:
                        ev_ = P.act("scalar", Gb[j][si][:, :], bk[:, 0:512],
                                    AF.Sigmoid, bias=sb2c)
                    else:
                        ev_ = P.act("scalar", tmpf3[j][si][:, 0:512], bk[:, 0:512],
                                    AF.Sigmoid, scale=-1.0,
                                    bias=P.BPn[:, 8 + f:8 + f + 1] if si == 0 else 0.0)
                    P.setguard(("bank", PB[idx % 3]), "chscalar", ev_)
                    return ev_

                marks = {}
                n = len(parts)
                for idx in range(min(2, n)):
                    marks[idx] = emit_part(idx)
                acts = {}
                for idx in range(n):
                    if idx + 2 < n:
                        marks[idx + 2] = emit_part(idx + 2)
                    acts[idx] = emit_act(idx, marks.pop(idx))
                lns = {}
                for j in range(2):
                    for si in range(3):
                        lns[(j, si)] = P.act("scalar", Bb[j][si][:, :],
                                             tmpf3[j][si][:, 0:512], AF.Ln)
                for j, f in enumerate(fpair):
                    for si in range(3):
                        P.wait("vector", "chscalar", lns[(j, si)])
                        if it == 0:
                            P.op("vector", "tensor_mul", vnb[si][:, :],
                                 Bb[j][si][:, :], dtb[j][:, :])
                        else:
                            P.op("vector", "tensor_mul", vnb[si][:, :],
                                 Ab[j][si][:, :], vst[:, f * 512:(f + 1) * 512])
                            P.op("vector", "tensor_mul", prod[:, :],
                                 Bb[j][si][:, :], dtb[j][:, :])
                            P.op("vector", "tensor_add", vnb[si][:, :],
                                 vnb[si][:, :], prod[:, :])
                        gn_ = P.op("vector", "tensor_mul", gnb[si][:, :],
                                   Gb[j][si][:, :], vnb[si][:, :])
                    P.setguard(("abg", j), "chvector", P.val("chvector"))
                    for (mix, src3) in ((mixv, vnb), (mixg, gnb)):
                        P.op("vector", "tensor_scalar", mix[0:128, 0:512],
                             src3[0][:, :], swcol[0:128, 0:1], None, OP.mult)
                        P.op("vector", "tensor_mul", prod[:, :], src3[1][:, :],
                             wAb[0:128, 0:512])
                        P.op("vector", "tensor_add", mix[0:128, 0:512],
                             mix[0:128, 0:512], prod[:, :])
                        P.op("vector", "tensor_mul", prod[:, :], src3[2][:, :],
                             wBb[0:128, 0:512])
                        P.op("vector", "tensor_add", mix[0:128, 0:512],
                             mix[0:128, 0:512], prod[:, :])
                    if it == 0:
                        P.op("vector", "tensor_mul", vst[:, f * 512:(f + 1) * 512],
                             haltb[0:128, 0:512], mixv[0:128, 0:512])
                    else:
                        P.op("vector", "tensor_sub", mixv[0:128, 0:512],
                             mixv[0:128, 0:512], vst[:, f * 512:(f + 1) * 512])
                        P.op("vector", "tensor_mul", mixv[0:128, 0:512],
                             haltb[0:128, 0:512], mixv[0:128, 0:512])
                        P.op("vector", "tensor_add", vst[:, f * 512:(f + 1) * 512],
                             vst[:, f * 512:(f + 1) * 512], mixv[0:128, 0:512])
                    P.op("vector", "tensor_mul", mixg[0:128, 0:512],
                         haltxb[0:128, 0:512], mixg[0:128, 0:512])
                    uv = P.op("vector", "tensor_add", intg[:, f * 512:(f + 1) * 512],
                              intg[:, f * 512:(f + 1) * 512], mixg[0:128, 0:512])
        P.wait("sync", "chvector", P.val("chvector"))
        for f in range(8):
            sv = P.dma("sync", intd[128 * f:128 * (f + 1), t0:t0 + 512],
                       intg[:, f * 512:(f + 1) * 512], "dsx")
            P.setguard(("s3x", f), "dsx", sv)
    P.pop()
    P.fence()

    P.push()
    fw1b = P.alloc_b(8 * 2048)
    fw2b = P.alloc_b(16 * 1024)
    for (buf, dn, nk, ncol) in ((fw1b, "f_w1", 8, 2048), (fw2b, "f_w2", 16, 1024)):
        wd = P.din(dn)
        for kc in range(nk):
            P.dma("gpsimd", buf[:, kc * ncol:(kc + 1) * ncol],
                  wd[128 * kc:128 * (kc + 1), :], "dlw2")
    P.wait("tensor", "dlw2", P.val("dlw2"))
    x1d = P.dsrc("x1")
    intd = P.dsrc("intg")
    x2d = P.dout_or_tmp("x2")

    inb8 = P.alloc_b(8 * 512)
    r1b = P.alloc_b(16 * 512)
    tmpb2 = P.alloc_b(512)
    tmph2 = P.alloc_b(512)
    ysb = P.alloc_f(8 * 512)
    x1t = [P.alloc_f(512), P.alloc_f(512)]
    ybf = P.alloc_b(512)
    y2b = P.alloc_b(512)
    ssum = P.alloc_f(512)
    sq = P.alloc_f(512)
    vrx = P.alloc_f(512)
    rstd = P.alloc_f(512)
    xo = [P.alloc_f(512), P.alloc_f(512)]
    ROWF = CFc(P, "ones_rowf", 0, 1)
    SCLF = CFc(P, "scale_rowf", 0, 1)
    OCOL = CB(P, "ones_col")
    sMM = "dmm"

    for tau in range(DBG_S3TAU):
        t0 = tau * 512
        P.wguard("gpsimd", ("s3in",))
        for kc in range(8):
            P.dma("gpsimd", inb8[:, kc * 512:(kc + 1) * 512],
                  intd[128 * kc:128 * (kc + 1), t0:t0 + 512], "dli")
        P.wait("tensor", "dli", P.val("dli"))
        for mh in range(16):
            bk = P.PS[mh % 2]
            P.wguard("tensor", ("bank", mh % 2))
            for kc in range(8):
                P.mm(bk[:, 0:512],
                     fw1b[:, kc * 2048 + mh * 128: kc * 2048 + (mh + 1) * 128],
                     inb8[:, kc * 512:(kc + 1) * 512], kc == 0, kc == 7,
                     sname=sMM if kc == 7 else None)
            if mh == 15:
                P.setguard(("s3in",), sMM, P.val(sMM))
            P.wait("scalar", sMM, P.val(sMM))
            gv = gelu_evac(P, r1b[:, mh * 512:(mh + 1) * 512], bk[:, 0:512],
                           9 + mh, tmpb2, tmph2)
            P.setguard(("bank", mh % 2), "chscalar", gv)
        for dm in range(8):
            bk = P.PS[2 + dm % 2]
            P.wguard("tensor", ("bank", 2 + dm % 2))
            P.wait("tensor", "chscalar", P.val("chscalar"))
            for kc in range(16):
                P.mm(bk[:, 0:512],
                     fw2b[:, kc * 1024 + dm * 128: kc * 1024 + (dm + 1) * 128],
                     r1b[:, kc * 512:(kc + 1) * 512], kc == 0, kc == 15,
                     sname=sMM if kc == 15 else None)
            mmv = P.val(sMM)
            rs = dm % 2
            P.wguard("sync", ("s3r", rs))
            P.dma("sync", x1t[rs][:, 0:512],
                  x1d[128 * dm:128 * (dm + 1), t0:t0 + 512], "dlr" + str(rs))
            P.wait("scalar", sMM, mmv)
            ev = P.act("scalar", ysb[:, dm * 512:(dm + 1) * 512], bk[:, 0:512],
                       AF.Identity, bias=BPc(P, "f_b2", dm))
            P.setguard(("bank", 2 + dm % 2), "chscalar", ev)
            P.wait("vector", "chscalar", ev)
            P.op("vector", "tensor_scalar", ysb[:, dm * 512:(dm + 1) * 512],
                 ysb[:, dm * 512:(dm + 1) * 512], BPc(P, "iw", dm), None, OP.mult)
            P.wait("vector", "dlr" + str(rs), P.val("dlr" + str(rs)))
            av = P.op("vector", "scalar_tensor_tensor",
                      ysb[:, dm * 512:(dm + 1) * 512], x1t[rs][:, 0:512], 2.0,
                      ysb[:, dm * 512:(dm + 1) * 512], OP.mult, OP.add)
            P.setguard(("s3r", rs), "chvector", av)
            P.op("vector", "tensor_copy", ybf[:, :], ysb[:, dm * 512:(dm + 1) * 512])
            cv = P.op("vector", "tensor_mul", y2b[:, :], ybf[:, :], ybf[:, :])
            P.wait("tensor", "chvector", cv)
            if dm == 0:
                P.wguard("tensor", ("bank", 4))
                P.wguard("tensor", ("bank", 5))
            P.mm(P.PS[4][0:1, 0:512], OCOL, ybf[:, :], dm == 0, dm == 7)
            P.mm(P.PS[5][0:1, 0:512], OCOL, y2b[:, :], dm == 0, dm == 7,
                 sname=sMM if dm == 7 else None)
        P.wait("scalar", sMM, P.val(sMM))
        P.act("scalar", ssum[0:1, 0:512], P.PS[4][0:1, 0:512], AF.Copy)
        c2 = P.act("scalar", sq[0:1, 0:512], P.PS[5][0:1, 0:512], AF.Copy)
        P.setguard(("bank", 4), "chscalar", c2)
        P.setguard(("bank", 5), "chscalar", c2)
        P.wait("vector", "chscalar", c2)
        P.op("vector", "tensor_mul", vrx[0:1, 0:512], ssum[0:1, 0:512],
             ssum[0:1, 0:512])
        P.op("vector", "scalar_tensor_tensor", vrx[0:1, 0:512], vrx[0:1, 0:512],
             -1.0 / 1024.0, sq[0:1, 0:512], OP.mult, OP.add)
        sv_ = P.op("vector", "tensor_scalar", vrx[0:1, 0:512], vrx[0:1, 0:512],
                   1.0 / 1024.0, 1e-5, OP.mult, OP.add)
        P.wait("scalar", "chvector", sv_)
        P.act("scalar", vrx[0:1, 0:512], vrx[0:1, 0:512], AF.Ln)
        rv = P.act("scalar", rstd[0:1, 0:512], vrx[0:1, 0:512], AF.Exp, scale=-0.5)
        P.wait("tensor", "chscalar", rv)
        P.wguard("tensor", ("bank", 6))
        P.wguard("tensor", ("bank", 7))
        P.mm(P.PS[6][0:128, 0:512], SCLF, ssum[0:1, 0:512], True, True)
        bm2 = P.mm(P.PS[7][0:128, 0:512], ROWF, rstd[0:1, 0:512], True, True,
                   sname=sMM)
        P.wait("vector", sMM, bm2)
        for dm in range(8):
            xs = xo[dm % 2]
            P.wguard("vector", ("s3o", dm % 2))
            P.op("vector", "tensor_sub", xs[:, 0:512],
                 ysb[:, dm * 512:(dm + 1) * 512], P.PS[6][0:128, 0:512])
            xv = P.op("vector", "tensor_mul", xs[:, 0:512], xs[:, 0:512],
                      P.PS[7][0:128, 0:512])
            P.wait("sync", "chvector", xv)
            so_ = P.dma("sync", x2d[128 * dm:128 * (dm + 1), t0:t0 + 512],
                        xs[:, 0:512], "dso" + str(dm % 2))
            P.setguard(("s3o", dm % 2), "dso" + str(dm % 2), so_)
        P.setguard(("bank", 6), "chvector", P.val("chvector"))
        P.setguard(("bank", 7), "chvector", P.val("chvector"))
    P.pop()


LMARK = {}


def s3(P):
    emit_ffn0(P)


STAGES["s3"] = (s3, ["x1", "r_w1", "r_w2", "h_w1", "h_w2", "e_w1", "e_w2", "e_b2",
                     "s_w1", "s_w2", "f_w1", "f_w2"], ["x2", "intg"])



def emit_perm(P):
    P.push()
    x2d = P.dsrc("x2")
    x2pd = P.dout_or_tmp("x2p")
    idxd = P.din("permidx")
    buf = [P.alloc_f(2048), P.alloc_f(2048)]
    idxt = [P.alloc_b(2), P.alloc_b(2)]
    x2v = x2d.rearrange("p (h t) -> (p h) t", h=2)
    x2pv = x2pd.rearrange("p (h t) -> (p h) t", h=2)
    for k in range(16):
        s = k % 2
        P.wguard("gpsimd", ("pidx", s))
        P.dma("gpsimd", idxt[s][:, 0:2].bitcast(I32),
              idxd[128 * k:128 * (k + 1), :], "pli" + str(s))
        P.wait("gpsimd", "pli" + str(s), P.val("pli" + str(s)))
        P.wguard("gpsimd", ("pbuf", s))
        P.idma(buf[s][:, 0:2048], x2v[:, :], idxt[s][:, 0:2].bitcast(I32),
               "pgi" + str(s))
        P.wait("sync", "pgi" + str(s), P.val("pgi" + str(s)))
        sv = P.dma("sync", x2pv[128 * k:128 * (k + 1), :], buf[s][:, 0:2048],
                   "pst" + str(s))
        P.setguard(("pbuf", s), "pst" + str(s), sv)
        P.setguard(("pidx", s), "pst" + str(s), sv)
    P.pop()


def sp(P):
    emit_perm(P)


STAGES["sp"] = (sp, ["x2", "permidx"], ["x2p"])



def emit_vsum(P):
    P.push()
    v1d = P.dsrc("v1")
    vsd = P.dout_or_tmp("vsum1")
    vgd = P.dout_or_tmp("vg1")
    vt = [P.alloc_b(1040), P.alloc_b(1040)]
    vsum = P.alloc_f(1040)
    vs65 = P.alloc_f(16)
    sMM = "gmm"
    P.dma("sync", vgd[0:32, :], v1d[:, 0, :], "gst")
    P.dma("sync", vgd[32:64, :], v1d[:, 64, :], "gst")
    for tb in range(32):
        s = tb % 2
        P.wguard("gpsimd", ("vsb", s))
        P.dma("gpsimd", vt[s][:, :], v1d[tb, :, :], "gld" + str(s))
        P.wait("tensor", "gld" + str(s), P.val("gld" + str(s)))
        for sl, (c0, c1) in enumerate(((0, 512), (512, 1024), (1024, 1040))):
            mk = P.mm(P.PS[sl][0:1, 0:c1 - c0], CB(P, "ones_col"),
                      vt[s][:, c0:c1], tb == 0, tb == 31,
                      sname=sMM if sl == 2 else None)
        P.setguard(("vsb", s), sMM, P.val(sMM))
    P.wait("scalar", sMM, P.val(sMM))
    for sl, (c0, c1) in enumerate(((0, 512), (512, 1024), (1024, 1040))):
        sv = P.act("scalar", vsum[0:1, c0:c1], P.PS[sl][0:1, 0:c1 - c0], AF.Copy)
    P.wait("tensor", "chscalar", sv)
    EY1 = CFc(P, "eye128f", 0, 1, 0, 1)
    for h in range(16):
        tv = P.op("tensor", "transpose", P.PS[3][0:64, h:h + 1],
                  vsum[0:1, h * 65: h * 65 + 64], EY1, sname=sMM)
    P.wait("vector", sMM, tv)
    P.op("vector", "tensor_copy", vs65[0:64, 0:16], P.PS[3][0:64, 0:16])
    P.op("vector", "memset", vs65[64:65, 0:16], 4096.0)
    P.wait("sync", "chvector", P.val("chvector"))
    P.dma("sync", vsd[:, :], vs65[0:65, 0:16], "gst")
    P.pop()


def s4(P):
    emit_qkv(P, "e", "x2p", ("wq1", 0), ("wk1", 0), ("wv1", 0),
             ("bq1", 0), ("bk1", 0), "bv1_row", "q1", "k1", "v1", nq_tiles=4)
    P.fence()
    emit_vsum(P)


STAGES["s4"] = (s4, ["x2p", "wq1", "wk1", "wv1", "bv1_row"],
                ["q1", "k1", "v1", "vsum1", "vg1"])



def emit_attn1(P):
    P.push()
    vb = P.alloc_b(32 * 1040)
    kpb = [P.alloc_b(8192), P.alloc_b(8192)]
    qpb = [P.alloc_b(4096), P.alloc_b(4096)]
    vgbE = P.alloc_b(1040)
    vgbO = P.alloc_b(1040)
    pgE = P.alloc_b(32)
    pgO = P.alloc_b(32)
    vs65 = P.alloc_f(16)
    kgb = P.alloc_b(64)
    qgb = P.alloc_b(32)
    e1 = P.alloc_b(32)
    e2 = P.alloc_b(32)
    corr = P.alloc_b(32)
    pg = P.alloc_b(32)
    expb = [P.alloc_b(512) for _ in range(3)]
    rawst = [P.alloc_f(512), P.alloc_f(512)]
    qd, kd, vd = P.dsrc("q1"), P.dsrc("k1"), P.dsrc("v1")
    vgd, vsd = P.dsrc("vg1"), P.dsrc("vsum1")
    rawd = P.dout_or_tmp("raw1")
    sMM = "hmm"

    for tb in range(32):
        P.dma("gpsimd", vb[:, tb * 1040:(tb + 1) * 1040], vd[tb, :, :], "hvld")
    for buf, nc_ in ((kpb[0], 16), (kpb[1], 16), (qpb[0], 8), (qpb[1], 8)):
        for c in range(nc_):
            P.op("vector", "memset", buf[64:128, c * 512:(c + 1) * 512], 0.0)
    P.wait("tensor", "chvector", P.val("chvector"))
    P.dma("sync", vgbE[0:32, :], vgd[0:32, :], "hgld")
    P.dma("sync", vgbO[0:32, :], vgd[32:64, :], "hgld")
    P.dma("sync", vs65[0:65, 0:16], vsd[:, :], "hgld")
    P.wait("tensor", "hvld", P.val("hvld"))
    P.wait("tensor", "hgld", P.val("hgld"))
    P.wait("vector", "hgld", P.val("hgld"))
    CMK = CB(P, "corrmask", 0, 64)

    for hp in range(DBG_S5HP):
        sl = hp % 2
        kp, qp = kpb[sl], qpb[sl]
        P.wguard("sync", ("q1slot", sl))
        P.dma("sync", kp[0:64, 0:4096], kd[128 * hp:128 * hp + 64, :], "hq" + str(sl))
        P.dma("sync", kp[0:64, 4096:8192], kd[128 * hp + 64:128 * hp + 128, :],
              "hq" + str(sl))
        P.dma("sync", qp[0:64, 0:2048], qd[128 * hp:128 * hp + 64, :], "hq" + str(sl))
        P.dma("sync", qp[0:64, 2048:4096], qd[128 * hp + 64:128 * hp + 128, :],
              "hq" + str(sl))
        P.wait("tensor", "hq" + str(sl), P.val("hq" + str(sl)))
        P.wait("vector", "hq" + str(sl), P.val("hq" + str(sl)))
        for h2 in range(2):
            h = 2 * hp + h2
            if not DBG_S5GLOB:
                pass
            P.wguard("vector", ("g1",))
            P.op("vector", "tensor_copy", kgb[0:64, 0:32],
                 kp[0:64, h2 * 4096: h2 * 4096 + 4096: 128])
            P.op("vector", "tensor_copy", kgb[0:64, 32:64],
                 kp[0:64, h2 * 4096 + 64: h2 * 4096 + 4096: 128])
            cq = P.op("vector", "tensor_copy", qgb[0:64, 0:32],
                      qp[0:64, h2 * 2048: h2 * 2048 + 2048: 64])
            P.wait("tensor", "chvector", cq)
            P.wguard("tensor", ("bank", 5))
            gv = P.mm(P.PS[5][0:64, 0:32], kgb[0:64, 0:64], qgb[0:64, 0:32],
                      True, True, sname=sMM)
            P.wait("scalar", sMM, gv)
            P.wguard("scalar", ("g1v",))
            s1_ = P.act("scalar", e1[0:64, 0:32], P.PS[5][0:64, 0:32], AF.Exp,
                        scale=0.125)
            s2_ = P.act("scalar", e2[0:64, 0:32], P.PS[5][0:64, 0:32], AF.Exp,
                        scale=0.25)
            P.setguard(("bank", 5), "chscalar", s2_)
            P.wait("vector", "chscalar", s2_)
            P.op("vector", "scalar_tensor_tensor", corr[0:64, 0:32], e1[0:64, 0:32],
                 -2.0, e2[0:64, 0:32], OP.mult, OP.add)
            P.op("vector", "tensor_scalar", corr[0:64, 0:32], corr[0:64, 0:32],
                 1.0, None, OP.add)
            P.op("vector", "tensor_mul", corr[0:64, 0:32], corr[0:64, 0:32], CMK[:, :])
            P.op("vector", "tensor_scalar", pg[0:64, 0:32], e1[0:64, 0:32],
                 -1.0, None, OP.add)
            P.op("vector", "tensor_add", pg[0:64, 0:32], pg[0:64, 0:32],
                 corr[0:64, 0:32])
            P.op("vector", "tensor_copy", pgE[0:32, 0:32], pg[0:32, 0:32])
            pgv = P.op("vector", "tensor_copy", pgO[0:32, 0:32], pg[32:64, 0:32])
            P.setguard(("g1v",), "chvector", pgv)
            P.wait("tensor", "chvector", pgv)
            for tau in range(4):
                avi = 2 + ((h2 * 4 + tau) % 2)
                av = P.PS[avi]
                P.wguard("tensor", ("bank", avi))

                def emit_sc1(i, tau=tau):
                    kc = 4 * tau + i
                    si = [0, 1, 6][i % 3]
                    P.wguard("tensor", ("sbank1", si))
                    smm = P.mm(P.PS[si][:, 0:512],
                               kp[0:128, h2 * 4096 + kc * 128:
                                  h2 * 4096 + (kc + 1) * 128],
                               qp[0:128, h2 * 2048 + tau * 512:
                                  h2 * 2048 + (tau + 1) * 512],
                               True, True, sname=sMM)
                    es = i % 3
                    P.wguard("scalar", ("exp1", es))
                    P.wait("scalar", sMM, smm)
                    ee = P.act("scalar", expb[es][:, :], P.PS[si][:, 0:512], AF.Exp,
                               scale=0.125)
                    P.setguard(("sbank1", si), "chscalar", ee)
                    P.wait("vector", "chscalar", ee)
                    em = P.op("vector", "tensor_scalar", expb[es][:, :],
                              expb[es][:, :], -1.0, None, OP.add)
                    return ("chvector", em)

                gts = {0: emit_sc1(0), 1: emit_sc1(1)}
                for i in range(4):
                    kc = 4 * tau + i
                    if i + 2 < 4:
                        gts[i + 2] = emit_sc1(i + 2)
                    gate = gts.pop(i)
                    es = i % 3
                    P.wait("tensor", gate[0], gate[1])
                    amv = P.mm(av[0:65, 0:512],
                               vb[:, kc * 1040 + h * 65: kc * 1040 + h * 65 + 65],
                               expb[es][:, :], i == 0, False, sname=sMM)
                    P.setguard(("exp1", es), sMM, amv)
                if DBG_S5GLOB:
                    P.mm(av[0:65, 0:512:64], vgbE[0:32, h * 65: h * 65 + 65],
                         pgE[0:32, 8 * tau: 8 * tau + 8], False, False, sname=sMM)
                    gmv = P.mm(av[0:65, 0:512:64],
                               vgbO[0:32, h * 65: h * 65 + 65],
                               pgO[0:32, 8 * tau: 8 * tau + 8], False, True,
                               sname=sMM)
                else:
                    gmv = P.mm(av[0:65, 0:512], vb[:, 4 * tau * 1040 + h * 65:
                                                   4 * tau * 1040 + h * 65 + 65],
                               expb[(3) % 3][:, :], False, True, sname=sMM)
                rsl = (h2 * 4 + tau) % 2
                P.wguard("scalar", ("rawst1", rsl))
                P.wait("scalar", sMM, gmv)
                ev = P.act("scalar", rawst[rsl][0:65, 0:512], av[0:65, 0:512],
                           AF.Identity, bias=vs65[0:65, h:h + 1])
                P.setguard(("bank", avi), "chscalar", ev)
                P.wait("sync", "chscalar", ev)
                stv = P.dma("sync", rawd[h, tau, :, :], rawst[rsl][0:65, 0:512],
                            "hr" + str(rsl))
                P.setguard(("rawst1", rsl), "hr" + str(rsl), stv)
            P.setguard(("g1",), sMM, P.val(sMM))
        P.setguard(("q1slot", hp % 2), sMM, P.val(sMM))
    P.pop()


def s5(P):
    emit_attn1(P)


STAGES["s5"] = (s5, ["q1", "k1", "v1", "vg1", "vsum1"], ["raw1"])


def s6(P):
    emit_proj_ln(P, "f", "raw1", 4, "wo1", "bo1", "x2p", "x3")


STAGES["s6"] = (s6, ["raw1", "wo1", "x2p"], ["x3"])



def emit_ffn1(P):
    P.push()
    x3b = P.alloc_b(8 * 2048)
    shgb = P.alloc_b(8 * 1024)
    shub = P.alloc_b(8 * 1024)
    shdb = P.alloc_b(8 * 1024)
    rt1wb = P.alloc_b(8 * 16)
    cwb = P.alloc_b(4 * 512)
    gateg = [P.alloc_b(8 * 512), P.alloc_b(8 * 512)]
    upg = [P.alloc_b(8 * 512), P.alloc_b(8 * 512)]
    downb = [P.alloc_b(1024) for _ in range(4)]
    hbt = [P.alloc_b(512) for _ in range(4)]
    gb = P.alloc_b(512)
    h1 = P.alloc_b(512)
    shb = P.alloc_b(8 * 512)
    logf = P.alloc_f(512)
    ltm = P.alloc_f(64)
    eqa = P.alloc_f(16)
    eqb = P.alloc_f(16)
    mx1 = P.alloc_f(1)
    mx2 = P.alloc_f(1)
    w1c = P.alloc_f(1)
    w2c = P.alloc_f(1)
    cwtm = P.alloc_f(16)
    ysb = P.alloc_f(8 * 512)
    x3t = [P.alloc_f(512), P.alloc_f(512)]
    ybf = P.alloc_b(512)
    y2b = P.alloc_b(512)
    ssum = P.alloc_f(512)
    sq = P.alloc_f(512)
    vrx = P.alloc_f(512)
    rstd = P.alloc_f(512)
    xo = [P.alloc_f(512), P.alloc_f(512)]

    x3d = P.dsrc("x3")
    hbd = P.dtmp("hb1")
    outd = P.dout_or_tmp("xout")
    EYE = CFc(P, "eye128f")
    ROWF = CFc(P, "ones_rowf", 0, 1)
    SCLF = CFc(P, "scale_rowf", 0, 1)
    OCOL = CB(P, "ones_col")
    B16 = CB(P, "B16", 0, 16)
    sMM = "jmm"

    for kc in range(8):
        P.dma("gpsimd", x3b[:, kc * 2048:(kc + 1) * 2048],
              x3d[128 * kc:128 * (kc + 1), :], "jlx")
        P.dma("gpsimd", shgb[:, kc * 1024:(kc + 1) * 1024],
              P.din("shg1")[128 * kc:128 * (kc + 1), :], "jlw")
        P.dma("gpsimd", shub[:, kc * 1024:(kc + 1) * 1024],
              P.din("shu1")[128 * kc:128 * (kc + 1), :], "jlw")
        P.dma("gpsimd", shdb[:, kc * 1024:(kc + 1) * 1024],
              P.din("shd1")[128 * kc:128 * (kc + 1), :], "jlw")
        P.dma("gpsimd", rt1wb[:, kc * 16:(kc + 1) * 16],
              P.din("rt1_w")[128 * kc:128 * (kc + 1), :], "jlw")
    P.wait("tensor", "jlx", P.val("jlx"))
    P.wait("tensor", "jlw", P.val("jlw"))

    for tau in range(4):
        P.wguard("tensor", ("bank", 0))
        for kc in range(8):
            P.mm(P.PS[0][0:16, 0:512], rt1wb[:, kc * 16:(kc + 1) * 16],
                 x3b[:, kc * 2048 + tau * 512: kc * 2048 + tau * 512 + 512],
                 kc == 0, kc == 7, sname=sMM if kc == 7 else None)
        P.wait("scalar", sMM, P.val(sMM))
        lv = P.act("scalar", logf[0:16, 0:512], P.PS[0][0:16, 0:512], AF.Identity,
                   bias=BPc(P, "rt1_b", 0, 0, 16))
        P.setguard(("bank", 0), "chscalar", lv)
        P.wait("tensor", "chscalar", lv)
        P.wguard("tensor", ("bank", 1))
        for j in range(4):
            tv = P.op("tensor", "transpose", P.PS[1][0:128, j * 16:(j + 1) * 16],
                      logf[0:16, 128 * j:128 * (j + 1)], EYE[0:16, 0:16], sname=sMM)
        P.wait("vector", sMM, tv)
        lc = P.op("vector", "tensor_copy", ltm[0:128, 0:64], P.PS[1][0:128, 0:64])
        P.setguard(("bank", 1), "chvector", lc)
        P.wguard("tensor", ("bank", 2))
        for j in range(4):
            lj = ltm[0:128, j * 16:(j + 1) * 16]
            if j > 0:
                P.wait("vector", "chscalar", P.val("chscalar"))
                P.wait("vector", sMM, P.val(sMM))
            P.op("vector", "reduce_max", mx1[0:128, 0:1], lj, AX)
            P.op("vector", "tensor_scalar", eqa[0:128, 0:16], lj, mx1[0:128, 0:1],
                 None, OP.is_equal)
            P.op("vector", "scalar_tensor_tensor", eqb[0:128, 0:16], eqa[0:128, 0:16],
                 -1e30, lj, OP.mult, OP.add)
            P.op("vector", "reduce_max", mx2[0:128, 0:1], eqb[0:128, 0:16], AX)
            P.op("vector", "tensor_scalar", eqb[0:128, 0:16], eqb[0:128, 0:16],
                 mx2[0:128, 0:1], None, OP.is_equal)
            wv_ = P.op("vector", "tensor_sub", w1c[0:128, 0:1], mx1[0:128, 0:1],
                       mx2[0:128, 0:1])
            P.wait("scalar", "chvector", wv_)
            sv_ = P.act("scalar", w1c[0:128, 0:1], w1c[0:128, 0:1], AF.Sigmoid)
            P.wait("vector", "chscalar", sv_)
            P.op("vector", "tensor_scalar", w2c[0:128, 0:1], w1c[0:128, 0:1],
                 -1.0, 1.0, OP.mult, OP.add)
            P.op("vector", "tensor_scalar", cwtm[0:128, 0:16], eqa[0:128, 0:16],
                 w1c[0:128, 0:1], None, OP.mult)
            P.op("vector", "tensor_scalar", eqb[0:128, 0:16], eqb[0:128, 0:16],
                 w2c[0:128, 0:1], None, OP.mult)
            cv_ = P.op("vector", "tensor_add", cwtm[0:128, 0:16], cwtm[0:128, 0:16],
                       eqb[0:128, 0:16])
            P.wait("tensor", "chvector", cv_)
            P.op("tensor", "transpose", P.PS[2][0:16, j * 128:(j + 1) * 128],
                 cwtm[0:128, 0:16], EYE[0:128, 0:128], sname=sMM)
        tv2 = P.val(sMM)
        P.wait("vector", sMM, tv2)
        cc_ = P.op("vector", "tensor_copy", cwb[0:16, tau * 512:(tau + 1) * 512],
                   P.PS[2][0:16, 0:512])
        P.setguard(("bank", 2), "chvector", cc_)

    P.wait("tensor", "chvector", P.val("chvector"))
    for g in range(16):
        s = g % 2
        P.wguard("gpsimd", ("jgu", s))
        for kc in range(8):
            P.dma("gpsimd", gateg[s][:, kc * 512:(kc + 1) * 512],
                  P.din("gate1")[128 * kc:128 * (kc + 1), g * 512:(g + 1) * 512],
                  "jlg" + str(s))
            P.dma("gpsimd", upg[s][:, kc * 512:(kc + 1) * 512],
                  P.din("up1")[128 * kc:128 * (kc + 1), g * 512:(g + 1) * 512],
                  "jlu" + str(s))
        P.wait("tensor", "jlg" + str(s), P.val("jlg" + str(s)))
        P.wait("tensor", "jlu" + str(s), P.val("jlu" + str(s)))
        for mq in range(4):
            mh = g * 4 + mq
            for tp in range(4):
                bG = P.PS[0 + 2 * (mh % 2)]
                bU = P.PS[1 + 2 * (mh % 2)]
                bM = P.PS[4 + (mh % 2)]
                P.wguard("tensor", ("bank", 0 + 2 * (mh % 2)))
                P.wguard("tensor", ("bank", 1 + 2 * (mh % 2)))
                P.wguard("tensor", ("bank", 4 + (mh % 2)))
                for kc in range(8):
                    P.mm(bG[:, 0:512],
                         gateg[s][:, kc * 512 + mq * 128: kc * 512 + (mq + 1) * 128],
                         x3b[:, kc * 2048 + tp * 512: kc * 2048 + tp * 512 + 512],
                         kc == 0, kc == 7)
                    P.mm(bU[:, 0:512],
                         upg[s][:, kc * 512 + mq * 128: kc * 512 + (mq + 1) * 128],
                         x3b[:, kc * 2048 + tp * 512: kc * 2048 + tp * 512 + 512],
                         kc == 0, kc == 7)
                bmv = P.mm(bM[:, 0:512], B16[0:16, mh * 128:(mh + 1) * 128],
                           cwb[0:16, tp * 512:(tp + 1) * 512], True, True, sname=sMM)
                P.wait("scalar", sMM, bmv)
                P.wguard("scalar", ("jgb",))
                sv_ = P.act("scalar", gb[:, :], bG[:, 0:512], AF.Silu)
                P.setguard(("bank", 0 + 2 * (mh % 2)), "chscalar", sv_)
                P.wait("vector", "chscalar", sv_)
                P.op("vector", "tensor_mul", h1[:, :], gb[:, :], bU[:, 0:512])
                P.wguard("vector", ("jhb", mh % 2))
                hv_ = P.op("vector", "tensor_mul", hbt[mh % 2][:, :], h1[:, :],
                           bM[:, 0:512])
                P.setguard(("jgb",), "chvector", hv_)
                P.setguard(("bank", 1 + 2 * (mh % 2)), "chvector", hv_)
                P.setguard(("bank", 4 + (mh % 2)), "chvector", hv_)
                P.wguard("sync", ("jhb", mh % 2))
                P.wait("sync", "chvector", hv_)
                so_ = P.dma("sync", hbd[mh, tp, :, :], hbt[mh % 2][:, :],
                            "jsh" + str(mh % 2))
                P.setguard(("jhb", mh % 2), "jsh" + str(mh % 2), so_)
        P.setguard(("jgu", s), sMM, P.val(sMM))
    P.fence()

    for tau in range(4):
        for m in range(8):
            bG = P.PS[0 + 2 * (m % 2)]
            bU = P.PS[1 + 2 * (m % 2)]
            P.wguard("tensor", ("bank", 0 + 2 * (m % 2)))
            P.wguard("tensor", ("bank", 1 + 2 * (m % 2)))
            for kc in range(8):
                P.mm(bG[:, 0:512],
                     shgb[:, kc * 1024 + m * 128: kc * 1024 + (m + 1) * 128],
                     x3b[:, kc * 2048 + tau * 512: kc * 2048 + tau * 512 + 512],
                     kc == 0, kc == 7)
                P.mm(bU[:, 0:512],
                     shub[:, kc * 1024 + m * 128: kc * 1024 + (m + 1) * 128],
                     x3b[:, kc * 2048 + tau * 512: kc * 2048 + tau * 512 + 512],
                     kc == 0, kc == 7, sname=sMM if kc == 7 else None)
            smv = P.val(sMM)
            P.wait("scalar", sMM, smv)
            P.wguard("scalar", ("jshb",))
            sv_ = P.act("scalar", gb[:, :], bG[:, 0:512], AF.Silu)
            P.setguard(("bank", 0 + 2 * (m % 2)), "chscalar", sv_)
            P.wait("vector", "chscalar", sv_)
            shv = P.op("vector", "tensor_mul", shb[:, m * 512:(m + 1) * 512],
                       gb[:, :], bU[:, 0:512])
            P.setguard(("bank", 1 + 2 * (m % 2)), "chvector", shv)
            P.setguard(("jshb",), "chvector", shv)
        P.wait("tensor", "chvector", P.val("chvector"))
        for dm in range(8):
            P.wguard("tensor", ("bank", dm))
        def emit_dload(mh):
            s = mh % 4
            P.wguard("gpsimd", ("jdw", s))
            dl = P.dma("gpsimd", downb[s][:, :],
                       P.din("down1")[128 * mh:128 * (mh + 1), :], "jld" + str(s))
            P.wguard("sync", ("jht", s))
            hl = P.dma("sync", hbt[s][:, :], hbd[mh, tau, :, :], "jlh" + str(s))
            return dl, hl

        lds = {0: emit_dload(0), 1: emit_dload(1), 2: emit_dload(2)}
        for mh in range(64):
            s = mh % 4
            if mh + 3 < 64:
                lds[mh + 3] = emit_dload(mh + 3)
            dl, hl = lds.pop(mh)
            P.wait("tensor", "jld" + str(s), dl)
            P.wait("tensor", "jlh" + str(s), hl)
            for dm in range(8):
                mk = P.mm(P.PS[dm][:, 0:512],
                          downb[s][:, dm * 128:(dm + 1) * 128], hbt[s][:, :],
                          mh == 0, False, sname=sMM if dm == 7 else None)
            P.setguard(("jdw", s), sMM, P.val(sMM))
            P.setguard(("jht", s), sMM, P.val(sMM))
        for kc in range(8):
            for dm in range(8):
                P.mm(P.PS[dm][:, 0:512],
                     shdb[:, kc * 1024 + dm * 128: kc * 1024 + (dm + 1) * 128],
                     shb[:, kc * 512:(kc + 1) * 512], False, kc == 7,
                     sname=sMM if dm == 7 else None)
        dnv = P.val(sMM)
        P.wait("scalar", sMM, dnv)
        for dm in range(8):
            rs = dm % 2
            P.wguard("sync", ("jxr", rs))
            P.dma("sync", x3t[rs][:, 0:512],
                  x3d[128 * dm:128 * (dm + 1), tau * 512:(tau + 1) * 512],
                  "jlr" + str(rs))
            ev = P.act("scalar", ysb[:, dm * 512:(dm + 1) * 512],
                       P.PS[dm][:, 0:512], AF.Copy)
            P.setguard(("bank", dm), "chscalar", ev)
            P.wait("vector", "chscalar", ev)
            P.wait("vector", "jlr" + str(rs), P.val("jlr" + str(rs)))
            av = P.op("vector", "tensor_add", ysb[:, dm * 512:(dm + 1) * 512],
                      ysb[:, dm * 512:(dm + 1) * 512], x3t[rs][:, 0:512])
            P.setguard(("jxr", rs), "chvector", av)
        for dm in range(8):
            P.wguard("vector", ("jyb",))
            P.op("vector", "tensor_copy", ybf[:, :], ysb[:, dm * 512:(dm + 1) * 512])
            cv = P.op("vector", "tensor_mul", y2b[:, :], ybf[:, :], ybf[:, :])
            P.wait("tensor", "chvector", cv)
            if dm == 0:
                P.wguard("tensor", ("bank", 4))
                P.wguard("tensor", ("bank", 5))
            P.mm(P.PS[4][0:1, 0:512], OCOL, ybf[:, :], dm == 0, dm == 7)
            sm = P.mm(P.PS[5][0:1, 0:512], OCOL, y2b[:, :], dm == 0, dm == 7,
                      sname=sMM)
            P.setguard(("jyb",), sMM, sm)
        stat_mm = P.val(sMM)
        P.wait("scalar", sMM, stat_mm)
        P.act("scalar", ssum[0:1, 0:512], P.PS[4][0:1, 0:512], AF.Copy)
        c2 = P.act("scalar", sq[0:1, 0:512], P.PS[5][0:1, 0:512], AF.Copy)
        P.setguard(("bank", 4), "chscalar", c2)
        P.setguard(("bank", 5), "chscalar", c2)
        P.wait("vector", "chscalar", c2)
        P.op("vector", "tensor_mul", vrx[0:1, 0:512], ssum[0:1, 0:512],
             ssum[0:1, 0:512])
        P.op("vector", "scalar_tensor_tensor", vrx[0:1, 0:512], vrx[0:1, 0:512],
             -1.0 / 1024.0, sq[0:1, 0:512], OP.mult, OP.add)
        sv2 = P.op("vector", "tensor_scalar", vrx[0:1, 0:512], vrx[0:1, 0:512],
                   1.0 / 1024.0, 1e-5, OP.mult, OP.add)
        P.wait("scalar", "chvector", sv2)
        P.act("scalar", vrx[0:1, 0:512], vrx[0:1, 0:512], AF.Ln)
        rv = P.act("scalar", rstd[0:1, 0:512], vrx[0:1, 0:512], AF.Exp, scale=-0.5)
        P.wait("tensor", "chscalar", rv)
        P.wguard("tensor", ("bank", 6))
        P.wguard("tensor", ("bank", 7))
        P.mm(P.PS[6][0:128, 0:512], SCLF, ssum[0:1, 0:512], True, True)
        bm2 = P.mm(P.PS[7][0:128, 0:512], ROWF, rstd[0:1, 0:512], True, True,
                   sname=sMM)
        P.wait("vector", sMM, bm2)
        for dm in range(8):
            xs = xo[dm % 2]
            P.wguard("vector", ("jxo", dm % 2))
            P.op("vector", "tensor_sub", xs[:, 0:512],
                 ysb[:, dm * 512:(dm + 1) * 512], P.PS[6][0:128, 0:512])
            xv = P.op("vector", "tensor_mul", xs[:, 0:512], xs[:, 0:512],
                      P.PS[7][0:128, 0:512])
            P.wait("sync", "chvector", xv)
            so_ = P.dma("sync", outd[128 * dm:128 * (dm + 1),
                                     tau * 512:(tau + 1) * 512], xs[:, 0:512],
                        "jso" + str(dm % 2))
            P.setguard(("jxo", dm % 2), "jso" + str(dm % 2), so_)
        P.setguard(("bank", 6), "chvector", P.val("chvector"))
        P.setguard(("bank", 7), "chvector", P.val("chvector"))
    P.pop()


def s7(P):
    emit_ffn1(P)


STAGES["s7"] = (s7, ["x3", "rt1_w", "up1", "gate1", "down1", "shg1", "shu1",
                     "shd1"], ["xout"])


def _dout_or_tmp(P, name):
    if name in P.export_names:
        return P.dout(name)
    return P.dtmp(name)


Prog.dout_or_tmp = _dout_or_tmp



def s0(P):
    emit_qkv(P, "a", "x0T",
             ("w_attn", 0), ("w_attn", 1024), ("w_attn", 2048),
             ("b_attn", 0), ("b_attn", 8), "bv_row",
             "q0", "k0", "v0", nq_tiles=8)


STAGES["s0"] = (s0, ["x0T", "w_attn", "bv_row"], ["q0", "k0", "v0"])


def build_launch(stages, export_names):
    P = Prog()
    P.export_names = export_names
    emit_consts(P)
    P.stage_bounds = {}
    for s in stages:
        P.cur_scope = s
        STAGES[s][0](P)
        P.fence()
        P.stage_bounds[s] = {e: len(P.streams[e]) for e in ENGS}
    P.cur_scope = None
    for name, rec in list(P.sems.items()):
        if rec[1] > 0:
            P.wait("sync", name, rec[1])
    P.finalize()
    return P


_BUILD_CACHE = {}


def kernel(**inputs):
    sh, per = host_prep(inputs)
    env = [dict() for _ in range(8)]
    produced_by_launch = []
    all_stage_list = [s for grp in LAUNCH_PLAN for s in grp]
    for li, grp in enumerate(LAUNCH_PLAN):
        later = set(s for g in LAUNCH_PLAN[li + 1:] for s in g)
        produces = set(p for s in grp for p in STAGES[s][2])
        consumed_later = set()
        for s in later:
            consumed_later.update(STAGES[s][1])
        exports = (produces & consumed_later) | (produces & {"xout"})
        if li == len(LAUNCH_PLAN) - 1:
            exports |= produces
        key = (li, tuple(grp), tuple(sorted(exports)))
        if key not in _BUILD_CACHE:
            _BUILD_CACHE[key] = build_launch(grp, exports)
        P = _BUILD_CACHE[key]
        in_maps = []
        for c in range(8):
            m = {}
            for name in P.inputs:
                if name in sh:
                    m[name] = sh[name]
                elif name in per[c]:
                    m[name] = per[c][name]
                else:
                    m[name] = env[c][name]
            in_maps.append(m)
        res = run_bass_kernel_spmd(P.nc, in_maps, core_ids=list(range(8)))
        for c in range(8):
            env[c].update(res.results[c])
    out = np.zeros((4, 4096, 1024), np.float32)
    for c in range(8):
        b, h = c // 2, c % 2
        if "xout" in env[c]:
            out[b, h * 2048:(h + 1) * 2048, :] = env[c]["xout"].T
    kernel.env = env
    return out


# version 16 (final)
# speedup vs baseline: 1.2338x; 1.0006x over previous
import numpy as np
import ml_dtypes
from contextlib import ExitStack

import concourse.bass as bass
import concourse.mybir as mybir
from concourse.bass_utils import run_bass_kernel_spmd

F32 = mybir.dt.float32
BF16 = mybir.dt.bfloat16
I32 = mybir.dt.int32
AF = mybir.ActivationFunctionType
OP = mybir.AluOpType
BF = ml_dtypes.bfloat16

RSQ2 = 0.70710678118654752

SEM_ALIAS = {}
for _n in ("amm", "bmm", "cmm", "dmm", "emm", "fmm", "gmm", "hmm", "jmm"):
    SEM_ALIAS[_n] = "Tmm"
SEM_ALIAS["bms"] = "Tms"
for _t in ("a", "e"):
    SEM_ALIAS[_t + "lx"] = "Glx"
    SEM_ALIAS[_t + "lb"] = "Glb"
    SEM_ALIAS[_t + "lw0"] = "Glw0"
    SEM_ALIAS[_t + "lw1"] = "Glw1"
    for _i in range(4):
        SEM_ALIAS[_t + "st" + str(_i)] = "So" + str(_i)
    for _i in range(2):
        SEM_ALIAS[_t + "stv" + str(_i)] = "Sv" + str(_i)
SEM_ALIAS.update({"bkld": "Gkl", "hkld": "Gkl", "bvld": "Gvl", "hvld": "Gvl",
                  "dlw": "Gw", "dlw2": "Gw", "jlw": "Gw", "dli": "Gli",
                  "jlx": "Gli", "hgld": "Sg", "gst": "Sgst"})
for _i in range(4):
    SEM_ALIAS["jld" + str(_i)] = "Gd" + str(_i)
    SEM_ALIAS["jlh" + str(_i)] = "Sjh" + str(_i)
for _i in range(2):
    SEM_ALIAS["bq" + str(_i)] = "SQ" + str(_i)
    SEM_ALIAS["hq" + str(_i)] = "SQ" + str(_i)
    SEM_ALIAS["br" + str(_i)] = "Sw" + str(_i)
    SEM_ALIAS["hr" + str(_i)] = "Sw" + str(_i)
    SEM_ALIAS["jlg" + str(_i)] = "Gg" + str(_i)
    SEM_ALIAS["jlu" + str(_i)] = "Gu" + str(_i)
    SEM_ALIAS["jld" + str(_i)] = "Gd" + str(_i)
    SEM_ALIAS["jlh" + str(_i)] = "Gh" + str(_i)
    SEM_ALIAS["gld" + str(_i)] = "Gw2" + str(_i)
    SEM_ALIAS["pli" + str(_i)] = "Gp" + str(_i)
    SEM_ALIAS["pgi" + str(_i)] = "Gi" + str(_i)
    SEM_ALIAS["pst" + str(_i)] = "Sp" + str(_i)
    SEM_ALIAS["jsh" + str(_i)] = "Sh" + str(_i)
    SEM_ALIAS["jso" + str(_i)] = "Sy" + str(_i)
    SEM_ALIAS["dso" + str(_i)] = "Sy" + str(_i)
    SEM_ALIAS["cso" + str(_i)] = "Sy" + str(_i)
    SEM_ALIAS["fso" + str(_i)] = "Sy" + str(_i)
    SEM_ALIAS["jlr" + str(_i)] = "Sr" + str(_i)
    SEM_ALIAS["dlr" + str(_i)] = "Sr" + str(_i)
    SEM_ALIAS["clr" + str(_i)] = "Sr" + str(_i)
    SEM_ALIAS["flr" + str(_i)] = "Sr" + str(_i)
    SEM_ALIAS["clx" + str(_i)] = "Sx" + str(_i)
    SEM_ALIAS["flx" + str(_i)] = "Sx" + str(_i)
for _i in range(4):
    SEM_ALIAS["clq" + str(_i)] = "Sq" + str(_i)
    SEM_ALIAS["flq" + str(_i)] = "Sq" + str(_i)
SEM_ALIAS.update({"cld": "Scl", "fld": "Scl", "clw": "Scw", "flw": "Scw",
                  "cst": "Sct", "fst": "Sct", "dlx": "Gdx", "dsx": "Sdx2"})
DBG_PHASES = ("q", "k", "v")
DBG_MLIM = 8
DBG_TBLIM = 32
DBG_TAULIM = 8
DBG_S3TAU = 8
DBG_S5HP = 8
DBG_S5GLOB = 1
ENGS = ("tensor", "scalar", "vector", "gpsimd", "sync")

LAUNCH_PLAN = [["s0", "s1", "s2", "s3", "sp", "s4", "s5", "s6", "s7"]]

ACT_SHAPES = {
    "x0T": ([1024, 4096], F32),
    "q0": ([1024, 4096], BF16),
    "k0": ([1024, 4096], BF16),
    "v0": ([32, 128, 1040], BF16),
    "raw0": ([16, 8, 65, 512], F32),
    "x1": ([1024, 4096], F32),
    "x2": ([1024, 4096], F32),
    "x2p": ([1024, 4096], F32),
    "q1": ([1024, 2048], BF16),
    "k1": ([1024, 4096], BF16),
    "v1": ([32, 128, 1040], BF16),
    "vg1": ([64, 1040], BF16),
    "vsum1": ([65, 16], F32),
    "raw1": ([16, 4, 65, 512], F32),
    "x3": ([1024, 2048], F32),
    "hb1": ([64, 4, 128, 512], BF16),
    "xout": ([1024, 2048], F32),
    "permidx": ([2048, 1], I32),
    "c_rd": ([128, 512], F32),
    "intg": ([1024, 4096], BF16),
    "f_rd": ([64, 512], F32),
}

WEIGHT_SHAPES = {
    "w_attn": [1024, 3072], "w_proj0": [1024, 1024],
    "wq1": [1024, 1024], "wk1": [1024, 1024], "wv1": [1024, 1024], "wo1": [1024, 1024],
    "r_w1": [1024, 256], "r_w2": [256, 8], "h_w1": [1024, 256], "h_w2": [256, 1],
    "e_w1": [2048, 512], "e_w2": [512, 3072], "e_b2": [8, 3072],
    "s_w1": [2048, 64], "s_w2": [64, 3072],
    "f_w1": [1024, 2048], "f_w2": [2048, 1024],
    "rt1_w": [1024, 16], "up1": [1024, 8192], "gate1": [1024, 8192],
    "down1": [8192, 1024], "shg1": [1024, 1024], "shu1": [1024, 1024],
    "shd1": [1024, 1024],
    "bv_row": [1, 1024], "bv1_row": [1, 1024],
}

STAGES = {}



BIAS_COLS = {}
N_BIAS_COLS = 0
CONST_COLS = {}
N_CONSTB = 0


def _build_bias_pack(inp):
    global BIAS_COLS, N_BIAS_COLS
    BIAS_COLS = {}
    cols = []

    def addvec(name, v):
        v = np.asarray(v, np.float32).reshape(-1)
        ncol = (v.size + 127) // 128
        a = np.zeros((ncol, 128), np.float32)
        a.reshape(-1)[:v.size] = v
        BIAS_COLS[name] = len(cols)
        cols.extend(list(a))

    addvec("r_b1", inp["L0_ffn_router_b1"])
    addvec("h_b1", inp["L0_ffn_halt_b1"])
    addvec("e_b1", inp["L0_ffn_expert_b1"].reshape(-1))
    addvec("s_b1", inp["L0_ffn_shared_b1"])
    addvec("f_b1", inp["L0_ffn_refine_b1"])
    BIAS_COLS["_gelu_end"] = len(cols)
    addvec("b_attn", inp["L0_attn_b_attn"])
    addvec("b_proj0", inp["L0_attn_b_proj"])
    addvec("r_b2", inp["L0_ffn_router_b2"])
    addvec("h_b2", inp["L0_ffn_halt_b2"])
    addvec("s_b2", inp["L0_ffn_shared_b2"])
    addvec("f_b2", inp["L0_ffn_refine_b2"])
    addvec("mu", inp["L0_ffn_mu"])
    addvec("iw", inp["L0_ffn_integration_weight"])
    addvec("sw", inp["L0_ffn_shared_weight"])
    addvec("bq1", inp["L1_attn_bq"])
    addvec("bk1", inp["L1_attn_bk"])
    addvec("bv1", inp["L1_attn_bv"])
    addvec("bo1", inp["L1_attn_bo"])
    addvec("rt1_b", inp["L1_ffn_router_bias"])
    N_BIAS_COLS = len(cols)
    return np.ascontiguousarray(np.stack(cols, axis=1), np.float32)


def _build_constb():
    global CONST_COLS, N_CONSTB
    CONST_COLS = {}
    parts = []
    off = 0

    def add(name, arr):
        nonlocal off
        a = np.zeros((128, arr.shape[1]), np.float32)
        a[:arr.shape[0], :] = arr
        CONST_COLS[name] = (off, arr.shape[1])
        parts.append(a)
        off += arr.shape[1]

    p = np.arange(128).reshape(128, 1)
    col = np.arange(896).reshape(1, 896)
    add("maskCM", (p <= col - 384).astype(np.float32))
    add("eye128", np.eye(128, dtype=np.float32))
    j = np.arange(512)
    add("B8", ((j // 64)[None, :] == np.arange(8)[:, None]).astype(np.float32))
    j = np.arange(8192)
    add("B16", ((j // 512)[None, :] == np.arange(16)[:, None]).astype(np.float32))
    add("ones_row", np.ones((1, 128), np.float32))
    add("ones_col", np.ones((128, 1), np.float32))
    m = np.zeros((64, 32), np.float32)
    for g in range(64):
        jj = 2 * g if g < 32 else 2 * (g - 32) + 1
        if jj < 32:
            for i in range(32):
                if i // 8 == jj // 8:
                    m[g, i] = 1.0
    add("corrmask", m)
    N_CONSTB = off
    return np.ascontiguousarray(np.concatenate(parts, axis=1).astype(BF))


def _build_constf():
    global CONSTF_COLS, N_CONSTF
    CONSTF_COLS = {}
    parts = []
    off = 0

    def add(name, arr):
        nonlocal off
        a = np.zeros((128, arr.shape[1]), np.float32)
        a[:arr.shape[0], :] = arr
        CONSTF_COLS[name] = (off, arr.shape[1])
        parts.append(a)
        off += arr.shape[1]

    add("eye128f", np.eye(128, dtype=np.float32))
    add("ones_rowf", np.ones((1, 128), np.float32))
    add("scale_rowf", np.full((1, 128), 1.0 / 1024.0, np.float32))
    N_CONSTF = off
    return np.ascontiguousarray(np.concatenate(parts, axis=1), np.float32)


def host_prep(inp):
    sh = {}
    sh["w_attn"] = np.ascontiguousarray(inp["L0_attn_w_attn"], np.float32)
    sh["w_proj0"] = np.ascontiguousarray(inp["L0_attn_w_proj"], np.float32)
    sh["wq1"] = np.ascontiguousarray(inp["L1_attn_wq"].T, np.float32)
    sh["wk1"] = np.ascontiguousarray(inp["L1_attn_wk"].T, np.float32)
    sh["wv1"] = np.ascontiguousarray(inp["L1_attn_wv"].T, np.float32)
    sh["wo1"] = np.ascontiguousarray(inp["L1_attn_wo"].T, np.float32)
    sh["r_w1"] = np.ascontiguousarray(inp["L0_ffn_router_w1"], np.float32)
    sh["r_w2"] = np.ascontiguousarray(inp["L0_ffn_router_w2"], np.float32)
    sh["h_w1"] = np.ascontiguousarray(inp["L0_ffn_halt_w1"], np.float32)
    sh["h_w2"] = np.ascontiguousarray(inp["L0_ffn_halt_w2"], np.float32)
    sh["e_w1"] = np.ascontiguousarray(
        inp["L0_ffn_expert_w1"].transpose(1, 0, 2).reshape(2048, 512), np.float32)
    sh["e_w2"] = np.ascontiguousarray(
        inp["L0_ffn_expert_w2"].reshape(512, 3072), np.float32)
    sh["e_b2"] = np.ascontiguousarray(inp["L0_ffn_expert_b2"], np.float32)
    sh["s_w1"] = np.ascontiguousarray(inp["L0_ffn_shared_w1"], np.float32)
    sh["s_w2"] = np.ascontiguousarray(inp["L0_ffn_shared_w2"], np.float32)
    sh["f_w1"] = np.ascontiguousarray(inp["L0_ffn_refine_w1"], np.float32)
    sh["f_w2"] = np.ascontiguousarray(inp["L0_ffn_refine_w2"], np.float32)
    sh["rt1_w"] = np.ascontiguousarray(inp["L1_ffn_router_w"].T, np.float32)
    sh["up1"] = np.ascontiguousarray(
        inp["L1_ffn_up_proj"].transpose(1, 0, 2).reshape(1024, 8192), np.float32)
    sh["gate1"] = np.ascontiguousarray(
        inp["L1_ffn_gate_proj"].transpose(1, 0, 2).reshape(1024, 8192), np.float32)
    sh["down1"] = np.ascontiguousarray(
        inp["L1_ffn_down_proj"].reshape(8192, 1024), np.float32)
    sh["shg1"] = np.ascontiguousarray(inp["L1_ffn_sh_gate"].T, np.float32)
    sh["shu1"] = np.ascontiguousarray(inp["L1_ffn_sh_up"].T, np.float32)
    sh["shd1"] = np.ascontiguousarray(inp["L1_ffn_sh_down"].T, np.float32)
    sh["bv_row"] = np.ascontiguousarray(
        inp["L0_attn_b_attn"][2048:3072].reshape(1, 1024), np.float32)
    sh["bv1_row"] = np.ascontiguousarray(
        inp["L1_attn_bv"].reshape(1, 1024), np.float32)
    sh["bias_pack"] = _build_bias_pack(inp)
    sh["constb"] = _build_constb()
    sh["constf"] = _build_constf()

    per = []
    for c in range(8):
        b, h = c // 2, c % 2
        d = {}
        d["x0T"] = np.ascontiguousarray(inp["L0_attn_hidden_states"][b].T, np.float32)
        d["permidx"] = (np.arange(2048, dtype=np.int32) ^ h).reshape(2048, 1)
        per.append(d)
    return sh, per



class Prog:
    def __init__(self):
        self.nc = bass.Bass("TRN2", target_bir_lowering=False)
        self.es = ExitStack()
        self.inputs = set()
        self.outputs = set()
        self.dram = {}
        self.sems = {}
        self.streams = {e: [] for e in ENGS}
        self.waited = {e: {} for e in ENGS}
        self.guards = {}
        self.AR = self.es.enter_context(self.nc.sbuf_tensor("AR", [128, 101376], BF16))
        self.ar_free = 0
        self.ar_base_stack = []
        self.PSH = [self.es.enter_context(
            self.nc.psum_tensor(f"PSH{i}", [128, 1024], F32)) for i in range(4)]
        self.PS = [self.PSH[i // 2][:, (i % 2) * 512:(i % 2) * 512 + 512]
                   for i in range(8)]

    def din(self, name):
        if name not in self.dram:
            if name == "bias_pack":
                shape, dt = [128, N_BIAS_COLS], F32
            elif name == "constb":
                shape, dt = [128, N_CONSTB], BF16
            elif name == "constf":
                shape, dt = [128, N_CONSTF], F32
            elif name in ACT_SHAPES:
                shape, dt = ACT_SHAPES[name]
            else:
                shape, dt = WEIGHT_SHAPES[name], F32
            self.dram[name] = self.nc.dram_tensor(
                name, list(shape), dt, kind="ExternalInput").ap()
            self.inputs.add(name)
        return self.dram[name]

    def dout(self, name):
        if name not in self.dram:
            shape, dt = ACT_SHAPES[name]
            self.dram[name] = self.nc.dram_tensor(
                name, list(shape), dt, kind="ExternalOutput").ap()
            self.outputs.add(name)
        return self.dram[name]

    def dsrc(self, name):
        if name in self.dram:
            return self.dram[name]
        return self.din(name)

    def fence(self):
        for sname, rec in list(self.sems.items()):
            if rec[1] > 0:
                for eng in ENGS:
                    self.wait(eng, sname, rec[1])

    def dtmp(self, name):
        if name not in self.dram:
            shape, dt = ACT_SHAPES[name]
            self.dram[name] = self.nc.dram_tensor(
                name, list(shape), dt, kind="Internal").ap()
        return self.dram[name]

    def alloc_b(self, n):
        off = self.ar_free
        self.ar_free += (n + 1) & ~1
        assert self.ar_free <= 101376, f"SBUF arena overflow: {self.ar_free}"
        return self.AR[:, off:off + n]

    def alloc_f(self, n):
        return self.alloc_b(2 * n).bitcast(F32)

    def push(self):
        self.ar_base_stack.append(self.ar_free)

    def pop(self):
        self.ar_free = self.ar_base_stack.pop()

    def sem(self, name):
        name = SEM_ALIAS.get(name, name)
        if name not in self.sems:
            h = self.es.enter_context(self.nc.semaphore(name))
            self.sems[name] = [h, 0]
        return self.sems[name][0]

    def tick(self, name, amt, agent=None):
        name = SEM_ALIAS.get(name, name)
        self.sem(name)
        s = self.sems[name]
        if agent is not None:
            if len(s) < 3:
                s.append(agent)
            else:
                assert s[2] == agent, (
                    f"semaphore {name} incremented by {agent} and {s[2]}")
        s[1] += amt
        return s[1]

    def val(self, name):
        name = SEM_ALIAS.get(name, name)
        return self.sems.get(name, [None, 0])[1]

    def emit(self, eng, fn):
        self.streams[eng].append((getattr(self, "cur_scope", None), fn))

    def wait(self, eng, name, v):
        name = SEM_ALIAS.get(name, name)
        if v is None or v <= 0:
            return
        w = self.waited[eng]
        if w.get(name, 0) >= v:
            return
        w[name] = v
        h = self.sem(name)
        self.emit(eng, lambda e, h=h, v=v: e.wait_ge(h, v))

    def wguard(self, eng, key):
        g = self.guards.get(key)
        if g is not None:
            self.wait(eng, g[0], g[1])

    def setguard(self, key, name, v):
        self.guards[key] = (name, v)

    def idma(self, out, in_, idx_ap, sname):
        h = self.sem(sname)
        self.emit("gpsimd", lambda e, o=out, i=in_, x=idx_ap, h=h:
                  e.indirect_dma_start(
                      out=o, out_offset=None, in_=i,
                      in_offset=bass.IndirectOffsetOnAxis(x, 0)).then_inc(h, 16))
        return self.tick(sname, 16, agent="dma-gpsimd")

    def dma(self, eng, out, in_, sname):
        h = self.sem(sname)
        self.emit(eng, lambda e, o=out, i=in_, h=h: e.dma_start(out=o, in_=i).then_inc(h, 16))
        return self.tick(sname, 16, agent="dma-" + eng)

    def mm(self, bank, lhsT, rhs, start, stop, sname=None):
        if sname is None:
            self.emit("tensor", lambda t, b=bank, l=lhsT, r=rhs, s1=start, s2=stop:
                      t.matmul(b, l, r, start=s1, stop=s2))
            return None
        h = self.sem(sname)
        self.emit("tensor", lambda t, b=bank, l=lhsT, r=rhs, s1=start, s2=stop, h=h:
                  t.matmul(b, l, r, start=s1, stop=s2).then_inc(h, 1))
        return self.tick(sname, 1, agent="tensor")

    def act(self, eng, out, in_, func, bias=0.0, scale=1.0, sname=None, accum=None):
        ch = "ch" + eng
        self.wait(eng, ch, self.val(ch))
        def f(e, o=out, i=in_, fu=func, b=bias, s=scale, a=accum):
            kw = {}
            if a is not None:
                kw["accum_out"] = a
            return e.activation(out=o, in_=i, func=fu, bias=b, scale=s, **kw)
        h = self.sem(ch)
        self.emit(eng, lambda e, h=h: f(e).then_inc(h, 1))
        return self.tick(ch, 1, agent=eng)

    def op(self, eng, meth, *args, sname=None, **kw):
        ch = "ch" + eng
        if eng != "tensor":
            self.wait(eng, ch, self.val(ch))
            assert sname in (None, ch), (eng, sname)
            sname = ch
        use = sname if sname is not None else ch
        h = self.sem(use)
        self.emit(eng, lambda e, a=args, k=kw, m=meth, h=h:
                  getattr(e, m)(*a, **k).then_inc(h, 1))
        return self.tick(use, 1, agent="tensor" if eng == "tensor" else eng)

    def finalize(self):
        def run(e, fns):
            cur = None
            cur_id = None
            for sc, f in fns:
                if sc != cur:
                    if cur is not None:
                        self.nc.leave_named_scope(cur, cur_id, False)
                    if sc is not None:
                        cur_id, _ = self.nc.enter_named_scope(sc, False)
                    cur = sc
                f(e)
            if cur is not None:
                self.nc.leave_named_scope(cur, cur_id, False)
        with self.nc.Block() as block:
            for eng in ENGS:
                fns = self.streams[eng]
                if not fns:
                    continue
                getattr(block, eng)(lambda e, fns=fns: run(e, fns))
        return self.nc



def emit_consts(P):
    cb = P.din("constb")
    cf = P.din("constf")
    bp = P.din("bias_pack")
    P.CB = P.alloc_b(N_CONSTB)
    P.CF = P.alloc_f(N_CONSTF)
    P.BP = P.alloc_f(N_BIAS_COLS)
    ng = BIAS_COLS["_gelu_end"]
    P.BPg_sc = P.alloc_f(ng)
    P.BPg_h = P.alloc_f(ng)
    P.BPn = P.alloc_f(24)
    P.dma("sync", P.CB[:, :], cb[:, :], "cld")
    P.dma("sync", P.CF[:, :], cf[:, :], "cld")
    v = P.dma("sync", P.BP[:, :], bp[:, :], "cld")
    P.wait("vector", "cld", v)
    P.op("vector", "tensor_scalar_mul", P.BPg_sc[:, :], P.BP[:, 0:ng], RSQ2)
    P.op("vector", "tensor_scalar_mul", P.BPg_h[:, :], P.BP[:, 0:ng], 0.5)
    sb2c0 = BIAS_COLS["s_b2"]
    P.op("vector", "tensor_scalar_mul", P.BPn[:, :],
         P.BP[:, sb2c0:sb2c0 + 24], -1.0)
    for eng in ENGS:
        P.wait(eng, "cld", P.val("cld"))
        P.wait(eng, "chvector", P.val("chvector"))


def CB(P, name, r0=0, r1=128, c0=0, c1=None):
    off, n = CONST_COLS[name]
    c1 = n if c1 is None else c1
    return P.CB[r0:r1, off + c0:off + c1]


def CFc(P, name, r0=0, r1=128, c0=0, c1=None):
    off, n = CONSTF_COLS[name]
    c1 = n if c1 is None else c1
    return P.CF[r0:r1, off + c0:off + c1]


def BPc(P, name, j=0, r0=0, r1=128):
    return P.BP[r0:r1, BIAS_COLS[name] + j: BIAS_COLS[name] + j + 1]



def emit_qkv(P, tag, xname, wq, wk, wv, bq, bk, bvrow_name, qname, kname, vname,
             nq_tiles):
    P.push()
    xb = P.alloc_b(8 * 4096)
    wbuf = [P.alloc_b(8 * 1024), P.alloc_b(8 * 1024)]
    ev = P.alloc_b(4 * 512)
    vsb = [P.alloc_b(1040), P.alloc_b(1040)]
    bvb = P.alloc_b(1024)
    sMM, sEV, sVV, sST = tag + "mm", "chscalar", "chvector", tag + "st"
    sLX, sLB = tag + "lx", tag + "lb"
    sLW = [tag + "lw0", tag + "lw1"]

    xsrc = P.dsrc(xname)
    bvrow = P.din(bvrow_name)
    qd, kd, vd = P.dout_or_tmp(qname), P.dout_or_tmp(kname), P.dout_or_tmp(vname)

    for kc in range(8):
        P.dma("gpsimd", xb[:, kc * 4096:(kc + 1) * 4096],
              xsrc[128 * kc:128 * (kc + 1), :], sLX)
    P.dma("gpsimd", bvb[0:1, :], bvrow[0:1, :], sLB)

    def load_w(phase, wspec):
        dname, col0 = wspec
        wd = P.din(dname)
        buf = wbuf[phase % 2]
        P.wguard("gpsimd", ("wbuf", tag, phase % 2))
        sem = sLW[phase % 2]
        for kc in range(8):
            P.dma("gpsimd", buf[:, kc * 1024:(kc + 1) * 1024],
                  wd[128 * kc:128 * (kc + 1), col0:col0 + 1024], sem)
        return buf, sem, P.val(sem)

    _phases = [("q", wq, nq_tiles, qd, bq), ("k", wk, 8, kd, bk)]
    _phases = [p for p in _phases if p[0] in DBG_PHASES]
    for phase, (_pn, wspec, ntile, outd, bcol) in enumerate(_phases):
        buf, wsem, wtot = load_w(phase, wspec)
        P.wait("tensor", wsem, wtot)
        P.wait("tensor", sLX, P.val(sLX))
        bname, bj = bcol
        for m in range(DBG_MLIM):
            for g0 in range(0, ntile, 4):
                gt = list(range(g0, min(g0 + 4, ntile)))
                bi = ((m * ((ntile + 3) // 4) + g0 // 4) % 2) * 4
                for ti in range(len(gt)):
                    P.wguard("tensor", ("bank", bi + ti))
                for kc in range(8):
                    lhs = buf[:, kc * 1024 + m * 128: kc * 1024 + (m + 1) * 128]
                    for ti, tile in enumerate(gt):
                        P.mm(P.PS[bi + ti][:, 0:512], lhs,
                             xb[:, kc * 4096 + tile * 512: kc * 4096 + tile * 512 + 512],
                             kc == 0, kc == 7,
                             sname=sMM if (kc == 7 and ti == len(gt) - 1) else None)
                mmv = P.val(sMM)
                P.wait("scalar", sMM, mmv)
                for ti, tile in enumerate(gt):
                    evslot = ev[:, ti * 512:(ti + 1) * 512]
                    P.wguard("scalar", ("ev", tag, ti))
                    e = P.act("scalar", evslot, P.PS[bi + ti][:, 0:512], AF.Identity,
                              bias=BPc(P, bname, bj + m), sname=sEV)
                    P.setguard(("bank", bi + ti), sEV, e)
                    P.wait("sync", sEV, e)
                    st = P.dma("sync", outd[128 * m:128 * (m + 1),
                                            tile * 512:(tile + 1) * 512], evslot,
                               sST + str(ti))
                    P.setguard(("ev", tag, ti), sST + str(ti), st)
        P.setguard(("wbuf", tag, phase % 2), sMM, P.val(sMM))

    if "v" not in DBG_PHASES:
        P.pop()
        return
    buf, wsem, wtot = load_w(len(_phases), wv)
    P.wait("tensor", wsem, wtot)
    P.wait("tensor", sLX, P.val(sLX))
    P.wait("tensor", sLB, P.val(sLB))
    for tb in range(DBG_TBLIM):
        sb = vsb[tb % 2]
        P.wguard("vector", ("vsb", tag, tb % 2))
        P.op("vector", "memset", sb[:, :], 1.0, sname=sVV)
        msetv = P.val(sVV)
        for dv in range(2):
            bank = P.PS[4 + 2 * (tb % 2) + dv]
            P.wguard("tensor", ("bank", 4 + 2 * (tb % 2) + dv))
            for kc in range(8):
                lhs = xb[:, kc * 4096 + tb * 128: kc * 4096 + (tb + 1) * 128]
                P.mm(bank[:, 0:512], lhs,
                     buf[:, kc * 1024 + dv * 512: kc * 1024 + dv * 512 + 512],
                     kc == 0, False)
            mmv = P.mm(bank[:, 0:512], CB(P, "ones_row", 0, 1),
                       bvb[0:1, dv * 512: dv * 512 + 512], False, True, sname=sMM)
            P.wait("vector", sMM, mmv)
            P.wait("vector", sVV, msetv)
            for hh in range(8):
                h = dv * 8 + hh
                P.op("vector", "tensor_copy",
                     sb[:, h * 65: h * 65 + 64],
                     bank[:, hh * 64:(hh + 1) * 64],
                     sname=sVV if hh == 7 else None)
            P.setguard(("bank", 4 + 2 * (tb % 2) + dv), sVV, P.val(sVV))
        P.wait("sync", sVV, P.val(sVV))
        st = P.dma("sync", vd[tb, :, :], sb[:, :], sST + "v" + str(tb % 2))
        P.setguard(("vsb", tag, tb % 2), sST + "v" + str(tb % 2), st)
    P.pop()



def emit_attn0(P):
    P.push()
    vb = P.alloc_b(32 * 1040)
    kpb = [P.alloc_b(8192), P.alloc_b(8192)]
    qpb = [P.alloc_b(8192), P.alloc_b(8192)]
    expp = [P.alloc_b(1024) for _ in range(3)]
    rawst = [P.alloc_f(512), P.alloc_f(512)]
    qd, kd, vd = P.dsrc("q0"), P.dsrc("k0"), P.dsrc("v0")
    rawd = P.dout_or_tmp("raw0")
    sMM, sSC, sVV = "bmm", "chscalar", "chvector"

    for tb in range(32):
        P.dma("gpsimd", vb[:, tb * 1040:(tb + 1) * 1040], vd[tb, :, :], "bvld")
    for buf in (kpb[0], kpb[1], qpb[0], qpb[1]):
        for c in range(16):
            P.op("vector", "memset", buf[64:128, c * 512:(c + 1) * 512], 0.0)
    P.wait("tensor", "chvector", P.val("chvector"))
    P.wait("tensor", "bvld", P.val("bvld"))

    CM = CB(P, "maskCM")
    SPAIR = [0, 2, 3]
    for hp in range(8):
        sl = hp % 2
        kp, qp = kpb[sl], qpb[sl]
        P.wguard("sync", ("qslot", sl))
        P.dma("sync", kp[0:64, 0:4096], kd[128 * hp:128 * hp + 64, :], "bq" + str(sl))
        P.dma("sync", kp[0:64, 4096:8192], kd[128 * hp + 64:128 * hp + 128, :],
              "bq" + str(sl))
        P.dma("sync", qp[0:64, 0:4096], qd[128 * hp:128 * hp + 64, :], "bq" + str(sl))
        P.dma("sync", qp[0:64, 4096:8192], qd[128 * hp + 64:128 * hp + 128, :],
              "bq" + str(sl))
        P.wait("tensor", "bq" + str(sl), P.val("bq" + str(sl)))
        for h2 in range(2):
            h = 2 * hp + h2

            def emit_pair(tau, pi):
                sp = SPAIR[pi % 3]
                ph = P.PSH[sp]
                P.wguard("tensor", ("spair", sp))
                for half in range(2):
                    kc = 2 * pi + half
                    P.mm(ph[:, half * 512:(half + 1) * 512],
                         kp[0:128, h2 * 4096 + kc * 128: h2 * 4096 + (kc + 1) * 128],
                         qp[0:128, h2 * 4096 + tau * 512: h2 * 4096 + (tau + 1) * 512],
                         True, True, sname="bms" if half == 1 else None)
                smm = P.val("bms")
                eslot = pi % 3
                P.wguard("scalar", ("expp", eslot))
                P.wait("scalar", "bms", smm)
                eex = P.act("scalar", expp[eslot][:, :], ph[:, 0:1024], AF.Exp,
                            scale=0.125, sname=sSC)
                P.setguard(("spair", sp), sSC, eex)
                gate = (sSC, eex)
                for half in range(2):
                    kc = 2 * pi + half
                    if kc >= 4 * tau:
                        i = kc - 4 * tau
                        P.wait("vector", sSC, eex)
                        emk = P.op("vector", "tensor_mul",
                                   expp[eslot][:, half * 512:(half + 1) * 512],
                                   expp[eslot][:, half * 512:(half + 1) * 512],
                                   CM[:, 384 - 128 * i: 384 - 128 * i + 512],
                                   sname=sVV)
                        gate = (sVV, emk)
                return gate

            for tau in range(DBG_TAULIM):
                npair = 2 * tau + 2
                avi = 2 + ((h2 * 8 + tau) % 2)
                av = P.PS[avi]
                P.wguard("tensor", ("bank", avi))
                gates = {}
                for pi in range(min(2, npair)):
                    gates[pi] = emit_pair(tau, pi)
                for pi in range(npair):
                    if pi + 2 < npair:
                        gates[pi + 2] = emit_pair(tau, pi + 2)
                    gate = gates.pop(pi)
                    eslot = pi % 3
                    P.wait("tensor", gate[0], gate[1])
                    for half in range(2):
                        kc = 2 * pi + half
                        P.mm(av[0:65, 0:512],
                             vb[:, kc * 1040 + h * 65: kc * 1040 + h * 65 + 65],
                             expp[eslot][:, half * 512:(half + 1) * 512],
                             kc == 0, kc == 4 * tau + 3,
                             sname=sMM if half == 1 else None)
                    P.setguard(("expp", eslot), sMM, P.val(sMM))
                rsl = (h2 * 8 + tau) % 2
                P.wguard("vector", ("rawst", rsl))
                P.wait("vector", sMM, P.val(sMM))
                ee = P.op("vector", "tensor_copy", rawst[rsl][0:65, 0:512],
                          av[0:65, 0:512])
                P.setguard(("bank", avi), "chvector", ee)
                P.wait("sync", "chvector", ee)
                stv = P.dma("sync", rawd[h, tau, :, :], rawst[rsl][0:65, 0:512],
                            "br" + str(rsl))
                P.setguard(("rawst", rsl), "br" + str(rsl), stv)
        P.setguard(("qslot", hp % 2), "bms", P.val("bms"))
    P.pop()


def s1(P):
    emit_attn0(P)


STAGES["s1"] = (s1, ["q0", "k0", "v0"], ["raw0"])



def emit_proj_ln(P, tag, raw_name, ntile, wname, bcol_name, res_name, out_name):
    P.push()
    nh = 16
    rawd = P.dsrc(raw_name)
    resd = P.dsrc(res_name)
    outd = P.dout_or_tmp(out_name)
    wd = P.din(wname)
    rdd = P.dtmp(tag + "_rd")
    sMM, sSC, sVV = tag + "mm", "chscalar", "chvector"

    wpb = P.alloc_b(8 * 1024)
    denf = P.alloc_f(512)
    recf = P.alloc_f(512)
    rdrow = [P.alloc_f(512) for _ in range(4)]
    rawt = [P.alloc_f(512) for _ in range(2)]
    attnb = P.alloc_b(8 * 512)
    ysb = P.alloc_f(8 * 512)
    ybf = P.alloc_b(512)
    y2b = P.alloc_b(512)
    xrt = [P.alloc_f(512) for _ in range(2)]
    ssum = P.alloc_f(512)
    sq = P.alloc_f(512)
    vrx = P.alloc_f(512)
    rstd = P.alloc_f(512)
    xo = [P.alloc_f(512) for _ in range(2)]

    nrow = nh * ntile
    for kc in range(8):
        P.dma("gpsimd", wpb[:, kc * 1024:(kc + 1) * 1024],
              wd[128 * kc:128 * (kc + 1), :], tag + "lw")
    P.dma("sync", denf[0:nrow, 0:512], rawd[:, :, 64, :], tag + "ld")
    P.wait("vector", tag + "ld", P.val(tag + "ld"))
    P.op("vector", "reciprocal", recf[0:nrow, 0:512], denf[0:nrow, 0:512], sname=sVV)
    P.wait("sync", sVV, P.val(sVV))
    P.dma("sync", rdd[0:nrow, :], recf[0:nrow, 0:512], tag + "st")
    P.wait("sync", tag + "st", P.val(tag + "st"))
    P.wait("tensor", tag + "lw", P.val(tag + "lw"))

    ROWF = CFc(P, "ones_rowf", 0, 1)
    SCLF = CFc(P, "scale_rowf", 0, 1)
    OCOL = CB(P, "ones_col")

    for tau in range(ntile):
        for m in range(8):
            rs = m % 2
            P.wguard("sync", ("rawt", tag, rs))
            for half in range(2):
                h = 2 * m + half
                P.dma("sync", rawt[rs][64 * half:64 * half + 64, 0:512],
                      rawd[h, tau, 0:64, :], tag + "lr" + str(rs))
                P.wguard("sync", ("rdrow", tag, 2 * rs + half))
                P.dma("sync", rdrow[2 * rs + half][0:1, 0:512],
                      rdd[h * ntile + tau: h * ntile + tau + 1, :],
                      tag + "lq" + str(2 * rs + half))
            P.wait("tensor", tag + "lq" + str(2 * rs), P.val(tag + "lq" + str(2 * rs)))
            P.wait("tensor", tag + "lq" + str(2 * rs + 1),
                   P.val(tag + "lq" + str(2 * rs + 1)))
            bcb = P.PS[6 + rs]
            P.wguard("tensor", ("bank", 6 + rs))
            P.mm(bcb[0:64, 0:512], ROWF[0:1, 0:64], rdrow[2 * rs][0:1, 0:512],
                 True, True)
            bm = P.mm(bcb[64:128, 0:512], ROWF[0:1, 0:64],
                      rdrow[2 * rs + 1][0:1, 0:512], True, True, sname=sMM)
            P.setguard(("rdrow", tag, 2 * rs), sMM, bm)
            P.setguard(("rdrow", tag, 2 * rs + 1), sMM, bm)
            P.wait("vector", sMM, bm)
            P.wait("vector", tag + "lr" + str(rs), P.val(tag + "lr" + str(rs)))
            vm = P.op("vector", "tensor_mul", attnb[:, m * 512:(m + 1) * 512],
                      rawt[rs][:, 0:512], bcb[0:128, 0:512], sname=sVV)
            P.setguard(("rawt", tag, rs), sVV, vm)
            P.setguard(("bank", 6 + rs), sVV, vm)
        attn_done = P.val(sVV)
        P.wait("tensor", sVV, attn_done)
        for dmg in range(2):
            for ti in range(4):
                P.wguard("tensor", ("bank", ti))
            for m in range(8):
                for ti in range(4):
                    dm = dmg * 4 + ti
                    P.mm(P.PS[ti][:, 0:512],
                         wpb[:, m * 1024 + dm * 128: m * 1024 + (dm + 1) * 128],
                         attnb[:, m * 512:(m + 1) * 512], m == 0, m == 7,
                         sname=sMM if (m == 7 and ti == 3) else None)
            mmv = P.val(sMM)
            P.wait("scalar", sMM, mmv)
            for ti in range(4):
                dm = dmg * 4 + ti
                rs = dm % 2
                P.wguard("sync", ("xrt", tag, rs))
                P.dma("sync", xrt[rs][:, 0:512],
                      resd[128 * dm:128 * (dm + 1), tau * 512:(tau + 1) * 512],
                      tag + "lx" + str(rs))
                ev = P.act("scalar", ysb[:, dm * 512:(dm + 1) * 512],
                           P.PS[ti][:, 0:512], AF.Identity,
                           bias=BPc(P, bcol_name, dm), sname=sSC)
                P.setguard(("bank", ti), sSC, ev)
                P.wait("vector", sSC, ev)
                P.wait("vector", tag + "lx" + str(rs), P.val(tag + "lx" + str(rs)))
                av = P.op("vector", "tensor_add", ysb[:, dm * 512:(dm + 1) * 512],
                          ysb[:, dm * 512:(dm + 1) * 512], xrt[rs][:, 0:512],
                          sname=sVV)
                P.setguard(("xrt", tag, rs), sVV, av)
                P.wguard("vector", ("ybf", tag))
                P.op("vector", "tensor_copy", ybf[:, :],
                     ysb[:, dm * 512:(dm + 1) * 512])
                cv = P.op("vector", "tensor_mul", y2b[:, :], ybf[:, :], ybf[:, :],
                          sname=sVV)
                P.wait("tensor", sVV, cv)
                if dm == 0:
                    P.wguard("tensor", ("bank", 4))
                    P.wguard("tensor", ("bank", 5))
                P.mm(P.PS[4][0:1, 0:512], OCOL, ybf[:, :], dm == 0, dm == 7)
                sm = P.mm(P.PS[5][0:1, 0:512], OCOL, y2b[:, :], dm == 0, dm == 7,
                          sname=sMM)
                P.setguard(("ybf", tag), sMM, sm)
        stat_mm = P.val(sMM)
        P.wait("scalar", sMM, stat_mm)
        c1 = P.act("scalar", ssum[0:1, 0:512], P.PS[4][0:1, 0:512], AF.Copy,
                   sname=sSC)
        c2 = P.act("scalar", sq[0:1, 0:512], P.PS[5][0:1, 0:512], AF.Copy,
                   sname=sSC)
        P.setguard(("bank", 4), sSC, c2)
        P.setguard(("bank", 5), sSC, c2)
        P.wait("vector", sSC, c2)
        P.op("vector", "tensor_mul", vrx[0:1, 0:512], ssum[0:1, 0:512],
             ssum[0:1, 0:512])
        sv = P.op("vector", "scalar_tensor_tensor", vrx[0:1, 0:512],
                  vrx[0:1, 0:512], -1.0 / 1024.0, sq[0:1, 0:512],
                  OP.mult, OP.add, sname=sVV)
        P.op("vector", "tensor_scalar", vrx[0:1, 0:512], vrx[0:1, 0:512],
             1.0 / 1024.0, 1e-5, OP.mult, OP.add, sname=sVV)
        P.wait("scalar", sVV, P.val(sVV))
        P.act("scalar", vrx[0:1, 0:512], vrx[0:1, 0:512], AF.Ln)
        rv = P.act("scalar", rstd[0:1, 0:512], vrx[0:1, 0:512], AF.Exp,
                   scale=-0.5, sname=sSC)
        P.wait("tensor", sSC, rv)
        P.wguard("tensor", ("bank", 6))
        P.wguard("tensor", ("bank", 7))
        P.mm(P.PS[6][0:128, 0:512], SCLF, ssum[0:1, 0:512], True, True)
        bm2 = P.mm(P.PS[7][0:128, 0:512], ROWF, rstd[0:1, 0:512], True, True,
                   sname=sMM)
        P.wait("vector", sMM, bm2)
        for dm in range(8):
            xs = xo[dm % 2]
            P.wguard("vector", ("xo", tag, dm % 2))
            P.op("vector", "tensor_sub", xs[:, 0:512],
                 ysb[:, dm * 512:(dm + 1) * 512], P.PS[6][0:128, 0:512])
            xv = P.op("vector", "tensor_mul", xs[:, 0:512], xs[:, 0:512],
                      P.PS[7][0:128, 0:512], sname=sVV)
            P.wait("sync", sVV, xv)
            so_ = P.dma("sync", outd[128 * dm:128 * (dm + 1),
                                     tau * 512:(tau + 1) * 512], xs[:, 0:512],
                        tag + "so" + str(dm % 2))
            P.setguard(("xo", tag, dm % 2), tag + "so" + str(dm % 2), so_)
        P.setguard(("bank", 6), sVV, P.val(sVV))
        P.setguard(("bank", 7), sVV, P.val(sVV))
    P.pop()


def s2(P):
    emit_proj_ln(P, "c", "raw0", 8, "w_proj0", "b_proj0", "x0T", "x1")


STAGES["s2"] = (s2, ["raw0", "w_proj0", "x0T"], ["x1"])



AX = mybir.AxisListType.X


def gelu_evac(P, outb, bank, gcol, tmpb, tmph, rows=128):
    return P.act("scalar", outb[0:rows, :], bank, AF.Gelu,
                 bias=P.BP[0:rows, gcol:gcol + 1])


def emit_ffn0(P):
    P.push()
    rw1b = P.alloc_b(2048)
    rw2b = P.alloc_b(16)
    hw1b = P.alloc_b(2048)
    hw2b = P.alloc_b(2)
    ew1b = P.alloc_b(16 * 512)
    ew2b = P.alloc_b(4 * 3072)
    eb2b = P.alloc_b(3072)
    sw1b = P.alloc_b(16 * 64)
    sw2b = P.alloc_b(3072)
    for (buf, dn, nk, ncol) in ((rw1b, "r_w1", 8, 256), (rw2b, "r_w2", 2, 8),
                                (hw1b, "h_w1", 8, 256), (hw2b, "h_w2", 2, 1),
                                (ew1b, "e_w1", 16, 512), (ew2b, "e_w2", 4, 3072),
                                (sw1b, "s_w1", 16, 64)):
        wd = P.din(dn)
        for kc in range(nk):
            P.dma("gpsimd", buf[:, kc * ncol:(kc + 1) * ncol],
                  wd[128 * kc:128 * (kc + 1), :], "dlw")
    P.dma("gpsimd", eb2b[0:8, :], P.din("e_b2")[:, :], "dlw")
    P.dma("gpsimd", sw2b[0:64, :], P.din("s_w2")[:, :], "dlw")
    P.wait("tensor", "dlw", P.val("dlw"))

    x1d = P.dsrc("x1")
    intd = P.dout_or_tmp("intg")

    intg = P.alloc_b(8 * 512)
    vst = P.alloc_b(8 * 512)
    habc = P.alloc_b(512)
    h1b = P.alloc_b(4 * 512)
    h2b = P.alloc_b(4 * 512)
    hsb = P.alloc_b(512)
    hhb = P.alloc_b(2 * 512)
    rhb = P.alloc_b(2 * 512)
    tmpb = P.alloc_b(512)
    tmph = P.alloc_b(512)
    tmpf = P.alloc_f(512)
    logf = P.alloc_f(512)
    hrow = P.alloc_f(512)
    ltm = P.alloc_f(32)
    eqa = P.alloc_f(32)
    eqb = P.alloc_f(32)
    mx1 = P.alloc_f(1)
    mx2 = P.alloc_f(1)
    w1c = P.alloc_f(1)
    m1fm = P.alloc_b(512)
    m2fm = P.alloc_b(512)
    rowa = P.alloc_f(512)
    rowb = P.alloc_f(512)
    swsig = P.alloc_f(1)
    onemsw = P.alloc_f(1)
    swcol = P.alloc_f(1)
    wAb = P.alloc_b(512)
    wBb = P.alloc_b(512)
    haltb = P.alloc_b(512)
    haltxb = P.alloc_b(512)
    dtb = [P.alloc_b(512) for _ in range(2)]
    Ab = [[P.alloc_b(512) for _ in range(3)] for _ in range(2)]
    Bb = [[P.alloc_b(512) for _ in range(3)] for _ in range(2)]
    Gb = [[P.alloc_b(512) for _ in range(3)] for _ in range(2)]
    tmpf3 = [[P.alloc_f(512) for _ in range(3)] for _ in range(2)]
    vnb = [P.alloc_b(512) for _ in range(3)]
    gnb = [P.alloc_b(512) for _ in range(3)]
    prod = P.alloc_b(512)
    mixv = P.alloc_b(512)
    mixg = P.alloc_b(512)

    EYE = CFc(P, "eye128f")
    ROWF = CFc(P, "ones_rowf", 0, 1)
    B8 = CB(P, "B8", 0, 8)
    sMM = "dmm"
    for c in range(6):
        P.op("vector", "memset", sw2b[64:128, c * 512:(c + 1) * 512], 0.0)
    P.op("vector", "memset", hsb[64:128, 0:512], 0.0)
    P.wait("tensor", "chvector", P.val("chvector"))

    P.act("scalar", swsig[0:1, 0:1], BPc(P, "sw", 0, 0, 1), AF.Sigmoid)
    P.wait("vector", "chscalar", P.val("chscalar"))
    P.op("vector", "tensor_scalar", onemsw[0:1, 0:1], swsig[0:1, 0:1], -1.0, 1.0,
         OP.mult, OP.add)
    ov = P.op("vector", "tensor_copy", rowa[0:1, 0:128], ROWF)
    P.wait("tensor", "chvector", ov)
    bmv = P.mm(P.PS[7][0:128, 0:1], rowa[0:1, 0:128], swsig[0:1, 0:1], True, True,
               sname=sMM)
    P.wait("scalar", sMM, bmv)
    scv = P.act("scalar", swcol[0:128, 0:1], P.PS[7][0:128, 0:1], AF.Copy)
    P.setguard(("bank", 7), "chscalar", scv)

    gc_r = 0
    gc_h = 2
    gc_e = 4
    gc_s = 8
    gc_f = 9

    for tau in range(DBG_S3TAU):
        t0 = tau * 512
        for f in range(8):
            P.wguard("gpsimd", ("s3x", f))
            P.dma("gpsimd", intg[:, f * 512:(f + 1) * 512],
                  x1d[128 * f:128 * (f + 1), t0:t0 + 512], "dlx")
        P.wait("vector", "dlx", P.val("dlx"))
        P.wait("tensor", "dlx", P.val("dlx"))

        for mh in range(2):
            P.wguard("tensor", ("bank", mh))
        for kc in range(8):
            for mh in range(2):
                P.mm(P.PS[mh][:, 0:512],
                     rw1b[:, kc * 256 + mh * 128: kc * 256 + (mh + 1) * 128],
                     intg[:, kc * 512:(kc + 1) * 512], kc == 0, kc == 7,
                     sname=sMM if mh == 1 else None)
        P.wait("scalar", sMM, P.val(sMM))
        for mh in range(2):
            gv = gelu_evac(P, rhb[:, mh * 512:(mh + 1) * 512],
                           P.PS[mh][:, 0:512], gc_r + mh, tmpb, tmph)
            P.setguard(("bank", mh), "chscalar", gv)
        P.wguard("tensor", ("bank", 2))
        P.wait("tensor", "chscalar", P.val("chscalar"))
        for kc2 in range(2):
            P.mm(P.PS[2][0:8, 0:512], rw2b[:, kc2 * 8:(kc2 + 1) * 8],
                 rhb[:, kc2 * 512:(kc2 + 1) * 512], kc2 == 0, kc2 == 1,
                 sname=sMM if kc2 == 1 else None)
        P.wait("scalar", sMM, P.val(sMM))
        lv = P.act("scalar", logf[0:8, 0:512], P.PS[2][0:8, 0:512], AF.Identity,
                   bias=BPc(P, "r_b2", 0, 0, 8))
        P.setguard(("bank", 2), "chscalar", lv)
        P.wait("tensor", "chscalar", lv)
        P.wguard("tensor", ("bank", 3))
        for j in range(4):
            tv = P.op("tensor", "transpose", P.PS[3][0:128, j * 8:(j + 1) * 8],
                      logf[0:8, 128 * j:128 * (j + 1)], EYE[0:8, 0:8], sname=sMM)
        P.wait("vector", sMM, tv)
        P.op("vector", "tensor_copy", ltm[0:128, 0:32], P.PS[3][0:128, 0:32])
        P.setguard(("bank", 3), "chvector", P.val("chvector"))
        P.wguard("tensor", ("bank", 4))
        P.wguard("tensor", ("bank", 5))
        P.wguard("tensor", ("bank", 6))
        for j in range(4):
            lj = ltm[0:128, j * 8:(j + 1) * 8]
            if j > 0:
                P.wait("vector", "chscalar", P.val("chscalar"))
                P.wait("vector", sMM, P.val(sMM))
            P.op("vector", "reduce_max", mx1[0:128, 0:1], lj, AX)
            P.op("vector", "tensor_scalar", eqa[0:128, 0:8], lj, mx1[0:128, 0:1],
                 None, OP.is_equal)
            P.op("vector", "scalar_tensor_tensor", eqb[0:128, 0:8], eqa[0:128, 0:8],
                 -1e30, lj, OP.mult, OP.add)
            P.op("vector", "reduce_max", mx2[0:128, 0:1], eqb[0:128, 0:8], AX)
            P.op("vector", "tensor_scalar", eqb[0:128, 0:8], eqb[0:128, 0:8],
                 mx2[0:128, 0:1], None, OP.is_equal)
            wv_ = P.op("vector", "tensor_sub", w1c[0:128, 0:1], mx1[0:128, 0:1],
                       mx2[0:128, 0:1])
            P.wait("scalar", "chvector", wv_)
            sv_ = P.act("scalar", w1c[0:128, 0:1], w1c[0:128, 0:1], AF.Sigmoid)
            P.wait("tensor", "chvector", wv_)
            P.wait("tensor", "chscalar", sv_)
            P.op("tensor", "transpose", P.PS[4][0:8, j * 128:(j + 1) * 128],
                 eqa[0:128, 0:8], EYE[0:128, 0:128], sname=sMM)
            P.op("tensor", "transpose", P.PS[5][0:8, j * 128:(j + 1) * 128],
                 eqb[0:128, 0:8], EYE[0:128, 0:128], sname=sMM)
            tv2 = P.op("tensor", "transpose", P.PS[6][0:1, j * 128:(j + 1) * 128],
                       w1c[0:128, 0:1], EYE[0:128, 0:128], sname=sMM)
        P.wait("vector", sMM, tv2)
        P.op("vector", "tensor_copy", m1fm[0:8, :], P.PS[4][0:8, 0:512])
        P.op("vector", "tensor_copy", m2fm[0:8, :], P.PS[5][0:8, 0:512])
        P.op("vector", "tensor_copy", rowa[0:1, 0:512], P.PS[6][0:1, 0:512])
        P.op("vector", "tensor_scalar", rowb[0:1, 0:512], rowa[0:1, 0:512],
             onemsw[0:1, 0:1], None, OP.mult)
        P.op("vector", "tensor_scalar", rowa[0:1, 0:512], rowa[0:1, 0:512],
             -1.0, 1.0, OP.mult, OP.add)
        rv_ = P.op("vector", "tensor_scalar", rowa[0:1, 0:512], rowa[0:1, 0:512],
                   onemsw[0:1, 0:1], None, OP.mult)
        P.setguard(("bank", 4), "chvector", rv_)
        P.setguard(("bank", 5), "chvector", rv_)
        P.setguard(("bank", 6), "chvector", rv_)
        P.wait("tensor", "chvector", rv_)
        P.wguard("tensor", ("bank", 4))
        P.wguard("tensor", ("bank", 5))
        P.mm(P.PS[4][0:128, 0:512], ROWF, rowb[0:1, 0:512], True, True)
        bc2 = P.mm(P.PS[5][0:128, 0:512], ROWF, rowa[0:1, 0:512], True, True,
                   sname=sMM)
        P.wait("vector", sMM, bc2)
        P.op("vector", "tensor_copy", wAb[0:128, 0:512], P.PS[4][0:128, 0:512])
        wv2 = P.op("vector", "tensor_copy", wBb[0:128, 0:512], P.PS[5][0:128, 0:512])
        P.setguard(("bank", 4), "chvector", wv2)
        P.setguard(("bank", 5), "chvector", wv2)

        for it in range(2):
            nkc = 8 if it == 0 else 16
            for b_ in range(7):
                P.wguard("tensor", ("bank", b_))
            P.wait("tensor", "chvector", P.val("chvector"))
            for kc in range(nkc):
                srcv = intg if kc < 8 else vst
                f8 = kc % 8
                ctx = srcv[:, f8 * 512:(f8 + 1) * 512]
                last = kc == nkc - 1
                for mh in range(4):
                    P.mm(P.PS[mh][:, 0:512],
                         ew1b[:, kc * 512 + mh * 128: kc * 512 + (mh + 1) * 128],
                         ctx, kc == 0, last)
                P.mm(P.PS[4][0:64, 0:512], sw1b[:, kc * 64:(kc + 1) * 64],
                     ctx, kc == 0, last,
                     sname=sMM if kc >= 8 else None)
                if kc < 8:
                    for mh2 in range(2):
                        P.mm(P.PS[5 + mh2][:, 0:512],
                             hw1b[:, kc * 256 + mh2 * 128: kc * 256 + (mh2 + 1) * 128],
                             ctx, kc == 0, kc == 7,
                             sname=sMM if mh2 == 1 else None)
            mmv = P.val(sMM)
            P.wait("scalar", sMM, mmv)
            for mh in range(4):
                P.wguard("scalar", ("habc",))
                gv = gelu_evac(P, habc, P.PS[mh][:, 0:512], gc_e + mh, tmpb, tmph)
                P.wguard("tensor", ("bank", 7))
                P.wait("tensor", "chscalar", gv)
                bm1 = P.mm(P.PS[7][0:128, 0:512], B8[0:8, mh * 128:(mh + 1) * 128],
                           m1fm[0:8, :], True, True, sname=sMM)
                P.wait("vector", sMM, bm1)
                P.wait("vector", "chscalar", gv)
                u1 = P.op("vector", "tensor_mul", h1b[:, mh * 512:(mh + 1) * 512],
                          habc[:, :], P.PS[7][0:128, 0:512])
                P.wait("tensor", "chvector", u1)
                bm2 = P.mm(P.PS[7][0:128, 0:512], B8[0:8, mh * 128:(mh + 1) * 128],
                           m2fm[0:8, :], True, True, sname=sMM)
                P.wait("vector", sMM, bm2)
                u2 = P.op("vector", "tensor_mul", h2b[:, mh * 512:(mh + 1) * 512],
                          habc[:, :], P.PS[7][0:128, 0:512])
                P.setguard(("bank", 7), "chvector", u2)
                P.setguard(("bank", mh), "chvector", u2)
                P.setguard(("habc",), "chvector", u2)
            gvs = gelu_evac(P, hsb, P.PS[4][0:64, 0:512], gc_s, tmpb, tmph, rows=64)
            P.setguard(("bank", 4), "chscalar", gvs)
            for mh2 in range(2):
                gvh = gelu_evac(P, hhb[:, mh2 * 512:(mh2 + 1) * 512],
                                P.PS[5 + mh2][:, 0:512], gc_h + mh2, tmpb, tmph)
                P.setguard(("bank", 5 + mh2), "chscalar", gvh)
            P.wguard("tensor", ("bank", 7))
            P.wait("tensor", "chscalar", P.val("chscalar"))
            for kc2 in range(2):
                P.mm(P.PS[7][0:1, 0:512], hw2b[:, kc2:kc2 + 1],
                     hhb[:, kc2 * 512:(kc2 + 1) * 512], kc2 == 0, kc2 == 1,
                     sname=sMM if kc2 == 1 else None)
            P.wait("scalar", sMM, P.val(sMM))
            hv = P.act("scalar", hrow[0:1, 0:512], P.PS[7][0:1, 0:512], AF.Sigmoid,
                       bias=BPc(P, "h_b2", 0, 0, 1))
            P.setguard(("bank", 7), "chscalar", hv)
            P.wait("tensor", "chscalar", hv)
            P.wguard("tensor", ("bank", 7))
            hb_ = P.mm(P.PS[7][0:128, 0:512], ROWF, hrow[0:1, 0:512], True, True,
                       sname=sMM)
            P.wait("vector", sMM, hb_)
            P.op("vector", "tensor_copy", haltb[0:128, 0:512], P.PS[7][0:128, 0:512])
            hx_ = P.op("vector", "tensor_scalar", haltxb[0:128, 0:512],
                       haltb[0:128, 0:512], 0.1, None, OP.mult)
            P.setguard(("bank", 7), "chvector", hx_)
            P.wait("tensor", "chscalar", P.val("chscalar"))
            PB = [0, 1, 2, 3, 5, 6]
            NPB = len(PB)
            for fp in range(4):
                fpair = (2 * fp, 2 * fp + 1)
                for j, f in enumerate(fpair):
                    P.op("vector", "tensor_scalar", dtb[j][:, :],
                         intg[:, f * 512:(f + 1) * 512], BPc(P, "mu", f), None,
                         OP.subtract)
                parts = []
                for j, f in enumerate(fpair):
                    for pt in (0, 2, 1):
                        for si in range(3):
                            if it == 0 and pt == 0:
                                continue
                            parts.append((j, f, si, pt))

                def emit_part(idx):
                    j, f, si, pt = parts[idx]
                    ch = (0, 8, 16)[pt] + f
                    bk = P.PS[PB[idx % NPB]]
                    P.wguard("tensor", ("bank", PB[idx % NPB]))
                    if si == 0:
                        P.mm(bk[:, 0:512],
                             sw2b[0:128, ch * 128:(ch + 1) * 128], hsb[0:128, :],
                             True, True, sname=sMM)
                    else:
                        hx = h1b if si == 1 else h2b
                        mf = m1fm if si == 1 else m2fm
                        for kc2 in range(4):
                            P.mm(bk[:, 0:512],
                                 ew2b[:, kc2 * 3072 + ch * 128:
                                      kc2 * 3072 + (ch + 1) * 128],
                                 hx[:, kc2 * 512:(kc2 + 1) * 512],
                                 kc2 == 0, False)
                        P.mm(bk[:, 0:512], eb2b[0:8, ch * 128:(ch + 1) * 128],
                             mf[0:8, :], False, True, sname=sMM)
                    return P.val(sMM)

                def emit_act(idx, mk):
                    j, f, si, pt = parts[idx]
                    bk = P.PS[PB[idx % NPB]]
                    P.wait("scalar", sMM, mk)
                    P.wguard("scalar", ("abg", j))
                    sb2c = BPc(P, "s_b2", (0, 8, 16)[pt] + f) if si == 0 else 0.0
                    if pt == 0:
                        ev_ = P.act("scalar", Ab[j][si][:, :], bk[:, 0:512],
                                    AF.Sigmoid, bias=sb2c)
                    elif pt == 2:
                        ev_ = P.act("scalar", Gb[j][si][:, :], bk[:, 0:512],
                                    AF.Sigmoid, bias=sb2c)
                    else:
                        ev_ = P.act("scalar", tmpf3[j][si][:, 0:512], bk[:, 0:512],
                                    AF.Sigmoid, scale=-1.0,
                                    bias=P.BPn[:, 8 + f:8 + f + 1] if si == 0 else 0.0)
                    P.setguard(("bank", PB[idx % NPB]), "chscalar", ev_)
                    return ev_

                marks = {}
                n = len(parts)
                nahead = NPB - 1
                for idx in range(min(nahead, n)):
                    marks[idx] = emit_part(idx)
                acts = {}
                for idx in range(n):
                    if idx + nahead < n:
                        marks[idx + nahead] = emit_part(idx + nahead)
                    acts[idx] = emit_act(idx, marks.pop(idx))
                lns = {}
                for j in range(2):
                    for si in range(3):
                        lns[(j, si)] = P.act("scalar", Bb[j][si][:, :],
                                             tmpf3[j][si][:, 0:512], AF.Ln)
                for j, f in enumerate(fpair):
                    for si in range(3):
                        P.wait("vector", "chscalar", lns[(j, si)])
                        if it == 0:
                            P.op("vector", "tensor_mul", vnb[si][:, :],
                                 Bb[j][si][:, :], dtb[j][:, :])
                        else:
                            P.op("vector", "tensor_mul", vnb[si][:, :],
                                 Ab[j][si][:, :], vst[:, f * 512:(f + 1) * 512])
                            P.op("vector", "tensor_mul", prod[:, :],
                                 Bb[j][si][:, :], dtb[j][:, :])
                            P.op("vector", "tensor_add", vnb[si][:, :],
                                 vnb[si][:, :], prod[:, :])
                        gn_ = P.op("vector", "tensor_mul", gnb[si][:, :],
                                   Gb[j][si][:, :], vnb[si][:, :])
                    P.setguard(("abg", j), "chvector", P.val("chvector"))
                    for (mix, src3) in ((mixv, vnb), (mixg, gnb)):
                        P.op("vector", "tensor_scalar", mix[0:128, 0:512],
                             src3[0][:, :], swcol[0:128, 0:1], None, OP.mult)
                        P.op("vector", "tensor_mul", prod[:, :], src3[1][:, :],
                             wAb[0:128, 0:512])
                        P.op("vector", "tensor_add", mix[0:128, 0:512],
                             mix[0:128, 0:512], prod[:, :])
                        P.op("vector", "tensor_mul", prod[:, :], src3[2][:, :],
                             wBb[0:128, 0:512])
                        P.op("vector", "tensor_add", mix[0:128, 0:512],
                             mix[0:128, 0:512], prod[:, :])
                    if it == 0:
                        P.op("vector", "tensor_mul", vst[:, f * 512:(f + 1) * 512],
                             haltb[0:128, 0:512], mixv[0:128, 0:512])
                    else:
                        P.op("vector", "tensor_sub", mixv[0:128, 0:512],
                             mixv[0:128, 0:512], vst[:, f * 512:(f + 1) * 512])
                        P.op("vector", "tensor_mul", mixv[0:128, 0:512],
                             haltb[0:128, 0:512], mixv[0:128, 0:512])
                        P.op("vector", "tensor_add", vst[:, f * 512:(f + 1) * 512],
                             vst[:, f * 512:(f + 1) * 512], mixv[0:128, 0:512])
                    P.op("vector", "tensor_mul", mixg[0:128, 0:512],
                         haltxb[0:128, 0:512], mixg[0:128, 0:512])
                    uv = P.op("vector", "tensor_add", intg[:, f * 512:(f + 1) * 512],
                              intg[:, f * 512:(f + 1) * 512], mixg[0:128, 0:512])
        P.wait("sync", "chvector", P.val("chvector"))
        for f in range(8):
            sv = P.dma("sync", intd[128 * f:128 * (f + 1), t0:t0 + 512],
                       intg[:, f * 512:(f + 1) * 512], "dsx")
        for f in range(8):
            P.setguard(("s3x", f), "dsx", sv)
    P.pop()
    P.fence()

    P.push()
    fw1b = P.alloc_b(8 * 2048)
    fw2b = P.alloc_b(16 * 1024)
    for (buf, dn, nk, ncol) in ((fw1b, "f_w1", 8, 2048), (fw2b, "f_w2", 16, 1024)):
        wd = P.din(dn)
        for kc in range(nk):
            P.dma("gpsimd", buf[:, kc * ncol:(kc + 1) * ncol],
                  wd[128 * kc:128 * (kc + 1), :], "dlw2")
    P.wait("tensor", "dlw2", P.val("dlw2"))
    x1d = P.dsrc("x1")
    intd = P.dsrc("intg")
    x2d = P.dout_or_tmp("x2")

    inb8 = P.alloc_b(8 * 512)
    r1b = P.alloc_b(16 * 512)
    tmpb2 = P.alloc_b(512)
    tmph2 = P.alloc_b(512)
    ysb = P.alloc_f(8 * 512)
    x1t = [P.alloc_f(512), P.alloc_f(512)]
    ybf = P.alloc_b(512)
    y2b = P.alloc_b(512)
    ssum = P.alloc_f(512)
    sq = P.alloc_f(512)
    vrx = P.alloc_f(512)
    rstd = P.alloc_f(512)
    xo = [P.alloc_f(512), P.alloc_f(512)]
    ROWF = CFc(P, "ones_rowf", 0, 1)
    SCLF = CFc(P, "scale_rowf", 0, 1)
    OCOL = CB(P, "ones_col")
    sMM = "dmm"

    for tau in range(DBG_S3TAU):
        t0 = tau * 512
        P.wguard("gpsimd", ("s3in",))
        for kc in range(8):
            P.dma("gpsimd", inb8[:, kc * 512:(kc + 1) * 512],
                  intd[128 * kc:128 * (kc + 1), t0:t0 + 512], "dli")
        P.wait("tensor", "dli", P.val("dli"))
        for mh in range(16):
            bk = P.PS[mh % 2]
            P.wguard("tensor", ("bank", mh % 2))
            for kc in range(8):
                P.mm(bk[:, 0:512],
                     fw1b[:, kc * 2048 + mh * 128: kc * 2048 + (mh + 1) * 128],
                     inb8[:, kc * 512:(kc + 1) * 512], kc == 0, kc == 7,
                     sname=sMM if kc == 7 else None)
            if mh == 15:
                P.setguard(("s3in",), sMM, P.val(sMM))
            P.wait("scalar", sMM, P.val(sMM))
            gv = gelu_evac(P, r1b[:, mh * 512:(mh + 1) * 512], bk[:, 0:512],
                           9 + mh, tmpb2, tmph2)
            P.setguard(("bank", mh % 2), "chscalar", gv)
        for dm in range(8):
            bk = P.PS[2 + dm % 2]
            P.wguard("tensor", ("bank", 2 + dm % 2))
            P.wait("tensor", "chscalar", P.val("chscalar"))
            for kc in range(16):
                P.mm(bk[:, 0:512],
                     fw2b[:, kc * 1024 + dm * 128: kc * 1024 + (dm + 1) * 128],
                     r1b[:, kc * 512:(kc + 1) * 512], kc == 0, kc == 15,
                     sname=sMM if kc == 15 else None)
            mmv = P.val(sMM)
            rs = dm % 2
            P.wguard("sync", ("s3r", rs))
            P.dma("sync", x1t[rs][:, 0:512],
                  x1d[128 * dm:128 * (dm + 1), t0:t0 + 512], "dlr" + str(rs))
            P.wait("scalar", sMM, mmv)
            ev = P.act("scalar", ysb[:, dm * 512:(dm + 1) * 512], bk[:, 0:512],
                       AF.Identity, bias=BPc(P, "f_b2", dm))
            P.setguard(("bank", 2 + dm % 2), "chscalar", ev)
            P.wait("vector", "chscalar", ev)
            P.op("vector", "tensor_scalar", ysb[:, dm * 512:(dm + 1) * 512],
                 ysb[:, dm * 512:(dm + 1) * 512], BPc(P, "iw", dm), None, OP.mult)
            P.wait("vector", "dlr" + str(rs), P.val("dlr" + str(rs)))
            av = P.op("vector", "scalar_tensor_tensor",
                      ysb[:, dm * 512:(dm + 1) * 512], x1t[rs][:, 0:512], 2.0,
                      ysb[:, dm * 512:(dm + 1) * 512], OP.mult, OP.add)
            P.setguard(("s3r", rs), "chvector", av)
            P.op("vector", "tensor_copy", ybf[:, :], ysb[:, dm * 512:(dm + 1) * 512])
            cv = P.op("vector", "tensor_mul", y2b[:, :], ybf[:, :], ybf[:, :])
            P.wait("tensor", "chvector", cv)
            if dm == 0:
                P.wguard("tensor", ("bank", 4))
                P.wguard("tensor", ("bank", 5))
            P.mm(P.PS[4][0:1, 0:512], OCOL, ybf[:, :], dm == 0, dm == 7)
            P.mm(P.PS[5][0:1, 0:512], OCOL, y2b[:, :], dm == 0, dm == 7,
                 sname=sMM if dm == 7 else None)
        P.wait("scalar", sMM, P.val(sMM))
        P.act("scalar", ssum[0:1, 0:512], P.PS[4][0:1, 0:512], AF.Copy)
        c2 = P.act("scalar", sq[0:1, 0:512], P.PS[5][0:1, 0:512], AF.Copy)
        P.setguard(("bank", 4), "chscalar", c2)
        P.setguard(("bank", 5), "chscalar", c2)
        P.wait("vector", "chscalar", c2)
        P.op("vector", "tensor_mul", vrx[0:1, 0:512], ssum[0:1, 0:512],
             ssum[0:1, 0:512])
        P.op("vector", "scalar_tensor_tensor", vrx[0:1, 0:512], vrx[0:1, 0:512],
             -1.0 / 1024.0, sq[0:1, 0:512], OP.mult, OP.add)
        sv_ = P.op("vector", "tensor_scalar", vrx[0:1, 0:512], vrx[0:1, 0:512],
                   1.0 / 1024.0, 1e-5, OP.mult, OP.add)
        P.wait("scalar", "chvector", sv_)
        P.act("scalar", vrx[0:1, 0:512], vrx[0:1, 0:512], AF.Ln)
        rv = P.act("scalar", rstd[0:1, 0:512], vrx[0:1, 0:512], AF.Exp, scale=-0.5)
        P.wait("tensor", "chscalar", rv)
        P.wguard("tensor", ("bank", 6))
        P.wguard("tensor", ("bank", 7))
        P.mm(P.PS[6][0:128, 0:512], SCLF, ssum[0:1, 0:512], True, True)
        bm2 = P.mm(P.PS[7][0:128, 0:512], ROWF, rstd[0:1, 0:512], True, True,
                   sname=sMM)
        P.wait("vector", sMM, bm2)
        for dm in range(8):
            xs = xo[dm % 2]
            P.wguard("vector", ("s3o", dm % 2))
            P.op("vector", "tensor_sub", xs[:, 0:512],
                 ysb[:, dm * 512:(dm + 1) * 512], P.PS[6][0:128, 0:512])
            xv = P.op("vector", "tensor_mul", xs[:, 0:512], xs[:, 0:512],
                      P.PS[7][0:128, 0:512])
            P.wait("sync", "chvector", xv)
            so_ = P.dma("sync", x2d[128 * dm:128 * (dm + 1), t0:t0 + 512],
                        xs[:, 0:512], "dso" + str(dm % 2))
            P.setguard(("s3o", dm % 2), "dso" + str(dm % 2), so_)
        P.setguard(("bank", 6), "chvector", P.val("chvector"))
        P.setguard(("bank", 7), "chvector", P.val("chvector"))
    P.pop()


LMARK = {}


def s3(P):
    emit_ffn0(P)


STAGES["s3"] = (s3, ["x1", "r_w1", "r_w2", "h_w1", "h_w2", "e_w1", "e_w2", "e_b2",
                     "s_w1", "s_w2", "f_w1", "f_w2"], ["x2", "intg"])



def emit_perm(P):
    P.push()
    x2d = P.dsrc("x2")
    x2pd = P.dout_or_tmp("x2p")
    idxd = P.din("permidx")
    buf = [P.alloc_f(2048), P.alloc_f(2048)]
    idxt = [P.alloc_b(2), P.alloc_b(2)]
    x2v = x2d.rearrange("p (h t) -> (p h) t", h=2)
    x2pv = x2pd.rearrange("p (h t) -> (p h) t", h=2)
    for k in range(16):
        s = k % 2
        P.wguard("gpsimd", ("pidx", s))
        P.dma("gpsimd", idxt[s][:, 0:2].bitcast(I32),
              idxd[128 * k:128 * (k + 1), :], "pli" + str(s))
        P.wait("gpsimd", "pli" + str(s), P.val("pli" + str(s)))
        P.wguard("gpsimd", ("pbuf", s))
        P.idma(buf[s][:, 0:2048], x2v[:, :], idxt[s][:, 0:2].bitcast(I32),
               "pgi" + str(s))
        P.wait("sync", "pgi" + str(s), P.val("pgi" + str(s)))
        sv = P.dma("sync", x2pv[128 * k:128 * (k + 1), :], buf[s][:, 0:2048],
                   "pst" + str(s))
        P.setguard(("pbuf", s), "pst" + str(s), sv)
        P.setguard(("pidx", s), "pst" + str(s), sv)
    P.pop()


def sp(P):
    emit_perm(P)


STAGES["sp"] = (sp, ["x2", "permidx"], ["x2p"])



def emit_vsum(P):
    P.push()
    v1d = P.dsrc("v1")
    vsd = P.dout_or_tmp("vsum1")
    vgd = P.dout_or_tmp("vg1")
    vt = [P.alloc_b(1040), P.alloc_b(1040)]
    vsum = P.alloc_f(1040)
    vs65 = P.alloc_f(16)
    sMM = "gmm"
    P.dma("sync", vgd[0:32, :], v1d[:, 0, :], "gst")
    P.dma("sync", vgd[32:64, :], v1d[:, 64, :], "gst")
    for tb in range(32):
        s = tb % 2
        P.wguard("gpsimd", ("vsb", s))
        P.dma("gpsimd", vt[s][:, :], v1d[tb, :, :], "gld" + str(s))
        P.wait("tensor", "gld" + str(s), P.val("gld" + str(s)))
        for sl, (c0, c1) in enumerate(((0, 512), (512, 1024), (1024, 1040))):
            mk = P.mm(P.PS[sl][0:1, 0:c1 - c0], CB(P, "ones_col"),
                      vt[s][:, c0:c1], tb == 0, tb == 31,
                      sname=sMM if sl == 2 else None)
        P.setguard(("vsb", s), sMM, P.val(sMM))
    P.wait("scalar", sMM, P.val(sMM))
    for sl, (c0, c1) in enumerate(((0, 512), (512, 1024), (1024, 1040))):
        sv = P.act("scalar", vsum[0:1, c0:c1], P.PS[sl][0:1, 0:c1 - c0], AF.Copy)
    P.wait("tensor", "chscalar", sv)
    EY1 = CFc(P, "eye128f", 0, 1, 0, 1)
    for h in range(16):
        tv = P.op("tensor", "transpose", P.PS[3][0:64, h:h + 1],
                  vsum[0:1, h * 65: h * 65 + 64], EY1, sname=sMM)
    P.wait("vector", sMM, tv)
    P.op("vector", "tensor_copy", vs65[0:64, 0:16], P.PS[3][0:64, 0:16])
    P.op("vector", "memset", vs65[64:65, 0:16], 4096.0)
    P.wait("sync", "chvector", P.val("chvector"))
    P.dma("sync", vsd[:, :], vs65[0:65, 0:16], "gst")
    P.pop()


def s4(P):
    emit_qkv(P, "e", "x2p", ("wq1", 0), ("wk1", 0), ("wv1", 0),
             ("bq1", 0), ("bk1", 0), "bv1_row", "q1", "k1", "v1", nq_tiles=4)
    P.fence()
    emit_vsum(P)


STAGES["s4"] = (s4, ["x2p", "wq1", "wk1", "wv1", "bv1_row"],
                ["q1", "k1", "v1", "vsum1", "vg1"])



def emit_attn1(P):
    P.push()
    vb = P.alloc_b(32 * 1040)
    kpb = [P.alloc_b(8192), P.alloc_b(8192)]
    qpb = [P.alloc_b(4096), P.alloc_b(4096)]
    vgbE = P.alloc_b(1040)
    vgbO = P.alloc_b(1040)
    pgE = P.alloc_b(32)
    pgO = P.alloc_b(32)
    vs65 = P.alloc_f(16)
    kgb = P.alloc_b(64)
    qgb = P.alloc_b(32)
    e1 = P.alloc_b(32)
    e2 = P.alloc_b(32)
    corr = P.alloc_b(32)
    pg = P.alloc_b(32)
    expb = [P.alloc_b(512) for _ in range(3)]
    rawst = [P.alloc_f(512), P.alloc_f(512)]
    qd, kd, vd = P.dsrc("q1"), P.dsrc("k1"), P.dsrc("v1")
    vgd, vsd = P.dsrc("vg1"), P.dsrc("vsum1")
    rawd = P.dout_or_tmp("raw1")
    sMM = "hmm"

    for tb in range(32):
        P.dma("gpsimd", vb[:, tb * 1040:(tb + 1) * 1040], vd[tb, :, :], "hvld")
    for buf, nc_ in ((kpb[0], 16), (kpb[1], 16), (qpb[0], 8), (qpb[1], 8)):
        for c in range(nc_):
            P.op("vector", "memset", buf[64:128, c * 512:(c + 1) * 512], 0.0)
    P.wait("tensor", "chvector", P.val("chvector"))
    P.dma("sync", vgbE[0:32, :], vgd[0:32, :], "hgld")
    P.dma("sync", vgbO[0:32, :], vgd[32:64, :], "hgld")
    P.dma("sync", vs65[0:65, 0:16], vsd[:, :], "hgld")
    P.wait("tensor", "hvld", P.val("hvld"))
    P.wait("tensor", "hgld", P.val("hgld"))
    P.wait("vector", "hgld", P.val("hgld"))
    CMK = CB(P, "corrmask", 0, 64)

    for hp in range(DBG_S5HP):
        sl = hp % 2
        kp, qp = kpb[sl], qpb[sl]
        P.wguard("sync", ("q1slot", sl))
        P.dma("sync", kp[0:64, 0:4096], kd[128 * hp:128 * hp + 64, :], "hq" + str(sl))
        P.dma("sync", kp[0:64, 4096:8192], kd[128 * hp + 64:128 * hp + 128, :],
              "hq" + str(sl))
        P.dma("sync", qp[0:64, 0:2048], qd[128 * hp:128 * hp + 64, :], "hq" + str(sl))
        P.dma("sync", qp[0:64, 2048:4096], qd[128 * hp + 64:128 * hp + 128, :],
              "hq" + str(sl))
        P.wait("tensor", "hq" + str(sl), P.val("hq" + str(sl)))
        P.wait("vector", "hq" + str(sl), P.val("hq" + str(sl)))
        for h2 in range(2):
            h = 2 * hp + h2
            if not DBG_S5GLOB:
                pass
            P.wguard("vector", ("g1",))
            P.op("vector", "tensor_copy", kgb[0:64, 0:32],
                 kp[0:64, h2 * 4096: h2 * 4096 + 4096: 128])
            P.op("vector", "tensor_copy", kgb[0:64, 32:64],
                 kp[0:64, h2 * 4096 + 64: h2 * 4096 + 4096: 128])
            cq = P.op("vector", "tensor_copy", qgb[0:64, 0:32],
                      qp[0:64, h2 * 2048: h2 * 2048 + 2048: 64])
            P.wait("tensor", "chvector", cq)
            P.wguard("tensor", ("bank", 5))
            gv = P.mm(P.PS[5][0:64, 0:32], kgb[0:64, 0:64], qgb[0:64, 0:32],
                      True, True, sname=sMM)
            P.wait("scalar", sMM, gv)
            P.wguard("scalar", ("g1v",))
            s1_ = P.act("scalar", e1[0:64, 0:32], P.PS[5][0:64, 0:32], AF.Exp,
                        scale=0.125)
            s2_ = P.act("scalar", e2[0:64, 0:32], P.PS[5][0:64, 0:32], AF.Exp,
                        scale=0.25)
            P.setguard(("bank", 5), "chscalar", s2_)
            P.wait("vector", "chscalar", s2_)
            P.op("vector", "scalar_tensor_tensor", corr[0:64, 0:32], e1[0:64, 0:32],
                 -2.0, e2[0:64, 0:32], OP.mult, OP.add)
            P.op("vector", "tensor_scalar", corr[0:64, 0:32], corr[0:64, 0:32],
                 1.0, None, OP.add)
            P.op("vector", "tensor_mul", corr[0:64, 0:32], corr[0:64, 0:32], CMK[:, :])
            P.op("vector", "tensor_scalar", pg[0:64, 0:32], e1[0:64, 0:32],
                 -1.0, None, OP.add)
            P.op("vector", "tensor_add", pg[0:64, 0:32], pg[0:64, 0:32],
                 corr[0:64, 0:32])
            P.op("vector", "tensor_copy", pgE[0:32, 0:32], pg[0:32, 0:32])
            pgv = P.op("vector", "tensor_copy", pgO[0:32, 0:32], pg[32:64, 0:32])
            P.setguard(("g1v",), "chvector", pgv)
            P.wait("tensor", "chvector", pgv)
            for tau in range(4):
                avi = 2 + ((h2 * 4 + tau) % 2)
                av = P.PS[avi]
                P.wguard("tensor", ("bank", avi))

                def emit_sc1(i, tau=tau):
                    kc = 4 * tau + i
                    si = [0, 1, 6][i % 3]
                    P.wguard("tensor", ("sbank1", si))
                    smm = P.mm(P.PS[si][:, 0:512],
                               kp[0:128, h2 * 4096 + kc * 128:
                                  h2 * 4096 + (kc + 1) * 128],
                               qp[0:128, h2 * 2048 + tau * 512:
                                  h2 * 2048 + (tau + 1) * 512],
                               True, True, sname=sMM)
                    es = i % 3
                    P.wguard("scalar", ("exp1", es))
                    P.wait("scalar", sMM, smm)
                    ee = P.act("scalar", expb[es][:, :], P.PS[si][:, 0:512], AF.Exp,
                               scale=0.125)
                    P.setguard(("sbank1", si), "chscalar", ee)
                    P.wait("vector", "chscalar", ee)
                    em = P.op("vector", "tensor_scalar", expb[es][:, :],
                              expb[es][:, :], -1.0, None, OP.add)
                    return ("chvector", em)

                gts = {0: emit_sc1(0), 1: emit_sc1(1)}
                for i in range(4):
                    kc = 4 * tau + i
                    if i + 2 < 4:
                        gts[i + 2] = emit_sc1(i + 2)
                    gate = gts.pop(i)
                    es = i % 3
                    P.wait("tensor", gate[0], gate[1])
                    amv = P.mm(av[0:65, 0:512],
                               vb[:, kc * 1040 + h * 65: kc * 1040 + h * 65 + 65],
                               expb[es][:, :], i == 0, False, sname=sMM)
                    P.setguard(("exp1", es), sMM, amv)
                if DBG_S5GLOB:
                    P.mm(av[0:65, 0:512:64], vgbE[0:32, h * 65: h * 65 + 65],
                         pgE[0:32, 8 * tau: 8 * tau + 8], False, False, sname=sMM)
                    gmv = P.mm(av[0:65, 0:512:64],
                               vgbO[0:32, h * 65: h * 65 + 65],
                               pgO[0:32, 8 * tau: 8 * tau + 8], False, True,
                               sname=sMM)
                else:
                    gmv = P.mm(av[0:65, 0:512], vb[:, 4 * tau * 1040 + h * 65:
                                                   4 * tau * 1040 + h * 65 + 65],
                               expb[(3) % 3][:, :], False, True, sname=sMM)
                rsl = (h2 * 4 + tau) % 2
                P.wguard("scalar", ("rawst1", rsl))
                P.wait("scalar", sMM, gmv)
                ev = P.act("scalar", rawst[rsl][0:65, 0:512], av[0:65, 0:512],
                           AF.Identity, bias=vs65[0:65, h:h + 1])
                P.setguard(("bank", avi), "chscalar", ev)
                P.wait("sync", "chscalar", ev)
                stv = P.dma("sync", rawd[h, tau, :, :], rawst[rsl][0:65, 0:512],
                            "hr" + str(rsl))
                P.setguard(("rawst1", rsl), "hr" + str(rsl), stv)
            P.setguard(("g1",), sMM, P.val(sMM))
        P.setguard(("q1slot", hp % 2), sMM, P.val(sMM))
    P.pop()


def s5(P):
    emit_attn1(P)


STAGES["s5"] = (s5, ["q1", "k1", "v1", "vg1", "vsum1"], ["raw1"])


def s6(P):
    emit_proj_ln(P, "f", "raw1", 4, "wo1", "bo1", "x2p", "x3")


STAGES["s6"] = (s6, ["raw1", "wo1", "x2p"], ["x3"])



def emit_ffn1(P):
    P.push()
    x3b = P.alloc_b(8 * 2048)
    shgb = P.alloc_b(8 * 1024)
    shub = P.alloc_b(8 * 1024)
    shdb = P.alloc_b(8 * 1024)
    rt1wb = P.alloc_b(8 * 16)
    cwb = P.alloc_b(4 * 512)
    gateg = [P.alloc_b(8 * 512), P.alloc_b(8 * 512)]
    upg = [P.alloc_b(8 * 512), P.alloc_b(8 * 512)]
    downb = [P.alloc_b(1024) for _ in range(4)]
    hbt = [P.alloc_b(512) for _ in range(4)]
    gb = P.alloc_b(512)
    h1 = P.alloc_b(512)
    shb = P.alloc_b(8 * 512)
    logf = P.alloc_f(512)
    ltm = P.alloc_f(64)
    eqa = P.alloc_f(16)
    eqb = P.alloc_f(16)
    mx1 = P.alloc_f(1)
    mx2 = P.alloc_f(1)
    w1c = P.alloc_f(1)
    w2c = P.alloc_f(1)
    cwtm = P.alloc_f(16)
    ysb = P.alloc_f(8 * 512)
    x3t = [P.alloc_f(512), P.alloc_f(512)]
    ybf = P.alloc_b(512)
    y2b = P.alloc_b(512)
    ssum = P.alloc_f(512)
    sq = P.alloc_f(512)
    vrx = P.alloc_f(512)
    rstd = P.alloc_f(512)
    xo = [P.alloc_f(512), P.alloc_f(512)]

    x3d = P.dsrc("x3")
    hbd = P.dtmp("hb1")
    outd = P.dout_or_tmp("xout")
    EYE = CFc(P, "eye128f")
    ROWF = CFc(P, "ones_rowf", 0, 1)
    SCLF = CFc(P, "scale_rowf", 0, 1)
    OCOL = CB(P, "ones_col")
    B16 = CB(P, "B16", 0, 16)
    sMM = "jmm"

    for kc in range(8):
        P.dma("gpsimd", x3b[:, kc * 2048:(kc + 1) * 2048],
              x3d[128 * kc:128 * (kc + 1), :], "jlx")
        P.dma("gpsimd", shgb[:, kc * 1024:(kc + 1) * 1024],
              P.din("shg1")[128 * kc:128 * (kc + 1), :], "jlw")
        P.dma("gpsimd", shub[:, kc * 1024:(kc + 1) * 1024],
              P.din("shu1")[128 * kc:128 * (kc + 1), :], "jlw")
        P.dma("gpsimd", shdb[:, kc * 1024:(kc + 1) * 1024],
              P.din("shd1")[128 * kc:128 * (kc + 1), :], "jlw")
        P.dma("gpsimd", rt1wb[:, kc * 16:(kc + 1) * 16],
              P.din("rt1_w")[128 * kc:128 * (kc + 1), :], "jlw")
    P.wait("tensor", "jlx", P.val("jlx"))
    P.wait("tensor", "jlw", P.val("jlw"))

    for tau in range(4):
        P.wguard("tensor", ("bank", 0))
        for kc in range(8):
            P.mm(P.PS[0][0:16, 0:512], rt1wb[:, kc * 16:(kc + 1) * 16],
                 x3b[:, kc * 2048 + tau * 512: kc * 2048 + tau * 512 + 512],
                 kc == 0, kc == 7, sname=sMM if kc == 7 else None)
        P.wait("scalar", sMM, P.val(sMM))
        lv = P.act("scalar", logf[0:16, 0:512], P.PS[0][0:16, 0:512], AF.Identity,
                   bias=BPc(P, "rt1_b", 0, 0, 16))
        P.setguard(("bank", 0), "chscalar", lv)
        P.wait("tensor", "chscalar", lv)
        P.wguard("tensor", ("bank", 1))
        for j in range(4):
            tv = P.op("tensor", "transpose", P.PS[1][0:128, j * 16:(j + 1) * 16],
                      logf[0:16, 128 * j:128 * (j + 1)], EYE[0:16, 0:16], sname=sMM)
        P.wait("vector", sMM, tv)
        lc = P.op("vector", "tensor_copy", ltm[0:128, 0:64], P.PS[1][0:128, 0:64])
        P.setguard(("bank", 1), "chvector", lc)
        P.wguard("tensor", ("bank", 2))
        for j in range(4):
            lj = ltm[0:128, j * 16:(j + 1) * 16]
            if j > 0:
                P.wait("vector", "chscalar", P.val("chscalar"))
                P.wait("vector", sMM, P.val(sMM))
            P.op("vector", "reduce_max", mx1[0:128, 0:1], lj, AX)
            P.op("vector", "tensor_scalar", eqa[0:128, 0:16], lj, mx1[0:128, 0:1],
                 None, OP.is_equal)
            P.op("vector", "scalar_tensor_tensor", eqb[0:128, 0:16], eqa[0:128, 0:16],
                 -1e30, lj, OP.mult, OP.add)
            P.op("vector", "reduce_max", mx2[0:128, 0:1], eqb[0:128, 0:16], AX)
            P.op("vector", "tensor_scalar", eqb[0:128, 0:16], eqb[0:128, 0:16],
                 mx2[0:128, 0:1], None, OP.is_equal)
            wv_ = P.op("vector", "tensor_sub", w1c[0:128, 0:1], mx1[0:128, 0:1],
                       mx2[0:128, 0:1])
            P.wait("scalar", "chvector", wv_)
            sv_ = P.act("scalar", w1c[0:128, 0:1], w1c[0:128, 0:1], AF.Sigmoid)
            P.wait("vector", "chscalar", sv_)
            P.op("vector", "tensor_scalar", w2c[0:128, 0:1], w1c[0:128, 0:1],
                 -1.0, 1.0, OP.mult, OP.add)
            P.op("vector", "tensor_scalar", cwtm[0:128, 0:16], eqa[0:128, 0:16],
                 w1c[0:128, 0:1], None, OP.mult)
            P.op("vector", "tensor_scalar", eqb[0:128, 0:16], eqb[0:128, 0:16],
                 w2c[0:128, 0:1], None, OP.mult)
            cv_ = P.op("vector", "tensor_add", cwtm[0:128, 0:16], cwtm[0:128, 0:16],
                       eqb[0:128, 0:16])
            P.wait("tensor", "chvector", cv_)
            P.op("tensor", "transpose", P.PS[2][0:16, j * 128:(j + 1) * 128],
                 cwtm[0:128, 0:16], EYE[0:128, 0:128], sname=sMM)
        tv2 = P.val(sMM)
        P.wait("vector", sMM, tv2)
        cc_ = P.op("vector", "tensor_copy", cwb[0:16, tau * 512:(tau + 1) * 512],
                   P.PS[2][0:16, 0:512])
        P.setguard(("bank", 2), "chvector", cc_)

    P.wait("tensor", "chvector", P.val("chvector"))
    for g in range(16):
        s = g % 2
        P.wguard("gpsimd", ("jgu", s))
        for kc in range(8):
            P.dma("gpsimd", gateg[s][:, kc * 512:(kc + 1) * 512],
                  P.din("gate1")[128 * kc:128 * (kc + 1), g * 512:(g + 1) * 512],
                  "jlg" + str(s))
            P.dma("gpsimd", upg[s][:, kc * 512:(kc + 1) * 512],
                  P.din("up1")[128 * kc:128 * (kc + 1), g * 512:(g + 1) * 512],
                  "jlu" + str(s))
        P.wait("tensor", "jlg" + str(s), P.val("jlg" + str(s)))
        P.wait("tensor", "jlu" + str(s), P.val("jlu" + str(s)))
        for mq in range(4):
            mh = g * 4 + mq
            for tp in range(4):
                bG = P.PS[0 + 2 * (mh % 2)]
                bU = P.PS[1 + 2 * (mh % 2)]
                bM = P.PS[4 + (mh % 2)]
                P.wguard("tensor", ("bank", 0 + 2 * (mh % 2)))
                P.wguard("tensor", ("bank", 1 + 2 * (mh % 2)))
                P.wguard("tensor", ("bank", 4 + (mh % 2)))
                for kc in range(8):
                    P.mm(bG[:, 0:512],
                         gateg[s][:, kc * 512 + mq * 128: kc * 512 + (mq + 1) * 128],
                         x3b[:, kc * 2048 + tp * 512: kc * 2048 + tp * 512 + 512],
                         kc == 0, kc == 7)
                    P.mm(bU[:, 0:512],
                         upg[s][:, kc * 512 + mq * 128: kc * 512 + (mq + 1) * 128],
                         x3b[:, kc * 2048 + tp * 512: kc * 2048 + tp * 512 + 512],
                         kc == 0, kc == 7)
                bmv = P.mm(bM[:, 0:512], B16[0:16, mh * 128:(mh + 1) * 128],
                           cwb[0:16, tp * 512:(tp + 1) * 512], True, True, sname=sMM)
                P.wait("scalar", sMM, bmv)
                P.wguard("scalar", ("jgb",))
                sv_ = P.act("scalar", gb[:, :], bG[:, 0:512], AF.Silu)
                P.setguard(("bank", 0 + 2 * (mh % 2)), "chscalar", sv_)
                P.wait("vector", "chscalar", sv_)
                P.op("vector", "tensor_mul", h1[:, :], gb[:, :], bU[:, 0:512])
                P.wguard("vector", ("jhb", mh % 2))
                hv_ = P.op("vector", "tensor_mul", hbt[mh % 2][:, :], h1[:, :],
                           bM[:, 0:512])
                P.setguard(("jgb",), "chvector", hv_)
                P.setguard(("bank", 1 + 2 * (mh % 2)), "chvector", hv_)
                P.setguard(("bank", 4 + (mh % 2)), "chvector", hv_)
                P.wguard("sync", ("jhb", mh % 2))
                P.wait("sync", "chvector", hv_)
                so_ = P.dma("sync", hbd[mh, tp, :, :], hbt[mh % 2][:, :],
                            "jsh" + str(mh % 2))
                P.setguard(("jhb", mh % 2), "jsh" + str(mh % 2), so_)
        P.setguard(("jgu", s), sMM, P.val(sMM))
    P.fence()

    for tau in range(4):
        for m in range(8):
            bG = P.PS[0 + 2 * (m % 2)]
            bU = P.PS[1 + 2 * (m % 2)]
            P.wguard("tensor", ("bank", 0 + 2 * (m % 2)))
            P.wguard("tensor", ("bank", 1 + 2 * (m % 2)))
            for kc in range(8):
                P.mm(bG[:, 0:512],
                     shgb[:, kc * 1024 + m * 128: kc * 1024 + (m + 1) * 128],
                     x3b[:, kc * 2048 + tau * 512: kc * 2048 + tau * 512 + 512],
                     kc == 0, kc == 7)
                P.mm(bU[:, 0:512],
                     shub[:, kc * 1024 + m * 128: kc * 1024 + (m + 1) * 128],
                     x3b[:, kc * 2048 + tau * 512: kc * 2048 + tau * 512 + 512],
                     kc == 0, kc == 7, sname=sMM if kc == 7 else None)
            smv = P.val(sMM)
            P.wait("scalar", sMM, smv)
            P.wguard("scalar", ("jshb",))
            sv_ = P.act("scalar", gb[:, :], bG[:, 0:512], AF.Silu)
            P.setguard(("bank", 0 + 2 * (m % 2)), "chscalar", sv_)
            P.wait("vector", "chscalar", sv_)
            shv = P.op("vector", "tensor_mul", shb[:, m * 512:(m + 1) * 512],
                       gb[:, :], bU[:, 0:512])
            P.setguard(("bank", 1 + 2 * (m % 2)), "chvector", shv)
            P.setguard(("jshb",), "chvector", shv)
        P.wait("tensor", "chvector", P.val("chvector"))
        for dm in range(8):
            P.wguard("tensor", ("bank", dm))
        def emit_dload(mh):
            s = mh % 4
            P.wguard("gpsimd", ("jdw", s))
            dl = P.dma("gpsimd", downb[s][:, :],
                       P.din("down1")[128 * mh:128 * (mh + 1), :], "jld" + str(s))
            P.wguard("sync", ("jht", s))
            hl = P.dma("sync", hbt[s][:, :], hbd[mh, tau, :, :], "jlh" + str(s))
            return dl, hl

        lds = {0: emit_dload(0), 1: emit_dload(1), 2: emit_dload(2)}
        for mh in range(64):
            s = mh % 4
            if mh + 3 < 64:
                lds[mh + 3] = emit_dload(mh + 3)
            dl, hl = lds.pop(mh)
            P.wait("tensor", "jld" + str(s), dl)
            P.wait("tensor", "jlh" + str(s), hl)
            for dm in range(8):
                mk = P.mm(P.PS[dm][:, 0:512],
                          downb[s][:, dm * 128:(dm + 1) * 128], hbt[s][:, :],
                          mh == 0, False, sname=sMM if dm == 7 else None)
            P.setguard(("jdw", s), sMM, P.val(sMM))
            P.setguard(("jht", s), sMM, P.val(sMM))
        for kc in range(8):
            for dm in range(8):
                P.mm(P.PS[dm][:, 0:512],
                     shdb[:, kc * 1024 + dm * 128: kc * 1024 + (dm + 1) * 128],
                     shb[:, kc * 512:(kc + 1) * 512], False, kc == 7,
                     sname=sMM if dm == 7 else None)
        dnv = P.val(sMM)
        P.wait("scalar", sMM, dnv)
        for dm in range(8):
            rs = dm % 2
            P.wguard("sync", ("jxr", rs))
            P.dma("sync", x3t[rs][:, 0:512],
                  x3d[128 * dm:128 * (dm + 1), tau * 512:(tau + 1) * 512],
                  "jlr" + str(rs))
            ev = P.act("scalar", ysb[:, dm * 512:(dm + 1) * 512],
                       P.PS[dm][:, 0:512], AF.Copy)
            P.setguard(("bank", dm), "chscalar", ev)
            P.wait("vector", "chscalar", ev)
            P.wait("vector", "jlr" + str(rs), P.val("jlr" + str(rs)))
            av = P.op("vector", "tensor_add", ysb[:, dm * 512:(dm + 1) * 512],
                      ysb[:, dm * 512:(dm + 1) * 512], x3t[rs][:, 0:512])
            P.setguard(("jxr", rs), "chvector", av)
        for dm in range(8):
            P.wguard("vector", ("jyb",))
            P.op("vector", "tensor_copy", ybf[:, :], ysb[:, dm * 512:(dm + 1) * 512])
            cv = P.op("vector", "tensor_mul", y2b[:, :], ybf[:, :], ybf[:, :])
            P.wait("tensor", "chvector", cv)
            if dm == 0:
                P.wguard("tensor", ("bank", 4))
                P.wguard("tensor", ("bank", 5))
            P.mm(P.PS[4][0:1, 0:512], OCOL, ybf[:, :], dm == 0, dm == 7)
            sm = P.mm(P.PS[5][0:1, 0:512], OCOL, y2b[:, :], dm == 0, dm == 7,
                      sname=sMM)
            P.setguard(("jyb",), sMM, sm)
        stat_mm = P.val(sMM)
        P.wait("scalar", sMM, stat_mm)
        P.act("scalar", ssum[0:1, 0:512], P.PS[4][0:1, 0:512], AF.Copy)
        c2 = P.act("scalar", sq[0:1, 0:512], P.PS[5][0:1, 0:512], AF.Copy)
        P.setguard(("bank", 4), "chscalar", c2)
        P.setguard(("bank", 5), "chscalar", c2)
        P.wait("vector", "chscalar", c2)
        P.op("vector", "tensor_mul", vrx[0:1, 0:512], ssum[0:1, 0:512],
             ssum[0:1, 0:512])
        P.op("vector", "scalar_tensor_tensor", vrx[0:1, 0:512], vrx[0:1, 0:512],
             -1.0 / 1024.0, sq[0:1, 0:512], OP.mult, OP.add)
        sv2 = P.op("vector", "tensor_scalar", vrx[0:1, 0:512], vrx[0:1, 0:512],
                   1.0 / 1024.0, 1e-5, OP.mult, OP.add)
        P.wait("scalar", "chvector", sv2)
        P.act("scalar", vrx[0:1, 0:512], vrx[0:1, 0:512], AF.Ln)
        rv = P.act("scalar", rstd[0:1, 0:512], vrx[0:1, 0:512], AF.Exp, scale=-0.5)
        P.wait("tensor", "chscalar", rv)
        P.wguard("tensor", ("bank", 6))
        P.wguard("tensor", ("bank", 7))
        P.mm(P.PS[6][0:128, 0:512], SCLF, ssum[0:1, 0:512], True, True)
        bm2 = P.mm(P.PS[7][0:128, 0:512], ROWF, rstd[0:1, 0:512], True, True,
                   sname=sMM)
        P.wait("vector", sMM, bm2)
        for dm in range(8):
            xs = xo[dm % 2]
            P.wguard("vector", ("jxo", dm % 2))
            P.op("vector", "tensor_sub", xs[:, 0:512],
                 ysb[:, dm * 512:(dm + 1) * 512], P.PS[6][0:128, 0:512])
            xv = P.op("vector", "tensor_mul", xs[:, 0:512], xs[:, 0:512],
                      P.PS[7][0:128, 0:512])
            P.wait("sync", "chvector", xv)
            so_ = P.dma("sync", outd[128 * dm:128 * (dm + 1),
                                     tau * 512:(tau + 1) * 512], xs[:, 0:512],
                        "jso" + str(dm % 2))
            P.setguard(("jxo", dm % 2), "jso" + str(dm % 2), so_)
        P.setguard(("bank", 6), "chvector", P.val("chvector"))
        P.setguard(("bank", 7), "chvector", P.val("chvector"))
    P.pop()


def s7(P):
    emit_ffn1(P)


STAGES["s7"] = (s7, ["x3", "rt1_w", "up1", "gate1", "down1", "shg1", "shu1",
                     "shd1"], ["xout"])


def _dout_or_tmp(P, name):
    if name in P.export_names:
        return P.dout(name)
    return P.dtmp(name)


Prog.dout_or_tmp = _dout_or_tmp



def s0(P):
    emit_qkv(P, "a", "x0T",
             ("w_attn", 0), ("w_attn", 1024), ("w_attn", 2048),
             ("b_attn", 0), ("b_attn", 8), "bv_row",
             "q0", "k0", "v0", nq_tiles=8)


STAGES["s0"] = (s0, ["x0T", "w_attn", "bv_row"], ["q0", "k0", "v0"])


def build_launch(stages, export_names):
    P = Prog()
    P.export_names = export_names
    emit_consts(P)
    P.stage_bounds = {}
    for s in stages:
        P.cur_scope = s
        STAGES[s][0](P)
        P.fence()
        P.stage_bounds[s] = {e: len(P.streams[e]) for e in ENGS}
    P.cur_scope = None
    for name, rec in list(P.sems.items()):
        if rec[1] > 0:
            P.wait("sync", name, rec[1])
    P.finalize()
    return P


_BUILD_CACHE = {}


def kernel(**inputs):
    sh, per = host_prep(inputs)
    env = [dict() for _ in range(8)]
    produced_by_launch = []
    all_stage_list = [s for grp in LAUNCH_PLAN for s in grp]
    for li, grp in enumerate(LAUNCH_PLAN):
        later = set(s for g in LAUNCH_PLAN[li + 1:] for s in g)
        produces = set(p for s in grp for p in STAGES[s][2])
        consumed_later = set()
        for s in later:
            consumed_later.update(STAGES[s][1])
        exports = (produces & consumed_later) | (produces & {"xout"})
        if li == len(LAUNCH_PLAN) - 1:
            exports |= produces
        key = (li, tuple(grp), tuple(sorted(exports)))
        if key not in _BUILD_CACHE:
            _BUILD_CACHE[key] = build_launch(grp, exports)
        P = _BUILD_CACHE[key]
        in_maps = []
        for c in range(8):
            m = {}
            for name in P.inputs:
                if name in sh:
                    m[name] = sh[name]
                elif name in per[c]:
                    m[name] = per[c][name]
                else:
                    m[name] = env[c][name]
            in_maps.append(m)
        res = run_bass_kernel_spmd(P.nc, in_maps, core_ids=list(range(8)))
        for c in range(8):
            env[c].update(res.results[c])
    out = np.zeros((4, 4096, 1024), np.float32)
    for c in range(8):
        b, h = c // 2, c % 2
        if "xout" in env[c]:
            out[b, h * 2048:(h + 1) * 2048, :] = env[c]["xout"].T
    kernel.env = env
    return out
